# version 10
# baseline (speedup 1.0000x reference)
import numpy as np

import concourse.bass as bass
import concourse.mybir as mybir
from concourse.bass_utils import run_bass_kernel_spmd

F32 = mybir.dt.float32
BF16 = mybir.dt.bfloat16
AF = mybir.ActivationFunctionType
ALU = mybir.AluOpType

NCORES = 8
B = 16
T = 2048
H = 1024
E = 2048
S = 128
RPC = T // NCORES
L = 512
INV_SQRT_S = 1.0 / float(np.sqrt(S))
EPS = 1e-5
KC = H // 128
ET = E // 128
RPCB = RPC + 16
_PERM = np.r_[64:128, 0:64]


def _rope_tables():
    try:
        import jax
        cpu = jax.devices("cpu")[0]
        import jax.numpy as jnp
        with jax.default_device(cpu):
            pos = jnp.arange(T, dtype=jnp.float32)
            inv_freq = 10000.0 ** (jnp.arange(64, dtype=jnp.float32) / 64)
            sinusoid = pos[:, None] * inv_freq[None, :]
            return (np.asarray(jnp.sin(sinusoid)), np.asarray(jnp.cos(sinusoid)))
    except Exception:
        pos = np.arange(T, dtype=np.float32)
        inv_freq = (10000.0 ** (np.arange(64, dtype=np.float32) /
                                np.float32(64))).astype(np.float32)
        sinusoid = pos[:, None] * inv_freq[None, :]
        return np.sin(sinusoid).astype(np.float32), np.cos(sinusoid).astype(np.float32)


def _bias_matrix(w):
    n = w.shape[0]
    rel = np.arange(T)[None, :] - np.arange(T)[:, None] + (L - 1)
    idx = np.where(rel < 0, rel + n, rel)
    idx = np.clip(idx, 0, n - 1)
    return np.ascontiguousarray(w[idx].T)


class Prog:
    ENGS = ("sync", "gpsimd", "tensor", "vector", "scalar")

    def __init__(self, nc, es):
        self.nc = nc
        self.es = es
        self.ops = {e: [] for e in self.ENGS}
        self.sem = {}
        self.cnt = {}
        self.dma_sems = {}
        for name in ("cc", "pe", "dv", "ac",
                     "b_sync", "b_gpsimd", "b_tensor", "b_vector", "b_scalar"):
            self.sem[name] = es.enter_context(nc.semaphore(name))
            self.cnt[name] = 0

    def dsem(self, name, issuer):
        if name not in self.sem:
            self.sem[name] = self.es.enter_context(self.nc.semaphore(name))
            self.cnt[name] = 0
            self.dma_sems[name] = issuer
        return name

    def wait(self, eng, sem, val=None):
        v = self.cnt[sem] if val is None else val
        h = self.sem[sem]
        self.ops[eng].append(lambda e, h=h, v=v: e.wait_ge(h, v))

    def emit(self, eng, fn, inc=None, amt=1):
        if eng == "vector":
            self.wait("vector", "dv")
            h = self.sem["dv"]
            self.cnt["dv"] += 1
            self.ops[eng].append(lambda e, fn=fn, h=h: fn(e).then_inc(h, 1))
            return self.cnt["dv"]
        if inc is None:
            self.ops[eng].append(fn)
            return None
        h = self.sem[inc]
        self.cnt[inc] += amt
        self.ops[eng].append(lambda e, fn=fn, h=h, a=amt: fn(e).then_inc(h, a))
        return self.cnt[inc]

    def barrier(self):
        for name, issuer in self.dma_sems.items():
            if self.cnt[name]:
                self.wait(issuer, name)
        self.wait("gpsimd", "cc")
        self.wait("vector", "dv")
        self.wait("scalar", "ac")
        self.wait("tensor", "pe")
        for e in self.ENGS:
            h = self.sem["b_" + e]
            self.cnt["b_" + e] += 1
            self.ops[e].append(lambda eng, h=h: eng.nop().then_inc(h, 1))
        for e in self.ENGS:
            for o in self.ENGS:
                if o != e:
                    self.wait(e, "b_" + o)


def build_program():
    nc = bass.Bass("TRN2", target_bir_lowering=False)
    dt = nc.dram_tensor

    x_own = dt("x_own", [B, H, RPC], F32, kind="ExternalInput").ap()
    biasT_own = dt("biasT_own", [T, RPC], F32, kind="ExternalInput").ap()
    sinT_in = dt("sinT", [128, RPC], F32, kind="ExternalInput").ap()
    cosT_in = dt("cosT", [128, RPC], F32, kind="ExternalInput").ap()
    ident_in = dt("ident_in", [128, 128], F32, kind="ExternalInput").ap()

    wuT = dt("wuT", [H, E], F32, kind="ExternalInput").ap()
    wvT = dt("wvT", [H, E], F32, kind="ExternalInput").ap()
    wbT = dt("wbT", [H, S], F32, kind="ExternalInput").ap()
    wbTs = dt("wbTs", [H, S], F32, kind="ExternalInput").ap()
    owT = dt("owT", [E, H], F32, kind="ExternalInput").ap()
    woT = dt("woT", [H, H], F32, kind="ExternalInput").ap()
    gateT = {g: dt(g + "T", [H, H], F32, kind="ExternalInput").ap()
             for g in ("wf", "uf", "wi", "ui", "wc", "uc", "uo")}

    vec_in = {}
    for name, n in (("ln_w", H), ("ln_b", H), ("o_b", H), ("bf", H), ("bi", H),
                    ("bc", H), ("bo", H), ("in_c", H), ("uvb_u", E), ("uvb_b", S), ("uvb_bs", S)):
        vec_in[name] = dt(name, [n], F32, kind="ExternalInput").ap()
    uvb_v_in = dt("uvb_v", [E], F32, kind="ExternalInput").ap()
    gamma_in = dt("gamma", [2, 128], F32, kind="ExternalInput").ap()
    gammas_in = dt("gammas", [2, 128], F32, kind="ExternalInput").ap()
    beta_in = dt("beta", [2, 128], F32, kind="ExternalInput").ap()
    betas_in = dt("betas", [2, 128], F32, kind="ExternalInput").ap()

    y_own = dt("y_own", [B, H, RPC], F32, kind="ExternalOutput").ap()

    wu_h = dt("wu_h", [H, E], BF16)
    wv_h = dt("wv_h", [H, E], BF16)
    wb_h = dt("wb_h", [H, S], BF16)
    wbs_h = dt("wbs_h", [H, S], BF16)
    ow_h = dt("ow_h", [E, H], BF16)
    uo_h = dt("uo_h", [H, H], BF16)
    wf_h = dt("wf_h", [H, H], BF16)
    wi_h = dt("wi_h", [H, H], BF16)
    wc_h = dt("wc_h", [H, H], BF16)
    ag_in_big = dt("ag_in_big", [B * RPCB, E], BF16)
    ag_out_big = dt("ag_out_big", [NCORES * B * RPCB, E], BF16)

    from contextlib import ExitStack
    es = ExitStack()
    with es:
        e = es.enter_context

        xnT = e(nc.sbuf_tensor("xnT", [128, KC * B * RPC], BF16))
        bb = [e(nc.sbuf_tensor(f"bb{i}", [128, 8192], BF16)) for i in range(6)]
        aux_h = e(nc.sbuf_tensor("aux_h", [128, 3072], BF16))
        c_f32 = e(nc.sbuf_tensor("c_f32", [128, KC * RPC], F32))
        c_bfA = e(nc.sbuf_tensor("c_bfA", [128, KC * RPC], BF16))
        c_bfB = e(nc.sbuf_tensor("c_bfB", [128, KC * RPC], BF16))
        scr = e(nc.sbuf_tensor("scr", [128, 512], F32))
        scr2 = e(nc.sbuf_tensor("scr2", [128, 512], F32))
        gates = [e(nc.sbuf_tensor(f"gate{i}", [128, RPC], F32)) for i in range(3)]
        upd = e(nc.sbuf_tensor("upd", [128, RPC], F32))
        sq_h = e(nc.sbuf_tensor("sq_h", [128, RPC], BF16))
        tmp_h = e(nc.sbuf_tensor("tmp_h", [128, RPC], BF16))
        muB = e(nc.sbuf_tensor("muB", [128, RPC], BF16))
        rsB = e(nc.sbuf_tensor("rsB", [128, RPC], BF16))
        hT_b = e(nc.sbuf_tensor("hT_b", [128, RPC], BF16))
        xres = [e(nc.sbuf_tensor(f"xres{i}", [128, RPC], F32)) for i in range(2)]
        outs = [e(nc.sbuf_tensor(f"outs{i}", [128, RPC], F32)) for i in range(2)]
        stat = e(nc.sbuf_tensor("stat", [1, 2 * RPC], F32))
        uvbv_row = e(nc.sbuf_tensor("uvbv_row", [1, E], BF16))
        ones_col = e(nc.sbuf_tensor("ones_col", [128, 1], BF16))
        ones_row = e(nc.sbuf_tensor("ones_row", [1, 128], BF16))
        ones_row_f = e(nc.sbuf_tensor("ones_row_f", [1, 128], F32))
        ident_h = e(nc.sbuf_tensor("ident_h", [128, 128], BF16))
        sin_sb = e(nc.sbuf_tensor("sin_sb", [128, RPC], F32))
        cos_sb = e(nc.sbuf_tensor("cos_sb", [128, RPC], F32))
        sm = e(nc.sbuf_tensor("sm", [128, 96], F32))

        p_m = [e(nc.psum_tensor(f"p_m{i}", [128, RPC], F32)) for i in range(3)]
        p_v = [e(nc.psum_tensor(f"p_v{i}", [128, 512], F32)) for i in range(2)]
        p_base = e(nc.psum_tensor("p_base", [128, RPC], F32))
        p_h = e(nc.psum_tensor("p_h", [128, RPC], F32))
        p_stat = e(nc.psum_tensor("p_stat", [128, 512], F32))
        p_s = [p_stat[0:1, 0:RPC], p_stat[0:1, RPC:2 * RPC]]
        p_b = p_stat

        P = Prog(nc, es)
        psum_owner = {}

        def group_begin(key):
            if key in psum_owner:
                s, v = psum_owner[key]
                P.wait("tensor", s, v)

        smcol = {}
        _c = [0]
        def alloc_sm(name, ncols):
            smcol[name] = _c[0]
            _c[0] += ncols
        for name in ("ln_w", "ln_b", "o_b", "bf", "bi", "bc", "bo", "in_c"):
            alloc_sm(name, KC)
        alloc_sm("uvb_u", ET)
        alloc_sm("uvb_b", 1)
        alloc_sm("uvb_bs", 1)
        alloc_sm("gamma", 2)
        alloc_sm("beta", 2)
        alloc_sm("gammas", 2)
        alloc_sm("betas", 2)
        alloc_sm("g0p", 1)
        alloc_sm("b0p", 1)
        alloc_sm("g0ps", 1)
        alloc_sm("b0ps", 1)
        assert _c[0] <= 96

        def smc(name, i=0):
            return sm[:, smcol[name] + i: smcol[name] + i + 1]

        WG = {"uf": bb[1], "ui": bb[3], "uc": bb[5]}
        xg_buf = [[bb[4][:, (2 * g + i) * 1024:(2 * g + i + 1) * 1024]
                   for i in range(2)] for g in range(3)]
        wb_vk = bb[0][:, 0:1024]
        wv_vk = [bb[0][:, 1024 + i * 512: 1024 + (i + 1) * 512] for i in range(2)]
        kernelT = bb[0]
        kT_sb = bb[0][:, 4096:6144]
        qT_bufs = [bb[0][:, 6144:6400], bb[0][:, 6400:6656]]
        uT_bufs = [bb[1][:, 0:4096], bb[1][:, 4096:8192]]
        eT = bb[3][:, 4096:8192]
        biasT_sb = bb[2][:, 0:4096]
        uocT = bb[2][:, 4096:6144]
        xnnT = bb[2][:, 6144:8192]
        ow_strip = [bb[3][:, i * 2048:(i + 1) * 2048] for i in range(2)]
        woT_sb = bb[4]
        wu_strip = [bb[5][:, i * 1024:(i + 1) * 1024] for i in range(2)]
        wv_buf = [bb[5][:, 2048 + i * 512: 2048 + (i + 1) * 512] for i in range(2)]
        wb_buf = bb[5][:, 3072:4096]
        vstrip = [bb[5][:, 4096 + i * 2048: 4096 + (i + 1) * 2048] for i in range(2)]
        vout = [aux_h[:, i * 512:(i + 1) * 512] for i in range(2)]
        uo_strip = [aux_h[:, 1024 + i * 1024: 1024 + (i + 1) * 1024] for i in range(2)]
        x_bf = c_bfA
        uvbv_hlp = c_bfA
        cast_h = c_bfB

        import dataclasses as _dc

        def mkap(base_ap, offset, dims):
            return _dc.replace(base_ap, offset=offset,
                               ap=type(base_ap.ap)(dims))

        def dma(eng, sem, out, in_, slow=False):
            P.dsem(sem, eng)
            if slow:
                return P.emit(eng, lambda e, o=out, i=in_: e.dma_start(
                    out=o, in_=i, allow_slow_non_contiguous=True), inc=sem, amt=16)
            return P.emit(eng, lambda e, o=out, i=in_: e.dma_start(out=o, in_=i),
                          inc=sem, amt=16)

        for name in ("ln_w", "ln_b", "o_b", "bf", "bi", "bc", "bo", "in_c"):
            dma("sync", "LS", sm[:, smcol[name]:smcol[name] + KC],
                vec_in[name].rearrange("(t p) -> p t", p=128), slow=True)
        dma("sync", "LS", sm[:, smcol["uvb_u"]:smcol["uvb_u"] + ET],
            vec_in["uvb_u"].rearrange("(t p) -> p t", p=128), slow=True)
        dma("sync", "LS", sm[:, smcol["uvb_b"]:smcol["uvb_b"] + 1],
            vec_in["uvb_b"].rearrange("(t p) -> p t", p=128), slow=True)
        dma("sync", "LS", sm[:, smcol["uvb_bs"]:smcol["uvb_bs"] + 1],
            vec_in["uvb_bs"].rearrange("(t p) -> p t", p=128), slow=True)
        dma("sync", "LS", sm[:, smcol["gamma"]:smcol["gamma"] + 2],
            gamma_in.rearrange("h s -> s h"), slow=True)
        dma("sync", "LS", sm[:, smcol["beta"]:smcol["beta"] + 2],
            beta_in.rearrange("h s -> s h"), slow=True)
        dma("sync", "LS", sm[:, smcol["gammas"]:smcol["gammas"] + 2],
            gammas_in.rearrange("h s -> s h"), slow=True)
        dma("sync", "LS", sm[:, smcol["betas"]:smcol["betas"] + 2],
            betas_in.rearrange("h s -> s h"), slow=True)
        dma("sync", "LS", scr[:, :128], ident_in)
        dma("sync", "LS", sin_sb[:, :], sinT_in)
        dma("sync", "LS", cos_sb[:, :], cosT_in)

        P.wait("vector", "LS")
        P.emit("vector", lambda e: e.memset(ones_col[:, :], 1.0))
        P.emit("vector", lambda e: e.memset(ones_row[:, :], 1.0))
        P.emit("vector", lambda e: e.memset(ones_row_f[:, :], 1.0))
        P.emit("vector", lambda e: e.tensor_copy(ident_h[:, :], scr[:, :128]))
        for q in range(4):
            dma("sync", "L1", scr2[0:1, :512],
                uvb_v_in[q * 512:(q + 1) * 512].rearrange("(a m) -> a m", a=1))
            P.wait("vector", "L1")
            dvq = P.emit("vector", lambda e, q_=q: e.tensor_copy(
                uvbv_hlp[0:1, q_ * 512:(q_ + 1) * 512], scr2[0:1, :512]), inc="dv")
            P.wait("sync", "dv", dvq)
        P.wait("gpsimd", "dv")
        dma("gpsimd", "SU", uvbv_row[0:1, :], uvbv_hlp[0:1, 0:E])
        P.emit("vector", lambda e: e.tensor_scalar_mul(smc("g0p"), smc("gamma", 0), INV_SQRT_S))
        P.emit("vector", lambda e: e.tensor_scalar_mul(smc("b0p"), smc("beta", 0), INV_SQRT_S))
        P.emit("vector", lambda e: e.tensor_scalar_mul(smc("g0ps"), smc("gammas", 0), INV_SQRT_S))
        P.emit("vector", lambda e: e.tensor_scalar_mul(smc("b0ps"), smc("betas", 0), INV_SQRT_S),
               inc="dv")

        scr_guard = [0, 0]

        def cast_chunks(chunks):
            pass

        def cast_resident(src_ap, dst, n_out):
            i = 0
            for kc in range(KC):
                for off in range(0, n_out, 512):
                    wdt = min(512, n_out - off)
                    par = i % 2
                    buf = (scr, scr2)[par]
                    if scr_guard[par]:
                        P.wait("sync", "dv", scr_guard[par])
                    dma("sync", "L%d" % par, buf[:, :wdt],
                        src_ap[kc * 128:(kc + 1) * 128, off:off + wdt])
                    P.wait("vector", "L%d" % par)
                    dvv = P.emit("vector", lambda e, b=buf[:, :wdt],
                                 d=dst[:, kc * n_out + off: kc * n_out + off + wdt]:
                                 e.tensor_copy(d, b), inc="dv")
                    scr_guard[par] = dvv
                    i += 1

        cast_dram_guard = [0, 0]

        def cast_to_dram(src_ap, dst_dram, rows, cols):
            i = 0
            for r0 in range(0, rows, 128):
                for off in range(0, cols, 512):
                    wdt = min(512, cols - off)
                    par = i % 2
                    buf = (scr, scr2)[par]
                    if scr_guard[par]:
                        P.wait("sync", "dv", scr_guard[par])
                    dma("sync", "L%d" % par, buf[:, :wdt],
                        src_ap[r0:r0 + 128, off:off + wdt])
                    hbuf = cast_h[:, par * 1024: par * 1024 + wdt]
                    P.wait("vector", "L%d" % par)
                    if cast_dram_guard[par]:
                        P.wait("vector", "SC%d" % par)
                    dvv = P.emit("vector", lambda e, b=buf[:, :wdt], h=hbuf:
                                 e.tensor_copy(h, b), inc="dv")
                    scr_guard[par] = dvv
                    P.wait("gpsimd", "dv", dvv)
                    dma("gpsimd", "SC%d" % par, dst_dram.ap()[r0:r0 + 128, off:off + wdt], hbuf)
                    cast_dram_guard[par] = 1
                    i += 1

        for g in ("uf", "ui", "uc"):
            cast_resident(gateT[g], WG[g], H)
        cast_to_dram(gateT["wf"], wf_h, H, H)
        cast_to_dram(gateT["wi"], wi_h, H, H)
        cast_to_dram(gateT["wc"], wc_h, H, H)
        cast_to_dram(gateT["uo"], uo_h, H, H)
        cast_to_dram(wuT, wu_h, H, E)
        cast_to_dram(wvT, wv_h, H, E)
        cast_to_dram(wbT, wb_h, H, S)
        cast_to_dram(wbTs, wbs_h, H, S)
        cast_to_dram(owT, ow_h, E, H)

        P.barrier()

        st_mu = stat[0:1, 0:RPC]
        st_b = stat[0:1, RPC:2 * RPC]

        stats_guard = {"sq": 0, "bank": 0}

        def ln_stats(src_slices, src_dv=0):
            P.wait("tensor", "dv", max(src_dv, stats_guard["bank"]))
            P.wait("scalar", "dv", max(stats_guard["sq"], src_dv))
            if P.cnt.get("SK"):
                P.wait("scalar", "SK")
                P.wait("vector", "SK")
            for kc in range(KC):
                P.emit("tensor", lambda e, s=src_slices[kc], k=kc:
                       e.matmul(p_s[0], ones_col[:, :], s,
                                start=(k == 0), stop=(k == KC - 1)),
                       inc="pe")
            for kc in range(KC):
                sq = (sq_h, tmp_h)[kc % 2]
                if kc >= 2:
                    P.wait("scalar", "pe", pe_sq[kc - 2])
                av = P.emit("scalar", lambda e, s=src_slices[kc], q=sq:
                            e.activation(out=q[:, :], in_=s, func=AF.Square), inc="ac")
                P.wait("tensor", "ac", av)
                pv = P.emit("tensor", lambda e, k=kc, q=sq:
                            e.matmul(p_s[1], ones_col[:, :], q[:, :],
                                     start=(k == 0), stop=(k == KC - 1)),
                            inc="pe")
                if kc == 0:
                    pe_sq = {}
                pe_sq[kc] = pv
            P.wait("vector", "pe")
            P.emit("vector", lambda e: e.tensor_scalar_mul(st_mu, p_s[0], 1.0 / H))
            P.emit("vector", lambda e: e.tensor_scalar_mul(st_b, p_s[1], 1.0 / H))
            P.emit("vector", lambda e: e.tensor_mul(p_s[0], st_mu, st_mu))
            P.emit("vector", lambda e: e.tensor_sub(st_b, st_b, p_s[0]))
            P.emit("vector", lambda e: e.tensor_scalar_add(st_b, st_b, EPS), inc="dv")
            P.wait("scalar", "dv")
            P.emit("scalar", lambda e: e.activation(out=st_b, in_=st_b, func=AF.Sqrt),
                   inc="ac")
            P.wait("vector", "ac")
            recip_dv = P.emit("vector", lambda e: e.reciprocal(st_b, st_b), inc="dv")
            P.wait("tensor", "dv", recip_dv)
            P.emit("tensor", lambda e: e.matmul(p_b[:, 0:RPC], ones_row_f[:, :], st_mu,
                                                start=True, stop=True))
            P.emit("tensor", lambda e: e.matmul(p_b[:, RPC:2 * RPC], ones_row_f[:, :], st_b,
                                                start=True, stop=True), inc="pe")
            P.wait("vector", "pe")
            P.emit("vector", lambda e: e.tensor_copy(muB[:, :], p_b[:, 0:RPC]))
            stats_guard["bank"] = P.emit(
                "vector", lambda e: e.tensor_copy(rsB[:, :], p_b[:, RPC:2 * RPC]),
                inc="dv")

        def xnT_sl(kc, t):
            off = kc * (B * RPC) + t * RPC
            return xnT[:, off: off + RPC]

        for t in range(B):
            P.wait("vector", "pe")
            for kc in range(KC):
                par = kc % 2
                buf = (scr, scr2)[par]
                if scr_guard[par]:
                    P.wait("sync", "dv", scr_guard[par])
                dma("sync", "L%d" % par, buf[:, :RPC],
                    x_own[t, kc * 128:(kc + 1) * 128, :])
                P.wait("vector", "L%d" % par)
                dvv = P.emit("vector", lambda e, b=buf[:, :RPC],
                             d=x_bf[:, kc * RPC:(kc + 1) * RPC]:
                             e.tensor_copy(d, b), inc="dv")
                scr_guard[par] = dvv
            xs = [x_bf[:, kc * RPC:(kc + 1) * RPC] for kc in range(KC)]
            ln_stats(xs, src_dv=dvv)
            for kc in range(KC):
                P.emit("vector", lambda e, s=xs[kc]: e.tensor_sub(tmp_h[:, :], s, muB[:, :]))
                stats_guard["sq"] = P.emit(
                    "vector", lambda e, d=xnT_sl(kc, t):
                    e.tensor_mul(d, tmp_h[:, :], rsB[:, :]), inc="dv")

        P.barrier()

        def xnn_make(b):
            xs = [xnT_sl(kc, b) for kc in range(KC)]
            ln_stats(xs)
            P.wait("vector", "pe", produce_pe_end[0])
            for kc in range(KC):
                P.emit("vector", lambda e, s=xs[kc]: e.tensor_sub(tmp_h[:, :], s, muB[:, :]))
                P.emit("vector", lambda e: e.tensor_mul(sq_h[:, :], tmp_h[:, :], rsB[:, :]))
                stats_guard["sq"] = P.emit("vector", lambda e, k=kc: e.tensor_scalar(
                    out=xnnT[:, k * RPC:(k + 1) * RPC], in0=sq_h[:, :],
                    scalar1=smc("ln_w", k), scalar2=smc("ln_b", k),
                    op0=ALU.mult, op1=ALU.add), inc="dv")

        base_sw = c_f32[:, 0:RPC]
        relu_t = c_f32[:, RPC:2 * RPC]
        rope_dv_last = [0]
        produce_pe_end = [0]
        cc_of = {}
        qk_pe_end = {}
        eT_dv_end = {}

        def rope_make(dst, gcol, bcol, gscol, bscol, a0, a1,
                      bsrc=None, bssrc=None, tq=None):
            bsrc = gates[2][:, :] if bsrc is None else bsrc
            bssrc = base_sw if bssrc is None else bssrc
            tq = upd[:, :] if tq is None else tq
            P.emit("vector", lambda e, g=gcol, b_=bcol: e.tensor_scalar(
                out=tq, in0=bsrc, scalar1=g, scalar2=b_,
                op0=ALU.mult, op1=ALU.add))
            P.emit("vector", lambda e, g=gscol, b_=bscol: e.tensor_scalar(
                out=a0, in0=bssrc, scalar1=g, scalar2=b_,
                op0=ALU.mult, op1=ALU.add))
            P.emit("vector", lambda e: e.tensor_mul(a1, a0, sin_sb[:, :]))
            P.emit("vector", lambda e: e.tensor_mul(tq, tq, cos_sb[:, :]))
            return P.emit("vector", lambda e, d=dst:
                          e.tensor_add(d, tq, a1), inc="dv")

        vout_guard = [0, 0]
        out_guard = [0, 0]
        phaseA_dv = [0]

        def produce_vk(b):
            xnn_make(b)
            dv_xnn = P.cnt["dv"]
            P.wait("tensor", "dv", dv_xnn)
            P.wait("scalar", "dv", rope_dv_last[0])
            for swp, dst_sb, bias_name in (
                    (0, xres[0][:, :], "uvb_b"),
                    (1, xres[1][:, :], "uvb_bs")):
                src = (wb_h if swp == 0 else wbs_h).ap()
                dma("sync", "LBk",
                    wb_vk[:, :1024].rearrange("p (kc c) -> p kc c", kc=KC),
                    mkap(src, 0, [[S, 128], [128 * S, KC], [1, S]]))
                group_begin("base")
                P.wait("tensor", "LBk")
                for kc in range(KC):
                    P.emit("tensor", lambda e, k=kc:
                           e.matmul(p_base[:, :], wb_vk[:, k * 128:(k + 1) * 128],
                                    xnnT[:, k * RPC:(k + 1) * RPC],
                                    start=(k == 0), stop=(k == KC - 1)),
                           inc="pe" if kc == KC - 1 else None)
                pv = P.cnt["pe"]
                P.wait("sync", "pe", pv)
                P.wait("scalar", "pe", pv)
                av = P.emit("scalar", lambda e, d=dst_sb, bn=bias_name: e.activation(
                    out=d, in_=p_base[:, :], func=AF.Silu,
                    bias=smc(bn)), inc="ac")
                psum_owner["base"] = ("ac", av)
            P.wait("vector", "ac", av)
            kv = rope_make(tmp_h[:, :], smc("gamma", 1), smc("beta", 1),
                           smc("gammas", 1), smc("betas", 1),
                           outs[0][:, :], outs[1][:, :],
                           bsrc=xres[0][:, :], bssrc=xres[1][:, :],
                           tq=xres[0][:, :])
            rope_dv_last[0] = kv
            stats_guard["sq"] = max(stats_guard["sq"], kv)
            P.wait("gpsimd", "dv", kv)
            dma("gpsimd", "SK",
                ag_in_big.ap()[b * RPCB + RPC: b * RPCB + RPC + 16, :]
                .rearrange("a (c i) -> (a c) i", i=RPC),
                tmp_h[:, :])
            for nch in range(4):
                for kc in range(KC):
                    par = (nch * KC + kc) % 2
                    wb2 = wv_vk[par]
                    if nch * KC + kc >= 2:
                        P.wait("sync", "pe", v_pe[nch * KC + kc - 2])
                    dma("sync", "LVk%d" % par, wb2[:, :],
                        wv_h.ap()[kc * 128:(kc + 1) * 128, nch * 512:(nch + 1) * 512])
                    for rt in range(2):
                        key = "pv%d" % rt
                        if kc == 0:
                            group_begin(key)
                            P.emit("tensor", lambda e, p=p_v[rt], n=nch: e.matmul(
                                p[:, :], ones_row[:, :],
                                uvbv_row[0:1, n * 512:(n + 1) * 512],
                                start=True, stop=False))
                        P.wait("tensor", "LVk%d" % par)
                        pv = P.emit("tensor", lambda e, p=p_v[rt], w=wb2, r=rt, k=kc:
                                    e.matmul(p[:, :],
                                             xnnT[:, k * RPC + r * 128: k * RPC + r * 128 + 128],
                                             w[:, :], start=False, stop=(k == KC - 1)),
                                    inc="pe")
                        if nch == 0 and kc == 0 and rt == 0:
                            v_pe = {}
                        v_pe[nch * KC + kc] = pv
                        if kc == KC - 1:
                            vo = vout[rt]
                            P.wait("scalar", "pe", pv)
                            if vout_guard[rt]:
                                P.wait("scalar", "SV%d" % rt)
                            av = P.emit("scalar", lambda e, p=p_v[rt], o=vo: e.activation(
                                out=o, in_=p[:, :], func=AF.Silu), inc="ac")
                            psum_owner[key] = ("ac", av)
                            P.wait("gpsimd", "ac", av)
                            dma("gpsimd", "SV%d" % rt,
                                ag_in_big.ap()[b * RPCB + rt * 128: b * RPCB + (rt + 1) * 128,
                                               nch * 512:(nch + 1) * 512], vo)
                            vout_guard[rt] = 1
            produce_pe_end[0] = P.cnt["pe"]

        def produce(b):
            uT = uT_bufs[b % 2]
            qT_sb = qT_bufs[b % 2]
            xnn_make(b)
            dv_xnn = P.cnt["dv"]
            P.wait("tensor", "dv", dv_xnn)
            P.wait("scalar", "dv", rope_dv_last[0])
            for swp, w_src, dst_sb, bias_name in (
                    (0, wb_h, gates[2][:, :], "uvb_b"),
                    (1, None, base_sw, "uvb_bs")):
                src = (wb_h if swp == 0 else wbs_h).ap()
                dma("sync", "LB",
                    wb_buf[:, :1024].rearrange("p (kc c) -> p kc c", kc=KC),
                    mkap(src, 0, [[S, 128], [128 * S, KC], [1, S]]))
                group_begin("base")
                P.wait("tensor", "LB")
                for kc in range(KC):
                    P.emit("tensor", lambda e, k=kc:
                           e.matmul(p_base[:, :], wb_buf[:, k * 128:(k + 1) * 128],
                                    xnnT[:, k * RPC:(k + 1) * RPC],
                                    start=(k == 0), stop=(k == KC - 1)),
                           inc="pe" if kc == KC - 1 else None)
                pv = P.cnt["pe"]
                P.wait("sync", "pe", pv)
                P.wait("scalar", "pe", pv)
                av = P.emit("scalar", lambda e, d=dst_sb, bn=bias_name: e.activation(
                    out=d, in_=p_base[:, :], func=AF.Silu,
                    bias=smc(bn)), inc="ac")
                psum_owner["base"] = ("ac", av)
            P.wait("vector", "ac", av)
            a0 = gates[0][:, :]
            a1 = gates[1][:, :]
            if b >= 2:
                P.wait("vector", "pe", qk_pe_end[b - 2])
            qv = rope_make(qT_sb, smc("g0p"), smc("b0p"),
                           smc("g0ps"), smc("b0ps"), a0, a1)
            rope_dv_last[0] = qv
            if b >= 2:
                P.wait("scalar", "dv", eT_dv_end[b - 2])
            for et in range(ET):
                sb = wu_strip[et % 2]
                if et >= 2:
                    P.wait("scalar", "pe", u_pe[et - 2])
                dma("scalar", "LU%d" % (et % 2),
                    sb[:, :1024].rearrange("p (kc c) -> p kc c", kc=KC),
                    mkap(wu_h.ap(), et * 128, [[E, 128], [128 * E, KC], [1, 128]]))
                key = "pv%d" % (et % 2)
                pm = p_v[et % 2][:, 0:RPC]
                group_begin(key)
                P.wait("tensor", "LU%d" % (et % 2))
                if et == 0:
                    P.wait("tensor", "dv", dv_xnn)
                for kc in range(KC):
                    pv = P.emit("tensor", lambda e, p=pm, s=sb, k=kc:
                                e.matmul(p, s[:, k * 128:(k + 1) * 128],
                                         xnnT[:, k * RPC:(k + 1) * RPC],
                                         start=(k == 0), stop=(k == KC - 1)),
                                inc="pe")
                if et == 0:
                    u_pe = {}
                u_pe[et] = pv
                P.wait("scalar", "pe", pv)
                av = P.emit("scalar", lambda e, p=pm, t_=et, u_=uT: e.activation(
                    out=u_[:, t_ * RPC:(t_ + 1) * RPC], in_=p, func=AF.Silu,
                    bias=smc("uvb_u", t_)), inc="ac")
                psum_owner[key] = ("ac", av)
            produce_pe_end[0] = P.cnt["pe"]

        def consume(b):
            uT = uT_bufs[b % 2]
            qT_sb = qT_bufs[b % 2]
            P.wait("gpsimd", "cc", 1)
            if b > 0:
                P.wait("gpsimd", "pe", qk_pe_end[b - 1])
            for r in range(NCORES):
                base_row = r * (B * RPCB) + b * RPCB + RPC
                blk = ag_out_big.ap()[base_row: base_row + 16, :]
                dma("gpsimd", "LK", kT_sb[:, r * RPC:(r + 1) * RPC],
                    blk.rearrange("a (c i) -> (a c) i", i=RPC))
            for jt in range(ET):
                key = "m%d" % (jt % 2)
                pm = p_m[jt % 2]
                group_begin(key)
                if jt == 0:
                    P.wait("tensor", "LK")
                P.emit("tensor", lambda e, p=pm, j=jt: e.matmul(
                    p[:, :], ident_h[:, :], biasT_sb[:, j * RPC:(j + 1) * RPC],
                    start=True, stop=False))
                pv = P.emit("tensor", lambda e, p=pm, j=jt: e.matmul(
                    p[:, :], kT_sb[:, j * 128:(j + 1) * 128], qT_sb,
                    start=False, stop=True), inc="pe")
                P.wait("scalar", "pe", pv)
                if jt > 0:
                    P.wait("scalar", "dv", sq_dv)
                av = P.emit("scalar", lambda e, p=pm: e.activation(
                    out=relu_t, in_=p[:, :], func=AF.Relu), inc="ac")
                psum_owner[key] = ("ac", av)
                P.wait("vector", "ac", av)
                sq_dv = P.emit("vector", lambda e, j=jt: e.tensor_mul(
                    kernelT[:, j * RPC:(j + 1) * RPC], relu_t, relu_t), inc="dv")
            qk_pe_end[b] = P.cnt["pe"]
            dv_kernel = P.cnt["dv"]
            for et in range(ET):
                vs = vstrip[et % 2]
                if et >= 2:
                    P.wait("gpsimd", "pe", kv_pe[et - 2])
                for q in range(2):
                    dma("gpsimd", "LW%d" % (et % 2),
                        mkap(vs, vs.offset + q * 128,
                             [[8192, 128], [256, NCORES], [1, 128]]),
                        mkap(ag_out_big.ap(),
                             (b * RPCB + q * 128) * E + et * 128,
                             [[E, 128], [B * RPCB * E, NCORES], [1, 128]]))
                key = "m%d" % (et % 2)
                pm = p_m[et % 2]
                group_begin(key)
                P.wait("tensor", "LW%d" % (et % 2))
                if et == 0:
                    P.wait("tensor", "dv", dv_kernel)
                for jt in range(ET):
                    pv = P.emit("tensor", lambda e, p=pm, v_=vs, j=jt: e.matmul(
                        p[:, :], v_[:, j * 128:(j + 1) * 128],
                        kernelT[:, j * RPC:(j + 1) * RPC],
                        start=(j == 0), stop=(j == ET - 1)),
                        inc="pe" if jt == ET - 1 else None)
                if et == 0:
                    kv_pe = {}
                kv_pe[et] = pv
                P.wait("vector", "pe", pv)
                dvv = P.emit("vector", lambda e, p=pm, t_=et: e.tensor_mul(
                    eT[:, t_ * RPC:(t_ + 1) * RPC], p[:, :],
                    uT[:, t_ * RPC:(t_ + 1) * RPC]), inc="dv")
                psum_owner[key] = ("dv", dvv)
            dv_eT = P.cnt["dv"]
            eT_dv_end[b] = dv_eT
            for ft in range(KC):
                osb = ow_strip[ft % 2]
                if ft >= 2:
                    P.wait("scalar", "pe", o_pe[ft - 2])
                dma("scalar", "LO%d" % (ft % 2),
                    osb[:, :2048].rearrange("p (ec c) -> p ec c", ec=ET),
                    mkap(ow_h.ap(), ft * 128, [[H, 128], [128 * H, ET], [1, 128]]))
                group_begin("h")
                P.emit("tensor", lambda e, f=ft: e.matmul(
                    p_h[:, :], ident_h[:, :], uocT[:, f * RPC:(f + 1) * RPC],
                    start=True, stop=False))
                for kc in range(KC):
                    P.emit("tensor", lambda e, f=ft, k=kc, b_=b: e.matmul(
                        p_h[:, :], woT_sb[:, k * H + f * 128: k * H + f * 128 + 128],
                        xnT_sl(k, b_), start=False, stop=(k == KC - 1)),
                        inc="pe" if kc == KC - 1 else None)
                pv_h = P.cnt["pe"]
                P.wait("scalar", "pe", pv_h)
                if ft > 0:
                    P.wait("scalar", "dv", comb_dv)
                av = P.emit("scalar", lambda e, f=ft: e.activation(
                    out=hT_b[:, :], in_=p_h[:, :], func=AF.Tanh, bias=smc("bo", f)),
                    inc="ac")
                psum_owner["h"] = ("ac", av)
                key = "m%d" % (ft % 3)
                pm = p_m[ft % 3]
                group_begin(key)
                P.emit("tensor", lambda e, p=pm, f=ft, b_=b: e.matmul(
                    p[:, :], ident_h[:, :], xnT_sl(f, b_), start=True, stop=False))
                P.wait("tensor", "LO%d" % (ft % 2))
                if ft == 0:
                    P.wait("tensor", "dv", dv_eT)
                for ec in range(ET):
                    P.emit("tensor", lambda e, p=pm, s=osb, c_=ec: e.matmul(
                        p[:, :], s[:, c_ * 128:(c_ + 1) * 128],
                        eT[:, c_ * RPC:(c_ + 1) * RPC],
                        start=False, stop=(c_ == ET - 1)),
                        inc="pe" if ec == ET - 1 else None)
                pv_o = P.cnt["pe"]
                if ft == 0:
                    o_pe = {}
                o_pe[ft] = pv_o
                xr = xres[ft % 2]
                if b == 0 and ft < 2:
                    P.wait("sync", "dv", phaseA_dv[0])
                dma("sync", "LX%d" % (ft % 2), xr[:, :],
                    x_own[b, ft * 128:(ft + 1) * 128, :])
                ob = outs[ft % 2]
                P.wait("vector", "pe", pv_o)
                P.wait("vector", "ac", av)
                if out_guard[ft % 2]:
                    P.wait("vector", "SO%d" % (ft % 2))
                P.emit("vector", lambda e, p=pm, f=ft, o=ob: e.scalar_tensor_tensor(
                    out=o[:, :], in0=p[:, :], scalar=smc("o_b", f), in1=hT_b[:, :],
                    op0=ALU.add, op1=ALU.add), inc="dv")
                psum_owner[key] = ("dv", P.cnt["dv"])
                P.wait("vector", "LX%d" % (ft % 2))
                comb_dv = P.emit("vector", lambda e, o=ob, x=xr: e.tensor_add(
                    o[:, :], o[:, :], x[:, :]), inc="dv")
                P.wait("gpsimd", "dv", comb_dv)
                dma("gpsimd", "SO%d" % (ft % 2), y_own[b, ft * 128:(ft + 1) * 128, :],
                    ob[:, :])
                out_guard[ft % 2] = 1
                P.wait("sync", "dv", comb_dv)

        for mt in range(KC):
            P.emit("vector", lambda e, m=mt: e.tensor_scalar(
                out=c_f32[:, m * RPC:(m + 1) * RPC], in0=muB[:, :],
                scalar1=0.0, scalar2=smc("in_c", m), op0=ALU.mult, op1=ALU.add))
        for mt in range(KC):
            P.emit("vector", lambda e, m=mt: e.tensor_copy(
                c_bfA[:, m * RPC:(m + 1) * RPC], c_f32[:, m * RPC:(m + 1) * RPC]),
                inc="dv")

        gatespec = [("wf", wf_h, "uf", "bf", AF.Sigmoid, 0),
                    ("wi", wi_h, "ui", "bi", AF.Sigmoid, 1),
                    ("wc", wc_h, "uc", "bc", AF.Tanh, 2)]
        xg_pe = [{} for _ in range(3)]

        def xg_dma(gi, gidx):
            t_, mt_ = divmod(gidx, KC)
            if t_ >= B:
                return
            wh = (wf_h, wi_h, wc_h)[gi]
            par = gidx % 2
            if gidx >= 2:
                P.wait("sync", "pe", xg_pe[gi][gidx - 2])
            dma("sync", "LG%d_%d" % (gi, par),
                xg_buf[gi][par][:, :1024].rearrange("p (kc c) -> p kc c", kc=KC),
                mkap(wh.ap(), mt_ * 128, [[H, 128], [128 * H, KC], [1, 128]]))

        dv_after_update = P.cnt["dv"]
        for gi in range(3):
            xg_dma(gi, 0)
            xg_dma(gi, 1)
        for t in range(B):
            cur = c_bfA if t % 2 == 0 else c_bfB
            nxt = c_bfB if t % 2 == 0 else c_bfA
            first_group = True
            for mt in range(KC):
                gidx = t * KC + mt
                for (wg, wh, ug, bg, func, gi) in gatespec:
                    key = "m%d" % ((mt * 3 + gi) % 3)
                    pm = p_m[(mt * 3 + gi) % 3]
                    sbx = xg_buf[gi][gidx % 2]
                    group_begin(key)
                    P.wait("tensor", "LG%d_%d" % (gi, gidx % 2))
                    for kc in range(KC):
                        P.emit("tensor", lambda e, p=pm, w=sbx, k=kc, t_=t:
                               e.matmul(p[:, :],
                                        w[:, k * 128:(k + 1) * 128],
                                        xnT_sl(k, t_), start=(k == 0), stop=False))
                    if first_group:
                        P.wait("tensor", "dv", dv_after_update)
                        first_group = False
                    for kc in range(KC):
                        P.emit("tensor", lambda e, p=pm, u=WG[ug], m=mt, k=kc, c=cur:
                               e.matmul(p[:, :],
                                        u[:, k * H + m * 128: k * H + m * 128 + 128],
                                        c[:, k * RPC:(k + 1) * RPC],
                                        start=False, stop=(k == KC - 1)),
                               inc="pe" if kc == KC - 1 else None)
                    pv = P.cnt["pe"]
                    xg_pe[gi][gidx] = pv
                    xg_dma(gi, gidx + 2)
                    P.wait("scalar", "pe", pv)
                    P.wait("scalar", "dv", dv_after_update if mt == 0 else dv_mt)
                    av = P.emit("scalar", lambda e, p=pm, f=func, b=bg, m=mt, g=gi:
                                e.activation(out=gates[g][:, :], in_=p[:, :], func=f,
                                             bias=smc(b, m)), inc="ac")
                    psum_owner[key] = ("ac", av)
                P.wait("vector", "ac")
                P.emit("vector", lambda e, m=mt: e.tensor_mul(
                    upd[:, :], gates[0][:, :], c_f32[:, m * RPC:(m + 1) * RPC]))
                P.emit("vector", lambda e: e.tensor_mul(
                    gates[2][:, :], gates[1][:, :], gates[2][:, :]))
                P.emit("vector", lambda e, m=mt: e.tensor_add(
                    c_f32[:, m * RPC:(m + 1) * RPC], upd[:, :], gates[2][:, :]))
                dv_mt = P.emit("vector", lambda e, m=mt, n=nxt: e.tensor_copy(
                    n[:, m * RPC:(m + 1) * RPC], c_f32[:, m * RPC:(m + 1) * RPC]),
                    inc="dv")
            dv_after_update = P.cnt["dv"]
            produce_vk(t)

        c_fin = c_bfA if B % 2 == 0 else c_bfB

        P.barrier()

        P.wait("gpsimd", "SV0")
        P.wait("gpsimd", "SV1")
        P.wait("gpsimd", "SK")
        P.emit("gpsimd", lambda e: e.collective_compute(
            "AllGather", ALU.bypass, replica_groups=[list(range(NCORES))],
            ins=[ag_in_big.ap().opt()], outs=[ag_out_big.ap().opt()]), inc="cc")
        phaseA_dv[0] = P.cnt["dv"]

        uoc_pe = {}
        for ot in range(KC):
            ustr = uo_strip[ot % 2]
            if ot >= 2:
                P.wait("sync", "pe", uoc_pe[ot - 2])
            dma("sync", "LR%d" % (ot % 2),
                ustr[:, :1024].rearrange("p (mc c) -> p mc c", mc=KC),
                mkap(uo_h.ap(), ot * 128, [[H, 128], [128 * H, KC], [1, 128]]))
            key = "base"
            pm = p_base
            group_begin(key)
            P.wait("tensor", "LR%d" % (ot % 2))
            for mc in range(KC):
                P.emit("tensor", lambda e, p=pm, s=ustr, m=mc:
                       e.matmul(p[:, :], s[:, m * 128:(m + 1) * 128],
                                c_fin[:, m * RPC:(m + 1) * RPC],
                                start=(m == 0), stop=(m == KC - 1)),
                       inc="pe" if mc == KC - 1 else None)
            pv = P.cnt["pe"]
            uoc_pe[ot] = pv
            P.wait("vector", "pe", pv)
            dvv = P.emit("vector", lambda e, o=ot, p=pm: e.tensor_copy(
                uocT[:, o * RPC:(o + 1) * RPC], p[:, :]), inc="dv")
            psum_owner[key] = ("dv", dvv)

        P.barrier()

        for jt in range(ET):
            par = jt % 2
            buf = (scr, scr2)[par]
            if scr_guard[par]:
                P.wait("sync", "dv", scr_guard[par])
            dma("sync", "L%d" % par, buf[:, :RPC],
                biasT_own[jt * 128:(jt + 1) * 128, :])
            P.wait("vector", "L%d" % par)
            dvv = P.emit("vector", lambda e, b=buf[:, :RPC], j=jt:
                         e.tensor_copy(biasT_sb[:, j * RPC:(j + 1) * RPC], b), inc="dv")
            scr_guard[par] = dvv
        for kc in range(KC):
            for hh in range(2):
                par = (kc * 2 + hh) % 2
                buf = (scr, scr2)[par]
                if scr_guard[par]:
                    P.wait("sync", "dv", scr_guard[par])
                dma("sync", "L%d" % par, buf[:, :],
                    woT[kc * 128:(kc + 1) * 128, hh * 512:(hh + 1) * 512])
                P.wait("vector", "L%d" % par)
                dvv = P.emit("vector", lambda e, b=buf[:, :], k=kc, h_=hh:
                             e.tensor_copy(woT_sb[:, k * H + h_ * 512:
                                                  k * H + h_ * 512 + 512], b),
                             inc="dv")
                scr_guard[par] = dvv

        P.barrier()

        produce(0)
        produce(1)
        for b in range(B):
            consume(b)
            if b + 2 < B:
                produce(b + 2)

        P.wait("gpsimd", "SO0")
        P.wait("gpsimd", "SO1")

        block = e(nc.Block())

        @block.sync
        def _(sync):
            for fn in P.ops["sync"]:
                fn(sync)

        @block.gpsimd
        def _(gpsimd):
            for fn in P.ops["gpsimd"]:
                fn(gpsimd)

        @block.tensor
        def _(tensor):
            for fn in P.ops["tensor"]:
                fn(tensor)

        @block.vector
        def _(vector):
            for fn in P.ops["vector"]:
                fn(vector)

        @block.scalar
        def _(scalar):
            for fn in P.ops["scalar"]:
                fn(scalar)

    return nc


_NC_CACHE = None


def kernel(**inputs):
    global _NC_CACHE
    x = np.asarray(inputs["L0_attn_x"], dtype=np.float32)
    sin_t, cos_t = _rope_tables()
    biasT = _bias_matrix(np.asarray(inputs["L0_attn_w"], np.float32))
    uv_w = np.asarray(inputs["L0_attn_uv_w"], np.float32)
    uv_b = np.asarray(inputs["L0_attn_uv_b"], np.float32)

    common = dict(
        ident_in=np.eye(128, dtype=np.float32),
        wuT=np.ascontiguousarray(uv_w[:E].T),
        wvT=np.ascontiguousarray(uv_w[E:2 * E].T),
        wbT=np.ascontiguousarray(uv_w[2 * E:].T),
        wbTs=np.ascontiguousarray(uv_w[2 * E:].T[:, _PERM]),
        owT=np.ascontiguousarray(np.asarray(inputs["L0_attn_o_w"], np.float32).T),
        woT=np.ascontiguousarray(np.asarray(inputs["L0_ffn_wo"], np.float32).T),
        wfT=np.ascontiguousarray(np.asarray(inputs["L0_ffn_wf"], np.float32).T),
        ufT=np.ascontiguousarray(np.asarray(inputs["L0_ffn_uf"], np.float32).T),
        wiT=np.ascontiguousarray(np.asarray(inputs["L0_ffn_wi"], np.float32).T),
        uiT=np.ascontiguousarray(np.asarray(inputs["L0_ffn_ui"], np.float32).T),
        wcT=np.ascontiguousarray(np.asarray(inputs["L0_ffn_wc"], np.float32).T),
        ucT=np.ascontiguousarray(np.asarray(inputs["L0_ffn_uc"], np.float32).T),
        uoT=np.ascontiguousarray(np.asarray(inputs["L0_ffn_uo"], np.float32).T),
        ln_w=np.asarray(inputs["L0_attn_ln_w"], np.float32),
        ln_b=np.asarray(inputs["L0_attn_ln_b"], np.float32),
        o_b=np.asarray(inputs["L0_attn_o_b"], np.float32),
        bf=np.asarray(inputs["L0_ffn_bf"], np.float32),
        bi=np.asarray(inputs["L0_ffn_bi"], np.float32),
        bc=np.asarray(inputs["L0_ffn_bc"], np.float32),
        bo=np.asarray(inputs["L0_ffn_bo"], np.float32),
        in_c=np.asarray(inputs["L0_ffn_in_c"], np.float32),
        uvb_u=np.ascontiguousarray(uv_b[:E]),
        uvb_v=np.ascontiguousarray(uv_b[E:2 * E]),
        uvb_b=np.ascontiguousarray(uv_b[2 * E:]),
        uvb_bs=np.ascontiguousarray(uv_b[2 * E:][_PERM]),
        gamma=np.asarray(inputs["L0_attn_gamma"], np.float32),
        beta=np.asarray(inputs["L0_attn_beta"], np.float32),
        gammas=np.ascontiguousarray(np.asarray(inputs["L0_attn_gamma"], np.float32)[:, _PERM]),
        betas=np.ascontiguousarray(np.asarray(inputs["L0_attn_beta"], np.float32)[:, _PERM]),
    )

    xT = np.ascontiguousarray(x.transpose(0, 2, 1))
    in_maps = []
    for c in range(NCORES):
        r0, r1 = c * RPC, (c + 1) * RPC
        m = dict(common)
        m["x_own"] = np.ascontiguousarray(xT[:, :, r0:r1])
        m["biasT_own"] = np.ascontiguousarray(biasT[:, r0:r1])
        st = sin_t[r0:r1].T
        ct = cos_t[r0:r1].T
        m["sinT"] = np.ascontiguousarray(np.concatenate([-st, st], axis=0))
        m["cosT"] = np.ascontiguousarray(np.concatenate([ct, ct], axis=0))
        in_maps.append(m)

    if _NC_CACHE is None:
        _NC_CACHE = build_program()
    res = run_bass_kernel_spmd(_NC_CACHE, in_maps, core_ids=list(range(NCORES)))
    shards = [res.results[c]["y_own"] for c in range(NCORES)]
    outT = np.concatenate(shards, axis=2)
    return np.ascontiguousarray(outT.transpose(0, 2, 1)).astype(np.float32)


# version 11
# speedup vs baseline: 1.0406x; 1.0406x over previous
import numpy as np

import concourse.bass as bass
import concourse.mybir as mybir
from concourse.bass_utils import run_bass_kernel_spmd

F32 = mybir.dt.float32
BF16 = mybir.dt.bfloat16
AF = mybir.ActivationFunctionType
ALU = mybir.AluOpType

NCORES = 8
B = 16
T = 2048
H = 1024
E = 2048
S = 128
RPC = T // NCORES
L = 512
INV_SQRT_S = 1.0 / float(np.sqrt(S))
EPS = 1e-5
KC = H // 128
ET = E // 128
RPCB = RPC + 16
_PERM = np.r_[64:128, 0:64]


def _rope_tables():
    try:
        import jax
        cpu = jax.devices("cpu")[0]
        import jax.numpy as jnp
        with jax.default_device(cpu):
            pos = jnp.arange(T, dtype=jnp.float32)
            inv_freq = 10000.0 ** (jnp.arange(64, dtype=jnp.float32) / 64)
            sinusoid = pos[:, None] * inv_freq[None, :]
            return (np.asarray(jnp.sin(sinusoid)), np.asarray(jnp.cos(sinusoid)))
    except Exception:
        pos = np.arange(T, dtype=np.float32)
        inv_freq = (10000.0 ** (np.arange(64, dtype=np.float32) /
                                np.float32(64))).astype(np.float32)
        sinusoid = pos[:, None] * inv_freq[None, :]
        return np.sin(sinusoid).astype(np.float32), np.cos(sinusoid).astype(np.float32)


def _bias_matrix(w):
    n = w.shape[0]
    rel = np.arange(T)[None, :] - np.arange(T)[:, None] + (L - 1)
    idx = np.where(rel < 0, rel + n, rel)
    idx = np.clip(idx, 0, n - 1)
    return np.ascontiguousarray(w[idx].T)


class Prog:
    ENGS = ("sync", "gpsimd", "tensor", "vector", "scalar")

    def __init__(self, nc, es):
        self.nc = nc
        self.es = es
        self.ops = {e: [] for e in self.ENGS}
        self.sem = {}
        self.cnt = {}
        self.dma_sems = {}
        for name in ("cc", "pe", "dv", "ac",
                     "b_sync", "b_gpsimd", "b_tensor", "b_vector", "b_scalar"):
            self.sem[name] = es.enter_context(nc.semaphore(name))
            self.cnt[name] = 0

    def dsem(self, name, issuer):
        if name not in self.sem:
            self.sem[name] = self.es.enter_context(self.nc.semaphore(name))
            self.cnt[name] = 0
            self.dma_sems[name] = issuer
        return name

    def wait(self, eng, sem, val=None):
        v = self.cnt[sem] if val is None else val
        h = self.sem[sem]
        self.ops[eng].append(lambda e, h=h, v=v: e.wait_ge(h, v))

    def emit(self, eng, fn, inc=None, amt=1):
        if eng == "vector":
            self.wait("vector", "dv")
            h = self.sem["dv"]
            self.cnt["dv"] += 1
            self.ops[eng].append(lambda e, fn=fn, h=h: fn(e).then_inc(h, 1))
            return self.cnt["dv"]
        if inc is None:
            self.ops[eng].append(fn)
            return None
        h = self.sem[inc]
        self.cnt[inc] += amt
        self.ops[eng].append(lambda e, fn=fn, h=h, a=amt: fn(e).then_inc(h, a))
        return self.cnt[inc]

    def barrier(self):
        for name, issuer in self.dma_sems.items():
            if self.cnt[name]:
                self.wait(issuer, name)
        self.wait("gpsimd", "cc")
        self.wait("vector", "dv")
        self.wait("scalar", "ac")
        self.wait("tensor", "pe")
        for e in self.ENGS:
            h = self.sem["b_" + e]
            self.cnt["b_" + e] += 1
            self.ops[e].append(lambda eng, h=h: eng.nop().then_inc(h, 1))
        for e in self.ENGS:
            for o in self.ENGS:
                if o != e:
                    self.wait(e, "b_" + o)


def build_program():
    nc = bass.Bass("TRN2", target_bir_lowering=False)
    dt = nc.dram_tensor

    x_own = dt("x_own", [B, H, RPC], F32, kind="ExternalInput").ap()
    biasT_own = dt("biasT_own", [T, RPC], F32, kind="ExternalInput").ap()
    sinT_in = dt("sinT", [128, RPC], F32, kind="ExternalInput").ap()
    cosT_in = dt("cosT", [128, RPC], F32, kind="ExternalInput").ap()
    ident_in = dt("ident_in", [128, 128], F32, kind="ExternalInput").ap()

    wuT = dt("wuT", [H, E], F32, kind="ExternalInput").ap()
    wvT = dt("wvT", [H, E], F32, kind="ExternalInput").ap()
    wbT = dt("wbT", [H, S], F32, kind="ExternalInput").ap()
    wbTs = dt("wbTs", [H, S], F32, kind="ExternalInput").ap()
    owT = dt("owT", [E, H], F32, kind="ExternalInput").ap()
    woT = dt("woT", [H, H], F32, kind="ExternalInput").ap()
    gateT = {g: dt(g + "T", [H, H], F32, kind="ExternalInput").ap()
             for g in ("wf", "uf", "wi", "ui", "wc", "uc", "uo")}

    vec_in = {}
    for name, n in (("ln_w", H), ("ln_b", H), ("o_b", H), ("bf", H), ("bi", H),
                    ("bc", H), ("bo", H), ("in_c", H), ("uvb_u", E), ("uvb_b", S), ("uvb_bs", S)):
        vec_in[name] = dt(name, [n], F32, kind="ExternalInput").ap()
    uvb_v_in = dt("uvb_v", [E], F32, kind="ExternalInput").ap()
    gamma_in = dt("gamma", [2, 128], F32, kind="ExternalInput").ap()
    gammas_in = dt("gammas", [2, 128], F32, kind="ExternalInput").ap()
    beta_in = dt("beta", [2, 128], F32, kind="ExternalInput").ap()
    betas_in = dt("betas", [2, 128], F32, kind="ExternalInput").ap()

    y_own = dt("y_own", [B, H, RPC], F32, kind="ExternalOutput").ap()

    wu_h = dt("wu_h", [H, E], BF16)
    wv_h = dt("wv_h", [H, E], BF16)
    wb_h = dt("wb_h", [H, S], BF16)
    wbs_h = dt("wbs_h", [H, S], BF16)
    ow_h = dt("ow_h", [E, H], BF16)
    uo_h = dt("uo_h", [H, H], BF16)
    wf_h = dt("wf_h", [H, H], BF16)
    wi_h = dt("wi_h", [H, H], BF16)
    wc_h = dt("wc_h", [H, H], BF16)
    ag_in_big = dt("ag_in_big", [B * RPCB, E], BF16)
    ag_out_pair = [dt(f"ag_out_p{p}", [NCORES * 2 * RPCB, E], BF16)
                   for p in range(B // 2)]

    from contextlib import ExitStack
    es = ExitStack()
    with es:
        e = es.enter_context

        xnT = e(nc.sbuf_tensor("xnT", [128, KC * B * RPC], BF16))
        bb = [e(nc.sbuf_tensor(f"bb{i}", [128, 8192], BF16)) for i in range(6)]
        aux_h = e(nc.sbuf_tensor("aux_h", [128, 3072], BF16))
        c_f32 = e(nc.sbuf_tensor("c_f32", [128, KC * RPC], F32))
        c_bfA = e(nc.sbuf_tensor("c_bfA", [128, KC * RPC], BF16))
        c_bfB = e(nc.sbuf_tensor("c_bfB", [128, KC * RPC], BF16))
        scr = e(nc.sbuf_tensor("scr", [128, 512], F32))
        scr2 = e(nc.sbuf_tensor("scr2", [128, 512], F32))
        gates = [e(nc.sbuf_tensor(f"gate{i}", [128, RPC], F32)) for i in range(3)]
        upd = e(nc.sbuf_tensor("upd", [128, RPC], F32))
        sq_h = e(nc.sbuf_tensor("sq_h", [128, RPC], BF16))
        tmp_h = e(nc.sbuf_tensor("tmp_h", [128, RPC], BF16))
        muB = e(nc.sbuf_tensor("muB", [128, RPC], BF16))
        rsB = e(nc.sbuf_tensor("rsB", [128, RPC], BF16))
        hT_b = e(nc.sbuf_tensor("hT_b", [128, RPC], BF16))
        xres = [e(nc.sbuf_tensor(f"xres{i}", [128, RPC], F32)) for i in range(2)]
        outs = [e(nc.sbuf_tensor(f"outs{i}", [128, RPC], F32)) for i in range(2)]
        stat = e(nc.sbuf_tensor("stat", [1, 2 * RPC], F32))
        uvbv_row = e(nc.sbuf_tensor("uvbv_row", [1, E], BF16))
        ones_col = e(nc.sbuf_tensor("ones_col", [128, 1], BF16))
        ones_row = e(nc.sbuf_tensor("ones_row", [1, 128], BF16))
        ones_row_f = e(nc.sbuf_tensor("ones_row_f", [1, 128], F32))
        ident_h = e(nc.sbuf_tensor("ident_h", [128, 128], BF16))
        sin_sb = e(nc.sbuf_tensor("sin_sb", [128, RPC], F32))
        cos_sb = e(nc.sbuf_tensor("cos_sb", [128, RPC], F32))
        sm = e(nc.sbuf_tensor("sm", [128, 96], F32))

        p_m = [e(nc.psum_tensor(f"p_m{i}", [128, RPC], F32)) for i in range(3)]
        p_v = [e(nc.psum_tensor(f"p_v{i}", [128, 512], F32)) for i in range(2)]
        p_base = e(nc.psum_tensor("p_base", [128, RPC], F32))
        p_h = e(nc.psum_tensor("p_h", [128, RPC], F32))
        p_stat = e(nc.psum_tensor("p_stat", [128, 512], F32))
        p_s = [p_stat[0:1, 0:RPC], p_stat[0:1, RPC:2 * RPC]]
        p_b = p_stat

        P = Prog(nc, es)
        psum_owner = {}

        def group_begin(key):
            if key in psum_owner:
                s, v = psum_owner[key]
                P.wait("tensor", s, v)

        smcol = {}
        _c = [0]
        def alloc_sm(name, ncols):
            smcol[name] = _c[0]
            _c[0] += ncols
        for name in ("ln_w", "ln_b", "o_b", "bf", "bi", "bc", "bo", "in_c"):
            alloc_sm(name, KC)
        alloc_sm("uvb_u", ET)
        alloc_sm("uvb_b", 1)
        alloc_sm("uvb_bs", 1)
        alloc_sm("gamma", 2)
        alloc_sm("beta", 2)
        alloc_sm("gammas", 2)
        alloc_sm("betas", 2)
        alloc_sm("g0p", 1)
        alloc_sm("b0p", 1)
        alloc_sm("g0ps", 1)
        alloc_sm("b0ps", 1)
        assert _c[0] <= 96

        def smc(name, i=0):
            return sm[:, smcol[name] + i: smcol[name] + i + 1]

        WG = {"uf": bb[1], "ui": bb[3], "uc": bb[5]}
        xg_buf = [[bb[4][:, (2 * g + i) * 1024:(2 * g + i + 1) * 1024]
                   for i in range(2)] for g in range(3)]
        wb_vk = bb[0][:, 0:1024]
        wv_vk = [bb[0][:, 1024 + i * 512: 1024 + (i + 1) * 512] for i in range(2)]
        kernelT = bb[0]
        kT_sb = bb[0][:, 4096:6144]
        qT_bufs = [bb[0][:, 6144:6400], bb[0][:, 6400:6656]]
        uT_bufs = [bb[1][:, 0:4096], bb[1][:, 4096:8192]]
        eT = bb[3][:, 4096:8192]
        biasT_sb = bb[2][:, 0:4096]
        uocT = bb[2][:, 4096:6144]
        xnnT = bb[2][:, 6144:8192]
        ow_strip = [bb[3][:, i * 2048:(i + 1) * 2048] for i in range(2)]
        woT_sb = bb[4]
        wu_strip = [bb[5][:, i * 1024:(i + 1) * 1024] for i in range(2)]
        wv_buf = [bb[5][:, 2048 + i * 512: 2048 + (i + 1) * 512] for i in range(2)]
        wb_buf = bb[5][:, 3072:4096]
        vstrip = [bb[5][:, 4096 + i * 2048: 4096 + (i + 1) * 2048] for i in range(2)]
        vout = [aux_h[:, i * 512:(i + 1) * 512] for i in range(2)]
        uo_strip = [aux_h[:, 1024 + i * 1024: 1024 + (i + 1) * 1024] for i in range(2)]
        x_bf = c_bfA
        uvbv_hlp = c_bfA
        cast_h = c_bfB

        import dataclasses as _dc

        def mkap(base_ap, offset, dims):
            return _dc.replace(base_ap, offset=offset,
                               ap=type(base_ap.ap)(dims))

        def dma(eng, sem, out, in_, slow=False):
            P.dsem(sem, eng)
            if slow:
                return P.emit(eng, lambda e, o=out, i=in_: e.dma_start(
                    out=o, in_=i, allow_slow_non_contiguous=True), inc=sem, amt=16)
            return P.emit(eng, lambda e, o=out, i=in_: e.dma_start(out=o, in_=i),
                          inc=sem, amt=16)

        for name in ("ln_w", "ln_b", "o_b", "bf", "bi", "bc", "bo", "in_c"):
            dma("sync", "LS", sm[:, smcol[name]:smcol[name] + KC],
                vec_in[name].rearrange("(t p) -> p t", p=128), slow=True)
        dma("sync", "LS", sm[:, smcol["uvb_u"]:smcol["uvb_u"] + ET],
            vec_in["uvb_u"].rearrange("(t p) -> p t", p=128), slow=True)
        dma("sync", "LS", sm[:, smcol["uvb_b"]:smcol["uvb_b"] + 1],
            vec_in["uvb_b"].rearrange("(t p) -> p t", p=128), slow=True)
        dma("sync", "LS", sm[:, smcol["uvb_bs"]:smcol["uvb_bs"] + 1],
            vec_in["uvb_bs"].rearrange("(t p) -> p t", p=128), slow=True)
        dma("sync", "LS", sm[:, smcol["gamma"]:smcol["gamma"] + 2],
            gamma_in.rearrange("h s -> s h"), slow=True)
        dma("sync", "LS", sm[:, smcol["beta"]:smcol["beta"] + 2],
            beta_in.rearrange("h s -> s h"), slow=True)
        dma("sync", "LS", sm[:, smcol["gammas"]:smcol["gammas"] + 2],
            gammas_in.rearrange("h s -> s h"), slow=True)
        dma("sync", "LS", sm[:, smcol["betas"]:smcol["betas"] + 2],
            betas_in.rearrange("h s -> s h"), slow=True)
        dma("sync", "LS", scr[:, :128], ident_in)
        dma("sync", "LS", sin_sb[:, :], sinT_in)
        dma("sync", "LS", cos_sb[:, :], cosT_in)

        P.wait("vector", "LS")
        P.emit("vector", lambda e: e.memset(ones_col[:, :], 1.0))
        P.emit("vector", lambda e: e.memset(ones_row[:, :], 1.0))
        P.emit("vector", lambda e: e.memset(ones_row_f[:, :], 1.0))
        P.emit("vector", lambda e: e.tensor_copy(ident_h[:, :], scr[:, :128]))
        for q in range(4):
            dma("sync", "L1", scr2[0:1, :512],
                uvb_v_in[q * 512:(q + 1) * 512].rearrange("(a m) -> a m", a=1))
            P.wait("vector", "L1")
            dvq = P.emit("vector", lambda e, q_=q: e.tensor_copy(
                uvbv_hlp[0:1, q_ * 512:(q_ + 1) * 512], scr2[0:1, :512]), inc="dv")
            P.wait("sync", "dv", dvq)
        P.wait("gpsimd", "dv")
        dma("gpsimd", "SU", uvbv_row[0:1, :], uvbv_hlp[0:1, 0:E])
        P.emit("vector", lambda e: e.tensor_scalar_mul(smc("g0p"), smc("gamma", 0), INV_SQRT_S))
        P.emit("vector", lambda e: e.tensor_scalar_mul(smc("b0p"), smc("beta", 0), INV_SQRT_S))
        P.emit("vector", lambda e: e.tensor_scalar_mul(smc("g0ps"), smc("gammas", 0), INV_SQRT_S))
        P.emit("vector", lambda e: e.tensor_scalar_mul(smc("b0ps"), smc("betas", 0), INV_SQRT_S),
               inc="dv")

        scr_guard = [0, 0]

        def cast_chunks(chunks):
            pass

        def cast_resident(src_ap, dst, n_out):
            i = 0
            for kc in range(KC):
                for off in range(0, n_out, 512):
                    wdt = min(512, n_out - off)
                    par = i % 2
                    buf = (scr, scr2)[par]
                    if scr_guard[par]:
                        P.wait("sync", "dv", scr_guard[par])
                    dma("sync", "L%d" % par, buf[:, :wdt],
                        src_ap[kc * 128:(kc + 1) * 128, off:off + wdt])
                    P.wait("vector", "L%d" % par)
                    dvv = P.emit("vector", lambda e, b=buf[:, :wdt],
                                 d=dst[:, kc * n_out + off: kc * n_out + off + wdt]:
                                 e.tensor_copy(d, b), inc="dv")
                    scr_guard[par] = dvv
                    i += 1

        cast_dram_guard = [0, 0]

        def cast_to_dram(src_ap, dst_dram, rows, cols):
            i = 0
            for r0 in range(0, rows, 128):
                for off in range(0, cols, 512):
                    wdt = min(512, cols - off)
                    par = i % 2
                    buf = (scr, scr2)[par]
                    if scr_guard[par]:
                        P.wait("sync", "dv", scr_guard[par])
                    dma("sync", "L%d" % par, buf[:, :wdt],
                        src_ap[r0:r0 + 128, off:off + wdt])
                    hbuf = cast_h[:, par * 1024: par * 1024 + wdt]
                    P.wait("vector", "L%d" % par)
                    if cast_dram_guard[par]:
                        P.wait("vector", "SC%d" % par)
                    dvv = P.emit("vector", lambda e, b=buf[:, :wdt], h=hbuf:
                                 e.tensor_copy(h, b), inc="dv")
                    scr_guard[par] = dvv
                    P.wait("gpsimd", "dv", dvv)
                    dma("gpsimd", "SC%d" % par, dst_dram.ap()[r0:r0 + 128, off:off + wdt], hbuf)
                    cast_dram_guard[par] = 1
                    i += 1

        for g in ("uf", "ui", "uc"):
            cast_resident(gateT[g], WG[g], H)
        cast_to_dram(gateT["wf"], wf_h, H, H)
        cast_to_dram(gateT["wi"], wi_h, H, H)
        cast_to_dram(gateT["wc"], wc_h, H, H)
        cast_to_dram(gateT["uo"], uo_h, H, H)
        cast_to_dram(wuT, wu_h, H, E)
        cast_to_dram(wvT, wv_h, H, E)
        cast_to_dram(wbT, wb_h, H, S)
        cast_to_dram(wbTs, wbs_h, H, S)
        cast_to_dram(owT, ow_h, E, H)

        P.barrier()

        st_mu = stat[0:1, 0:RPC]
        st_b = stat[0:1, RPC:2 * RPC]

        stats_guard = {"sq": 0, "bank": 0}

        def ln_stats(src_slices, src_dv=0):
            P.wait("tensor", "dv", max(src_dv, stats_guard["bank"]))
            P.wait("scalar", "dv", max(stats_guard["sq"], src_dv))
            if P.cnt.get("SK"):
                P.wait("scalar", "SK")
                P.wait("vector", "SK")
            for kc in range(KC):
                P.emit("tensor", lambda e, s=src_slices[kc], k=kc:
                       e.matmul(p_s[0], ones_col[:, :], s,
                                start=(k == 0), stop=(k == KC - 1)),
                       inc="pe")
            for kc in range(KC):
                sq = (sq_h, tmp_h)[kc % 2]
                if kc >= 2:
                    P.wait("scalar", "pe", pe_sq[kc - 2])
                av = P.emit("scalar", lambda e, s=src_slices[kc], q=sq:
                            e.activation(out=q[:, :], in_=s, func=AF.Square), inc="ac")
                P.wait("tensor", "ac", av)
                pv = P.emit("tensor", lambda e, k=kc, q=sq:
                            e.matmul(p_s[1], ones_col[:, :], q[:, :],
                                     start=(k == 0), stop=(k == KC - 1)),
                            inc="pe")
                if kc == 0:
                    pe_sq = {}
                pe_sq[kc] = pv
            P.wait("vector", "pe")
            P.emit("vector", lambda e: e.tensor_scalar_mul(st_mu, p_s[0], 1.0 / H))
            P.emit("vector", lambda e: e.tensor_scalar_mul(st_b, p_s[1], 1.0 / H))
            P.emit("vector", lambda e: e.tensor_mul(p_s[0], st_mu, st_mu))
            P.emit("vector", lambda e: e.tensor_sub(st_b, st_b, p_s[0]))
            P.emit("vector", lambda e: e.tensor_scalar_add(st_b, st_b, EPS), inc="dv")
            P.wait("scalar", "dv")
            P.emit("scalar", lambda e: e.activation(out=st_b, in_=st_b, func=AF.Sqrt),
                   inc="ac")
            P.wait("vector", "ac")
            recip_dv = P.emit("vector", lambda e: e.reciprocal(st_b, st_b), inc="dv")
            P.wait("tensor", "dv", recip_dv)
            P.emit("tensor", lambda e: e.matmul(p_b[:, 0:RPC], ones_row_f[:, :], st_mu,
                                                start=True, stop=True))
            P.emit("tensor", lambda e: e.matmul(p_b[:, RPC:2 * RPC], ones_row_f[:, :], st_b,
                                                start=True, stop=True), inc="pe")
            P.wait("vector", "pe")
            P.emit("vector", lambda e: e.tensor_copy(muB[:, :], p_b[:, 0:RPC]))
            stats_guard["bank"] = P.emit(
                "vector", lambda e: e.tensor_copy(rsB[:, :], p_b[:, RPC:2 * RPC]),
                inc="dv")

        def xnT_sl(kc, t):
            off = kc * (B * RPC) + t * RPC
            return xnT[:, off: off + RPC]

        for t in range(B):
            P.wait("vector", "pe")
            for kc in range(KC):
                par = kc % 2
                buf = (scr, scr2)[par]
                if scr_guard[par]:
                    P.wait("sync", "dv", scr_guard[par])
                dma("sync", "L%d" % par, buf[:, :RPC],
                    x_own[t, kc * 128:(kc + 1) * 128, :])
                P.wait("vector", "L%d" % par)
                dvv = P.emit("vector", lambda e, b=buf[:, :RPC],
                             d=x_bf[:, kc * RPC:(kc + 1) * RPC]:
                             e.tensor_copy(d, b), inc="dv")
                scr_guard[par] = dvv
            xs = [x_bf[:, kc * RPC:(kc + 1) * RPC] for kc in range(KC)]
            ln_stats(xs, src_dv=dvv)
            for kc in range(KC):
                P.emit("vector", lambda e, s=xs[kc]: e.tensor_sub(tmp_h[:, :], s, muB[:, :]))
                stats_guard["sq"] = P.emit(
                    "vector", lambda e, d=xnT_sl(kc, t):
                    e.tensor_mul(d, tmp_h[:, :], rsB[:, :]), inc="dv")

        P.barrier()

        def xnn_make(b):
            xs = [xnT_sl(kc, b) for kc in range(KC)]
            ln_stats(xs)
            P.wait("vector", "pe", produce_pe_end[0])
            for kc in range(KC):
                P.emit("vector", lambda e, s=xs[kc]: e.tensor_sub(tmp_h[:, :], s, muB[:, :]))
                P.emit("vector", lambda e: e.tensor_mul(sq_h[:, :], tmp_h[:, :], rsB[:, :]))
                stats_guard["sq"] = P.emit("vector", lambda e, k=kc: e.tensor_scalar(
                    out=xnnT[:, k * RPC:(k + 1) * RPC], in0=sq_h[:, :],
                    scalar1=smc("ln_w", k), scalar2=smc("ln_b", k),
                    op0=ALU.mult, op1=ALU.add), inc="dv")

        base_sw = c_f32[:, 0:RPC]
        relu_t = c_f32[:, RPC:2 * RPC]
        rope_dv_last = [0]
        produce_pe_end = [0]
        cc_of = {}
        qk_pe_end = {}
        eT_dv_end = {}

        def rope_make(dst, gcol, bcol, gscol, bscol, a0, a1,
                      bsrc=None, bssrc=None, tq=None):
            bsrc = gates[2][:, :] if bsrc is None else bsrc
            bssrc = base_sw if bssrc is None else bssrc
            tq = upd[:, :] if tq is None else tq
            P.emit("vector", lambda e, g=gcol, b_=bcol: e.tensor_scalar(
                out=tq, in0=bsrc, scalar1=g, scalar2=b_,
                op0=ALU.mult, op1=ALU.add))
            P.emit("vector", lambda e, g=gscol, b_=bscol: e.tensor_scalar(
                out=a0, in0=bssrc, scalar1=g, scalar2=b_,
                op0=ALU.mult, op1=ALU.add))
            P.emit("vector", lambda e: e.tensor_mul(a1, a0, sin_sb[:, :]))
            P.emit("vector", lambda e: e.tensor_mul(tq, tq, cos_sb[:, :]))
            return P.emit("vector", lambda e, d=dst:
                          e.tensor_add(d, tq, a1), inc="dv")

        vout_guard = [0, 0]
        out_guard = [0, 0]
        phaseA_dv = [0]

        def produce_vk(b):
            xnn_make(b)
            dv_xnn = P.cnt["dv"]
            P.wait("tensor", "dv", dv_xnn)
            P.wait("scalar", "dv", rope_dv_last[0])
            for swp, dst_sb, bias_name in (
                    (0, xres[0][:, :], "uvb_b"),
                    (1, xres[1][:, :], "uvb_bs")):
                src = (wb_h if swp == 0 else wbs_h).ap()
                dma("sync", "LBk",
                    wb_vk[:, :1024].rearrange("p (kc c) -> p kc c", kc=KC),
                    mkap(src, 0, [[S, 128], [128 * S, KC], [1, S]]))
                group_begin("base")
                P.wait("tensor", "LBk")
                for kc in range(KC):
                    P.emit("tensor", lambda e, k=kc:
                           e.matmul(p_base[:, :], wb_vk[:, k * 128:(k + 1) * 128],
                                    xnnT[:, k * RPC:(k + 1) * RPC],
                                    start=(k == 0), stop=(k == KC - 1)),
                           inc="pe" if kc == KC - 1 else None)
                pv = P.cnt["pe"]
                P.wait("sync", "pe", pv)
                P.wait("scalar", "pe", pv)
                av = P.emit("scalar", lambda e, d=dst_sb, bn=bias_name: e.activation(
                    out=d, in_=p_base[:, :], func=AF.Silu,
                    bias=smc(bn)), inc="ac")
                psum_owner["base"] = ("ac", av)
            P.wait("vector", "ac", av)
            kv = rope_make(tmp_h[:, :], smc("gamma", 1), smc("beta", 1),
                           smc("gammas", 1), smc("betas", 1),
                           outs[0][:, :], outs[1][:, :],
                           bsrc=xres[0][:, :], bssrc=xres[1][:, :],
                           tq=xres[0][:, :])
            rope_dv_last[0] = kv
            stats_guard["sq"] = max(stats_guard["sq"], kv)
            P.wait("gpsimd", "dv", kv)
            dma("gpsimd", "SK",
                ag_in_big.ap()[b * RPCB + RPC: b * RPCB + RPC + 16, :]
                .rearrange("a (c i) -> (a c) i", i=RPC),
                tmp_h[:, :])
            for nch in range(4):
                for kc in range(KC):
                    par = (nch * KC + kc) % 2
                    wb2 = wv_vk[par]
                    if nch * KC + kc >= 2:
                        P.wait("sync", "pe", v_pe[nch * KC + kc - 2])
                    dma("sync", "LVk%d" % par, wb2[:, :],
                        wv_h.ap()[kc * 128:(kc + 1) * 128, nch * 512:(nch + 1) * 512])
                    for rt in range(2):
                        key = "pv%d" % rt
                        if kc == 0:
                            group_begin(key)
                            P.emit("tensor", lambda e, p=p_v[rt], n=nch: e.matmul(
                                p[:, :], ones_row[:, :],
                                uvbv_row[0:1, n * 512:(n + 1) * 512],
                                start=True, stop=False))
                        P.wait("tensor", "LVk%d" % par)
                        pv = P.emit("tensor", lambda e, p=p_v[rt], w=wb2, r=rt, k=kc:
                                    e.matmul(p[:, :],
                                             xnnT[:, k * RPC + r * 128: k * RPC + r * 128 + 128],
                                             w[:, :], start=False, stop=(k == KC - 1)),
                                    inc="pe")
                        if nch == 0 and kc == 0 and rt == 0:
                            v_pe = {}
                        v_pe[nch * KC + kc] = pv
                        if kc == KC - 1:
                            vo = vout[rt]
                            P.wait("scalar", "pe", pv)
                            if vout_guard[rt]:
                                P.wait("scalar", "SV%d" % rt)
                            av = P.emit("scalar", lambda e, p=p_v[rt], o=vo: e.activation(
                                out=o, in_=p[:, :], func=AF.Silu), inc="ac")
                            psum_owner[key] = ("ac", av)
                            P.wait("gpsimd", "ac", av)
                            dma("gpsimd", "SV%d" % rt,
                                ag_in_big.ap()[b * RPCB + rt * 128: b * RPCB + (rt + 1) * 128,
                                               nch * 512:(nch + 1) * 512], vo)
                            vout_guard[rt] = 1
            produce_pe_end[0] = P.cnt["pe"]

        def produce(b):
            uT = uT_bufs[b % 2]
            qT_sb = qT_bufs[b % 2]
            xnn_make(b)
            dv_xnn = P.cnt["dv"]
            P.wait("tensor", "dv", dv_xnn)
            P.wait("scalar", "dv", rope_dv_last[0])
            for swp, w_src, dst_sb, bias_name in (
                    (0, wb_h, gates[2][:, :], "uvb_b"),
                    (1, None, base_sw, "uvb_bs")):
                src = (wb_h if swp == 0 else wbs_h).ap()
                dma("sync", "LB",
                    wb_buf[:, :1024].rearrange("p (kc c) -> p kc c", kc=KC),
                    mkap(src, 0, [[S, 128], [128 * S, KC], [1, S]]))
                group_begin("base")
                P.wait("tensor", "LB")
                for kc in range(KC):
                    P.emit("tensor", lambda e, k=kc:
                           e.matmul(p_base[:, :], wb_buf[:, k * 128:(k + 1) * 128],
                                    xnnT[:, k * RPC:(k + 1) * RPC],
                                    start=(k == 0), stop=(k == KC - 1)),
                           inc="pe" if kc == KC - 1 else None)
                pv = P.cnt["pe"]
                P.wait("sync", "pe", pv)
                P.wait("scalar", "pe", pv)
                av = P.emit("scalar", lambda e, d=dst_sb, bn=bias_name: e.activation(
                    out=d, in_=p_base[:, :], func=AF.Silu,
                    bias=smc(bn)), inc="ac")
                psum_owner["base"] = ("ac", av)
            P.wait("vector", "ac", av)
            a0 = gates[0][:, :]
            a1 = gates[1][:, :]
            if b >= 2:
                P.wait("vector", "pe", qk_pe_end[b - 2])
            qv = rope_make(qT_sb, smc("g0p"), smc("b0p"),
                           smc("g0ps"), smc("b0ps"), a0, a1)
            rope_dv_last[0] = qv
            if b >= 2:
                P.wait("scalar", "dv", eT_dv_end[b - 2])
            for et in range(ET):
                sb = wu_strip[et % 2]
                if et >= 2:
                    P.wait("scalar", "pe", u_pe[et - 2])
                dma("scalar", "LU%d" % (et % 2),
                    sb[:, :1024].rearrange("p (kc c) -> p kc c", kc=KC),
                    mkap(wu_h.ap(), et * 128, [[E, 128], [128 * E, KC], [1, 128]]))
                key = "pv%d" % (et % 2)
                pm = p_v[et % 2][:, 0:RPC]
                group_begin(key)
                P.wait("tensor", "LU%d" % (et % 2))
                if et == 0:
                    P.wait("tensor", "dv", dv_xnn)
                for kc in range(KC):
                    pv = P.emit("tensor", lambda e, p=pm, s=sb, k=kc:
                                e.matmul(p, s[:, k * 128:(k + 1) * 128],
                                         xnnT[:, k * RPC:(k + 1) * RPC],
                                         start=(k == 0), stop=(k == KC - 1)),
                                inc="pe")
                if et == 0:
                    u_pe = {}
                u_pe[et] = pv
                P.wait("scalar", "pe", pv)
                av = P.emit("scalar", lambda e, p=pm, t_=et, u_=uT: e.activation(
                    out=u_[:, t_ * RPC:(t_ + 1) * RPC], in_=p, func=AF.Silu,
                    bias=smc("uvb_u", t_)), inc="ac")
                psum_owner[key] = ("ac", av)
            produce_pe_end[0] = P.cnt["pe"]

        def consume(b):
            uT = uT_bufs[b % 2]
            qT_sb = qT_bufs[b % 2]
            P.wait("gpsimd", "cc", b // 2 + 1)
            if b > 0:
                P.wait("gpsimd", "pe", qk_pe_end[b - 1])
            agp = ag_out_pair[b // 2]
            for r in range(NCORES):
                base_row = r * (2 * RPCB) + (b % 2) * RPCB + RPC
                blk = agp.ap()[base_row: base_row + 16, :]
                dma("gpsimd", "LK", kT_sb[:, r * RPC:(r + 1) * RPC],
                    blk.rearrange("a (c i) -> (a c) i", i=RPC))
            for jt in range(ET):
                key = "m%d" % (jt % 2)
                pm = p_m[jt % 2]
                group_begin(key)
                if jt == 0:
                    P.wait("tensor", "LK")
                P.emit("tensor", lambda e, p=pm, j=jt: e.matmul(
                    p[:, :], ident_h[:, :], biasT_sb[:, j * RPC:(j + 1) * RPC],
                    start=True, stop=False))
                pv = P.emit("tensor", lambda e, p=pm, j=jt: e.matmul(
                    p[:, :], kT_sb[:, j * 128:(j + 1) * 128], qT_sb,
                    start=False, stop=True), inc="pe")
                P.wait("scalar", "pe", pv)
                if jt > 0:
                    P.wait("scalar", "dv", sq_dv)
                av = P.emit("scalar", lambda e, p=pm: e.activation(
                    out=relu_t, in_=p[:, :], func=AF.Relu), inc="ac")
                psum_owner[key] = ("ac", av)
                P.wait("vector", "ac", av)
                sq_dv = P.emit("vector", lambda e, j=jt: e.tensor_mul(
                    kernelT[:, j * RPC:(j + 1) * RPC], relu_t, relu_t), inc="dv")
            qk_pe_end[b] = P.cnt["pe"]
            dv_kernel = P.cnt["dv"]
            for et in range(ET):
                vs = vstrip[et % 2]
                if et >= 2:
                    P.wait("gpsimd", "pe", kv_pe[et - 2])
                for q in range(2):
                    dma("gpsimd", "LW%d" % (et % 2),
                        mkap(vs, vs.offset + q * 128,
                             [[8192, 128], [256, NCORES], [1, 128]]),
                        mkap(agp.ap(),
                             ((b % 2) * RPCB + q * 128) * E + et * 128,
                             [[E, 128], [2 * RPCB * E, NCORES], [1, 128]]))
                key = "m%d" % (et % 2)
                pm = p_m[et % 2]
                group_begin(key)
                P.wait("tensor", "LW%d" % (et % 2))
                if et == 0:
                    P.wait("tensor", "dv", dv_kernel)
                for jt in range(ET):
                    pv = P.emit("tensor", lambda e, p=pm, v_=vs, j=jt: e.matmul(
                        p[:, :], v_[:, j * 128:(j + 1) * 128],
                        kernelT[:, j * RPC:(j + 1) * RPC],
                        start=(j == 0), stop=(j == ET - 1)),
                        inc="pe" if jt == ET - 1 else None)
                if et == 0:
                    kv_pe = {}
                kv_pe[et] = pv
                P.wait("vector", "pe", pv)
                dvv = P.emit("vector", lambda e, p=pm, t_=et: e.tensor_mul(
                    eT[:, t_ * RPC:(t_ + 1) * RPC], p[:, :],
                    uT[:, t_ * RPC:(t_ + 1) * RPC]), inc="dv")
                psum_owner[key] = ("dv", dvv)
            dv_eT = P.cnt["dv"]
            eT_dv_end[b] = dv_eT
            for ft in range(KC):
                osb = ow_strip[ft % 2]
                if ft >= 2:
                    P.wait("scalar", "pe", o_pe[ft - 2])
                dma("scalar", "LO%d" % (ft % 2),
                    osb[:, :2048].rearrange("p (ec c) -> p ec c", ec=ET),
                    mkap(ow_h.ap(), ft * 128, [[H, 128], [128 * H, ET], [1, 128]]))
                group_begin("h")
                P.emit("tensor", lambda e, f=ft: e.matmul(
                    p_h[:, :], ident_h[:, :], uocT[:, f * RPC:(f + 1) * RPC],
                    start=True, stop=False))
                for kc in range(KC):
                    P.emit("tensor", lambda e, f=ft, k=kc, b_=b: e.matmul(
                        p_h[:, :], woT_sb[:, k * H + f * 128: k * H + f * 128 + 128],
                        xnT_sl(k, b_), start=False, stop=(k == KC - 1)),
                        inc="pe" if kc == KC - 1 else None)
                pv_h = P.cnt["pe"]
                P.wait("scalar", "pe", pv_h)
                if ft > 0:
                    P.wait("scalar", "dv", comb_dv)
                av = P.emit("scalar", lambda e, f=ft: e.activation(
                    out=hT_b[:, :], in_=p_h[:, :], func=AF.Tanh, bias=smc("bo", f)),
                    inc="ac")
                psum_owner["h"] = ("ac", av)
                key = "m%d" % (ft % 3)
                pm = p_m[ft % 3]
                group_begin(key)
                P.emit("tensor", lambda e, p=pm, f=ft, b_=b: e.matmul(
                    p[:, :], ident_h[:, :], xnT_sl(f, b_), start=True, stop=False))
                P.wait("tensor", "LO%d" % (ft % 2))
                if ft == 0:
                    P.wait("tensor", "dv", dv_eT)
                for ec in range(ET):
                    P.emit("tensor", lambda e, p=pm, s=osb, c_=ec: e.matmul(
                        p[:, :], s[:, c_ * 128:(c_ + 1) * 128],
                        eT[:, c_ * RPC:(c_ + 1) * RPC],
                        start=False, stop=(c_ == ET - 1)),
                        inc="pe" if ec == ET - 1 else None)
                pv_o = P.cnt["pe"]
                if ft == 0:
                    o_pe = {}
                o_pe[ft] = pv_o
                xr = xres[ft % 2]
                if b == 0 and ft < 2:
                    P.wait("sync", "dv", phaseA_dv[0])
                dma("sync", "LX%d" % (ft % 2), xr[:, :],
                    x_own[b, ft * 128:(ft + 1) * 128, :])
                ob = outs[ft % 2]
                P.wait("vector", "pe", pv_o)
                P.wait("vector", "ac", av)
                if out_guard[ft % 2]:
                    P.wait("vector", "SO%d" % (ft % 2))
                P.emit("vector", lambda e, p=pm, f=ft, o=ob: e.scalar_tensor_tensor(
                    out=o[:, :], in0=p[:, :], scalar=smc("o_b", f), in1=hT_b[:, :],
                    op0=ALU.add, op1=ALU.add), inc="dv")
                psum_owner[key] = ("dv", P.cnt["dv"])
                P.wait("vector", "LX%d" % (ft % 2))
                comb_dv = P.emit("vector", lambda e, o=ob, x=xr: e.tensor_add(
                    o[:, :], o[:, :], x[:, :]), inc="dv")
                P.wait("gpsimd", "dv", comb_dv)
                dma("gpsimd", "SO%d" % (ft % 2), y_own[b, ft * 128:(ft + 1) * 128, :],
                    ob[:, :])
                out_guard[ft % 2] = 1
                P.wait("sync", "dv", comb_dv)

        for mt in range(KC):
            P.emit("vector", lambda e, m=mt: e.tensor_scalar(
                out=c_f32[:, m * RPC:(m + 1) * RPC], in0=muB[:, :],
                scalar1=0.0, scalar2=smc("in_c", m), op0=ALU.mult, op1=ALU.add))
        for mt in range(KC):
            P.emit("vector", lambda e, m=mt: e.tensor_copy(
                c_bfA[:, m * RPC:(m + 1) * RPC], c_f32[:, m * RPC:(m + 1) * RPC]),
                inc="dv")

        gatespec = [("wf", wf_h, "uf", "bf", AF.Sigmoid, 0),
                    ("wi", wi_h, "ui", "bi", AF.Sigmoid, 1),
                    ("wc", wc_h, "uc", "bc", AF.Tanh, 2)]
        xg_pe = [{} for _ in range(3)]

        def xg_dma(gi, gidx):
            t_, mt_ = divmod(gidx, KC)
            if t_ >= B:
                return
            wh = (wf_h, wi_h, wc_h)[gi]
            par = gidx % 2
            if gidx >= 2:
                P.wait("sync", "pe", xg_pe[gi][gidx - 2])
            dma("sync", "LG%d_%d" % (gi, par),
                xg_buf[gi][par][:, :1024].rearrange("p (kc c) -> p kc c", kc=KC),
                mkap(wh.ap(), mt_ * 128, [[H, 128], [128 * H, KC], [1, 128]]))

        dv_after_update = P.cnt["dv"]
        for gi in range(3):
            xg_dma(gi, 0)
            xg_dma(gi, 1)
        for t in range(B):
            cur = c_bfA if t % 2 == 0 else c_bfB
            nxt = c_bfB if t % 2 == 0 else c_bfA
            first_group = True
            for mt in range(KC):
                gidx = t * KC + mt
                for (wg, wh, ug, bg, func, gi) in gatespec:
                    key = "m%d" % ((mt * 3 + gi) % 3)
                    pm = p_m[(mt * 3 + gi) % 3]
                    sbx = xg_buf[gi][gidx % 2]
                    group_begin(key)
                    P.wait("tensor", "LG%d_%d" % (gi, gidx % 2))
                    for kc in range(KC):
                        P.emit("tensor", lambda e, p=pm, w=sbx, k=kc, t_=t:
                               e.matmul(p[:, :],
                                        w[:, k * 128:(k + 1) * 128],
                                        xnT_sl(k, t_), start=(k == 0), stop=False))
                    if first_group:
                        P.wait("tensor", "dv", dv_after_update)
                        first_group = False
                    for kc in range(KC):
                        P.emit("tensor", lambda e, p=pm, u=WG[ug], m=mt, k=kc, c=cur:
                               e.matmul(p[:, :],
                                        u[:, k * H + m * 128: k * H + m * 128 + 128],
                                        c[:, k * RPC:(k + 1) * RPC],
                                        start=False, stop=(k == KC - 1)),
                               inc="pe" if kc == KC - 1 else None)
                    pv = P.cnt["pe"]
                    xg_pe[gi][gidx] = pv
                    xg_dma(gi, gidx + 2)
                    P.wait("scalar", "pe", pv)
                    P.wait("scalar", "dv", dv_after_update if mt == 0 else dv_mt)
                    av = P.emit("scalar", lambda e, p=pm, f=func, b=bg, m=mt, g=gi:
                                e.activation(out=gates[g][:, :], in_=p[:, :], func=f,
                                             bias=smc(b, m)), inc="ac")
                    psum_owner[key] = ("ac", av)
                P.wait("vector", "ac")
                P.emit("vector", lambda e, m=mt: e.tensor_mul(
                    upd[:, :], gates[0][:, :], c_f32[:, m * RPC:(m + 1) * RPC]))
                P.emit("vector", lambda e: e.tensor_mul(
                    gates[2][:, :], gates[1][:, :], gates[2][:, :]))
                P.emit("vector", lambda e, m=mt: e.tensor_add(
                    c_f32[:, m * RPC:(m + 1) * RPC], upd[:, :], gates[2][:, :]))
                dv_mt = P.emit("vector", lambda e, m=mt, n=nxt: e.tensor_copy(
                    n[:, m * RPC:(m + 1) * RPC], c_f32[:, m * RPC:(m + 1) * RPC]),
                    inc="dv")
            dv_after_update = P.cnt["dv"]
            produce_vk(t)
            if t % 2 == 1:
                pr = t // 2
                P.wait("gpsimd", "SV0")
                P.wait("gpsimd", "SV1")
                P.wait("gpsimd", "SK")
                P.emit("gpsimd", lambda e, p_=pr: e.collective_compute(
                    "AllGather", ALU.bypass, replica_groups=[list(range(NCORES))],
                    ins=[ag_in_big.ap()[(2 * p_) * RPCB:(2 * p_ + 2) * RPCB, :].opt()],
                    outs=[ag_out_pair[p_].ap().opt()]), inc="cc")

        c_fin = c_bfA if B % 2 == 0 else c_bfB

        P.barrier()

        phaseA_dv[0] = P.cnt["dv"]

        uoc_pe = {}
        for ot in range(KC):
            ustr = uo_strip[ot % 2]
            if ot >= 2:
                P.wait("sync", "pe", uoc_pe[ot - 2])
            dma("sync", "LR%d" % (ot % 2),
                ustr[:, :1024].rearrange("p (mc c) -> p mc c", mc=KC),
                mkap(uo_h.ap(), ot * 128, [[H, 128], [128 * H, KC], [1, 128]]))
            key = "base"
            pm = p_base
            group_begin(key)
            P.wait("tensor", "LR%d" % (ot % 2))
            for mc in range(KC):
                P.emit("tensor", lambda e, p=pm, s=ustr, m=mc:
                       e.matmul(p[:, :], s[:, m * 128:(m + 1) * 128],
                                c_fin[:, m * RPC:(m + 1) * RPC],
                                start=(m == 0), stop=(m == KC - 1)),
                       inc="pe" if mc == KC - 1 else None)
            pv = P.cnt["pe"]
            uoc_pe[ot] = pv
            P.wait("vector", "pe", pv)
            dvv = P.emit("vector", lambda e, o=ot, p=pm: e.tensor_copy(
                uocT[:, o * RPC:(o + 1) * RPC], p[:, :]), inc="dv")
            psum_owner[key] = ("dv", dvv)

        P.barrier()

        for jt in range(ET):
            par = jt % 2
            buf = (scr, scr2)[par]
            if scr_guard[par]:
                P.wait("sync", "dv", scr_guard[par])
            dma("sync", "L%d" % par, buf[:, :RPC],
                biasT_own[jt * 128:(jt + 1) * 128, :])
            P.wait("vector", "L%d" % par)
            dvv = P.emit("vector", lambda e, b=buf[:, :RPC], j=jt:
                         e.tensor_copy(biasT_sb[:, j * RPC:(j + 1) * RPC], b), inc="dv")
            scr_guard[par] = dvv
        for kc in range(KC):
            for hh in range(2):
                par = (kc * 2 + hh) % 2
                buf = (scr, scr2)[par]
                if scr_guard[par]:
                    P.wait("sync", "dv", scr_guard[par])
                dma("sync", "L%d" % par, buf[:, :],
                    woT[kc * 128:(kc + 1) * 128, hh * 512:(hh + 1) * 512])
                P.wait("vector", "L%d" % par)
                dvv = P.emit("vector", lambda e, b=buf[:, :], k=kc, h_=hh:
                             e.tensor_copy(woT_sb[:, k * H + h_ * 512:
                                                  k * H + h_ * 512 + 512], b),
                             inc="dv")
                scr_guard[par] = dvv

        P.barrier()

        produce(0)
        produce(1)
        for b in range(B):
            consume(b)
            if b + 2 < B:
                produce(b + 2)

        P.wait("gpsimd", "SO0")
        P.wait("gpsimd", "SO1")

        block = e(nc.Block())

        @block.sync
        def _(sync):
            for fn in P.ops["sync"]:
                fn(sync)

        @block.gpsimd
        def _(gpsimd):
            for fn in P.ops["gpsimd"]:
                fn(gpsimd)

        @block.tensor
        def _(tensor):
            for fn in P.ops["tensor"]:
                fn(tensor)

        @block.vector
        def _(vector):
            for fn in P.ops["vector"]:
                fn(vector)

        @block.scalar
        def _(scalar):
            for fn in P.ops["scalar"]:
                fn(scalar)

    return nc


_NC_CACHE = None


def kernel(**inputs):
    global _NC_CACHE
    x = np.asarray(inputs["L0_attn_x"], dtype=np.float32)
    sin_t, cos_t = _rope_tables()
    biasT = _bias_matrix(np.asarray(inputs["L0_attn_w"], np.float32))
    uv_w = np.asarray(inputs["L0_attn_uv_w"], np.float32)
    uv_b = np.asarray(inputs["L0_attn_uv_b"], np.float32)

    common = dict(
        ident_in=np.eye(128, dtype=np.float32),
        wuT=np.ascontiguousarray(uv_w[:E].T),
        wvT=np.ascontiguousarray(uv_w[E:2 * E].T),
        wbT=np.ascontiguousarray(uv_w[2 * E:].T),
        wbTs=np.ascontiguousarray(uv_w[2 * E:].T[:, _PERM]),
        owT=np.ascontiguousarray(np.asarray(inputs["L0_attn_o_w"], np.float32).T),
        woT=np.ascontiguousarray(np.asarray(inputs["L0_ffn_wo"], np.float32).T),
        wfT=np.ascontiguousarray(np.asarray(inputs["L0_ffn_wf"], np.float32).T),
        ufT=np.ascontiguousarray(np.asarray(inputs["L0_ffn_uf"], np.float32).T),
        wiT=np.ascontiguousarray(np.asarray(inputs["L0_ffn_wi"], np.float32).T),
        uiT=np.ascontiguousarray(np.asarray(inputs["L0_ffn_ui"], np.float32).T),
        wcT=np.ascontiguousarray(np.asarray(inputs["L0_ffn_wc"], np.float32).T),
        ucT=np.ascontiguousarray(np.asarray(inputs["L0_ffn_uc"], np.float32).T),
        uoT=np.ascontiguousarray(np.asarray(inputs["L0_ffn_uo"], np.float32).T),
        ln_w=np.asarray(inputs["L0_attn_ln_w"], np.float32),
        ln_b=np.asarray(inputs["L0_attn_ln_b"], np.float32),
        o_b=np.asarray(inputs["L0_attn_o_b"], np.float32),
        bf=np.asarray(inputs["L0_ffn_bf"], np.float32),
        bi=np.asarray(inputs["L0_ffn_bi"], np.float32),
        bc=np.asarray(inputs["L0_ffn_bc"], np.float32),
        bo=np.asarray(inputs["L0_ffn_bo"], np.float32),
        in_c=np.asarray(inputs["L0_ffn_in_c"], np.float32),
        uvb_u=np.ascontiguousarray(uv_b[:E]),
        uvb_v=np.ascontiguousarray(uv_b[E:2 * E]),
        uvb_b=np.ascontiguousarray(uv_b[2 * E:]),
        uvb_bs=np.ascontiguousarray(uv_b[2 * E:][_PERM]),
        gamma=np.asarray(inputs["L0_attn_gamma"], np.float32),
        beta=np.asarray(inputs["L0_attn_beta"], np.float32),
        gammas=np.ascontiguousarray(np.asarray(inputs["L0_attn_gamma"], np.float32)[:, _PERM]),
        betas=np.ascontiguousarray(np.asarray(inputs["L0_attn_beta"], np.float32)[:, _PERM]),
    )

    xT = np.ascontiguousarray(x.transpose(0, 2, 1))
    in_maps = []
    for c in range(NCORES):
        r0, r1 = c * RPC, (c + 1) * RPC
        m = dict(common)
        m["x_own"] = np.ascontiguousarray(xT[:, :, r0:r1])
        m["biasT_own"] = np.ascontiguousarray(biasT[:, r0:r1])
        st = sin_t[r0:r1].T
        ct = cos_t[r0:r1].T
        m["sinT"] = np.ascontiguousarray(np.concatenate([-st, st], axis=0))
        m["cosT"] = np.ascontiguousarray(np.concatenate([ct, ct], axis=0))
        in_maps.append(m)

    if _NC_CACHE is None:
        _NC_CACHE = build_program()
    res = run_bass_kernel_spmd(_NC_CACHE, in_maps, core_ids=list(range(NCORES)))
    shards = [res.results[c]["y_own"] for c in range(NCORES)]
    outT = np.concatenate(shards, axis=2)
    return np.ascontiguousarray(outT.transpose(0, 2, 1)).astype(np.float32)


# version 12
# speedup vs baseline: 1.0486x; 1.0078x over previous
import numpy as np

import concourse.bass as bass
import concourse.mybir as mybir
from concourse.bass_utils import run_bass_kernel_spmd

F32 = mybir.dt.float32
BF16 = mybir.dt.bfloat16
AF = mybir.ActivationFunctionType
ALU = mybir.AluOpType

NCORES = 8
B = 16
T = 2048
H = 1024
E = 2048
S = 128
RPC = T // NCORES
L = 512
INV_SQRT_S = 1.0 / float(np.sqrt(S))
EPS = 1e-5
KC = H // 128
ET = E // 128
RPCB = RPC + 16
_PERM = np.r_[64:128, 0:64]


def _rope_tables():
    try:
        import jax
        cpu = jax.devices("cpu")[0]
        import jax.numpy as jnp
        with jax.default_device(cpu):
            pos = jnp.arange(T, dtype=jnp.float32)
            inv_freq = 10000.0 ** (jnp.arange(64, dtype=jnp.float32) / 64)
            sinusoid = pos[:, None] * inv_freq[None, :]
            return (np.asarray(jnp.sin(sinusoid)), np.asarray(jnp.cos(sinusoid)))
    except Exception:
        pos = np.arange(T, dtype=np.float32)
        inv_freq = (10000.0 ** (np.arange(64, dtype=np.float32) /
                                np.float32(64))).astype(np.float32)
        sinusoid = pos[:, None] * inv_freq[None, :]
        return np.sin(sinusoid).astype(np.float32), np.cos(sinusoid).astype(np.float32)


def _bias_matrix(w):
    n = w.shape[0]
    rel = np.arange(T)[None, :] - np.arange(T)[:, None] + (L - 1)
    idx = np.where(rel < 0, rel + n, rel)
    idx = np.clip(idx, 0, n - 1)
    return np.ascontiguousarray(w[idx].T)


class Prog:
    ENGS = ("sync", "gpsimd", "tensor", "vector", "scalar")

    def __init__(self, nc, es):
        self.nc = nc
        self.es = es
        self.ops = {e: [] for e in self.ENGS}
        self.sem = {}
        self.cnt = {}
        self.dma_sems = {}
        for name in ("cc", "pe", "dv", "ac",
                     "b_sync", "b_gpsimd", "b_tensor", "b_vector", "b_scalar"):
            self.sem[name] = es.enter_context(nc.semaphore(name))
            self.cnt[name] = 0

    def dsem(self, name, issuer):
        if name not in self.sem:
            self.sem[name] = self.es.enter_context(self.nc.semaphore(name))
            self.cnt[name] = 0
            self.dma_sems[name] = issuer
        return name

    def wait(self, eng, sem, val=None):
        v = self.cnt[sem] if val is None else val
        h = self.sem[sem]
        self.ops[eng].append(lambda e, h=h, v=v: e.wait_ge(h, v))

    def emit(self, eng, fn, inc=None, amt=1):
        if eng == "vector":
            self.wait("vector", "dv")
            h = self.sem["dv"]
            self.cnt["dv"] += 1
            self.ops[eng].append(lambda e, fn=fn, h=h: fn(e).then_inc(h, 1))
            return self.cnt["dv"]
        if inc is None:
            self.ops[eng].append(fn)
            return None
        h = self.sem[inc]
        self.cnt[inc] += amt
        self.ops[eng].append(lambda e, fn=fn, h=h, a=amt: fn(e).then_inc(h, a))
        return self.cnt[inc]

    def barrier(self):
        for name, issuer in self.dma_sems.items():
            if self.cnt[name]:
                self.wait(issuer, name)
        self.wait("gpsimd", "cc")
        self.wait("vector", "dv")
        self.wait("scalar", "ac")
        self.wait("tensor", "pe")
        for e in self.ENGS:
            h = self.sem["b_" + e]
            self.cnt["b_" + e] += 1
            self.ops[e].append(lambda eng, h=h: eng.nop().then_inc(h, 1))
        for e in self.ENGS:
            for o in self.ENGS:
                if o != e:
                    self.wait(e, "b_" + o)


def build_program():
    nc = bass.Bass("TRN2", target_bir_lowering=False)
    dt = nc.dram_tensor

    x_own = dt("x_own", [B, H, RPC], F32, kind="ExternalInput").ap()
    biasT_own = dt("biasT_own", [T, RPC], F32, kind="ExternalInput").ap()
    sinT_in = dt("sinT", [128, RPC], F32, kind="ExternalInput").ap()
    cosT_in = dt("cosT", [128, RPC], F32, kind="ExternalInput").ap()
    ident_in = dt("ident_in", [128, 128], F32, kind="ExternalInput").ap()

    wuT = dt("wuT", [H, E], F32, kind="ExternalInput").ap()
    wvT = dt("wvT", [H, E], F32, kind="ExternalInput").ap()
    wbT = dt("wbT", [H, S], F32, kind="ExternalInput").ap()
    wbTs = dt("wbTs", [H, S], F32, kind="ExternalInput").ap()
    owT = dt("owT", [E, H], F32, kind="ExternalInput").ap()
    woT = dt("woT", [H, H], F32, kind="ExternalInput").ap()
    gateT = {g: dt(g + "T", [H, H], F32, kind="ExternalInput").ap()
             for g in ("wf", "uf", "wi", "ui", "wc", "uc", "uo")}

    vec_in = {}
    for name, n in (("ln_w", H), ("ln_b", H), ("o_b", H), ("bf", H), ("bi", H),
                    ("bc", H), ("bo", H), ("in_c", H), ("uvb_u", E), ("uvb_b", S), ("uvb_bs", S)):
        vec_in[name] = dt(name, [n], F32, kind="ExternalInput").ap()
    uvb_v_in = dt("uvb_v", [E], F32, kind="ExternalInput").ap()
    gamma_in = dt("gamma", [2, 128], F32, kind="ExternalInput").ap()
    gammas_in = dt("gammas", [2, 128], F32, kind="ExternalInput").ap()
    beta_in = dt("beta", [2, 128], F32, kind="ExternalInput").ap()
    betas_in = dt("betas", [2, 128], F32, kind="ExternalInput").ap()

    y_own = dt("y_own", [B, H, RPC], F32, kind="ExternalOutput").ap()

    wu_h = dt("wu_h", [H, E], BF16)
    wv_h = dt("wv_h", [H, E], BF16)
    wb_h = dt("wb_h", [H, S], BF16)
    wbs_h = dt("wbs_h", [H, S], BF16)
    ow_h = dt("ow_h", [E, H], BF16)
    uo_h = dt("uo_h", [H, H], BF16)
    wf_h = dt("wf_h", [H, H], BF16)
    wi_h = dt("wi_h", [H, H], BF16)
    wc_h = dt("wc_h", [H, H], BF16)
    ag_in_big = dt("ag_in_big", [B * RPCB, E], BF16)
    ag_out_pair = [dt(f"ag_out_p{p}", [NCORES * 2 * RPCB, E], BF16)
                   for p in range(B // 2)]

    from contextlib import ExitStack
    es = ExitStack()
    with es:
        e = es.enter_context

        xnT = e(nc.sbuf_tensor("xnT", [128, KC * B * RPC], BF16))
        bb = [e(nc.sbuf_tensor(f"bb{i}", [128, 8192], BF16)) for i in range(6)]
        aux_h = e(nc.sbuf_tensor("aux_h", [128, 3072], BF16))
        c_f32 = e(nc.sbuf_tensor("c_f32", [128, KC * RPC], F32))
        c_bfA = e(nc.sbuf_tensor("c_bfA", [128, KC * RPC], BF16))
        c_bfB = e(nc.sbuf_tensor("c_bfB", [128, KC * RPC], BF16))
        scr = e(nc.sbuf_tensor("scr", [128, 512], F32))
        scr2 = e(nc.sbuf_tensor("scr2", [128, 512], F32))
        gates = [e(nc.sbuf_tensor(f"gate{i}", [128, RPC], F32)) for i in range(3)]
        upd = e(nc.sbuf_tensor("upd", [128, RPC], F32))
        sq_h = e(nc.sbuf_tensor("sq_h", [128, RPC], BF16))
        tmp_h = e(nc.sbuf_tensor("tmp_h", [128, RPC], BF16))
        muB = e(nc.sbuf_tensor("muB", [128, RPC], BF16))
        rsB = e(nc.sbuf_tensor("rsB", [128, RPC], BF16))
        hT_b = e(nc.sbuf_tensor("hT_b", [128, RPC], BF16))
        xres = [e(nc.sbuf_tensor(f"xres{i}", [128, RPC], F32)) for i in range(2)]
        outs = [e(nc.sbuf_tensor(f"outs{i}", [128, RPC], F32)) for i in range(2)]
        stat = e(nc.sbuf_tensor("stat", [1, 2 * RPC], F32))
        uvbv_row = e(nc.sbuf_tensor("uvbv_row", [1, E], BF16))
        ones_col = e(nc.sbuf_tensor("ones_col", [128, 1], BF16))
        ones_row = e(nc.sbuf_tensor("ones_row", [1, 128], BF16))
        ones_row_f = e(nc.sbuf_tensor("ones_row_f", [1, 128], F32))
        ident_h = e(nc.sbuf_tensor("ident_h", [128, 128], BF16))
        sin_sb = e(nc.sbuf_tensor("sin_sb", [128, RPC], F32))
        cos_sb = e(nc.sbuf_tensor("cos_sb", [128, RPC], F32))
        sm = e(nc.sbuf_tensor("sm", [128, 96], F32))

        p_m = [e(nc.psum_tensor(f"p_m{i}", [128, RPC], F32)) for i in range(3)]
        p_v = [e(nc.psum_tensor(f"p_v{i}", [128, 512], F32)) for i in range(2)]
        p_base = e(nc.psum_tensor("p_base", [128, RPC], F32))
        p_h = e(nc.psum_tensor("p_h", [128, RPC], F32))
        p_stat = e(nc.psum_tensor("p_stat", [128, 512], F32))
        p_s = [p_stat[0:1, 0:RPC], p_stat[0:1, RPC:2 * RPC]]
        p_b = p_stat

        P = Prog(nc, es)
        psum_owner = {}

        def group_begin(key):
            if key in psum_owner:
                s, v = psum_owner[key]
                P.wait("tensor", s, v)

        smcol = {}
        _c = [0]
        def alloc_sm(name, ncols):
            smcol[name] = _c[0]
            _c[0] += ncols
        for name in ("ln_w", "ln_b", "o_b", "bf", "bi", "bc", "bo", "in_c"):
            alloc_sm(name, KC)
        alloc_sm("uvb_u", ET)
        alloc_sm("uvb_b", 1)
        alloc_sm("uvb_bs", 1)
        alloc_sm("gamma", 2)
        alloc_sm("beta", 2)
        alloc_sm("gammas", 2)
        alloc_sm("betas", 2)
        alloc_sm("g0p", 1)
        alloc_sm("b0p", 1)
        alloc_sm("g0ps", 1)
        alloc_sm("b0ps", 1)
        assert _c[0] <= 96

        def smc(name, i=0):
            return sm[:, smcol[name] + i: smcol[name] + i + 1]

        WG = {"uf": bb[1], "ui": bb[3], "uc": bb[5]}
        xg_buf = [[bb[4][:, (2 * g + i) * 1024:(2 * g + i + 1) * 1024]
                   for i in range(2)] for g in range(3)]
        wb_vk = bb[0][:, 0:1024]
        wv_vk = [bb[0][:, 1024 + i * 512: 1024 + (i + 1) * 512] for i in range(2)]
        kernelT = bb[0]
        kT_sb = bb[0][:, 4096:6144]
        qT_bufs = [bb[0][:, 6144:6400], bb[0][:, 6400:6656]]
        uT_bufs = [bb[1][:, 0:4096], bb[1][:, 4096:8192]]
        eT = bb[3][:, 4096:8192]
        biasT_sb = bb[2][:, 0:4096]
        uocT = bb[2][:, 4096:6144]
        xnnT = bb[2][:, 6144:8192]
        ow_strip = [bb[3][:, i * 2048:(i + 1) * 2048] for i in range(2)]
        woT_sb = bb[4]
        wu_strip = [bb[5][:, i * 1024:(i + 1) * 1024] for i in range(2)]
        wv_buf = [bb[5][:, 2048 + i * 512: 2048 + (i + 1) * 512] for i in range(2)]
        wb_buf = bb[5][:, 3072:4096]
        vstrip = [bb[5][:, 4096 + i * 2048: 4096 + (i + 1) * 2048] for i in range(2)]
        vout = [aux_h[:, i * 512:(i + 1) * 512] for i in range(2)]
        uo_strip = [aux_h[:, 1024 + i * 1024: 1024 + (i + 1) * 1024] for i in range(2)]
        x_bf = c_bfA
        uvbv_hlp = c_bfA
        cast_h = c_bfB

        import dataclasses as _dc

        def mkap(base_ap, offset, dims):
            return _dc.replace(base_ap, offset=offset,
                               ap=type(base_ap.ap)(dims))

        def dma(eng, sem, out, in_, slow=False):
            P.dsem(sem, eng)
            if slow:
                return P.emit(eng, lambda e, o=out, i=in_: e.dma_start(
                    out=o, in_=i, allow_slow_non_contiguous=True), inc=sem, amt=16)
            return P.emit(eng, lambda e, o=out, i=in_: e.dma_start(out=o, in_=i),
                          inc=sem, amt=16)

        for name in ("ln_w", "ln_b", "o_b", "bf", "bi", "bc", "bo", "in_c"):
            dma("sync", "LS", sm[:, smcol[name]:smcol[name] + KC],
                vec_in[name].rearrange("(t p) -> p t", p=128), slow=True)
        dma("sync", "LS", sm[:, smcol["uvb_u"]:smcol["uvb_u"] + ET],
            vec_in["uvb_u"].rearrange("(t p) -> p t", p=128), slow=True)
        dma("sync", "LS", sm[:, smcol["uvb_b"]:smcol["uvb_b"] + 1],
            vec_in["uvb_b"].rearrange("(t p) -> p t", p=128), slow=True)
        dma("sync", "LS", sm[:, smcol["uvb_bs"]:smcol["uvb_bs"] + 1],
            vec_in["uvb_bs"].rearrange("(t p) -> p t", p=128), slow=True)
        dma("sync", "LS", sm[:, smcol["gamma"]:smcol["gamma"] + 2],
            gamma_in.rearrange("h s -> s h"), slow=True)
        dma("sync", "LS", sm[:, smcol["beta"]:smcol["beta"] + 2],
            beta_in.rearrange("h s -> s h"), slow=True)
        dma("sync", "LS", sm[:, smcol["gammas"]:smcol["gammas"] + 2],
            gammas_in.rearrange("h s -> s h"), slow=True)
        dma("sync", "LS", sm[:, smcol["betas"]:smcol["betas"] + 2],
            betas_in.rearrange("h s -> s h"), slow=True)
        dma("sync", "LS", scr[:, :128], ident_in)
        dma("sync", "LS", sin_sb[:, :], sinT_in)
        dma("sync", "LS", cos_sb[:, :], cosT_in)

        P.wait("vector", "LS")
        P.emit("vector", lambda e: e.memset(ones_col[:, :], 1.0))
        P.emit("vector", lambda e: e.memset(ones_row[:, :], 1.0))
        P.emit("vector", lambda e: e.memset(ones_row_f[:, :], 1.0))
        P.emit("vector", lambda e: e.tensor_copy(ident_h[:, :], scr[:, :128]))
        for q in range(4):
            dma("sync", "L1", scr2[0:1, :512],
                uvb_v_in[q * 512:(q + 1) * 512].rearrange("(a m) -> a m", a=1))
            P.wait("vector", "L1")
            dvq = P.emit("vector", lambda e, q_=q: e.tensor_copy(
                uvbv_hlp[0:1, q_ * 512:(q_ + 1) * 512], scr2[0:1, :512]), inc="dv")
            P.wait("sync", "dv", dvq)
        P.wait("gpsimd", "dv")
        dma("gpsimd", "SU", uvbv_row[0:1, :], uvbv_hlp[0:1, 0:E])
        P.emit("vector", lambda e: e.tensor_scalar_mul(smc("g0p"), smc("gamma", 0), INV_SQRT_S))
        P.emit("vector", lambda e: e.tensor_scalar_mul(smc("b0p"), smc("beta", 0), INV_SQRT_S))
        P.emit("vector", lambda e: e.tensor_scalar_mul(smc("g0ps"), smc("gammas", 0), INV_SQRT_S))
        P.emit("vector", lambda e: e.tensor_scalar_mul(smc("b0ps"), smc("betas", 0), INV_SQRT_S),
               inc="dv")

        scr_guard = [0, 0]

        def cast_chunks(chunks):
            pass

        def cast_resident(src_ap, dst, n_out):
            i = 0
            for kc in range(KC):
                for off in range(0, n_out, 512):
                    wdt = min(512, n_out - off)
                    par = i % 2
                    buf = (scr, scr2)[par]
                    if scr_guard[par]:
                        P.wait("sync", "dv", scr_guard[par])
                    dma("sync", "L%d" % par, buf[:, :wdt],
                        src_ap[kc * 128:(kc + 1) * 128, off:off + wdt])
                    P.wait("vector", "L%d" % par)
                    dvv = P.emit("vector", lambda e, b=buf[:, :wdt],
                                 d=dst[:, kc * n_out + off: kc * n_out + off + wdt]:
                                 e.tensor_copy(d, b), inc="dv")
                    scr_guard[par] = dvv
                    i += 1

        cast_dram_guard = [0, 0]

        def cast_to_dram(src_ap, dst_dram, rows, cols):
            i = 0
            for r0 in range(0, rows, 128):
                for off in range(0, cols, 512):
                    wdt = min(512, cols - off)
                    par = i % 2
                    buf = (scr, scr2)[par]
                    if scr_guard[par]:
                        P.wait("sync", "dv", scr_guard[par])
                    dma("sync", "L%d" % par, buf[:, :wdt],
                        src_ap[r0:r0 + 128, off:off + wdt])
                    hbuf = cast_h[:, par * 1024: par * 1024 + wdt]
                    P.wait("vector", "L%d" % par)
                    if cast_dram_guard[par]:
                        P.wait("vector", "SC%d" % par)
                    dvv = P.emit("vector", lambda e, b=buf[:, :wdt], h=hbuf:
                                 e.tensor_copy(h, b), inc="dv")
                    scr_guard[par] = dvv
                    P.wait("gpsimd", "dv", dvv)
                    dma("gpsimd", "SC%d" % par, dst_dram.ap()[r0:r0 + 128, off:off + wdt], hbuf)
                    cast_dram_guard[par] = 1
                    i += 1

        for g in ("uf", "ui", "uc"):
            cast_resident(gateT[g], WG[g], H)
        cast_to_dram(gateT["wf"], wf_h, H, H)
        cast_to_dram(gateT["wi"], wi_h, H, H)
        cast_to_dram(gateT["wc"], wc_h, H, H)
        cast_to_dram(gateT["uo"], uo_h, H, H)
        cast_to_dram(wuT, wu_h, H, E)
        cast_to_dram(wvT, wv_h, H, E)
        cast_to_dram(wbT, wb_h, H, S)
        cast_to_dram(wbTs, wbs_h, H, S)
        cast_to_dram(owT, ow_h, E, H)

        P.barrier()

        st_mu = stat[0:1, 0:RPC]
        st_b = stat[0:1, RPC:2 * RPC]

        stats_guard = {"sq": 0, "bank": 0}

        def ln_stats(src_slices, src_dv=0):
            P.wait("tensor", "dv", max(src_dv, stats_guard["bank"]))
            P.wait("scalar", "dv", max(stats_guard["sq"], src_dv))
            if P.cnt.get("SK"):
                P.wait("scalar", "SK")
                P.wait("vector", "SK")
            for kc in range(KC):
                P.emit("tensor", lambda e, s=src_slices[kc], k=kc:
                       e.matmul(p_s[0], ones_col[:, :], s,
                                start=(k == 0), stop=(k == KC - 1)),
                       inc="pe")
            for kc in range(KC):
                sq = (sq_h, tmp_h)[kc % 2]
                if kc >= 2:
                    P.wait("scalar", "pe", pe_sq[kc - 2])
                av = P.emit("scalar", lambda e, s=src_slices[kc], q=sq:
                            e.activation(out=q[:, :], in_=s, func=AF.Square), inc="ac")
                P.wait("tensor", "ac", av)
                pv = P.emit("tensor", lambda e, k=kc, q=sq:
                            e.matmul(p_s[1], ones_col[:, :], q[:, :],
                                     start=(k == 0), stop=(k == KC - 1)),
                            inc="pe")
                if kc == 0:
                    pe_sq = {}
                pe_sq[kc] = pv
            P.wait("vector", "pe")
            P.emit("vector", lambda e: e.tensor_scalar_mul(st_mu, p_s[0], 1.0 / H))
            P.emit("vector", lambda e: e.tensor_scalar_mul(st_b, p_s[1], 1.0 / H))
            P.emit("vector", lambda e: e.tensor_mul(p_s[0], st_mu, st_mu))
            P.emit("vector", lambda e: e.tensor_sub(st_b, st_b, p_s[0]))
            P.emit("vector", lambda e: e.tensor_scalar_add(st_b, st_b, EPS), inc="dv")
            P.wait("scalar", "dv")
            P.emit("scalar", lambda e: e.activation(out=st_b, in_=st_b, func=AF.Sqrt),
                   inc="ac")
            P.wait("vector", "ac")
            recip_dv = P.emit("vector", lambda e: e.reciprocal(st_b, st_b), inc="dv")
            P.wait("tensor", "dv", recip_dv)
            P.emit("tensor", lambda e: e.matmul(p_b[:, 0:RPC], ones_row_f[:, :], st_mu,
                                                start=True, stop=True))
            P.emit("tensor", lambda e: e.matmul(p_b[:, RPC:2 * RPC], ones_row_f[:, :], st_b,
                                                start=True, stop=True), inc="pe")
            P.wait("vector", "pe")
            P.emit("vector", lambda e: e.tensor_copy(muB[:, :], p_b[:, 0:RPC]))
            stats_guard["bank"] = P.emit(
                "vector", lambda e: e.tensor_copy(rsB[:, :], p_b[:, RPC:2 * RPC]),
                inc="dv")

        def xnT_sl(kc, t):
            off = kc * (B * RPC) + t * RPC
            return xnT[:, off: off + RPC]

        for t in range(B):
            P.wait("vector", "pe")
            for kc in range(KC):
                par = kc % 2
                buf = (scr, scr2)[par]
                if scr_guard[par]:
                    P.wait("sync", "dv", scr_guard[par])
                dma("sync", "L%d" % par, buf[:, :RPC],
                    x_own[t, kc * 128:(kc + 1) * 128, :])
                P.wait("vector", "L%d" % par)
                dvv = P.emit("vector", lambda e, b=buf[:, :RPC],
                             d=x_bf[:, kc * RPC:(kc + 1) * RPC]:
                             e.tensor_copy(d, b), inc="dv")
                scr_guard[par] = dvv
            xs = [x_bf[:, kc * RPC:(kc + 1) * RPC] for kc in range(KC)]
            ln_stats(xs, src_dv=dvv)
            for kc in range(KC):
                P.emit("vector", lambda e, s=xs[kc]: e.tensor_sub(tmp_h[:, :], s, muB[:, :]))
                stats_guard["sq"] = P.emit(
                    "vector", lambda e, d=xnT_sl(kc, t):
                    e.tensor_mul(d, tmp_h[:, :], rsB[:, :]), inc="dv")

        P.barrier()

        def xnn_make(b):
            xs = [xnT_sl(kc, b) for kc in range(KC)]
            ln_stats(xs)
            P.wait("vector", "pe", produce_pe_end[0])
            for kc in range(KC):
                P.emit("vector", lambda e, s=xs[kc]: e.tensor_sub(tmp_h[:, :], s, muB[:, :]))
                P.emit("vector", lambda e: e.tensor_mul(sq_h[:, :], tmp_h[:, :], rsB[:, :]))
                stats_guard["sq"] = P.emit("vector", lambda e, k=kc: e.tensor_scalar(
                    out=xnnT[:, k * RPC:(k + 1) * RPC], in0=sq_h[:, :],
                    scalar1=smc("ln_w", k), scalar2=smc("ln_b", k),
                    op0=ALU.mult, op1=ALU.add), inc="dv")

        base_sw = c_f32[:, 0:RPC]
        relu_t = c_f32[:, RPC:2 * RPC]
        rope_dv_last = [0]
        produce_pe_end = [0]
        cc_of = {}
        qk_pe_end = {}
        eT_dv_end = {}

        def rope_make(dst, gcol, bcol, gscol, bscol, a0, a1,
                      bsrc=None, bssrc=None, tq=None):
            bsrc = gates[2][:, :] if bsrc is None else bsrc
            bssrc = base_sw if bssrc is None else bssrc
            tq = upd[:, :] if tq is None else tq
            P.emit("vector", lambda e, g=gcol, b_=bcol: e.tensor_scalar(
                out=tq, in0=bsrc, scalar1=g, scalar2=b_,
                op0=ALU.mult, op1=ALU.add))
            P.emit("vector", lambda e, g=gscol, b_=bscol: e.tensor_scalar(
                out=a0, in0=bssrc, scalar1=g, scalar2=b_,
                op0=ALU.mult, op1=ALU.add))
            P.emit("vector", lambda e: e.tensor_mul(a1, a0, sin_sb[:, :]))
            P.emit("vector", lambda e: e.tensor_mul(tq, tq, cos_sb[:, :]))
            return P.emit("vector", lambda e, d=dst:
                          e.tensor_add(d, tq, a1), inc="dv")

        vout_guard = [0, 0]
        out_guard = [0, 0]
        phaseA_dv = [0]

        vk_dv_xnn = [0]

        def produce_vk_pre(b):
            xnn_make(b)
            vk_dv_xnn[0] = P.cnt["dv"]

        def produce_vk(b):
            dv_xnn = vk_dv_xnn[0]
            P.wait("tensor", "dv", dv_xnn)
            P.wait("scalar", "dv", rope_dv_last[0])
            for swp, dst_sb, bias_name in (
                    (0, xres[0][:, :], "uvb_b"),
                    (1, xres[1][:, :], "uvb_bs")):
                src = (wb_h if swp == 0 else wbs_h).ap()
                dma("sync", "LBk",
                    wb_vk[:, :1024].rearrange("p (kc c) -> p kc c", kc=KC),
                    mkap(src, 0, [[S, 128], [128 * S, KC], [1, S]]))
                group_begin("base")
                P.wait("tensor", "LBk")
                for kc in range(KC):
                    P.emit("tensor", lambda e, k=kc:
                           e.matmul(p_base[:, :], wb_vk[:, k * 128:(k + 1) * 128],
                                    xnnT[:, k * RPC:(k + 1) * RPC],
                                    start=(k == 0), stop=(k == KC - 1)),
                           inc="pe" if kc == KC - 1 else None)
                pv = P.cnt["pe"]
                P.wait("sync", "pe", pv)
                P.wait("scalar", "pe", pv)
                av = P.emit("scalar", lambda e, d=dst_sb, bn=bias_name: e.activation(
                    out=d, in_=p_base[:, :], func=AF.Silu,
                    bias=smc(bn)), inc="ac")
                psum_owner["base"] = ("ac", av)
            P.wait("vector", "ac", av)
            kv = rope_make(tmp_h[:, :], smc("gamma", 1), smc("beta", 1),
                           smc("gammas", 1), smc("betas", 1),
                           outs[0][:, :], outs[1][:, :],
                           bsrc=xres[0][:, :], bssrc=xres[1][:, :],
                           tq=xres[0][:, :])
            rope_dv_last[0] = kv
            stats_guard["sq"] = max(stats_guard["sq"], kv)
            P.wait("gpsimd", "dv", kv)
            dma("gpsimd", "SK",
                ag_in_big.ap()[b * RPCB + RPC: b * RPCB + RPC + 16, :]
                .rearrange("a (c i) -> (a c) i", i=RPC),
                tmp_h[:, :])
            for nch in range(4):
                for kc in range(KC):
                    par = (nch * KC + kc) % 2
                    wb2 = wv_vk[par]
                    if nch * KC + kc >= 2:
                        P.wait("sync", "pe", v_pe[nch * KC + kc - 2])
                    dma("sync", "LVk%d" % par, wb2[:, :],
                        wv_h.ap()[kc * 128:(kc + 1) * 128, nch * 512:(nch + 1) * 512])
                    for rt in range(2):
                        key = "pv%d" % rt
                        if kc == 0:
                            group_begin(key)
                            P.emit("tensor", lambda e, p=p_v[rt], n=nch: e.matmul(
                                p[:, :], ones_row[:, :],
                                uvbv_row[0:1, n * 512:(n + 1) * 512],
                                start=True, stop=False))
                        P.wait("tensor", "LVk%d" % par)
                        pv = P.emit("tensor", lambda e, p=p_v[rt], w=wb2, r=rt, k=kc:
                                    e.matmul(p[:, :],
                                             xnnT[:, k * RPC + r * 128: k * RPC + r * 128 + 128],
                                             w[:, :], start=False, stop=(k == KC - 1)),
                                    inc="pe")
                        if nch == 0 and kc == 0 and rt == 0:
                            v_pe = {}
                        v_pe[nch * KC + kc] = pv
                        if kc == KC - 1:
                            vo = vout[rt]
                            P.wait("scalar", "pe", pv)
                            if vout_guard[rt]:
                                P.wait("scalar", "SV%d" % rt)
                            av = P.emit("scalar", lambda e, p=p_v[rt], o=vo: e.activation(
                                out=o, in_=p[:, :], func=AF.Silu), inc="ac")
                            psum_owner[key] = ("ac", av)
                            P.wait("gpsimd", "ac", av)
                            dma("gpsimd", "SV%d" % rt,
                                ag_in_big.ap()[b * RPCB + rt * 128: b * RPCB + (rt + 1) * 128,
                                               nch * 512:(nch + 1) * 512], vo)
                            vout_guard[rt] = 1
            produce_pe_end[0] = P.cnt["pe"]

        def produce(b):
            uT = uT_bufs[b % 2]
            qT_sb = qT_bufs[b % 2]
            xnn_make(b)
            dv_xnn = P.cnt["dv"]
            P.wait("tensor", "dv", dv_xnn)
            P.wait("scalar", "dv", rope_dv_last[0])
            for swp, w_src, dst_sb, bias_name in (
                    (0, wb_h, gates[2][:, :], "uvb_b"),
                    (1, None, base_sw, "uvb_bs")):
                src = (wb_h if swp == 0 else wbs_h).ap()
                dma("sync", "LB",
                    wb_buf[:, :1024].rearrange("p (kc c) -> p kc c", kc=KC),
                    mkap(src, 0, [[S, 128], [128 * S, KC], [1, S]]))
                group_begin("base")
                P.wait("tensor", "LB")
                for kc in range(KC):
                    P.emit("tensor", lambda e, k=kc:
                           e.matmul(p_base[:, :], wb_buf[:, k * 128:(k + 1) * 128],
                                    xnnT[:, k * RPC:(k + 1) * RPC],
                                    start=(k == 0), stop=(k == KC - 1)),
                           inc="pe" if kc == KC - 1 else None)
                pv = P.cnt["pe"]
                P.wait("sync", "pe", pv)
                P.wait("scalar", "pe", pv)
                av = P.emit("scalar", lambda e, d=dst_sb, bn=bias_name: e.activation(
                    out=d, in_=p_base[:, :], func=AF.Silu,
                    bias=smc(bn)), inc="ac")
                psum_owner["base"] = ("ac", av)
            P.wait("vector", "ac", av)
            a0 = gates[0][:, :]
            a1 = gates[1][:, :]
            if b >= 2:
                P.wait("vector", "pe", qk_pe_end[b - 2])
            qv = rope_make(qT_sb, smc("g0p"), smc("b0p"),
                           smc("g0ps"), smc("b0ps"), a0, a1)
            rope_dv_last[0] = qv
            if b >= 2:
                P.wait("scalar", "dv", eT_dv_end[b - 2])
            for et in range(ET):
                sb = wu_strip[et % 2]
                if et >= 2:
                    P.wait("scalar", "pe", u_pe[et - 2])
                dma("scalar", "LU%d" % (et % 2),
                    sb[:, :1024].rearrange("p (kc c) -> p kc c", kc=KC),
                    mkap(wu_h.ap(), et * 128, [[E, 128], [128 * E, KC], [1, 128]]))
                key = "pv%d" % (et % 2)
                pm = p_v[et % 2][:, 0:RPC]
                group_begin(key)
                P.wait("tensor", "LU%d" % (et % 2))
                if et == 0:
                    P.wait("tensor", "dv", dv_xnn)
                for kc in range(KC):
                    pv = P.emit("tensor", lambda e, p=pm, s=sb, k=kc:
                                e.matmul(p, s[:, k * 128:(k + 1) * 128],
                                         xnnT[:, k * RPC:(k + 1) * RPC],
                                         start=(k == 0), stop=(k == KC - 1)),
                                inc="pe")
                if et == 0:
                    u_pe = {}
                u_pe[et] = pv
                P.wait("scalar", "pe", pv)
                av = P.emit("scalar", lambda e, p=pm, t_=et, u_=uT: e.activation(
                    out=u_[:, t_ * RPC:(t_ + 1) * RPC], in_=p, func=AF.Silu,
                    bias=smc("uvb_u", t_)), inc="ac")
                psum_owner[key] = ("ac", av)
            produce_pe_end[0] = P.cnt["pe"]

        def consume(b):
            uT = uT_bufs[b % 2]
            qT_sb = qT_bufs[b % 2]
            P.wait("gpsimd", "cc", b // 2 + 1)
            if b > 0:
                P.wait("gpsimd", "pe", qk_pe_end[b - 1])
            agp = ag_out_pair[b // 2]
            for r in range(NCORES):
                base_row = r * (2 * RPCB) + (b % 2) * RPCB + RPC
                blk = agp.ap()[base_row: base_row + 16, :]
                dma("gpsimd", "LK", kT_sb[:, r * RPC:(r + 1) * RPC],
                    blk.rearrange("a (c i) -> (a c) i", i=RPC))
            for jt in range(ET):
                key = "m%d" % (jt % 3)
                pm = p_m[jt % 3]
                group_begin(key)
                if jt == 0:
                    P.wait("tensor", "LK")
                P.emit("tensor", lambda e, p=pm, j=jt: e.matmul(
                    p[:, :], ident_h[:, :], biasT_sb[:, j * RPC:(j + 1) * RPC],
                    start=True, stop=False))
                pv = P.emit("tensor", lambda e, p=pm, j=jt: e.matmul(
                    p[:, :], kT_sb[:, j * 128:(j + 1) * 128], qT_sb,
                    start=False, stop=True), inc="pe")
                P.wait("scalar", "pe", pv)
                if jt > 0:
                    P.wait("scalar", "dv", sq_dv)
                av = P.emit("scalar", lambda e, p=pm: e.activation(
                    out=relu_t, in_=p[:, :], func=AF.Relu), inc="ac")
                psum_owner[key] = ("ac", av)
                P.wait("vector", "ac", av)
                sq_dv = P.emit("vector", lambda e, j=jt: e.tensor_mul(
                    kernelT[:, j * RPC:(j + 1) * RPC], relu_t, relu_t), inc="dv")
            qk_pe_end[b] = P.cnt["pe"]
            dv_kernel = P.cnt["dv"]
            for et in range(ET):
                vs = vstrip[et % 2]
                if et >= 2:
                    P.wait("gpsimd", "pe", kv_pe[et - 2])

                for q in range(2):
                    dma("gpsimd", "LW%d" % (et % 2),
                        mkap(vs, vs.offset + q * 128,
                             [[8192, 128], [256, NCORES], [1, 128]]),
                        mkap(agp.ap(),
                             ((b % 2) * RPCB + q * 128) * E + et * 128,
                             [[E, 128], [2 * RPCB * E, NCORES], [1, 128]]))
                key = "m%d" % (et % 3)
                pm = p_m[et % 3]
                group_begin(key)
                P.wait("tensor", "LW%d" % (et % 2))
                if et == 0:
                    P.wait("tensor", "dv", dv_kernel)
                for jt in range(ET):
                    pv = P.emit("tensor", lambda e, p=pm, v_=vs, j=jt: e.matmul(
                        p[:, :], v_[:, j * 128:(j + 1) * 128],
                        kernelT[:, j * RPC:(j + 1) * RPC],
                        start=(j == 0), stop=(j == ET - 1)),
                        inc="pe" if jt == ET - 1 else None)
                if et == 0:
                    kv_pe = {}
                kv_pe[et] = pv
                P.wait("vector", "pe", pv)
                dvv = P.emit("vector", lambda e, p=pm, t_=et: e.tensor_mul(
                    eT[:, t_ * RPC:(t_ + 1) * RPC], p[:, :],
                    uT[:, t_ * RPC:(t_ + 1) * RPC]), inc="dv")
                psum_owner[key] = ("dv", dvv)
            dv_eT = P.cnt["dv"]
            eT_dv_end[b] = dv_eT
            for ft in range(KC):
                osb = ow_strip[ft % 2]
                if ft >= 2:
                    P.wait("scalar", "pe", o_pe[ft - 2])
                dma("scalar", "LO%d" % (ft % 2),
                    osb[:, :2048].rearrange("p (ec c) -> p ec c", ec=ET),
                    mkap(ow_h.ap(), ft * 128, [[H, 128], [128 * H, ET], [1, 128]]))
                group_begin("h")
                P.emit("tensor", lambda e, f=ft: e.matmul(
                    p_h[:, :], ident_h[:, :], uocT[:, f * RPC:(f + 1) * RPC],
                    start=True, stop=False))
                for kc in range(KC):
                    P.emit("tensor", lambda e, f=ft, k=kc, b_=b: e.matmul(
                        p_h[:, :], woT_sb[:, k * H + f * 128: k * H + f * 128 + 128],
                        xnT_sl(k, b_), start=False, stop=(k == KC - 1)),
                        inc="pe" if kc == KC - 1 else None)
                pv_h = P.cnt["pe"]
                P.wait("scalar", "pe", pv_h)
                if ft > 0:
                    P.wait("scalar", "dv", comb_dv)
                av = P.emit("scalar", lambda e, f=ft: e.activation(
                    out=hT_b[:, :], in_=p_h[:, :], func=AF.Tanh, bias=smc("bo", f)),
                    inc="ac")
                psum_owner["h"] = ("ac", av)
                key = "m%d" % (ft % 3)
                pm = p_m[ft % 3]
                group_begin(key)
                P.emit("tensor", lambda e, p=pm, f=ft, b_=b: e.matmul(
                    p[:, :], ident_h[:, :], xnT_sl(f, b_), start=True, stop=False))
                P.wait("tensor", "LO%d" % (ft % 2))
                if ft == 0:
                    P.wait("tensor", "dv", dv_eT)
                for ec in range(ET):
                    P.emit("tensor", lambda e, p=pm, s=osb, c_=ec: e.matmul(
                        p[:, :], s[:, c_ * 128:(c_ + 1) * 128],
                        eT[:, c_ * RPC:(c_ + 1) * RPC],
                        start=False, stop=(c_ == ET - 1)),
                        inc="pe" if ec == ET - 1 else None)
                pv_o = P.cnt["pe"]
                if ft == 0:
                    o_pe = {}
                o_pe[ft] = pv_o
                xr = xres[ft % 2]
                if b == 0 and ft < 2:
                    P.wait("sync", "dv", phaseA_dv[0])
                dma("sync", "LX%d" % (ft % 2), xr[:, :],
                    x_own[b, ft * 128:(ft + 1) * 128, :])
                ob = outs[ft % 2]
                P.wait("vector", "pe", pv_o)
                P.wait("vector", "ac", av)
                if out_guard[ft % 2]:
                    P.wait("vector", "SO%d" % (ft % 2))
                P.emit("vector", lambda e, p=pm, f=ft, o=ob: e.scalar_tensor_tensor(
                    out=o[:, :], in0=p[:, :], scalar=smc("o_b", f), in1=hT_b[:, :],
                    op0=ALU.add, op1=ALU.add), inc="dv")
                psum_owner[key] = ("dv", P.cnt["dv"])
                P.wait("vector", "LX%d" % (ft % 2))
                comb_dv = P.emit("vector", lambda e, o=ob, x=xr: e.tensor_add(
                    o[:, :], o[:, :], x[:, :]), inc="dv")
                P.wait("gpsimd", "dv", comb_dv)
                dma("gpsimd", "SO%d" % (ft % 2), y_own[b, ft * 128:(ft + 1) * 128, :],
                    ob[:, :])
                out_guard[ft % 2] = 1
                P.wait("sync", "dv", comb_dv)

        for mt in range(KC):
            P.emit("vector", lambda e, m=mt: e.tensor_scalar(
                out=c_f32[:, m * RPC:(m + 1) * RPC], in0=muB[:, :],
                scalar1=0.0, scalar2=smc("in_c", m), op0=ALU.mult, op1=ALU.add))
        for mt in range(KC):
            P.emit("vector", lambda e, m=mt: e.tensor_copy(
                c_bfA[:, m * RPC:(m + 1) * RPC], c_f32[:, m * RPC:(m + 1) * RPC]),
                inc="dv")

        gatespec = [("wf", wf_h, "uf", "bf", AF.Sigmoid, 0),
                    ("wi", wi_h, "ui", "bi", AF.Sigmoid, 1),
                    ("wc", wc_h, "uc", "bc", AF.Tanh, 2)]
        xg_pe = [{} for _ in range(3)]

        def xg_dma(gi, gidx):
            t_, mt_ = divmod(gidx, KC)
            if t_ >= B:
                return
            wh = (wf_h, wi_h, wc_h)[gi]
            par = gidx % 2
            if gidx >= 2:
                P.wait("sync", "pe", xg_pe[gi][gidx - 2])
            dma("sync", "LG%d_%d" % (gi, par),
                xg_buf[gi][par][:, :1024].rearrange("p (kc c) -> p kc c", kc=KC),
                mkap(wh.ap(), mt_ * 128, [[H, 128], [128 * H, KC], [1, 128]]))

        dv_after_update = P.cnt["dv"]
        for gi in range(3):
            xg_dma(gi, 0)
            xg_dma(gi, 1)
        for t in range(B):
            produce_vk_pre(t)
            cur = c_bfA if t % 2 == 0 else c_bfB
            nxt = c_bfB if t % 2 == 0 else c_bfA
            first_group = True
            for mt in range(KC):
                gidx = t * KC + mt
                for (wg, wh, ug, bg, func, gi) in gatespec:
                    key = "m%d" % ((mt * 3 + gi) % 3)
                    pm = p_m[(mt * 3 + gi) % 3]
                    sbx = xg_buf[gi][gidx % 2]
                    group_begin(key)
                    P.wait("tensor", "LG%d_%d" % (gi, gidx % 2))
                    for kc in range(KC):
                        P.emit("tensor", lambda e, p=pm, w=sbx, k=kc, t_=t:
                               e.matmul(p[:, :],
                                        w[:, k * 128:(k + 1) * 128],
                                        xnT_sl(k, t_), start=(k == 0), stop=False))
                    if first_group:
                        P.wait("tensor", "dv", dv_after_update)
                        first_group = False
                    for kc in range(KC):
                        P.emit("tensor", lambda e, p=pm, u=WG[ug], m=mt, k=kc, c=cur:
                               e.matmul(p[:, :],
                                        u[:, k * H + m * 128: k * H + m * 128 + 128],
                                        c[:, k * RPC:(k + 1) * RPC],
                                        start=False, stop=(k == KC - 1)),
                               inc="pe" if kc == KC - 1 else None)
                    pv = P.cnt["pe"]
                    xg_pe[gi][gidx] = pv
                    xg_dma(gi, gidx + 2)
                    P.wait("scalar", "pe", pv)
                    P.wait("scalar", "dv", dv_after_update if mt == 0 else dv_mt)
                    av = P.emit("scalar", lambda e, p=pm, f=func, b=bg, m=mt, g=gi:
                                e.activation(out=gates[g][:, :], in_=p[:, :], func=f,
                                             bias=smc(b, m)), inc="ac")
                    psum_owner[key] = ("ac", av)
                P.wait("vector", "ac")
                P.emit("vector", lambda e, m=mt: e.tensor_mul(
                    upd[:, :], gates[0][:, :], c_f32[:, m * RPC:(m + 1) * RPC]))
                P.emit("vector", lambda e: e.tensor_mul(
                    gates[2][:, :], gates[1][:, :], gates[2][:, :]))
                P.emit("vector", lambda e, m=mt: e.tensor_add(
                    c_f32[:, m * RPC:(m + 1) * RPC], upd[:, :], gates[2][:, :]))
                dv_mt = P.emit("vector", lambda e, m=mt, n=nxt: e.tensor_copy(
                    n[:, m * RPC:(m + 1) * RPC], c_f32[:, m * RPC:(m + 1) * RPC]),
                    inc="dv")
            dv_after_update = P.cnt["dv"]
            produce_vk(t)
            if t % 2 == 1:
                pr = t // 2
                P.wait("gpsimd", "SV0")
                P.wait("gpsimd", "SV1")
                P.wait("gpsimd", "SK")
                P.emit("gpsimd", lambda e, p_=pr: e.collective_compute(
                    "AllGather", ALU.bypass, replica_groups=[list(range(NCORES))],
                    ins=[ag_in_big.ap()[(2 * p_) * RPCB:(2 * p_ + 2) * RPCB, :].opt()],
                    outs=[ag_out_pair[p_].ap().opt()]), inc="cc")

        c_fin = c_bfA if B % 2 == 0 else c_bfB

        P.barrier()

        phaseA_dv[0] = P.cnt["dv"]

        uoc_pe = {}
        for ot in range(KC):
            ustr = uo_strip[ot % 2]
            if ot >= 2:
                P.wait("sync", "pe", uoc_pe[ot - 2])
            dma("sync", "LR%d" % (ot % 2),
                ustr[:, :1024].rearrange("p (mc c) -> p mc c", mc=KC),
                mkap(uo_h.ap(), ot * 128, [[H, 128], [128 * H, KC], [1, 128]]))
            key = "base"
            pm = p_base
            group_begin(key)
            P.wait("tensor", "LR%d" % (ot % 2))
            for mc in range(KC):
                P.emit("tensor", lambda e, p=pm, s=ustr, m=mc:
                       e.matmul(p[:, :], s[:, m * 128:(m + 1) * 128],
                                c_fin[:, m * RPC:(m + 1) * RPC],
                                start=(m == 0), stop=(m == KC - 1)),
                       inc="pe" if mc == KC - 1 else None)
            pv = P.cnt["pe"]
            uoc_pe[ot] = pv
            P.wait("vector", "pe", pv)
            dvv = P.emit("vector", lambda e, o=ot, p=pm: e.tensor_copy(
                uocT[:, o * RPC:(o + 1) * RPC], p[:, :]), inc="dv")
            psum_owner[key] = ("dv", dvv)

        P.barrier()

        for jt in range(ET):
            par = jt % 2
            buf = (scr, scr2)[par]
            if scr_guard[par]:
                P.wait("sync", "dv", scr_guard[par])
            dma("sync", "L%d" % par, buf[:, :RPC],
                biasT_own[jt * 128:(jt + 1) * 128, :])
            P.wait("vector", "L%d" % par)
            dvv = P.emit("vector", lambda e, b=buf[:, :RPC], j=jt:
                         e.tensor_copy(biasT_sb[:, j * RPC:(j + 1) * RPC], b), inc="dv")
            scr_guard[par] = dvv
        for kc in range(KC):
            for hh in range(2):
                par = (kc * 2 + hh) % 2
                buf = (scr, scr2)[par]
                if scr_guard[par]:
                    P.wait("sync", "dv", scr_guard[par])
                dma("sync", "L%d" % par, buf[:, :],
                    woT[kc * 128:(kc + 1) * 128, hh * 512:(hh + 1) * 512])
                P.wait("vector", "L%d" % par)
                dvv = P.emit("vector", lambda e, b=buf[:, :], k=kc, h_=hh:
                             e.tensor_copy(woT_sb[:, k * H + h_ * 512:
                                                  k * H + h_ * 512 + 512], b),
                             inc="dv")
                scr_guard[par] = dvv

        P.barrier()

        produce(0)
        produce(1)
        for b in range(B):
            consume(b)
            if b + 2 < B:
                produce(b + 2)

        P.wait("gpsimd", "SO0")
        P.wait("gpsimd", "SO1")

        block = e(nc.Block())

        @block.sync
        def _(sync):
            for fn in P.ops["sync"]:
                fn(sync)

        @block.gpsimd
        def _(gpsimd):
            for fn in P.ops["gpsimd"]:
                fn(gpsimd)

        @block.tensor
        def _(tensor):
            for fn in P.ops["tensor"]:
                fn(tensor)

        @block.vector
        def _(vector):
            for fn in P.ops["vector"]:
                fn(vector)

        @block.scalar
        def _(scalar):
            for fn in P.ops["scalar"]:
                fn(scalar)

    return nc


_NC_CACHE = None


def kernel(**inputs):
    global _NC_CACHE
    x = np.asarray(inputs["L0_attn_x"], dtype=np.float32)
    sin_t, cos_t = _rope_tables()
    biasT = _bias_matrix(np.asarray(inputs["L0_attn_w"], np.float32))
    uv_w = np.asarray(inputs["L0_attn_uv_w"], np.float32)
    uv_b = np.asarray(inputs["L0_attn_uv_b"], np.float32)

    common = dict(
        ident_in=np.eye(128, dtype=np.float32),
        wuT=np.ascontiguousarray(uv_w[:E].T),
        wvT=np.ascontiguousarray(uv_w[E:2 * E].T),
        wbT=np.ascontiguousarray(uv_w[2 * E:].T),
        wbTs=np.ascontiguousarray(uv_w[2 * E:].T[:, _PERM]),
        owT=np.ascontiguousarray(np.asarray(inputs["L0_attn_o_w"], np.float32).T),
        woT=np.ascontiguousarray(np.asarray(inputs["L0_ffn_wo"], np.float32).T),
        wfT=np.ascontiguousarray(np.asarray(inputs["L0_ffn_wf"], np.float32).T),
        ufT=np.ascontiguousarray(np.asarray(inputs["L0_ffn_uf"], np.float32).T),
        wiT=np.ascontiguousarray(np.asarray(inputs["L0_ffn_wi"], np.float32).T),
        uiT=np.ascontiguousarray(np.asarray(inputs["L0_ffn_ui"], np.float32).T),
        wcT=np.ascontiguousarray(np.asarray(inputs["L0_ffn_wc"], np.float32).T),
        ucT=np.ascontiguousarray(np.asarray(inputs["L0_ffn_uc"], np.float32).T),
        uoT=np.ascontiguousarray(np.asarray(inputs["L0_ffn_uo"], np.float32).T),
        ln_w=np.asarray(inputs["L0_attn_ln_w"], np.float32),
        ln_b=np.asarray(inputs["L0_attn_ln_b"], np.float32),
        o_b=np.asarray(inputs["L0_attn_o_b"], np.float32),
        bf=np.asarray(inputs["L0_ffn_bf"], np.float32),
        bi=np.asarray(inputs["L0_ffn_bi"], np.float32),
        bc=np.asarray(inputs["L0_ffn_bc"], np.float32),
        bo=np.asarray(inputs["L0_ffn_bo"], np.float32),
        in_c=np.asarray(inputs["L0_ffn_in_c"], np.float32),
        uvb_u=np.ascontiguousarray(uv_b[:E]),
        uvb_v=np.ascontiguousarray(uv_b[E:2 * E]),
        uvb_b=np.ascontiguousarray(uv_b[2 * E:]),
        uvb_bs=np.ascontiguousarray(uv_b[2 * E:][_PERM]),
        gamma=np.asarray(inputs["L0_attn_gamma"], np.float32),
        beta=np.asarray(inputs["L0_attn_beta"], np.float32),
        gammas=np.ascontiguousarray(np.asarray(inputs["L0_attn_gamma"], np.float32)[:, _PERM]),
        betas=np.ascontiguousarray(np.asarray(inputs["L0_attn_beta"], np.float32)[:, _PERM]),
    )

    xT = np.ascontiguousarray(x.transpose(0, 2, 1))
    in_maps = []
    for c in range(NCORES):
        r0, r1 = c * RPC, (c + 1) * RPC
        m = dict(common)
        m["x_own"] = np.ascontiguousarray(xT[:, :, r0:r1])
        m["biasT_own"] = np.ascontiguousarray(biasT[:, r0:r1])
        st = sin_t[r0:r1].T
        ct = cos_t[r0:r1].T
        m["sinT"] = np.ascontiguousarray(np.concatenate([-st, st], axis=0))
        m["cosT"] = np.ascontiguousarray(np.concatenate([ct, ct], axis=0))
        in_maps.append(m)

    if _NC_CACHE is None:
        _NC_CACHE = build_program()
    res = run_bass_kernel_spmd(_NC_CACHE, in_maps, core_ids=list(range(NCORES)))
    shards = [res.results[c]["y_own"] for c in range(NCORES)]
    outT = np.concatenate(shards, axis=2)
    return np.ascontiguousarray(outT.transpose(0, 2, 1)).astype(np.float32)


# version 13
# speedup vs baseline: 1.1580x; 1.1043x over previous
import numpy as np

import concourse.bass as bass
import concourse.mybir as mybir
from concourse.bass_utils import run_bass_kernel_spmd

F32 = mybir.dt.float32
BF16 = mybir.dt.bfloat16
AF = mybir.ActivationFunctionType
ALU = mybir.AluOpType

NCORES = 8
B = 16
T = 2048
H = 1024
E = 2048
S = 128
RPC = T // NCORES
L = 512
INV_SQRT_S = 1.0 / float(np.sqrt(S))
EPS = 1e-5
KC = H // 128
ET = E // 128
RPCB = RPC + 16
_PERM = np.r_[64:128, 0:64]


def _rope_tables():
    try:
        import jax
        cpu = jax.devices("cpu")[0]
        import jax.numpy as jnp
        with jax.default_device(cpu):
            pos = jnp.arange(T, dtype=jnp.float32)
            inv_freq = 10000.0 ** (jnp.arange(64, dtype=jnp.float32) / 64)
            sinusoid = pos[:, None] * inv_freq[None, :]
            return (np.asarray(jnp.sin(sinusoid)), np.asarray(jnp.cos(sinusoid)))
    except Exception:
        pos = np.arange(T, dtype=np.float32)
        inv_freq = (10000.0 ** (np.arange(64, dtype=np.float32) /
                                np.float32(64))).astype(np.float32)
        sinusoid = pos[:, None] * inv_freq[None, :]
        return np.sin(sinusoid).astype(np.float32), np.cos(sinusoid).astype(np.float32)


def _bias_matrix(w):
    n = w.shape[0]
    rel = np.arange(T)[None, :] - np.arange(T)[:, None] + (L - 1)
    idx = np.where(rel < 0, rel + n, rel)
    idx = np.clip(idx, 0, n - 1)
    return np.ascontiguousarray(w[idx].T)


class Prog:
    ENGS = ("sync", "gpsimd", "tensor", "vector", "scalar")

    def __init__(self, nc, es):
        self.nc = nc
        self.es = es
        self.ops = {e: [] for e in self.ENGS}
        self.sem = {}
        self.cnt = {}
        self.dma_sems = {}
        for name in ("cc", "pe", "dv", "ac",
                     "b_sync", "b_gpsimd", "b_tensor", "b_vector", "b_scalar"):
            self.sem[name] = es.enter_context(nc.semaphore(name))
            self.cnt[name] = 0

    def dsem(self, name, issuer):
        if name not in self.sem:
            self.sem[name] = self.es.enter_context(self.nc.semaphore(name))
            self.cnt[name] = 0
            self.dma_sems[name] = issuer
        return name

    def wait(self, eng, sem, val=None):
        v = self.cnt[sem] if val is None else val
        h = self.sem[sem]
        self.ops[eng].append(lambda e, h=h, v=v: e.wait_ge(h, v))

    def emit(self, eng, fn, inc=None, amt=1):
        if eng == "vector":
            self.wait("vector", "dv")
            h = self.sem["dv"]
            self.cnt["dv"] += 1
            self.ops[eng].append(lambda e, fn=fn, h=h: fn(e).then_inc(h, 1))
            return self.cnt["dv"]
        if inc is None:
            self.ops[eng].append(fn)
            return None
        h = self.sem[inc]
        self.cnt[inc] += amt
        self.ops[eng].append(lambda e, fn=fn, h=h, a=amt: fn(e).then_inc(h, a))
        return self.cnt[inc]

    def barrier(self):
        for name, issuer in self.dma_sems.items():
            if self.cnt[name]:
                self.wait(issuer, name)
        self.wait("gpsimd", "cc")
        self.wait("vector", "dv")
        self.wait("scalar", "ac")
        self.wait("tensor", "pe")
        for e in self.ENGS:
            h = self.sem["b_" + e]
            self.cnt["b_" + e] += 1
            self.ops[e].append(lambda eng, h=h: eng.nop().then_inc(h, 1))
        for e in self.ENGS:
            for o in self.ENGS:
                if o != e:
                    self.wait(e, "b_" + o)


def build_program():
    nc = bass.Bass("TRN2", target_bir_lowering=False)
    dt = nc.dram_tensor

    x_own = dt("x_own", [B, H, RPC], F32, kind="ExternalInput").ap()
    biasT_own = dt("biasT_own", [T, RPC], F32, kind="ExternalInput").ap()
    sinT_in = dt("sinT", [128, RPC], F32, kind="ExternalInput").ap()
    cosT_in = dt("cosT", [128, RPC], F32, kind="ExternalInput").ap()
    ident_in = dt("ident_in", [128, 128], F32, kind="ExternalInput").ap()

    wuT = dt("wuT", [H, E], F32, kind="ExternalInput").ap()
    wvT = dt("wvT", [H, E], F32, kind="ExternalInput").ap()
    wbT = dt("wbT", [H, S], F32, kind="ExternalInput").ap()
    wbTs = dt("wbTs", [H, S], F32, kind="ExternalInput").ap()
    owT = dt("owT", [E, H], F32, kind="ExternalInput").ap()
    woT = dt("woT", [H, H], F32, kind="ExternalInput").ap()
    gateT = {g: dt(g + "T", [H, H], F32, kind="ExternalInput").ap()
             for g in ("wf", "uf", "wi", "ui", "wc", "uc", "uo")}

    vec_in = {}
    for name, n in (("ln_w", H), ("ln_b", H), ("o_b", H), ("bf", H), ("bi", H),
                    ("bc", H), ("bo", H), ("in_c", H), ("uvb_u", E), ("uvb_b", S), ("uvb_bs", S)):
        vec_in[name] = dt(name, [n], F32, kind="ExternalInput").ap()
    uvb_v_in = dt("uvb_v", [E], F32, kind="ExternalInput").ap()
    gamma_in = dt("gamma", [2, 128], F32, kind="ExternalInput").ap()
    gammas_in = dt("gammas", [2, 128], F32, kind="ExternalInput").ap()
    beta_in = dt("beta", [2, 128], F32, kind="ExternalInput").ap()
    betas_in = dt("betas", [2, 128], F32, kind="ExternalInput").ap()

    y_own = dt("y_own", [B, H, RPC], F32, kind="ExternalOutput").ap()

    wu_h = dt("wu_h", [H, E], BF16)
    wv_h = dt("wv_h", [H, E], BF16)
    wb_h = dt("wb_h", [H, S], BF16)
    wbs_h = dt("wbs_h", [H, S], BF16)
    ow_h = dt("ow_h", [E, H], BF16)
    uo_h = dt("uo_h", [H, H], BF16)
    wf_h = dt("wf_h", [H, H], BF16)
    wi_h = dt("wi_h", [H, H], BF16)
    wc_h = dt("wc_h", [H, H], BF16)
    ag_in_big = dt("ag_in_big", [B * RPCB, E], BF16)
    ag_out_pair = [dt(f"ag_out_p{p}", [NCORES * 2 * RPCB, E], BF16)
                   for p in range(B // 2)]

    from contextlib import ExitStack
    es = ExitStack()
    with es:
        e = es.enter_context

        xnT = e(nc.sbuf_tensor("xnT", [128, KC * B * RPC], BF16))
        bb = [e(nc.sbuf_tensor(f"bb{i}", [128, 8192], BF16)) for i in range(6)]
        aux_h = e(nc.sbuf_tensor("aux_h", [128, 3072], BF16))
        c_f32 = e(nc.sbuf_tensor("c_f32", [128, KC * RPC], F32))
        c_bfA = e(nc.sbuf_tensor("c_bfA", [128, KC * RPC], BF16))
        c_bfB = e(nc.sbuf_tensor("c_bfB", [128, KC * RPC], BF16))
        scr = e(nc.sbuf_tensor("scr", [128, 512], F32))
        scr2 = e(nc.sbuf_tensor("scr2", [128, 512], F32))
        gates = [e(nc.sbuf_tensor(f"gate{i}", [128, RPC], F32)) for i in range(3)]
        upd = e(nc.sbuf_tensor("upd", [128, RPC], F32))
        sq_h = e(nc.sbuf_tensor("sq_h", [128, RPC], BF16))
        tmp_h = e(nc.sbuf_tensor("tmp_h", [128, RPC], BF16))
        muB = e(nc.sbuf_tensor("muB", [128, RPC], BF16))
        rsB = e(nc.sbuf_tensor("rsB", [128, RPC], BF16))
        hT_b = e(nc.sbuf_tensor("hT_b", [128, RPC], BF16))
        xres = [e(nc.sbuf_tensor(f"xres{i}", [128, RPC], F32)) for i in range(2)]
        outs = [e(nc.sbuf_tensor(f"outs{i}", [128, RPC], F32)) for i in range(2)]
        stat = e(nc.sbuf_tensor("stat", [1, 2 * RPC], F32))
        uvbv_row = e(nc.sbuf_tensor("uvbv_row", [1, E], BF16))
        ones_col = e(nc.sbuf_tensor("ones_col", [128, 1], BF16))
        ones_row = e(nc.sbuf_tensor("ones_row", [1, 128], BF16))
        ones_row_f = e(nc.sbuf_tensor("ones_row_f", [1, 128], F32))
        ident_h = e(nc.sbuf_tensor("ident_h", [128, 128], BF16))
        sin_sb = e(nc.sbuf_tensor("sin_sb", [128, RPC], F32))
        cos_sb = e(nc.sbuf_tensor("cos_sb", [128, RPC], F32))
        sm = e(nc.sbuf_tensor("sm", [128, 96], F32))

        p_m = [e(nc.psum_tensor(f"p_m{i}", [128, RPC], F32)) for i in range(3)]
        p_v = [e(nc.psum_tensor(f"p_v{i}", [128, 512], F32)) for i in range(2)]
        p_base = e(nc.psum_tensor("p_base", [128, RPC], F32))
        p_h = e(nc.psum_tensor("p_h", [128, RPC], F32))
        p_stat = e(nc.psum_tensor("p_stat", [128, 512], F32))
        p_s = [p_stat[0:1, 0:RPC], p_stat[0:1, RPC:2 * RPC]]
        p_b = p_stat

        P = Prog(nc, es)
        psum_owner = {}

        def group_begin(key):
            if key in psum_owner:
                s, v = psum_owner[key]
                P.wait("tensor", s, v)

        smcol = {}
        _c = [0]
        def alloc_sm(name, ncols):
            smcol[name] = _c[0]
            _c[0] += ncols
        for name in ("ln_w", "ln_b", "o_b", "bf", "bi", "bc", "bo", "in_c"):
            alloc_sm(name, KC)
        alloc_sm("uvb_u", ET)
        alloc_sm("uvb_b", 1)
        alloc_sm("uvb_bs", 1)
        alloc_sm("gamma", 2)
        alloc_sm("beta", 2)
        alloc_sm("gammas", 2)
        alloc_sm("betas", 2)
        alloc_sm("g0p", 1)
        alloc_sm("b0p", 1)
        alloc_sm("g0ps", 1)
        alloc_sm("b0ps", 1)
        assert _c[0] <= 96

        def smc(name, i=0):
            return sm[:, smcol[name] + i: smcol[name] + i + 1]

        WG = {"uf": bb[1], "ui": bb[3], "uc": bb[5]}
        xg_buf = [[bb[4][:, (2 * g + i) * 1024:(2 * g + i + 1) * 1024]
                   for i in range(2)] for g in range(3)]
        wb_vk2 = [bb[0][:, i * 1024:(i + 1) * 1024] for i in range(2)]
        wv_vk = [bb[0][:, 2048 + i * 512: 2048 + (i + 1) * 512] for i in range(2)]
        kernelT = bb[0]
        kT_sb = bb[0][:, 4096:6144]
        qT_bufs = [bb[0][:, 6144:6400], bb[0][:, 6400:6656]]
        uT_bufs = [bb[1][:, 0:4096], bb[1][:, 4096:8192]]
        eT = bb[3][:, 4096:8192]
        biasT_sb = bb[2][:, 0:4096]
        uocT = bb[2][:, 4096:6144]
        xnnT = bb[2][:, 6144:8192]
        ow_strip = [bb[3][:, i * 2048:(i + 1) * 2048] for i in range(2)]
        woT_sb = bb[4]
        wu_strip = [bb[5][:, i * 1024:(i + 1) * 1024] for i in range(2)]
        wb_buf2 = [bb[5][:, 2048 + i * 1024: 2048 + (i + 1) * 1024] for i in range(2)]
        vstrip = [bb[5][:, 4096 + i * 2048: 4096 + (i + 1) * 2048] for i in range(2)]
        vout = [aux_h[:, i * 512:(i + 1) * 512] for i in range(2)]
        uo_strip = [aux_h[:, 1024 + i * 1024: 1024 + (i + 1) * 1024] for i in range(2)]
        x_bf = c_bfA
        uvbv_hlp = c_bfA
        cast_h = c_bfB

        import dataclasses as _dc

        def mkap(base_ap, offset, dims):
            return _dc.replace(base_ap, offset=offset,
                               ap=type(base_ap.ap)(dims))

        def dma(eng, sem, out, in_, slow=False):
            P.dsem(sem, eng)
            if slow:
                return P.emit(eng, lambda e, o=out, i=in_: e.dma_start(
                    out=o, in_=i, allow_slow_non_contiguous=True), inc=sem, amt=16)
            return P.emit(eng, lambda e, o=out, i=in_: e.dma_start(out=o, in_=i),
                          inc=sem, amt=16)

        for name in ("ln_w", "ln_b", "o_b", "bf", "bi", "bc", "bo", "in_c"):
            dma("sync", "LS", sm[:, smcol[name]:smcol[name] + KC],
                vec_in[name].rearrange("(t p) -> p t", p=128), slow=True)
        dma("sync", "LS", sm[:, smcol["uvb_u"]:smcol["uvb_u"] + ET],
            vec_in["uvb_u"].rearrange("(t p) -> p t", p=128), slow=True)
        dma("sync", "LS", sm[:, smcol["uvb_b"]:smcol["uvb_b"] + 1],
            vec_in["uvb_b"].rearrange("(t p) -> p t", p=128), slow=True)
        dma("sync", "LS", sm[:, smcol["uvb_bs"]:smcol["uvb_bs"] + 1],
            vec_in["uvb_bs"].rearrange("(t p) -> p t", p=128), slow=True)
        dma("sync", "LS", sm[:, smcol["gamma"]:smcol["gamma"] + 2],
            gamma_in.rearrange("h s -> s h"), slow=True)
        dma("sync", "LS", sm[:, smcol["beta"]:smcol["beta"] + 2],
            beta_in.rearrange("h s -> s h"), slow=True)
        dma("sync", "LS", sm[:, smcol["gammas"]:smcol["gammas"] + 2],
            gammas_in.rearrange("h s -> s h"), slow=True)
        dma("sync", "LS", sm[:, smcol["betas"]:smcol["betas"] + 2],
            betas_in.rearrange("h s -> s h"), slow=True)
        dma("sync", "LS", scr[:, :128], ident_in)
        dma("sync", "LS", sin_sb[:, :], sinT_in)
        dma("sync", "LS", cos_sb[:, :], cosT_in)

        P.wait("vector", "LS")
        P.emit("vector", lambda e: e.memset(ones_col[:, :], 1.0))
        P.emit("vector", lambda e: e.memset(ones_row[:, :], 1.0))
        P.emit("vector", lambda e: e.memset(ones_row_f[:, :], 1.0))
        P.emit("vector", lambda e: e.tensor_copy(ident_h[:, :], scr[:, :128]))
        for q in range(4):
            dma("sync", "L1", scr2[0:1, :512],
                uvb_v_in[q * 512:(q + 1) * 512].rearrange("(a m) -> a m", a=1))
            P.wait("vector", "L1")
            dvq = P.emit("vector", lambda e, q_=q: e.tensor_copy(
                uvbv_hlp[0:1, q_ * 512:(q_ + 1) * 512], scr2[0:1, :512]), inc="dv")
            P.wait("sync", "dv", dvq)
        P.wait("gpsimd", "dv")
        dma("gpsimd", "SU", uvbv_row[0:1, :], uvbv_hlp[0:1, 0:E])
        P.emit("vector", lambda e: e.tensor_scalar_mul(smc("g0p"), smc("gamma", 0), INV_SQRT_S))
        P.emit("vector", lambda e: e.tensor_scalar_mul(smc("b0p"), smc("beta", 0), INV_SQRT_S))
        P.emit("vector", lambda e: e.tensor_scalar_mul(smc("g0ps"), smc("gammas", 0), INV_SQRT_S))
        P.emit("vector", lambda e: e.tensor_scalar_mul(smc("b0ps"), smc("betas", 0), INV_SQRT_S),
               inc="dv")

        scr_guard = [0, 0]

        def cast_chunks(chunks):
            pass

        def cast_resident(src_ap, dst, n_out):
            i = 0
            for kc in range(KC):
                for off in range(0, n_out, 512):
                    wdt = min(512, n_out - off)
                    par = i % 2
                    buf = (scr, scr2)[par]
                    if scr_guard[par]:
                        P.wait("sync", "dv", scr_guard[par])
                    dma("sync", "L%d" % par, buf[:, :wdt],
                        src_ap[kc * 128:(kc + 1) * 128, off:off + wdt])
                    P.wait("vector", "L%d" % par)
                    dvv = P.emit("vector", lambda e, b=buf[:, :wdt],
                                 d=dst[:, kc * n_out + off: kc * n_out + off + wdt]:
                                 e.tensor_copy(d, b), inc="dv")
                    scr_guard[par] = dvv
                    i += 1

        cast_dram_guard = [0, 0]

        def cast_to_dram(src_ap, dst_dram, rows, cols):
            i = 0
            for r0 in range(0, rows, 128):
                for off in range(0, cols, 512):
                    wdt = min(512, cols - off)
                    par = i % 2
                    buf = (scr, scr2)[par]
                    if scr_guard[par]:
                        P.wait("sync", "dv", scr_guard[par])
                    dma("sync", "L%d" % par, buf[:, :wdt],
                        src_ap[r0:r0 + 128, off:off + wdt])
                    hbuf = cast_h[:, par * 1024: par * 1024 + wdt]
                    P.wait("vector", "L%d" % par)
                    if cast_dram_guard[par]:
                        P.wait("vector", "SC%d" % par)
                    dvv = P.emit("vector", lambda e, b=buf[:, :wdt], h=hbuf:
                                 e.tensor_copy(h, b), inc="dv")
                    scr_guard[par] = dvv
                    P.wait("gpsimd", "dv", dvv)
                    dma("gpsimd", "SC%d" % par, dst_dram.ap()[r0:r0 + 128, off:off + wdt], hbuf)
                    cast_dram_guard[par] = 1
                    i += 1

        for g in ("uf", "ui", "uc"):
            cast_resident(gateT[g], WG[g], H)
        cast_to_dram(gateT["wf"], wf_h, H, H)
        cast_to_dram(gateT["wi"], wi_h, H, H)
        cast_to_dram(gateT["wc"], wc_h, H, H)
        cast_to_dram(gateT["uo"], uo_h, H, H)
        cast_to_dram(wuT, wu_h, H, E)
        cast_to_dram(wvT, wv_h, H, E)
        cast_to_dram(wbT, wb_h, H, S)
        cast_to_dram(wbTs, wbs_h, H, S)
        cast_to_dram(owT, ow_h, E, H)

        P.barrier()

        st_mu = stat[0:1, 0:RPC]
        st_b = stat[0:1, RPC:2 * RPC]

        stats_guard = {"sq": 0, "bank": 0}

        def ln_stats(src_slices, src_dv=0):
            P.wait("tensor", "dv", max(src_dv, stats_guard["bank"]))
            P.wait("scalar", "dv", max(stats_guard["sq"], src_dv))
            if P.cnt.get("SK"):
                P.wait("scalar", "SK")
                P.wait("vector", "SK")
            for kc in range(KC):
                P.emit("tensor", lambda e, s=src_slices[kc], k=kc:
                       e.matmul(p_s[0], ones_col[:, :], s,
                                start=(k == 0), stop=(k == KC - 1)),
                       inc="pe")
            for kc in range(KC):
                sq = (sq_h, tmp_h)[kc % 2]
                if kc >= 2:
                    P.wait("scalar", "pe", pe_sq[kc - 2])
                av = P.emit("scalar", lambda e, s=src_slices[kc], q=sq:
                            e.activation(out=q[:, :], in_=s, func=AF.Square), inc="ac")
                P.wait("tensor", "ac", av)
                pv = P.emit("tensor", lambda e, k=kc, q=sq:
                            e.matmul(p_s[1], ones_col[:, :], q[:, :],
                                     start=(k == 0), stop=(k == KC - 1)),
                            inc="pe")
                if kc == 0:
                    pe_sq = {}
                pe_sq[kc] = pv
            P.wait("vector", "pe")
            P.emit("vector", lambda e: e.tensor_scalar_mul(st_mu, p_s[0], 1.0 / H))
            P.emit("vector", lambda e: e.tensor_scalar_mul(st_b, p_s[1], 1.0 / H))
            P.emit("vector", lambda e: e.tensor_mul(p_s[0], st_mu, st_mu))
            P.emit("vector", lambda e: e.tensor_sub(st_b, st_b, p_s[0]))
            P.emit("vector", lambda e: e.tensor_scalar_add(st_b, st_b, EPS), inc="dv")
            P.wait("scalar", "dv")
            P.emit("scalar", lambda e: e.activation(out=st_b, in_=st_b, func=AF.Sqrt),
                   inc="ac")
            P.wait("vector", "ac")
            recip_dv = P.emit("vector", lambda e: e.reciprocal(st_b, st_b), inc="dv")
            P.wait("tensor", "dv", recip_dv)
            P.emit("tensor", lambda e: e.matmul(p_b[:, 0:RPC], ones_row_f[:, :], st_mu,
                                                start=True, stop=True))
            P.emit("tensor", lambda e: e.matmul(p_b[:, RPC:2 * RPC], ones_row_f[:, :], st_b,
                                                start=True, stop=True), inc="pe")
            P.wait("vector", "pe")
            P.emit("vector", lambda e: e.tensor_copy(muB[:, :], p_b[:, 0:RPC]))
            stats_guard["bank"] = P.emit(
                "vector", lambda e: e.tensor_copy(rsB[:, :], p_b[:, RPC:2 * RPC]),
                inc="dv")

        def xnT_sl(kc, t):
            off = kc * (B * RPC) + t * RPC
            return xnT[:, off: off + RPC]

        for t in range(B):
            P.wait("vector", "pe")
            for kc in range(KC):
                par = kc % 2
                buf = (scr, scr2)[par]
                if scr_guard[par]:
                    P.wait("sync", "dv", scr_guard[par])
                dma("sync", "L%d" % par, buf[:, :RPC],
                    x_own[t, kc * 128:(kc + 1) * 128, :])
                P.wait("vector", "L%d" % par)
                dvv = P.emit("vector", lambda e, b=buf[:, :RPC],
                             d=x_bf[:, kc * RPC:(kc + 1) * RPC]:
                             e.tensor_copy(d, b), inc="dv")
                scr_guard[par] = dvv
            xs = [x_bf[:, kc * RPC:(kc + 1) * RPC] for kc in range(KC)]
            ln_stats(xs, src_dv=dvv)
            for kc in range(KC):
                P.emit("vector", lambda e, s=xs[kc]: e.tensor_sub(tmp_h[:, :], s, muB[:, :]))
                stats_guard["sq"] = P.emit(
                    "vector", lambda e, d=xnT_sl(kc, t):
                    e.tensor_mul(d, tmp_h[:, :], rsB[:, :]), inc="dv")

        P.barrier()

        def xnn_make(b):
            xs = [xnT_sl(kc, b) for kc in range(KC)]
            ln_stats(xs)
            P.wait("vector", "pe", produce_pe_end[0])
            for kc in range(KC):
                P.emit("vector", lambda e, s=xs[kc]: e.tensor_sub(tmp_h[:, :], s, muB[:, :]))
                P.emit("vector", lambda e: e.tensor_mul(sq_h[:, :], tmp_h[:, :], rsB[:, :]))
                stats_guard["sq"] = P.emit("vector", lambda e, k=kc: e.tensor_scalar(
                    out=xnnT[:, k * RPC:(k + 1) * RPC], in0=sq_h[:, :],
                    scalar1=smc("ln_w", k), scalar2=smc("ln_b", k),
                    op0=ALU.mult, op1=ALU.add), inc="dv")

        base_sw = c_f32[:, 0:RPC]
        relu_t = c_f32[:, RPC:2 * RPC]
        rope_dv_last = [0]
        produce_pe_end = [0]
        cc_of = {}
        qk_pe_end = {}
        eT_dv_end = {}

        def rope_make(dst, gcol, bcol, gscol, bscol, a0, a1,
                      bsrc=None, bssrc=None, tq=None):
            bsrc = gates[2][:, :] if bsrc is None else bsrc
            bssrc = base_sw if bssrc is None else bssrc
            tq = upd[:, :] if tq is None else tq
            P.emit("vector", lambda e, g=gcol, b_=bcol: e.tensor_scalar(
                out=tq, in0=bsrc, scalar1=g, scalar2=b_,
                op0=ALU.mult, op1=ALU.add))
            P.emit("vector", lambda e, g=gscol, b_=bscol: e.tensor_scalar(
                out=a0, in0=bssrc, scalar1=g, scalar2=b_,
                op0=ALU.mult, op1=ALU.add))
            P.emit("vector", lambda e: e.tensor_mul(a1, a0, sin_sb[:, :]))
            P.emit("vector", lambda e: e.tensor_mul(tq, tq, cos_sb[:, :]))
            return P.emit("vector", lambda e, d=dst:
                          e.tensor_add(d, tq, a1), inc="dv")

        vout_guard = [0, 0]
        out_guard = [0, 0]
        phaseA_dv = [0]

        vk_dv_xnn = [0]
        vk_base_pe = [0]

        vout_guard = [0, 0]
        out_guard = [0, 0]
        phaseA_dv = [0]

        vk_dv_xnn = [0]
        vk_base_pe = [0]

        def produce_vk_pre(b):
            xnn_make(b)
            vk_dv_xnn[0] = P.cnt["dv"]
            if vk_base_pe[0]:
                P.wait("gpsimd", "pe", vk_base_pe[0])
            for swp in range(2):
                src = (wb_h if swp == 0 else wbs_h).ap()
                dma("gpsimd", "LBk",
                    wb_vk2[swp][:, :1024].rearrange("p (kc c) -> p kc c", kc=KC),
                    mkap(src, 0, [[S, 128], [128 * S, KC], [1, S]]))

        def produce_vk(b):
            dv_xnn = vk_dv_xnn[0]
            P.wait("tensor", "dv", dv_xnn)
            P.wait("scalar", "dv", rope_dv_last[0])
            for swp, dst_sb, bias_name in (
                    (0, xres[0][:, :], "uvb_b"),
                    (1, xres[1][:, :], "uvb_bs")):
                group_begin("base")
                if swp == 0:
                    P.wait("tensor", "LBk")
                for kc in range(KC):
                    P.emit("tensor", lambda e, k=kc, w=wb_vk2[swp]:
                           e.matmul(p_base[:, :], w[:, k * 128:(k + 1) * 128],
                                    xnnT[:, k * RPC:(k + 1) * RPC],
                                    start=(k == 0), stop=(k == KC - 1)),
                           inc="pe" if kc == KC - 1 else None)
                pv = P.cnt["pe"]
                P.wait("scalar", "pe", pv)
                av = P.emit("scalar", lambda e, d=dst_sb, bn=bias_name: e.activation(
                    out=d, in_=p_base[:, :], func=AF.Silu,
                    bias=smc(bn)), inc="ac")
                psum_owner["base"] = ("ac", av)
            vk_base_pe[0] = P.cnt["pe"]
            P.wait("vector", "ac", av)
            kv = rope_make(tmp_h[:, :], smc("gamma", 1), smc("beta", 1),
                           smc("gammas", 1), smc("betas", 1),
                           outs[0][:, :], outs[1][:, :],
                           bsrc=xres[0][:, :], bssrc=xres[1][:, :],
                           tq=xres[0][:, :])
            rope_dv_last[0] = kv
            stats_guard["sq"] = max(stats_guard["sq"], kv)
            P.wait("gpsimd", "dv", kv)
            dma("gpsimd", "SK",
                ag_in_big.ap()[b * RPCB + RPC: b * RPCB + RPC + 16, :]
                .rearrange("a (c i) -> (a c) i", i=RPC),
                tmp_h[:, :])
            for nch in range(4):
                for kc in range(KC):
                    par = (nch * KC + kc) % 2
                    wb2 = wv_vk[par]
                    if nch * KC + kc >= 2:
                        P.wait("sync", "pe", v_pe[nch * KC + kc - 2])
                    dma("sync", "LVk%d" % par, wb2[:, :],
                        wv_h.ap()[kc * 128:(kc + 1) * 128, nch * 512:(nch + 1) * 512])
                    for rt in range(2):
                        key = "pv%d" % rt
                        if kc == 0:
                            group_begin(key)
                            P.emit("tensor", lambda e, p=p_v[rt], n=nch: e.matmul(
                                p[:, :], ones_row[:, :],
                                uvbv_row[0:1, n * 512:(n + 1) * 512],
                                start=True, stop=False))
                        P.wait("tensor", "LVk%d" % par)
                        pv = P.emit("tensor", lambda e, p=p_v[rt], w=wb2, r=rt, k=kc:
                                    e.matmul(p[:, :],
                                             xnnT[:, k * RPC + r * 128: k * RPC + r * 128 + 128],
                                             w[:, :], start=False, stop=(k == KC - 1)),
                                    inc="pe")
                        if nch == 0 and kc == 0 and rt == 0:
                            v_pe = {}
                        v_pe[nch * KC + kc] = pv
                        if kc == KC - 1:
                            vo = vout[rt]
                            P.wait("scalar", "pe", pv)
                            if vout_guard[rt]:
                                P.wait("scalar", "SV%d" % rt)
                            av = P.emit("scalar", lambda e, p=p_v[rt], o=vo: e.activation(
                                out=o, in_=p[:, :], func=AF.Silu), inc="ac")
                            psum_owner[key] = ("ac", av)
                            P.wait("gpsimd", "ac", av)
                            dma("gpsimd", "SV%d" % rt,
                                ag_in_big.ap()[b * RPCB + rt * 128: b * RPCB + (rt + 1) * 128,
                                               nch * 512:(nch + 1) * 512], vo)
                            vout_guard[rt] = 1
            produce_pe_end[0] = P.cnt["pe"]

        b_dv_xnn = {}
        b_base_pe = [0]
        b_dv_kernel = {}
        b_qv = {}

        def produce_pre(b):
            xnn_make(b)
            b_dv_xnn[b] = P.cnt["dv"]
            if b_base_pe[0]:
                P.wait("sync", "pe", b_base_pe[0])
            for swp in range(2):
                src = (wb_h if swp == 0 else wbs_h).ap()
                dma("sync", "LB",
                    wb_buf2[swp][:, :1024].rearrange("p (kc c) -> p kc c", kc=KC),
                    mkap(src, 0, [[S, 128], [128 * S, KC], [1, S]]))

        def produce(b):
            uT = uT_bufs[b % 2]
            qT_sb = qT_bufs[b % 2]
            dv_xnn = b_dv_xnn[b]
            P.wait("tensor", "dv", dv_xnn)
            P.wait("scalar", "dv", rope_dv_last[0])
            for swp, dst_sb, bias_name in (
                    (0, gates[2][:, :], "uvb_b"),
                    (1, base_sw, "uvb_bs")):
                group_begin("base")
                if swp == 0:
                    P.wait("tensor", "LB")
                for kc in range(KC):
                    P.emit("tensor", lambda e, k=kc, w=wb_buf2[swp]:
                           e.matmul(p_base[:, :], w[:, k * 128:(k + 1) * 128],
                                    xnnT[:, k * RPC:(k + 1) * RPC],
                                    start=(k == 0), stop=(k == KC - 1)),
                           inc="pe" if kc == KC - 1 else None)
                pv = P.cnt["pe"]
                P.wait("scalar", "pe", pv)
                av = P.emit("scalar", lambda e, d=dst_sb, bn=bias_name: e.activation(
                    out=d, in_=p_base[:, :], func=AF.Silu,
                    bias=smc(bn)), inc="ac")
                psum_owner["base"] = ("ac", av)
            b_base_pe[0] = P.cnt["pe"]
            P.wait("vector", "ac", av)
            a0 = gates[0][:, :]
            a1 = gates[1][:, :]
            if b >= 2:
                P.wait("vector", "pe", qk_pe_end[b - 2])
            qv = rope_make(qT_sb, smc("g0p"), smc("b0p"),
                           smc("g0ps"), smc("b0ps"), a0, a1)
            rope_dv_last[0] = qv
            b_qv[b] = qv
            if b >= 2:
                P.wait("scalar", "dv", eT_dv_end[b - 2])

            def u_dma(et):
                if et >= ET:
                    return
                if et >= 2:
                    P.wait("scalar", "pe", u_pe[et - 2])
                dma("scalar", "LU%d" % (et % 2),
                    wu_strip[et % 2][:, :1024].rearrange("p (kc c) -> p kc c", kc=KC),
                    mkap(wu_h.ap(), et * 128, [[E, 128], [128 * E, KC], [1, 128]]))

            u_dma(0)
            u_dma(1)
            for et in range(ET):
                sb = wu_strip[et % 2]
                key = "pv%d" % (et % 2)
                pm = p_v[et % 2][:, 0:RPC]
                group_begin(key)
                P.wait("tensor", "LU%d" % (et % 2))
                if et == 0:
                    P.wait("tensor", "dv", dv_xnn)
                for kc in range(KC):
                    pv = P.emit("tensor", lambda e, p=pm, s=sb, k=kc:
                                e.matmul(p, s[:, k * 128:(k + 1) * 128],
                                         xnnT[:, k * RPC:(k + 1) * RPC],
                                         start=(k == 0), stop=(k == KC - 1)),
                                inc="pe")
                if et == 0:
                    u_pe = {}
                u_pe[et] = pv
                u_dma(et + 2)
                P.wait("scalar", "pe", pv)
                av = P.emit("scalar", lambda e, p=pm, t_=et, u_=uT: e.activation(
                    out=u_[:, t_ * RPC:(t_ + 1) * RPC], in_=p, func=AF.Silu,
                    bias=smc("uvb_u", t_)), inc="ac")
                psum_owner[key] = ("ac", av)
            produce_pe_end[0] = P.cnt["pe"]

        def consume_qk(b):
            qT_sb = qT_bufs[b % 2]
            agp = ag_out_pair[b // 2]
            P.wait("gpsimd", "cc", b // 2 + 1)
            if b > 0:
                P.wait("gpsimd", "pe", qk_pe_end[b - 1])
            for r in range(NCORES):
                base_row = r * (2 * RPCB) + (b % 2) * RPCB + RPC
                blk = agp.ap()[base_row: base_row + 16, :]
                dma("gpsimd", "LK", kT_sb[:, r * RPC:(r + 1) * RPC],
                    blk.rearrange("a (c i) -> (a c) i", i=RPC))
            for jt in range(ET):
                key = "m%d" % (jt % 3)
                pm = p_m[jt % 3]
                group_begin(key)
                if jt == 0:
                    P.wait("tensor", "LK")
                    P.wait("tensor", "dv", b_qv[b])
                P.emit("tensor", lambda e, p=pm, j=jt: e.matmul(
                    p[:, :], ident_h[:, :], biasT_sb[:, j * RPC:(j + 1) * RPC],
                    start=True, stop=False))
                pv = P.emit("tensor", lambda e, p=pm, j=jt: e.matmul(
                    p[:, :], kT_sb[:, j * 128:(j + 1) * 128], qT_sb,
                    start=False, stop=True), inc="pe")
                P.wait("scalar", "pe", pv)
                if jt > 0:
                    P.wait("scalar", "dv", sq_dv)
                av = P.emit("scalar", lambda e, p=pm: e.activation(
                    out=relu_t, in_=p[:, :], func=AF.Relu), inc="ac")
                psum_owner[key] = ("ac", av)
                P.wait("vector", "ac", av)
                sq_dv = P.emit("vector", lambda e, j=jt: e.tensor_mul(
                    kernelT[:, j * RPC:(j + 1) * RPC], relu_t, relu_t), inc="dv")
            qk_pe_end[b] = P.cnt["pe"]
            b_dv_kernel[b] = P.cnt["dv"]

        def consume(b):
            uT = uT_bufs[b % 2]
            dv_kernel = b_dv_kernel[b]
            agp = ag_out_pair[b // 2]
            for et in range(ET):
                vs = vstrip[et % 2]
                if et >= 2:
                    P.wait("gpsimd", "pe", kv_pe[et - 2])

                for q in range(2):
                    dma("gpsimd", "LW%d" % (et % 2),
                        mkap(vs, vs.offset + q * 128,
                             [[8192, 128], [256, NCORES], [1, 128]]),
                        mkap(agp.ap(),
                             ((b % 2) * RPCB + q * 128) * E + et * 128,
                             [[E, 128], [2 * RPCB * E, NCORES], [1, 128]]))
                key = "m%d" % (et % 3)
                pm = p_m[et % 3]
                group_begin(key)
                P.wait("tensor", "LW%d" % (et % 2))
                if et == 0:
                    P.wait("tensor", "dv", dv_kernel)
                for jt in range(ET):
                    pv = P.emit("tensor", lambda e, p=pm, v_=vs, j=jt: e.matmul(
                        p[:, :], v_[:, j * 128:(j + 1) * 128],
                        kernelT[:, j * RPC:(j + 1) * RPC],
                        start=(j == 0), stop=(j == ET - 1)),
                        inc="pe" if jt == ET - 1 else None)
                if et == 0:
                    kv_pe = {}
                kv_pe[et] = pv
                P.wait("vector", "pe", pv)
                dvv = P.emit("vector", lambda e, p=pm, t_=et: e.tensor_mul(
                    eT[:, t_ * RPC:(t_ + 1) * RPC], p[:, :],
                    uT[:, t_ * RPC:(t_ + 1) * RPC]), inc="dv")
                psum_owner[key] = ("dv", dvv)
            dv_eT = P.cnt["dv"]
            eT_dv_end[b] = dv_eT
            for ft in range(KC):
                osb = ow_strip[ft % 2]
                if ft >= 2:
                    P.wait("scalar", "pe", o_pe[ft - 2])
                dma("scalar", "LO%d" % (ft % 2),
                    osb[:, :2048].rearrange("p (ec c) -> p ec c", ec=ET),
                    mkap(ow_h.ap(), ft * 128, [[H, 128], [128 * H, ET], [1, 128]]))
                group_begin("h")
                P.emit("tensor", lambda e, f=ft: e.matmul(
                    p_h[:, :], ident_h[:, :], uocT[:, f * RPC:(f + 1) * RPC],
                    start=True, stop=False))
                for kc in range(KC):
                    P.emit("tensor", lambda e, f=ft, k=kc, b_=b: e.matmul(
                        p_h[:, :], woT_sb[:, k * H + f * 128: k * H + f * 128 + 128],
                        xnT_sl(k, b_), start=False, stop=(k == KC - 1)),
                        inc="pe" if kc == KC - 1 else None)
                pv_h = P.cnt["pe"]
                P.wait("scalar", "pe", pv_h)
                if ft > 0:
                    P.wait("scalar", "dv", comb_dv)
                av = P.emit("scalar", lambda e, f=ft: e.activation(
                    out=hT_b[:, :], in_=p_h[:, :], func=AF.Tanh, bias=smc("bo", f)),
                    inc="ac")
                psum_owner["h"] = ("ac", av)
                key = "m%d" % (ft % 3)
                pm = p_m[ft % 3]
                group_begin(key)
                P.emit("tensor", lambda e, p=pm, f=ft, b_=b: e.matmul(
                    p[:, :], ident_h[:, :], xnT_sl(f, b_), start=True, stop=False))
                P.wait("tensor", "LO%d" % (ft % 2))
                if ft == 0:
                    P.wait("tensor", "dv", dv_eT)
                for ec in range(ET):
                    P.emit("tensor", lambda e, p=pm, s=osb, c_=ec: e.matmul(
                        p[:, :], s[:, c_ * 128:(c_ + 1) * 128],
                        eT[:, c_ * RPC:(c_ + 1) * RPC],
                        start=False, stop=(c_ == ET - 1)),
                        inc="pe" if ec == ET - 1 else None)
                pv_o = P.cnt["pe"]
                if ft == 0:
                    o_pe = {}
                o_pe[ft] = pv_o
                xr = xres[ft % 2]
                if b == 0 and ft < 2:
                    P.wait("sync", "dv", phaseA_dv[0])
                dma("sync", "LX%d" % (ft % 2), xr[:, :],
                    x_own[b, ft * 128:(ft + 1) * 128, :])
                ob = outs[ft % 2]
                P.wait("vector", "pe", pv_o)
                P.wait("vector", "ac", av)
                if out_guard[ft % 2]:
                    P.wait("vector", "SO%d" % (ft % 2))
                P.emit("vector", lambda e, p=pm, f=ft, o=ob: e.scalar_tensor_tensor(
                    out=o[:, :], in0=p[:, :], scalar=smc("o_b", f), in1=hT_b[:, :],
                    op0=ALU.add, op1=ALU.add), inc="dv")
                psum_owner[key] = ("dv", P.cnt["dv"])
                P.wait("vector", "LX%d" % (ft % 2))
                comb_dv = P.emit("vector", lambda e, o=ob, x=xr: e.tensor_add(
                    o[:, :], o[:, :], x[:, :]), inc="dv")
                P.wait("gpsimd", "dv", comb_dv)
                dma("gpsimd", "SO%d" % (ft % 2), y_own[b, ft * 128:(ft + 1) * 128, :],
                    ob[:, :])
                out_guard[ft % 2] = 1
                P.wait("sync", "dv", comb_dv)

        for mt in range(KC):
            P.emit("vector", lambda e, m=mt: e.tensor_scalar(
                out=c_f32[:, m * RPC:(m + 1) * RPC], in0=muB[:, :],
                scalar1=0.0, scalar2=smc("in_c", m), op0=ALU.mult, op1=ALU.add))
        for mt in range(KC):
            P.emit("vector", lambda e, m=mt: e.tensor_copy(
                c_bfA[:, m * RPC:(m + 1) * RPC], c_f32[:, m * RPC:(m + 1) * RPC]),
                inc="dv")

        gatespec = [("wf", wf_h, "uf", "bf", AF.Sigmoid, 0),
                    ("wi", wi_h, "ui", "bi", AF.Sigmoid, 1),
                    ("wc", wc_h, "uc", "bc", AF.Tanh, 2)]
        xg_pe = [{} for _ in range(3)]

        def xg_dma(gi, gidx):
            t_, mt_ = divmod(gidx, KC)
            if t_ >= B:
                return
            wh = (wf_h, wi_h, wc_h)[gi]
            par = gidx % 2
            if gidx >= 2:
                P.wait("sync", "pe", xg_pe[gi][gidx - 2])
            dma("sync", "LG%d_%d" % (gi, par),
                xg_buf[gi][par][:, :1024].rearrange("p (kc c) -> p kc c", kc=KC),
                mkap(wh.ap(), mt_ * 128, [[H, 128], [128 * H, KC], [1, 128]]))

        dv_after_update = P.cnt["dv"]
        for gi in range(3):
            xg_dma(gi, 0)
            xg_dma(gi, 1)
        for t in range(B):
            produce_vk_pre(t)
            cur = c_bfA if t % 2 == 0 else c_bfB
            nxt = c_bfB if t % 2 == 0 else c_bfA
            first_group = True
            for mt in range(KC):
                gidx = t * KC + mt
                for (wg, wh, ug, bg, func, gi) in gatespec:
                    key = "m%d" % ((mt * 3 + gi) % 3)
                    pm = p_m[(mt * 3 + gi) % 3]
                    sbx = xg_buf[gi][gidx % 2]
                    group_begin(key)
                    P.wait("tensor", "LG%d_%d" % (gi, gidx % 2))
                    for kc in range(KC):
                        P.emit("tensor", lambda e, p=pm, w=sbx, k=kc, t_=t:
                               e.matmul(p[:, :],
                                        w[:, k * 128:(k + 1) * 128],
                                        xnT_sl(k, t_), start=(k == 0), stop=False))
                    if first_group:
                        P.wait("tensor", "dv", dv_after_update)
                        first_group = False
                    for kc in range(KC):
                        P.emit("tensor", lambda e, p=pm, u=WG[ug], m=mt, k=kc, c=cur:
                               e.matmul(p[:, :],
                                        u[:, k * H + m * 128: k * H + m * 128 + 128],
                                        c[:, k * RPC:(k + 1) * RPC],
                                        start=False, stop=(k == KC - 1)),
                               inc="pe" if kc == KC - 1 else None)
                    pv = P.cnt["pe"]
                    xg_pe[gi][gidx] = pv
                    xg_dma(gi, gidx + 2)
                    P.wait("scalar", "pe", pv)
                    P.wait("scalar", "dv", dv_after_update if mt == 0 else dv_mt)
                    av = P.emit("scalar", lambda e, p=pm, f=func, b=bg, m=mt, g=gi:
                                e.activation(out=gates[g][:, :], in_=p[:, :], func=f,
                                             bias=smc(b, m)), inc="ac")
                    psum_owner[key] = ("ac", av)
                P.wait("vector", "ac")
                P.emit("vector", lambda e, m=mt: e.tensor_mul(
                    upd[:, :], gates[0][:, :], c_f32[:, m * RPC:(m + 1) * RPC]))
                P.emit("vector", lambda e: e.tensor_mul(
                    gates[2][:, :], gates[1][:, :], gates[2][:, :]))
                P.emit("vector", lambda e, m=mt: e.tensor_add(
                    c_f32[:, m * RPC:(m + 1) * RPC], upd[:, :], gates[2][:, :]))
                dv_mt = P.emit("vector", lambda e, m=mt, n=nxt: e.tensor_copy(
                    n[:, m * RPC:(m + 1) * RPC], c_f32[:, m * RPC:(m + 1) * RPC]),
                    inc="dv")
            dv_after_update = P.cnt["dv"]
            produce_vk(t)
            if t % 2 == 1:
                pr = t // 2
                P.wait("gpsimd", "SV0")
                P.wait("gpsimd", "SV1")
                P.wait("gpsimd", "SK")
                P.emit("gpsimd", lambda e, p_=pr: e.collective_compute(
                    "AllGather", ALU.bypass, replica_groups=[list(range(NCORES))],
                    ins=[ag_in_big.ap()[(2 * p_) * RPCB:(2 * p_ + 2) * RPCB, :].opt()],
                    outs=[ag_out_pair[p_].ap().opt()]), inc="cc")

        c_fin = c_bfA if B % 2 == 0 else c_bfB

        P.barrier()

        phaseA_dv[0] = P.cnt["dv"]

        uoc_pe = {}
        for ot in range(KC):
            ustr = uo_strip[ot % 2]
            if ot >= 2:
                P.wait("sync", "pe", uoc_pe[ot - 2])
            dma("sync", "LR%d" % (ot % 2),
                ustr[:, :1024].rearrange("p (mc c) -> p mc c", mc=KC),
                mkap(uo_h.ap(), ot * 128, [[H, 128], [128 * H, KC], [1, 128]]))
            key = "base"
            pm = p_base
            group_begin(key)
            P.wait("tensor", "LR%d" % (ot % 2))
            for mc in range(KC):
                P.emit("tensor", lambda e, p=pm, s=ustr, m=mc:
                       e.matmul(p[:, :], s[:, m * 128:(m + 1) * 128],
                                c_fin[:, m * RPC:(m + 1) * RPC],
                                start=(m == 0), stop=(m == KC - 1)),
                       inc="pe" if mc == KC - 1 else None)
            pv = P.cnt["pe"]
            uoc_pe[ot] = pv
            P.wait("vector", "pe", pv)
            dvv = P.emit("vector", lambda e, o=ot, p=pm: e.tensor_copy(
                uocT[:, o * RPC:(o + 1) * RPC], p[:, :]), inc="dv")
            psum_owner[key] = ("dv", dvv)

        P.barrier()

        for jt in range(ET):
            par = jt % 2
            buf = (scr, scr2)[par]
            if scr_guard[par]:
                P.wait("sync", "dv", scr_guard[par])
            dma("sync", "L%d" % par, buf[:, :RPC],
                biasT_own[jt * 128:(jt + 1) * 128, :])
            P.wait("vector", "L%d" % par)
            dvv = P.emit("vector", lambda e, b=buf[:, :RPC], j=jt:
                         e.tensor_copy(biasT_sb[:, j * RPC:(j + 1) * RPC], b), inc="dv")
            scr_guard[par] = dvv
        for kc in range(KC):
            for hh in range(2):
                par = (kc * 2 + hh) % 2
                buf = (scr, scr2)[par]
                if scr_guard[par]:
                    P.wait("sync", "dv", scr_guard[par])
                dma("sync", "L%d" % par, buf[:, :],
                    woT[kc * 128:(kc + 1) * 128, hh * 512:(hh + 1) * 512])
                P.wait("vector", "L%d" % par)
                dvv = P.emit("vector", lambda e, b=buf[:, :], k=kc, h_=hh:
                             e.tensor_copy(woT_sb[:, k * H + h_ * 512:
                                                  k * H + h_ * 512 + 512], b),
                             inc="dv")
                scr_guard[par] = dvv

        P.barrier()

        produce_pre(0)
        produce(0)
        for b in range(B):
            consume_qk(b)
            if b + 1 < B:
                produce_pre(b + 1)
                produce(b + 1)
            consume(b)

        P.wait("gpsimd", "SO0")
        P.wait("gpsimd", "SO1")

        block = e(nc.Block())

        @block.sync
        def _(sync):
            for fn in P.ops["sync"]:
                fn(sync)

        @block.gpsimd
        def _(gpsimd):
            for fn in P.ops["gpsimd"]:
                fn(gpsimd)

        @block.tensor
        def _(tensor):
            for fn in P.ops["tensor"]:
                fn(tensor)

        @block.vector
        def _(vector):
            for fn in P.ops["vector"]:
                fn(vector)

        @block.scalar
        def _(scalar):
            for fn in P.ops["scalar"]:
                fn(scalar)

    return nc


_NC_CACHE = None


def kernel(**inputs):
    global _NC_CACHE
    x = np.asarray(inputs["L0_attn_x"], dtype=np.float32)
    sin_t, cos_t = _rope_tables()
    biasT = _bias_matrix(np.asarray(inputs["L0_attn_w"], np.float32))
    uv_w = np.asarray(inputs["L0_attn_uv_w"], np.float32)
    uv_b = np.asarray(inputs["L0_attn_uv_b"], np.float32)

    common = dict(
        ident_in=np.eye(128, dtype=np.float32),
        wuT=np.ascontiguousarray(uv_w[:E].T),
        wvT=np.ascontiguousarray(uv_w[E:2 * E].T),
        wbT=np.ascontiguousarray(uv_w[2 * E:].T),
        wbTs=np.ascontiguousarray(uv_w[2 * E:].T[:, _PERM]),
        owT=np.ascontiguousarray(np.asarray(inputs["L0_attn_o_w"], np.float32).T),
        woT=np.ascontiguousarray(np.asarray(inputs["L0_ffn_wo"], np.float32).T),
        wfT=np.ascontiguousarray(np.asarray(inputs["L0_ffn_wf"], np.float32).T),
        ufT=np.ascontiguousarray(np.asarray(inputs["L0_ffn_uf"], np.float32).T),
        wiT=np.ascontiguousarray(np.asarray(inputs["L0_ffn_wi"], np.float32).T),
        uiT=np.ascontiguousarray(np.asarray(inputs["L0_ffn_ui"], np.float32).T),
        wcT=np.ascontiguousarray(np.asarray(inputs["L0_ffn_wc"], np.float32).T),
        ucT=np.ascontiguousarray(np.asarray(inputs["L0_ffn_uc"], np.float32).T),
        uoT=np.ascontiguousarray(np.asarray(inputs["L0_ffn_uo"], np.float32).T),
        ln_w=np.asarray(inputs["L0_attn_ln_w"], np.float32),
        ln_b=np.asarray(inputs["L0_attn_ln_b"], np.float32),
        o_b=np.asarray(inputs["L0_attn_o_b"], np.float32),
        bf=np.asarray(inputs["L0_ffn_bf"], np.float32),
        bi=np.asarray(inputs["L0_ffn_bi"], np.float32),
        bc=np.asarray(inputs["L0_ffn_bc"], np.float32),
        bo=np.asarray(inputs["L0_ffn_bo"], np.float32),
        in_c=np.asarray(inputs["L0_ffn_in_c"], np.float32),
        uvb_u=np.ascontiguousarray(uv_b[:E]),
        uvb_v=np.ascontiguousarray(uv_b[E:2 * E]),
        uvb_b=np.ascontiguousarray(uv_b[2 * E:]),
        uvb_bs=np.ascontiguousarray(uv_b[2 * E:][_PERM]),
        gamma=np.asarray(inputs["L0_attn_gamma"], np.float32),
        beta=np.asarray(inputs["L0_attn_beta"], np.float32),
        gammas=np.ascontiguousarray(np.asarray(inputs["L0_attn_gamma"], np.float32)[:, _PERM]),
        betas=np.ascontiguousarray(np.asarray(inputs["L0_attn_beta"], np.float32)[:, _PERM]),
    )

    xT = np.ascontiguousarray(x.transpose(0, 2, 1))
    in_maps = []
    for c in range(NCORES):
        r0, r1 = c * RPC, (c + 1) * RPC
        m = dict(common)
        m["x_own"] = np.ascontiguousarray(xT[:, :, r0:r1])
        m["biasT_own"] = np.ascontiguousarray(biasT[:, r0:r1])
        st = sin_t[r0:r1].T
        ct = cos_t[r0:r1].T
        m["sinT"] = np.ascontiguousarray(np.concatenate([-st, st], axis=0))
        m["cosT"] = np.ascontiguousarray(np.concatenate([ct, ct], axis=0))
        in_maps.append(m)

    if _NC_CACHE is None:
        _NC_CACHE = build_program()
    res = run_bass_kernel_spmd(_NC_CACHE, in_maps, core_ids=list(range(NCORES)))
    shards = [res.results[c]["y_own"] for c in range(NCORES)]
    outT = np.concatenate(shards, axis=2)
    return np.ascontiguousarray(outT.transpose(0, 2, 1)).astype(np.float32)


# version 15
# speedup vs baseline: 1.2124x; 1.0470x over previous
import numpy as np

import concourse.bass as bass
import concourse.mybir as mybir
from concourse.bass_utils import run_bass_kernel_spmd

F32 = mybir.dt.float32
BF16 = mybir.dt.bfloat16
AF = mybir.ActivationFunctionType
ALU = mybir.AluOpType

NCORES = 8
B = 16
T = 2048
H = 1024
E = 2048
S = 128
RPC = T // NCORES
L = 512
INV_SQRT_S = 1.0 / float(np.sqrt(S))
EPS = 1e-5
KC = H // 128
ET = E // 128
RPCB = RPC + 16
_PERM = np.r_[64:128, 0:64]


def _rope_tables():
    try:
        import jax
        cpu = jax.devices("cpu")[0]
        import jax.numpy as jnp
        with jax.default_device(cpu):
            pos = jnp.arange(T, dtype=jnp.float32)
            inv_freq = 10000.0 ** (jnp.arange(64, dtype=jnp.float32) / 64)
            sinusoid = pos[:, None] * inv_freq[None, :]
            return (np.asarray(jnp.sin(sinusoid)), np.asarray(jnp.cos(sinusoid)))
    except Exception:
        pos = np.arange(T, dtype=np.float32)
        inv_freq = (10000.0 ** (np.arange(64, dtype=np.float32) /
                                np.float32(64))).astype(np.float32)
        sinusoid = pos[:, None] * inv_freq[None, :]
        return np.sin(sinusoid).astype(np.float32), np.cos(sinusoid).astype(np.float32)


def _bias_matrix(w):
    n = w.shape[0]
    rel = np.arange(T)[None, :] - np.arange(T)[:, None] + (L - 1)
    idx = np.where(rel < 0, rel + n, rel)
    idx = np.clip(idx, 0, n - 1)
    return np.ascontiguousarray(w[idx].T)


class Prog:
    ENGS = ("sync", "gpsimd", "tensor", "vector", "scalar")

    def __init__(self, nc, es):
        self.nc = nc
        self.es = es
        self.ops = {e: [] for e in self.ENGS}
        self.sem = {}
        self.cnt = {}
        self.dma_sems = {}
        for name in ("cc", "pe", "dv", "ac",
                     "b_sync", "b_gpsimd", "b_tensor", "b_vector", "b_scalar"):
            self.sem[name] = es.enter_context(nc.semaphore(name))
            self.cnt[name] = 0

    def dsem(self, name, issuer):
        if name not in self.sem:
            self.sem[name] = self.es.enter_context(self.nc.semaphore(name))
            self.cnt[name] = 0
            self.dma_sems[name] = issuer
        return name

    def wait(self, eng, sem, val=None):
        v = self.cnt[sem] if val is None else val
        h = self.sem[sem]
        self.ops[eng].append(lambda e, h=h, v=v: e.wait_ge(h, v))

    def emit(self, eng, fn, inc=None, amt=1):
        if eng == "vector":
            self.wait("vector", "dv")
            h = self.sem["dv"]
            self.cnt["dv"] += 1
            self.ops[eng].append(lambda e, fn=fn, h=h: fn(e).then_inc(h, 1))
            return self.cnt["dv"]
        if inc is None:
            self.ops[eng].append(fn)
            return None
        h = self.sem[inc]
        self.cnt[inc] += amt
        self.ops[eng].append(lambda e, fn=fn, h=h, a=amt: fn(e).then_inc(h, a))
        return self.cnt[inc]

    def barrier(self):
        for name, issuer in self.dma_sems.items():
            if self.cnt[name]:
                self.wait(issuer, name)
        self.wait("gpsimd", "cc")
        self.wait("vector", "dv")
        self.wait("scalar", "ac")
        self.wait("tensor", "pe")
        for e in self.ENGS:
            h = self.sem["b_" + e]
            self.cnt["b_" + e] += 1
            self.ops[e].append(lambda eng, h=h: eng.nop().then_inc(h, 1))
        for e in self.ENGS:
            for o in self.ENGS:
                if o != e:
                    self.wait(e, "b_" + o)


def build_program():
    nc = bass.Bass("TRN2", target_bir_lowering=False)
    dt = nc.dram_tensor

    x_own = dt("x_own", [B, H, RPC], F32, kind="ExternalInput").ap()
    biasT_own = dt("biasT_own", [T, RPC], F32, kind="ExternalInput").ap()
    sinT_in = dt("sinT", [128, RPC], F32, kind="ExternalInput").ap()
    cosT_in = dt("cosT", [128, RPC], F32, kind="ExternalInput").ap()
    ident_in = dt("ident_in", [128, 128], F32, kind="ExternalInput").ap()

    wuT = dt("wuT", [H, E], F32, kind="ExternalInput").ap()
    wvT = dt("wvT", [H, E], F32, kind="ExternalInput").ap()
    wbT = dt("wbT", [H, S], F32, kind="ExternalInput").ap()
    wbTs = dt("wbTs", [H, S], F32, kind="ExternalInput").ap()
    owT = dt("owT", [E, H], F32, kind="ExternalInput").ap()
    woT = dt("woT", [H, H], F32, kind="ExternalInput").ap()
    gateT = {g: dt(g + "T", [H, H], F32, kind="ExternalInput").ap()
             for g in ("wf", "uf", "wi", "ui", "wc", "uc", "uo")}

    vec_in = {}
    for name, n in (("ln_w", H), ("ln_b", H), ("o_b", H), ("bf", H), ("bi", H),
                    ("bc", H), ("bo", H), ("in_c", H), ("uvb_u", E), ("uvb_b", S), ("uvb_bs", S)):
        vec_in[name] = dt(name, [n], F32, kind="ExternalInput").ap()
    uvb_v_in = dt("uvb_v", [E], F32, kind="ExternalInput").ap()
    gamma_in = dt("gamma", [2, 128], F32, kind="ExternalInput").ap()
    gammas_in = dt("gammas", [2, 128], F32, kind="ExternalInput").ap()
    beta_in = dt("beta", [2, 128], F32, kind="ExternalInput").ap()
    betas_in = dt("betas", [2, 128], F32, kind="ExternalInput").ap()

    y_own = dt("y_own", [B, H, RPC], F32, kind="ExternalOutput").ap()

    wu_h = dt("wu_h", [H, E], BF16)
    wv_h = dt("wv_h", [H, E], BF16)
    wb_h = dt("wb_h", [H, S], BF16)
    wbs_h = dt("wbs_h", [H, S], BF16)
    ow_h = dt("ow_h", [E, H], BF16)
    uo_h = dt("uo_h", [H, H], BF16)
    wf_h = dt("wf_h", [H, H], BF16)
    wi_h = dt("wi_h", [H, H], BF16)
    wc_h = dt("wc_h", [H, H], BF16)
    ag_in_big = dt("ag_in_big", [B * RPCB, E], BF16)
    ag_out_pair = [dt(f"ag_out_p{p}", [NCORES * 2 * RPCB, E], BF16)
                   for p in range(B // 2)]

    from contextlib import ExitStack
    es = ExitStack()
    with es:
        e = es.enter_context

        xnT = e(nc.sbuf_tensor("xnT", [128, KC * B * RPC], BF16))
        bb = [e(nc.sbuf_tensor(f"bb{i}", [128, 8192], BF16)) for i in range(6)]
        aux_h = e(nc.sbuf_tensor("aux_h", [128, 3072], BF16))
        c_f32 = e(nc.sbuf_tensor("c_f32", [128, KC * RPC], F32))
        c_bfA = e(nc.sbuf_tensor("c_bfA", [128, KC * RPC], BF16))
        c_bfB = e(nc.sbuf_tensor("c_bfB", [128, KC * RPC], BF16))
        scr = e(nc.sbuf_tensor("scr", [128, 512], F32))
        scr2 = e(nc.sbuf_tensor("scr2", [128, 512], F32))
        gates = [e(nc.sbuf_tensor(f"gate{i}", [128, RPC], F32)) for i in range(3)]
        upd = e(nc.sbuf_tensor("upd", [128, RPC], F32))
        sq_h = e(nc.sbuf_tensor("sq_h", [128, RPC], BF16))
        tmp_h = e(nc.sbuf_tensor("tmp_h", [128, RPC], BF16))
        muB = e(nc.sbuf_tensor("muB", [128, RPC], BF16))
        rsB = e(nc.sbuf_tensor("rsB", [128, RPC], BF16))
        hT_b = e(nc.sbuf_tensor("hT_b", [128, RPC], BF16))
        xres = [e(nc.sbuf_tensor(f"xres{i}", [128, RPC], F32)) for i in range(2)]
        outs = [e(nc.sbuf_tensor(f"outs{i}", [128, RPC], F32)) for i in range(2)]
        stat = e(nc.sbuf_tensor("stat", [1, 2 * RPC], F32))
        uvbv_row = e(nc.sbuf_tensor("uvbv_row", [1, E], BF16))
        ones_col = e(nc.sbuf_tensor("ones_col", [128, 1], BF16))
        ones_row = e(nc.sbuf_tensor("ones_row", [1, 128], BF16))
        ones_row_f = e(nc.sbuf_tensor("ones_row_f", [1, 128], F32))
        ident_h = e(nc.sbuf_tensor("ident_h", [128, 128], BF16))
        sin_sb = e(nc.sbuf_tensor("sin_sb", [128, RPC], F32))
        cos_sb = e(nc.sbuf_tensor("cos_sb", [128, RPC], F32))
        sm = e(nc.sbuf_tensor("sm", [128, 96], F32))

        p_m = [e(nc.psum_tensor(f"p_m{i}", [128, RPC], F32)) for i in range(3)]
        p_v = [e(nc.psum_tensor(f"p_v{i}", [128, 512], F32)) for i in range(2)]
        p_base = e(nc.psum_tensor("p_base", [128, RPC], F32))
        p_h = e(nc.psum_tensor("p_h", [128, RPC], F32))
        p_stat = e(nc.psum_tensor("p_stat", [128, 512], F32))
        p_s = [p_stat[0:1, 0:RPC], p_stat[0:1, RPC:2 * RPC]]
        p_b = p_stat

        P = Prog(nc, es)
        psum_owner = {}

        def group_begin(key):
            if key in psum_owner:
                s, v = psum_owner[key]
                P.wait("tensor", s, v)

        smcol = {}
        _c = [0]
        def alloc_sm(name, ncols):
            smcol[name] = _c[0]
            _c[0] += ncols
        for name in ("ln_w", "ln_b", "o_b", "bf", "bi", "bc", "bo", "in_c"):
            alloc_sm(name, KC)
        alloc_sm("uvb_u", ET)
        alloc_sm("uvb_b", 1)
        alloc_sm("uvb_bs", 1)
        alloc_sm("gamma", 2)
        alloc_sm("beta", 2)
        alloc_sm("gammas", 2)
        alloc_sm("betas", 2)
        alloc_sm("g0p", 1)
        alloc_sm("b0p", 1)
        alloc_sm("g0ps", 1)
        alloc_sm("b0ps", 1)
        assert _c[0] <= 96

        def smc(name, i=0):
            return sm[:, smcol[name] + i: smcol[name] + i + 1]

        WG = {"uf": bb[1], "ui": bb[3], "uc": bb[5]}
        xg_buf = [[bb[4][:, (2 * g + i) * 1024:(2 * g + i + 1) * 1024]
                   for i in range(2)] for g in range(3)]
        wb_vk2 = [bb[0][:, i * 1024:(i + 1) * 1024] for i in range(2)]
        wv_vk = [bb[0][:, 2048 + i * 1024: 2048 + (i + 1) * 1024] for i in range(2)]
        kernelT = bb[0]
        kT_sb = bb[0][:, 4096:6144]
        qT_bufs = [bb[0][:, 6144:6400], bb[0][:, 6400:6656]]
        uT_bufs = [bb[1][:, 0:4096], bb[1][:, 4096:8192]]
        eT = bb[3][:, 4096:8192]
        biasT_sb = bb[2][:, 0:4096]
        uocT = bb[2][:, 4096:6144]
        xnnT = bb[2][:, 6144:8192]
        ow_strip = [bb[3][:, i * 2048:(i + 1) * 2048] for i in range(2)]
        woT_sb = bb[4]
        wu_strip = [bb[5][:, i * 1024:(i + 1) * 1024] for i in range(2)]
        wb_buf2 = [bb[5][:, 2048 + i * 1024: 2048 + (i + 1) * 1024] for i in range(2)]
        vstrip = [bb[5][:, 4096 + i * 2048: 4096 + (i + 1) * 2048] for i in range(2)]
        vout = [aux_h[:, i * 512:(i + 1) * 512] for i in range(2)]
        uo_strip = [aux_h[:, 1024 + i * 1024: 1024 + (i + 1) * 1024] for i in range(2)]
        x_bf = c_bfA
        uvbv_hlp = c_bfA
        cast_h = c_bfB

        import dataclasses as _dc

        def mkap(base_ap, offset, dims):
            return _dc.replace(base_ap, offset=offset,
                               ap=type(base_ap.ap)(dims))

        def dma(eng, sem, out, in_, slow=False):
            P.dsem(sem, eng)
            if slow:
                return P.emit(eng, lambda e, o=out, i=in_: e.dma_start(
                    out=o, in_=i, allow_slow_non_contiguous=True), inc=sem, amt=16)
            return P.emit(eng, lambda e, o=out, i=in_: e.dma_start(out=o, in_=i),
                          inc=sem, amt=16)

        for name in ("ln_w", "ln_b", "o_b", "bf", "bi", "bc", "bo", "in_c"):
            dma("sync", "LS", sm[:, smcol[name]:smcol[name] + KC],
                vec_in[name].rearrange("(t p) -> p t", p=128), slow=True)
        dma("sync", "LS", sm[:, smcol["uvb_u"]:smcol["uvb_u"] + ET],
            vec_in["uvb_u"].rearrange("(t p) -> p t", p=128), slow=True)
        dma("sync", "LS", sm[:, smcol["uvb_b"]:smcol["uvb_b"] + 1],
            vec_in["uvb_b"].rearrange("(t p) -> p t", p=128), slow=True)
        dma("sync", "LS", sm[:, smcol["uvb_bs"]:smcol["uvb_bs"] + 1],
            vec_in["uvb_bs"].rearrange("(t p) -> p t", p=128), slow=True)
        dma("sync", "LS", sm[:, smcol["gamma"]:smcol["gamma"] + 2],
            gamma_in.rearrange("h s -> s h"), slow=True)
        dma("sync", "LS", sm[:, smcol["beta"]:smcol["beta"] + 2],
            beta_in.rearrange("h s -> s h"), slow=True)
        dma("sync", "LS", sm[:, smcol["gammas"]:smcol["gammas"] + 2],
            gammas_in.rearrange("h s -> s h"), slow=True)
        dma("sync", "LS", sm[:, smcol["betas"]:smcol["betas"] + 2],
            betas_in.rearrange("h s -> s h"), slow=True)
        dma("sync", "LS", scr[:, :128], ident_in)
        dma("sync", "LS", sin_sb[:, :], sinT_in)
        dma("sync", "LS", cos_sb[:, :], cosT_in)

        P.wait("vector", "LS")
        P.emit("vector", lambda e: e.memset(ones_col[:, :], 1.0))
        P.emit("vector", lambda e: e.memset(ones_row[:, :], 1.0))
        P.emit("vector", lambda e: e.memset(ones_row_f[:, :], 1.0))
        P.emit("vector", lambda e: e.tensor_copy(ident_h[:, :], scr[:, :128]))
        for q in range(4):
            dma("sync", "L1", scr2[0:1, :512],
                uvb_v_in[q * 512:(q + 1) * 512].rearrange("(a m) -> a m", a=1))
            P.wait("vector", "L1")
            dvq = P.emit("vector", lambda e, q_=q: e.tensor_copy(
                uvbv_hlp[0:1, q_ * 512:(q_ + 1) * 512], scr2[0:1, :512]), inc="dv")
            P.wait("sync", "dv", dvq)
        P.wait("gpsimd", "dv")
        dma("gpsimd", "SU", uvbv_row[0:1, :], uvbv_hlp[0:1, 0:E])
        P.emit("vector", lambda e: e.tensor_scalar_mul(smc("g0p"), smc("gamma", 0), INV_SQRT_S))
        P.emit("vector", lambda e: e.tensor_scalar_mul(smc("b0p"), smc("beta", 0), INV_SQRT_S))
        P.emit("vector", lambda e: e.tensor_scalar_mul(smc("g0ps"), smc("gammas", 0), INV_SQRT_S))
        P.emit("vector", lambda e: e.tensor_scalar_mul(smc("b0ps"), smc("betas", 0), INV_SQRT_S),
               inc="dv")

        scr_guard = [0, 0]

        def cast_chunks(chunks):
            pass

        def cast_resident(src_ap, dst, n_out):
            i = 0
            for kc in range(KC):
                for off in range(0, n_out, 512):
                    wdt = min(512, n_out - off)
                    par = i % 2
                    buf = (scr, scr2)[par]
                    if scr_guard[par]:
                        P.wait("sync", "dv", scr_guard[par])
                    dma("sync", "L%d" % par, buf[:, :wdt],
                        src_ap[kc * 128:(kc + 1) * 128, off:off + wdt])
                    P.wait("vector", "L%d" % par)
                    dvv = P.emit("vector", lambda e, b=buf[:, :wdt],
                                 d=dst[:, kc * n_out + off: kc * n_out + off + wdt]:
                                 e.tensor_copy(d, b), inc="dv")
                    scr_guard[par] = dvv
                    i += 1

        cast_dram_guard = [0, 0]

        def cast_to_dram(src_ap, dst_dram, rows, cols):
            i = 0
            for r0 in range(0, rows, 128):
                for off in range(0, cols, 512):
                    wdt = min(512, cols - off)
                    par = i % 2
                    buf = (scr, scr2)[par]
                    if scr_guard[par]:
                        P.wait("sync", "dv", scr_guard[par])
                    dma("sync", "L%d" % par, buf[:, :wdt],
                        src_ap[r0:r0 + 128, off:off + wdt])
                    hbuf = cast_h[:, par * 1024: par * 1024 + wdt]
                    P.wait("vector", "L%d" % par)
                    if cast_dram_guard[par]:
                        P.wait("vector", "SC%d" % par)
                    dvv = P.emit("vector", lambda e, b=buf[:, :wdt], h=hbuf:
                                 e.tensor_copy(h, b), inc="dv")
                    scr_guard[par] = dvv
                    P.wait("gpsimd", "dv", dvv)
                    dma("gpsimd", "SC%d" % par, dst_dram.ap()[r0:r0 + 128, off:off + wdt], hbuf)
                    cast_dram_guard[par] = 1
                    i += 1

        for g in ("uf", "ui", "uc"):
            cast_resident(gateT[g], WG[g], H)
        cast_to_dram(gateT["wf"], wf_h, H, H)
        cast_to_dram(gateT["wi"], wi_h, H, H)
        cast_to_dram(gateT["wc"], wc_h, H, H)
        cast_to_dram(gateT["uo"], uo_h, H, H)
        cast_to_dram(wuT, wu_h, H, E)
        cast_to_dram(wvT, wv_h, H, E)
        cast_to_dram(wbT, wb_h, H, S)
        cast_to_dram(wbTs, wbs_h, H, S)
        cast_to_dram(owT, ow_h, E, H)

        P.barrier()

        st_mu = stat[0:1, 0:RPC]
        st_b = stat[0:1, RPC:2 * RPC]

        stats_guard = {"sq": 0, "bank": 0}

        def ln_stats(src_slices, src_dv=0):
            P.wait("tensor", "dv", max(src_dv, stats_guard["bank"]))
            P.wait("scalar", "dv", max(stats_guard["sq"], src_dv))
            if P.cnt.get("SK"):
                P.wait("scalar", "SK")
                P.wait("vector", "SK")
            for kc in range(KC):
                P.emit("tensor", lambda e, s=src_slices[kc], k=kc:
                       e.matmul(p_s[0], ones_col[:, :], s,
                                start=(k == 0), stop=(k == KC - 1)),
                       inc="pe")
            for kc in range(KC):
                sq = (sq_h, tmp_h)[kc % 2]
                if kc >= 2:
                    P.wait("scalar", "pe", pe_sq[kc - 2])
                av = P.emit("scalar", lambda e, s=src_slices[kc], q=sq:
                            e.activation(out=q[:, :], in_=s, func=AF.Square), inc="ac")
                P.wait("tensor", "ac", av)
                pv = P.emit("tensor", lambda e, k=kc, q=sq:
                            e.matmul(p_s[1], ones_col[:, :], q[:, :],
                                     start=(k == 0), stop=(k == KC - 1)),
                            inc="pe")
                if kc == 0:
                    pe_sq = {}
                pe_sq[kc] = pv
            P.wait("vector", "pe")
            P.emit("vector", lambda e: e.tensor_scalar_mul(st_mu, p_s[0], 1.0 / H))
            P.emit("vector", lambda e: e.tensor_scalar_mul(st_b, p_s[1], 1.0 / H))
            P.emit("vector", lambda e: e.tensor_mul(p_s[0], st_mu, st_mu))
            P.emit("vector", lambda e: e.tensor_sub(st_b, st_b, p_s[0]))
            P.emit("vector", lambda e: e.tensor_scalar_add(st_b, st_b, EPS), inc="dv")
            P.wait("scalar", "dv")
            P.emit("scalar", lambda e: e.activation(out=st_b, in_=st_b, func=AF.Sqrt),
                   inc="ac")
            P.wait("vector", "ac")
            recip_dv = P.emit("vector", lambda e: e.reciprocal(st_b, st_b), inc="dv")
            P.wait("tensor", "dv", recip_dv)
            P.emit("tensor", lambda e: e.matmul(p_b[:, 0:RPC], ones_row_f[:, :], st_mu,
                                                start=True, stop=True))
            P.emit("tensor", lambda e: e.matmul(p_b[:, RPC:2 * RPC], ones_row_f[:, :], st_b,
                                                start=True, stop=True), inc="pe")
            P.wait("vector", "pe")
            P.emit("vector", lambda e: e.tensor_copy(muB[:, :], p_b[:, 0:RPC]))
            stats_guard["bank"] = P.emit(
                "vector", lambda e: e.tensor_copy(rsB[:, :], p_b[:, RPC:2 * RPC]),
                inc="dv")

        def xnT_sl(kc, t):
            off = kc * (B * RPC) + t * RPC
            return xnT[:, off: off + RPC]

        for t in range(B):
            P.wait("vector", "pe")
            for kc in range(KC):
                par = kc % 2
                buf = (scr, scr2)[par]
                if scr_guard[par]:
                    P.wait("sync", "dv", scr_guard[par])
                dma("sync", "L%d" % par, buf[:, :RPC],
                    x_own[t, kc * 128:(kc + 1) * 128, :])
                P.wait("vector", "L%d" % par)
                dvv = P.emit("vector", lambda e, b=buf[:, :RPC],
                             d=x_bf[:, kc * RPC:(kc + 1) * RPC]:
                             e.tensor_copy(d, b), inc="dv")
                scr_guard[par] = dvv
            xs = [x_bf[:, kc * RPC:(kc + 1) * RPC] for kc in range(KC)]
            ln_stats(xs, src_dv=dvv)
            for kc in range(KC):
                P.emit("vector", lambda e, s=xs[kc]: e.tensor_sub(tmp_h[:, :], s, muB[:, :]))
                stats_guard["sq"] = P.emit(
                    "vector", lambda e, d=xnT_sl(kc, t):
                    e.tensor_mul(d, tmp_h[:, :], rsB[:, :]), inc="dv")

        P.barrier()

        def xnn_make(b):
            xs = [xnT_sl(kc, b) for kc in range(KC)]
            ln_stats(xs)
            P.wait("vector", "pe", produce_pe_end[0])
            for kc in range(KC):
                P.emit("vector", lambda e, s=xs[kc]: e.tensor_sub(tmp_h[:, :], s, muB[:, :]))
                P.emit("vector", lambda e: e.tensor_mul(sq_h[:, :], tmp_h[:, :], rsB[:, :]))
                stats_guard["sq"] = P.emit("vector", lambda e, k=kc: e.tensor_scalar(
                    out=xnnT[:, k * RPC:(k + 1) * RPC], in0=sq_h[:, :],
                    scalar1=smc("ln_w", k), scalar2=smc("ln_b", k),
                    op0=ALU.mult, op1=ALU.add), inc="dv")

        base_sw = c_f32[:, 0:RPC]
        relu_t = c_f32[:, RPC:2 * RPC]
        rope_dv_last = [0]
        produce_pe_end = [0]
        cc_of = {}
        qk_pe_end = {}
        eT_dv_end = {}

        def rope_make(dst, gcol, bcol, gscol, bscol, a0, a1,
                      bsrc=None, bssrc=None, tq=None):
            bsrc = gates[2][:, :] if bsrc is None else bsrc
            bssrc = base_sw if bssrc is None else bssrc
            tq = upd[:, :] if tq is None else tq
            P.emit("vector", lambda e, g=gcol, b_=bcol: e.tensor_scalar(
                out=tq, in0=bsrc, scalar1=g, scalar2=b_,
                op0=ALU.mult, op1=ALU.add))
            P.emit("vector", lambda e, g=gscol, b_=bscol: e.tensor_scalar(
                out=a0, in0=bssrc, scalar1=g, scalar2=b_,
                op0=ALU.mult, op1=ALU.add))
            P.emit("vector", lambda e: e.tensor_mul(a1, a0, sin_sb[:, :]))
            P.emit("vector", lambda e: e.tensor_mul(tq, tq, cos_sb[:, :]))
            return P.emit("vector", lambda e, d=dst:
                          e.tensor_add(d, tq, a1), inc="dv")

        vout_guard = [0, 0]
        out_guard = [0, 0]
        phaseA_dv = [0]

        vk_dv_xnn = [0]
        vk_base_pe = [0]

        vout_guard = [0, 0]
        out_guard = [0, 0]
        phaseA_dv = [0]

        vk_dv_xnn = [0]
        vk_base_pe = [0]
        v_pe = {}

        def produce_vk_pre(b):
            xnn_make(b)
            vk_dv_xnn[0] = P.cnt["dv"]
            if vk_base_pe[0]:
                P.wait("gpsimd", "pe", vk_base_pe[0])
            for swp in range(2):
                src = (wb_h if swp == 0 else wbs_h).ap()
                dma("gpsimd", "LBk",
                    wb_vk2[swp][:, :1024].rearrange("p (kc c) -> p kc c", kc=KC),
                    mkap(src, 0, [[S, 128], [128 * S, KC], [1, S]]))

        def produce_vk(b):
            dv_xnn = vk_dv_xnn[0]
            P.wait("tensor", "dv", dv_xnn)
            P.wait("scalar", "dv", rope_dv_last[0])
            for swp, dst_sb, bias_name in (
                    (0, xres[0][:, :], "uvb_b"),
                    (1, xres[1][:, :], "uvb_bs")):
                group_begin("base")
                if swp == 0:
                    P.wait("tensor", "LBk")
                for kc in range(KC):
                    P.emit("tensor", lambda e, k=kc, w=wb_vk2[swp]:
                           e.matmul(p_base[:, :], w[:, k * 128:(k + 1) * 128],
                                    xnnT[:, k * RPC:(k + 1) * RPC],
                                    start=(k == 0), stop=(k == KC - 1)),
                           inc="pe" if kc == KC - 1 else None)
                pv = P.cnt["pe"]
                P.wait("scalar", "pe", pv)
                av = P.emit("scalar", lambda e, d=dst_sb, bn=bias_name: e.activation(
                    out=d, in_=p_base[:, :], func=AF.Silu,
                    bias=smc(bn)), inc="ac")
                psum_owner["base"] = ("ac", av)
            vk_base_pe[0] = P.cnt["pe"]
            P.wait("vector", "ac", av)
            kv = rope_make(tmp_h[:, :], smc("gamma", 1), smc("beta", 1),
                           smc("gammas", 1), smc("betas", 1),
                           outs[0][:, :], outs[1][:, :],
                           bsrc=xres[0][:, :], bssrc=xres[1][:, :],
                           tq=xres[0][:, :])
            rope_dv_last[0] = kv
            stats_guard["sq"] = max(stats_guard["sq"], kv)
            P.wait("gpsimd", "dv", kv)
            dma("gpsimd", "SK",
                ag_in_big.ap()[b * RPCB + RPC: b * RPCB + RPC + 16, :]
                .rearrange("a (c i) -> (a c) i", i=RPC),
                tmp_h[:, :])
            for nch in range(4):
                for kh in range(4):
                    idx = nch * 4 + kh
                    par = idx % 2
                    wb2 = wv_vk[par]
                    prev = idx - 2 if idx >= 2 else idx + 14
                    if prev in v_pe:
                        P.wait(("sync", "gpsimd")[par], "pe", v_pe[prev])
                    dma(("sync", "gpsimd")[par], "LVk%d" % par,
                        wb2[:, :].rearrange("p (kk c) -> p kk c", kk=2),
                        mkap(wv_h.ap(), (2 * kh) * 128 * E + nch * 512,
                             [[E, 128], [128 * E, 2], [1, 512]]))
                    pv = None
                    for kk in range(2):
                      kc = 2 * kh + kk
                      for rt in range(2):
                        key = "pv%d" % rt
                        if kc == 0:
                            group_begin(key)
                            P.emit("tensor", lambda e, p=p_v[rt], n=nch: e.matmul(
                                p[:, :], ones_row[:, :],
                                uvbv_row[0:1, n * 512:(n + 1) * 512],
                                start=True, stop=False))
                        if kk == 0 and rt == 0:
                            P.wait("tensor", "LVk%d" % par)
                        pv = P.emit("tensor",
                                    lambda e, p=p_v[rt], w=wb2, r=rt, k=kc, q=kk:
                                    e.matmul(p[:, :],
                                             xnnT[:, k * RPC + r * 128: k * RPC + r * 128 + 128],
                                             w[:, q * 512:(q + 1) * 512],
                                             start=False, stop=(k == KC - 1)),
                                    inc="pe")
                        v_pe[idx] = pv
                        if kc == KC - 1:
                            vo = vout[rt]
                            P.wait("scalar", "pe", pv)
                            if vout_guard[rt]:
                                P.wait("scalar", "SV%d" % rt)
                            av = P.emit("scalar", lambda e, p=p_v[rt], o=vo: e.activation(
                                out=o, in_=p[:, :], func=AF.Silu), inc="ac")
                            psum_owner[key] = ("ac", av)
                            P.wait("gpsimd", "ac", av)
                            dma("gpsimd", "SV%d" % rt,
                                ag_in_big.ap()[b * RPCB + rt * 128: b * RPCB + (rt + 1) * 128,
                                               nch * 512:(nch + 1) * 512], vo)
                            vout_guard[rt] = 1
            produce_pe_end[0] = P.cnt["pe"]

        b_dv_xnn = {}
        b_base_pe = [0]
        b_dv_kernel = {}
        b_qv = {}

        def produce_pre(b):
            xnn_make(b)
            b_dv_xnn[b] = P.cnt["dv"]
            if b_base_pe[0]:
                P.wait("sync", "pe", b_base_pe[0])
            for swp in range(2):
                src = (wb_h if swp == 0 else wbs_h).ap()
                dma("sync", "LB",
                    wb_buf2[swp][:, :1024].rearrange("p (kc c) -> p kc c", kc=KC),
                    mkap(src, 0, [[S, 128], [128 * S, KC], [1, S]]))

        def produce(b):
            uT = uT_bufs[b % 2]
            qT_sb = qT_bufs[b % 2]
            dv_xnn = b_dv_xnn[b]
            P.wait("tensor", "dv", dv_xnn)
            P.wait("scalar", "dv", rope_dv_last[0])
            for swp, dst_sb, bias_name in (
                    (0, gates[2][:, :], "uvb_b"),
                    (1, base_sw, "uvb_bs")):
                group_begin("base")
                if swp == 0:
                    P.wait("tensor", "LB")
                for kc in range(KC):
                    P.emit("tensor", lambda e, k=kc, w=wb_buf2[swp]:
                           e.matmul(p_base[:, :], w[:, k * 128:(k + 1) * 128],
                                    xnnT[:, k * RPC:(k + 1) * RPC],
                                    start=(k == 0), stop=(k == KC - 1)),
                           inc="pe" if kc == KC - 1 else None)
                pv = P.cnt["pe"]
                P.wait("scalar", "pe", pv)
                av = P.emit("scalar", lambda e, d=dst_sb, bn=bias_name: e.activation(
                    out=d, in_=p_base[:, :], func=AF.Silu,
                    bias=smc(bn)), inc="ac")
                psum_owner["base"] = ("ac", av)
            b_base_pe[0] = P.cnt["pe"]
            P.wait("vector", "ac", av)
            a0 = gates[0][:, :]
            a1 = gates[1][:, :]
            if b >= 2:
                P.wait("vector", "pe", qk_pe_end[b - 2])
            qv = rope_make(qT_sb, smc("g0p"), smc("b0p"),
                           smc("g0ps"), smc("b0ps"), a0, a1)
            rope_dv_last[0] = qv
            b_qv[b] = qv
            if b >= 2:
                P.wait("scalar", "dv", eT_dv_end[b - 2])

            def u_dma(et):
                if et >= ET:
                    return
                if et >= 2:
                    P.wait("scalar", "pe", u_pe[et - 2])
                dma("scalar", "LU%d" % (et % 2),
                    wu_strip[et % 2][:, :1024].rearrange("p (kc c) -> p kc c", kc=KC),
                    mkap(wu_h.ap(), et * 128, [[E, 128], [128 * E, KC], [1, 128]]))

            u_dma(0)
            u_dma(1)
            for et in range(ET):
                sb = wu_strip[et % 2]
                key = "pv%d" % (et % 2)
                pm = p_v[et % 2][:, 0:RPC]
                group_begin(key)
                P.wait("tensor", "LU%d" % (et % 2))
                if et == 0:
                    P.wait("tensor", "dv", dv_xnn)
                for kc in range(KC):
                    pv = P.emit("tensor", lambda e, p=pm, s=sb, k=kc:
                                e.matmul(p, s[:, k * 128:(k + 1) * 128],
                                         xnnT[:, k * RPC:(k + 1) * RPC],
                                         start=(k == 0), stop=(k == KC - 1)),
                                inc="pe")
                if et == 0:
                    u_pe = {}
                u_pe[et] = pv
                u_dma(et + 2)
                P.wait("scalar", "pe", pv)
                av = P.emit("scalar", lambda e, p=pm, t_=et, u_=uT: e.activation(
                    out=u_[:, t_ * RPC:(t_ + 1) * RPC], in_=p, func=AF.Silu,
                    bias=smc("uvb_u", t_)), inc="ac")
                psum_owner[key] = ("ac", av)
            produce_pe_end[0] = P.cnt["pe"]

        def consume_qk(b):
            qT_sb = qT_bufs[b % 2]
            agp = ag_out_pair[b // 2]
            P.wait("gpsimd", "cc", b // 2 + 1)
            if b > 0:
                P.wait("gpsimd", "pe", qk_pe_end[b - 1])
            for r in range(NCORES):
                base_row = r * (2 * RPCB) + (b % 2) * RPCB + RPC
                blk = agp.ap()[base_row: base_row + 16, :]
                dma("gpsimd", "LK", kT_sb[:, r * RPC:(r + 1) * RPC],
                    blk.rearrange("a (c i) -> (a c) i", i=RPC))
            for jt in range(ET):
                key = "m%d" % (jt % 3)
                pm = p_m[jt % 3]
                group_begin(key)
                if jt == 0:
                    P.wait("tensor", "LK")
                    P.wait("tensor", "dv", b_qv[b])
                P.emit("tensor", lambda e, p=pm, j=jt: e.matmul(
                    p[:, :], ident_h[:, :], biasT_sb[:, j * RPC:(j + 1) * RPC],
                    start=True, stop=False))
                pv = P.emit("tensor", lambda e, p=pm, j=jt: e.matmul(
                    p[:, :], kT_sb[:, j * 128:(j + 1) * 128], qT_sb,
                    start=False, stop=True), inc="pe")
                P.wait("scalar", "pe", pv)
                if jt > 0:
                    P.wait("scalar", "dv", sq_dv)
                av = P.emit("scalar", lambda e, p=pm: e.activation(
                    out=relu_t, in_=p[:, :], func=AF.Relu), inc="ac")
                psum_owner[key] = ("ac", av)
                P.wait("vector", "ac", av)
                sq_dv = P.emit("vector", lambda e, j=jt: e.tensor_mul(
                    kernelT[:, j * RPC:(j + 1) * RPC], relu_t, relu_t), inc="dv")
            qk_pe_end[b] = P.cnt["pe"]
            b_dv_kernel[b] = P.cnt["dv"]

        def consume(b):
            uT = uT_bufs[b % 2]
            dv_kernel = b_dv_kernel[b]
            agp = ag_out_pair[b // 2]
            for et in range(ET):
                vs = vstrip[et % 2]
                if et >= 2:
                    P.wait("gpsimd", "pe", kv_pe[et - 2])
                    P.wait("sync", "pe", kv_pe[et - 2])

                for q in range(2):
                    dma(("gpsimd", "sync")[q], "LW%d_%d" % (et % 2, q),
                        mkap(vs, vs.offset + q * 128,
                             [[8192, 128], [256, NCORES], [1, 128]]),
                        mkap(agp.ap(),
                             ((b % 2) * RPCB + q * 128) * E + et * 128,
                             [[E, 128], [2 * RPCB * E, NCORES], [1, 128]]))
                key = "m%d" % (et % 3)
                pm = p_m[et % 3]
                group_begin(key)
                P.wait("tensor", "LW%d_0" % (et % 2))
                P.wait("tensor", "LW%d_1" % (et % 2))
                if et == 0:
                    P.wait("tensor", "dv", dv_kernel)
                for jt in range(ET):
                    pv = P.emit("tensor", lambda e, p=pm, v_=vs, j=jt: e.matmul(
                        p[:, :], v_[:, j * 128:(j + 1) * 128],
                        kernelT[:, j * RPC:(j + 1) * RPC],
                        start=(j == 0), stop=(j == ET - 1)),
                        inc="pe" if jt == ET - 1 else None)
                if et == 0:
                    kv_pe = {}
                kv_pe[et] = pv
                P.wait("vector", "pe", pv)
                dvv = P.emit("vector", lambda e, p=pm, t_=et: e.tensor_mul(
                    eT[:, t_ * RPC:(t_ + 1) * RPC], p[:, :],
                    uT[:, t_ * RPC:(t_ + 1) * RPC]), inc="dv")
                psum_owner[key] = ("dv", dvv)
            dv_eT = P.cnt["dv"]
            eT_dv_end[b] = dv_eT
            for ft in range(KC):
                osb = ow_strip[ft % 2]
                if ft >= 2:
                    P.wait("scalar", "pe", o_pe[ft - 2])
                dma("scalar", "LO%d" % (ft % 2),
                    osb[:, :2048].rearrange("p (ec c) -> p ec c", ec=ET),
                    mkap(ow_h.ap(), ft * 128, [[H, 128], [128 * H, ET], [1, 128]]))
                group_begin("h")
                P.emit("tensor", lambda e, f=ft: e.matmul(
                    p_h[:, :], ident_h[:, :], uocT[:, f * RPC:(f + 1) * RPC],
                    start=True, stop=False))
                for kc in range(KC):
                    P.emit("tensor", lambda e, f=ft, k=kc, b_=b: e.matmul(
                        p_h[:, :], woT_sb[:, k * H + f * 128: k * H + f * 128 + 128],
                        xnT_sl(k, b_), start=False, stop=(k == KC - 1)),
                        inc="pe" if kc == KC - 1 else None)
                pv_h = P.cnt["pe"]
                P.wait("scalar", "pe", pv_h)
                if ft > 0:
                    P.wait("scalar", "dv", comb_dv)
                av = P.emit("scalar", lambda e, f=ft: e.activation(
                    out=hT_b[:, :], in_=p_h[:, :], func=AF.Tanh, bias=smc("bo", f)),
                    inc="ac")
                psum_owner["h"] = ("ac", av)
                key = "m%d" % (ft % 3)
                pm = p_m[ft % 3]
                group_begin(key)
                P.emit("tensor", lambda e, p=pm, f=ft, b_=b: e.matmul(
                    p[:, :], ident_h[:, :], xnT_sl(f, b_), start=True, stop=False))
                P.wait("tensor", "LO%d" % (ft % 2))
                if ft == 0:
                    P.wait("tensor", "dv", dv_eT)
                for ec in range(ET):
                    P.emit("tensor", lambda e, p=pm, s=osb, c_=ec: e.matmul(
                        p[:, :], s[:, c_ * 128:(c_ + 1) * 128],
                        eT[:, c_ * RPC:(c_ + 1) * RPC],
                        start=False, stop=(c_ == ET - 1)),
                        inc="pe" if ec == ET - 1 else None)
                pv_o = P.cnt["pe"]
                if ft == 0:
                    o_pe = {}
                o_pe[ft] = pv_o
                xr = xres[ft % 2]
                if b == 0 and ft < 2:
                    P.wait("sync", "dv", phaseA_dv[0])
                dma("sync", "LX%d" % (ft % 2), xr[:, :],
                    x_own[b, ft * 128:(ft + 1) * 128, :])
                ob = outs[ft % 2]
                P.wait("vector", "pe", pv_o)
                P.wait("vector", "ac", av)
                if out_guard[ft % 2]:
                    P.wait("vector", "SO%d" % (ft % 2))
                P.emit("vector", lambda e, p=pm, f=ft, o=ob: e.scalar_tensor_tensor(
                    out=o[:, :], in0=p[:, :], scalar=smc("o_b", f), in1=hT_b[:, :],
                    op0=ALU.add, op1=ALU.add), inc="dv")
                psum_owner[key] = ("dv", P.cnt["dv"])
                P.wait("vector", "LX%d" % (ft % 2))
                comb_dv = P.emit("vector", lambda e, o=ob, x=xr: e.tensor_add(
                    o[:, :], o[:, :], x[:, :]), inc="dv")
                P.wait("gpsimd", "dv", comb_dv)
                dma("gpsimd", "SO%d" % (ft % 2), y_own[b, ft * 128:(ft + 1) * 128, :],
                    ob[:, :])
                out_guard[ft % 2] = 1
                P.wait("sync", "dv", comb_dv)

        for mt in range(KC):
            P.emit("vector", lambda e, m=mt: e.tensor_scalar(
                out=c_f32[:, m * RPC:(m + 1) * RPC], in0=muB[:, :],
                scalar1=0.0, scalar2=smc("in_c", m), op0=ALU.mult, op1=ALU.add))
        for mt in range(KC):
            P.emit("vector", lambda e, m=mt: e.tensor_copy(
                c_bfA[:, m * RPC:(m + 1) * RPC], c_f32[:, m * RPC:(m + 1) * RPC]),
                inc="dv")

        gatespec = [("wf", wf_h, "uf", "bf", AF.Sigmoid, 0),
                    ("wi", wi_h, "ui", "bi", AF.Sigmoid, 1),
                    ("wc", wc_h, "uc", "bc", AF.Tanh, 2)]
        xg_pe = [{} for _ in range(3)]

        def xg_dma(gi, gidx):
            t_, mt_ = divmod(gidx, KC)
            if t_ >= B:
                return
            wh = (wf_h, wi_h, wc_h)[gi]
            par = gidx % 2
            if gidx >= 2:
                P.wait("sync", "pe", xg_pe[gi][gidx - 2])
            dma("sync", "LG%d_%d" % (gi, par),
                xg_buf[gi][par][:, :1024].rearrange("p (kc c) -> p kc c", kc=KC),
                mkap(wh.ap(), mt_ * 128, [[H, 128], [128 * H, KC], [1, 128]]))

        dv_after_update = P.cnt["dv"]
        for gi in range(3):
            xg_dma(gi, 0)
            xg_dma(gi, 1)
        for t in range(B):
            produce_vk_pre(t)
            cur = c_bfA if t % 2 == 0 else c_bfB
            nxt = c_bfB if t % 2 == 0 else c_bfA
            first_group = True
            for mt in range(KC):
                gidx = t * KC + mt
                for (wg, wh, ug, bg, func, gi) in gatespec:
                    key = "m%d" % ((mt * 3 + gi) % 3)
                    pm = p_m[(mt * 3 + gi) % 3]
                    sbx = xg_buf[gi][gidx % 2]
                    group_begin(key)
                    P.wait("tensor", "LG%d_%d" % (gi, gidx % 2))
                    for kc in range(KC):
                        P.emit("tensor", lambda e, p=pm, w=sbx, k=kc, t_=t:
                               e.matmul(p[:, :],
                                        w[:, k * 128:(k + 1) * 128],
                                        xnT_sl(k, t_), start=(k == 0), stop=False))
                    if first_group:
                        P.wait("tensor", "dv", dv_after_update)
                        first_group = False
                    for kc in range(KC):
                        P.emit("tensor", lambda e, p=pm, u=WG[ug], m=mt, k=kc, c=cur:
                               e.matmul(p[:, :],
                                        u[:, k * H + m * 128: k * H + m * 128 + 128],
                                        c[:, k * RPC:(k + 1) * RPC],
                                        start=False, stop=(k == KC - 1)),
                               inc="pe" if kc == KC - 1 else None)
                    pv = P.cnt["pe"]
                    xg_pe[gi][gidx] = pv
                    xg_dma(gi, gidx + 2)
                    P.wait("scalar", "pe", pv)
                    P.wait("scalar", "dv", dv_after_update if mt == 0 else dv_mt)
                    av = P.emit("scalar", lambda e, p=pm, f=func, b=bg, m=mt, g=gi:
                                e.activation(out=gates[g][:, :], in_=p[:, :], func=f,
                                             bias=smc(b, m)), inc="ac")
                    psum_owner[key] = ("ac", av)
                P.wait("vector", "ac")
                P.emit("vector", lambda e, m=mt: e.tensor_mul(
                    upd[:, :], gates[0][:, :], c_f32[:, m * RPC:(m + 1) * RPC]))
                P.emit("vector", lambda e: e.tensor_mul(
                    gates[2][:, :], gates[1][:, :], gates[2][:, :]))
                P.emit("vector", lambda e, m=mt: e.tensor_add(
                    c_f32[:, m * RPC:(m + 1) * RPC], upd[:, :], gates[2][:, :]))
                dv_mt = P.emit("vector", lambda e, m=mt, n=nxt: e.tensor_copy(
                    n[:, m * RPC:(m + 1) * RPC], c_f32[:, m * RPC:(m + 1) * RPC]),
                    inc="dv")
            dv_after_update = P.cnt["dv"]
            produce_vk(t)
            if t % 2 == 1:
                pr = t // 2
                P.wait("gpsimd", "SV0")
                P.wait("gpsimd", "SV1")
                P.wait("gpsimd", "SK")
                P.emit("gpsimd", lambda e, p_=pr: e.collective_compute(
                    "AllGather", ALU.bypass, replica_groups=[list(range(NCORES))],
                    ins=[ag_in_big.ap()[(2 * p_) * RPCB:(2 * p_ + 2) * RPCB, :].opt()],
                    outs=[ag_out_pair[p_].ap().opt()]), inc="cc")

        c_fin = c_bfA if B % 2 == 0 else c_bfB

        P.barrier()

        phaseA_dv[0] = P.cnt["dv"]

        uoc_pe = {}
        for ot in range(KC):
            ustr = uo_strip[ot % 2]
            if ot >= 2:
                P.wait("sync", "pe", uoc_pe[ot - 2])
            dma("sync", "LR%d" % (ot % 2),
                ustr[:, :1024].rearrange("p (mc c) -> p mc c", mc=KC),
                mkap(uo_h.ap(), ot * 128, [[H, 128], [128 * H, KC], [1, 128]]))
            key = "base"
            pm = p_base
            group_begin(key)
            P.wait("tensor", "LR%d" % (ot % 2))
            for mc in range(KC):
                P.emit("tensor", lambda e, p=pm, s=ustr, m=mc:
                       e.matmul(p[:, :], s[:, m * 128:(m + 1) * 128],
                                c_fin[:, m * RPC:(m + 1) * RPC],
                                start=(m == 0), stop=(m == KC - 1)),
                       inc="pe" if mc == KC - 1 else None)
            pv = P.cnt["pe"]
            uoc_pe[ot] = pv
            P.wait("vector", "pe", pv)
            dvv = P.emit("vector", lambda e, o=ot, p=pm: e.tensor_copy(
                uocT[:, o * RPC:(o + 1) * RPC], p[:, :]), inc="dv")
            psum_owner[key] = ("dv", dvv)

        P.barrier()

        for jt in range(ET):
            par = jt % 2
            buf = (scr, scr2)[par]
            if scr_guard[par]:
                P.wait("sync", "dv", scr_guard[par])
            dma("sync", "L%d" % par, buf[:, :RPC],
                biasT_own[jt * 128:(jt + 1) * 128, :])
            P.wait("vector", "L%d" % par)
            dvv = P.emit("vector", lambda e, b=buf[:, :RPC], j=jt:
                         e.tensor_copy(biasT_sb[:, j * RPC:(j + 1) * RPC], b), inc="dv")
            scr_guard[par] = dvv
        for kc in range(KC):
            for hh in range(2):
                par = (kc * 2 + hh) % 2
                buf = (scr, scr2)[par]
                if scr_guard[par]:
                    P.wait("sync", "dv", scr_guard[par])
                dma("sync", "L%d" % par, buf[:, :],
                    woT[kc * 128:(kc + 1) * 128, hh * 512:(hh + 1) * 512])
                P.wait("vector", "L%d" % par)
                dvv = P.emit("vector", lambda e, b=buf[:, :], k=kc, h_=hh:
                             e.tensor_copy(woT_sb[:, k * H + h_ * 512:
                                                  k * H + h_ * 512 + 512], b),
                             inc="dv")
                scr_guard[par] = dvv

        P.barrier()

        produce_pre(0)
        produce(0)
        for b in range(B):
            if b + 1 < B:
                produce_pre(b + 1)
            consume_qk(b)
            if b + 1 < B:
                produce(b + 1)
            consume(b)

        P.wait("gpsimd", "SO0")
        P.wait("gpsimd", "SO1")

        block = e(nc.Block())

        @block.sync
        def _(sync):
            for fn in P.ops["sync"]:
                fn(sync)

        @block.gpsimd
        def _(gpsimd):
            for fn in P.ops["gpsimd"]:
                fn(gpsimd)

        @block.tensor
        def _(tensor):
            for fn in P.ops["tensor"]:
                fn(tensor)

        @block.vector
        def _(vector):
            for fn in P.ops["vector"]:
                fn(vector)

        @block.scalar
        def _(scalar):
            for fn in P.ops["scalar"]:
                fn(scalar)

    return nc


_NC_CACHE = None


def kernel(**inputs):
    global _NC_CACHE
    x = np.asarray(inputs["L0_attn_x"], dtype=np.float32)
    sin_t, cos_t = _rope_tables()
    biasT = _bias_matrix(np.asarray(inputs["L0_attn_w"], np.float32))
    uv_w = np.asarray(inputs["L0_attn_uv_w"], np.float32)
    uv_b = np.asarray(inputs["L0_attn_uv_b"], np.float32)

    common = dict(
        ident_in=np.eye(128, dtype=np.float32),
        wuT=np.ascontiguousarray(uv_w[:E].T),
        wvT=np.ascontiguousarray(uv_w[E:2 * E].T),
        wbT=np.ascontiguousarray(uv_w[2 * E:].T),
        wbTs=np.ascontiguousarray(uv_w[2 * E:].T[:, _PERM]),
        owT=np.ascontiguousarray(np.asarray(inputs["L0_attn_o_w"], np.float32).T),
        woT=np.ascontiguousarray(np.asarray(inputs["L0_ffn_wo"], np.float32).T),
        wfT=np.ascontiguousarray(np.asarray(inputs["L0_ffn_wf"], np.float32).T),
        ufT=np.ascontiguousarray(np.asarray(inputs["L0_ffn_uf"], np.float32).T),
        wiT=np.ascontiguousarray(np.asarray(inputs["L0_ffn_wi"], np.float32).T),
        uiT=np.ascontiguousarray(np.asarray(inputs["L0_ffn_ui"], np.float32).T),
        wcT=np.ascontiguousarray(np.asarray(inputs["L0_ffn_wc"], np.float32).T),
        ucT=np.ascontiguousarray(np.asarray(inputs["L0_ffn_uc"], np.float32).T),
        uoT=np.ascontiguousarray(np.asarray(inputs["L0_ffn_uo"], np.float32).T),
        ln_w=np.asarray(inputs["L0_attn_ln_w"], np.float32),
        ln_b=np.asarray(inputs["L0_attn_ln_b"], np.float32),
        o_b=np.asarray(inputs["L0_attn_o_b"], np.float32),
        bf=np.asarray(inputs["L0_ffn_bf"], np.float32),
        bi=np.asarray(inputs["L0_ffn_bi"], np.float32),
        bc=np.asarray(inputs["L0_ffn_bc"], np.float32),
        bo=np.asarray(inputs["L0_ffn_bo"], np.float32),
        in_c=np.asarray(inputs["L0_ffn_in_c"], np.float32),
        uvb_u=np.ascontiguousarray(uv_b[:E]),
        uvb_v=np.ascontiguousarray(uv_b[E:2 * E]),
        uvb_b=np.ascontiguousarray(uv_b[2 * E:]),
        uvb_bs=np.ascontiguousarray(uv_b[2 * E:][_PERM]),
        gamma=np.asarray(inputs["L0_attn_gamma"], np.float32),
        beta=np.asarray(inputs["L0_attn_beta"], np.float32),
        gammas=np.ascontiguousarray(np.asarray(inputs["L0_attn_gamma"], np.float32)[:, _PERM]),
        betas=np.ascontiguousarray(np.asarray(inputs["L0_attn_beta"], np.float32)[:, _PERM]),
    )

    xT = np.ascontiguousarray(x.transpose(0, 2, 1))
    in_maps = []
    for c in range(NCORES):
        r0, r1 = c * RPC, (c + 1) * RPC
        m = dict(common)
        m["x_own"] = np.ascontiguousarray(xT[:, :, r0:r1])
        m["biasT_own"] = np.ascontiguousarray(biasT[:, r0:r1])
        st = sin_t[r0:r1].T
        ct = cos_t[r0:r1].T
        m["sinT"] = np.ascontiguousarray(np.concatenate([-st, st], axis=0))
        m["cosT"] = np.ascontiguousarray(np.concatenate([ct, ct], axis=0))
        in_maps.append(m)

    if _NC_CACHE is None:
        _NC_CACHE = build_program()
    res = run_bass_kernel_spmd(_NC_CACHE, in_maps, core_ids=list(range(NCORES)))
    shards = [res.results[c]["y_own"] for c in range(NCORES)]
    outT = np.concatenate(shards, axis=2)
    return np.ascontiguousarray(outT.transpose(0, 2, 1)).astype(np.float32)


# version 16
# speedup vs baseline: 1.2487x; 1.0300x over previous
import numpy as np

import concourse.bass as bass
import concourse.mybir as mybir
from concourse.bass_utils import run_bass_kernel_spmd

F32 = mybir.dt.float32
BF16 = mybir.dt.bfloat16
AF = mybir.ActivationFunctionType
ALU = mybir.AluOpType

NCORES = 8
B = 16
T = 2048
H = 1024
E = 2048
S = 128
RPC = T // NCORES
L = 512
INV_SQRT_S = 1.0 / float(np.sqrt(S))
EPS = 1e-5
KC = H // 128
ET = E // 128
RPCB = RPC + 16
_PERM = np.r_[64:128, 0:64]


def _rope_tables():
    try:
        import jax
        cpu = jax.devices("cpu")[0]
        import jax.numpy as jnp
        with jax.default_device(cpu):
            pos = jnp.arange(T, dtype=jnp.float32)
            inv_freq = 10000.0 ** (jnp.arange(64, dtype=jnp.float32) / 64)
            sinusoid = pos[:, None] * inv_freq[None, :]
            return (np.asarray(jnp.sin(sinusoid)), np.asarray(jnp.cos(sinusoid)))
    except Exception:
        pos = np.arange(T, dtype=np.float32)
        inv_freq = (10000.0 ** (np.arange(64, dtype=np.float32) /
                                np.float32(64))).astype(np.float32)
        sinusoid = pos[:, None] * inv_freq[None, :]
        return np.sin(sinusoid).astype(np.float32), np.cos(sinusoid).astype(np.float32)


def _bias_matrix(w):
    n = w.shape[0]
    rel = np.arange(T)[None, :] - np.arange(T)[:, None] + (L - 1)
    idx = np.where(rel < 0, rel + n, rel)
    idx = np.clip(idx, 0, n - 1)
    return np.ascontiguousarray(w[idx].T)


class Prog:
    ENGS = ("sync", "gpsimd", "tensor", "vector", "scalar")

    def __init__(self, nc, es):
        self.nc = nc
        self.es = es
        self.ops = {e: [] for e in self.ENGS}
        self.sem = {}
        self.cnt = {}
        self.dma_sems = {}
        for name in ("cc", "pe", "dv", "ac",
                     "b_sync", "b_gpsimd", "b_tensor", "b_vector", "b_scalar"):
            self.sem[name] = es.enter_context(nc.semaphore(name))
            self.cnt[name] = 0

    def dsem(self, name, issuer):
        if name not in self.sem:
            self.sem[name] = self.es.enter_context(self.nc.semaphore(name))
            self.cnt[name] = 0
            self.dma_sems[name] = issuer
        return name

    def wait(self, eng, sem, val=None):
        v = self.cnt[sem] if val is None else val
        h = self.sem[sem]
        self.ops[eng].append(lambda e, h=h, v=v: e.wait_ge(h, v))

    def emit(self, eng, fn, inc=None, amt=1):
        if eng == "vector":
            self.wait("vector", "dv")
            h = self.sem["dv"]
            self.cnt["dv"] += 1
            self.ops[eng].append(lambda e, fn=fn, h=h: fn(e).then_inc(h, 1))
            return self.cnt["dv"]
        if inc is None:
            self.ops[eng].append(fn)
            return None
        h = self.sem[inc]
        self.cnt[inc] += amt
        self.ops[eng].append(lambda e, fn=fn, h=h, a=amt: fn(e).then_inc(h, a))
        return self.cnt[inc]

    def barrier(self):
        for name, issuer in self.dma_sems.items():
            if self.cnt[name]:
                self.wait(issuer, name)
        self.wait("gpsimd", "cc")
        self.wait("vector", "dv")
        self.wait("scalar", "ac")
        self.wait("tensor", "pe")
        for e in self.ENGS:
            h = self.sem["b_" + e]
            self.cnt["b_" + e] += 1
            self.ops[e].append(lambda eng, h=h: eng.nop().then_inc(h, 1))
        for e in self.ENGS:
            for o in self.ENGS:
                if o != e:
                    self.wait(e, "b_" + o)


def build_program():
    nc = bass.Bass("TRN2", target_bir_lowering=False)
    dt = nc.dram_tensor

    x_own = dt("x_own", [B, H, RPC], F32, kind="ExternalInput").ap()
    biasT_own = dt("biasT_own", [T, RPC], F32, kind="ExternalInput").ap()
    sinT_in = dt("sinT", [128, RPC], F32, kind="ExternalInput").ap()
    cosT_in = dt("cosT", [128, RPC], F32, kind="ExternalInput").ap()
    ident_in = dt("ident_in", [128, 128], F32, kind="ExternalInput").ap()

    wuT = dt("wuT", [H, E], F32, kind="ExternalInput").ap()
    wvT = dt("wvT", [H, E], F32, kind="ExternalInput").ap()
    wbT = dt("wbT", [H, S], F32, kind="ExternalInput").ap()
    wbTs = dt("wbTs", [H, S], F32, kind="ExternalInput").ap()
    owT = dt("owT", [E, H], F32, kind="ExternalInput").ap()
    woT = dt("woT", [H, H], F32, kind="ExternalInput").ap()
    gateT = {g: dt(g + "T", [H, H], F32, kind="ExternalInput").ap()
             for g in ("wf", "uf", "wi", "ui", "wc", "uc", "uo")}

    vec_in = {}
    for name, n in (("ln_w", H), ("ln_b", H), ("o_b", H), ("bf", H), ("bi", H),
                    ("bc", H), ("bo", H), ("in_c", H), ("uvb_u", E), ("uvb_b", S), ("uvb_bs", S)):
        vec_in[name] = dt(name, [n], F32, kind="ExternalInput").ap()
    uvb_v_in = dt("uvb_v", [E], F32, kind="ExternalInput").ap()
    gamma_in = dt("gamma", [2, 128], F32, kind="ExternalInput").ap()
    gammas_in = dt("gammas", [2, 128], F32, kind="ExternalInput").ap()
    beta_in = dt("beta", [2, 128], F32, kind="ExternalInput").ap()
    betas_in = dt("betas", [2, 128], F32, kind="ExternalInput").ap()

    y_own = dt("y_own", [B, H, RPC], F32, kind="ExternalOutput").ap()

    wu_h = dt("wu_h", [H, E], BF16)
    wv_h = dt("wv_h", [H, E], BF16)
    wb_h = dt("wb_h", [H, S], BF16)
    wbs_h = dt("wbs_h", [H, S], BF16)
    ow_h = dt("ow_h", [E, H], BF16)
    uo_h = dt("uo_h", [H, H], BF16)
    wf_h = dt("wf_h", [H, H], BF16)
    wi_h = dt("wi_h", [H, H], BF16)
    wc_h = dt("wc_h", [H, H], BF16)
    ag_in_big = dt("ag_in_big", [B * RPCB, E], BF16)
    ag_out_pair = [dt(f"ag_out_p{p}", [NCORES * 2 * RPCB, E], BF16)
                   for p in range(B // 2)]

    from contextlib import ExitStack
    es = ExitStack()
    with es:
        e = es.enter_context

        xnT = e(nc.sbuf_tensor("xnT", [128, KC * B * RPC], BF16))
        bb = [e(nc.sbuf_tensor(f"bb{i}", [128, 8192], BF16)) for i in range(6)]
        aux_h = e(nc.sbuf_tensor("aux_h", [128, 3072], BF16))
        c_f32 = e(nc.sbuf_tensor("c_f32", [128, KC * RPC], F32))
        c_bfA = e(nc.sbuf_tensor("c_bfA", [128, KC * RPC], BF16))
        c_bfB = e(nc.sbuf_tensor("c_bfB", [128, KC * RPC], BF16))
        scr = e(nc.sbuf_tensor("scr", [128, 512], F32))
        scr2 = e(nc.sbuf_tensor("scr2", [128, 512], F32))
        gates = [e(nc.sbuf_tensor(f"gate{i}", [128, RPC], F32)) for i in range(3)]
        upd = e(nc.sbuf_tensor("upd", [128, RPC], F32))
        sq_h = e(nc.sbuf_tensor("sq_h", [128, RPC], BF16))
        tmp_h = e(nc.sbuf_tensor("tmp_h", [128, RPC], BF16))
        muB = e(nc.sbuf_tensor("muB", [128, RPC], BF16))
        rsB = e(nc.sbuf_tensor("rsB", [128, RPC], BF16))
        hT_b = e(nc.sbuf_tensor("hT_b", [128, RPC], BF16))
        xres = [e(nc.sbuf_tensor(f"xres{i}", [128, RPC], F32)) for i in range(2)]
        outs = [e(nc.sbuf_tensor(f"outs{i}", [128, RPC], F32)) for i in range(2)]
        stat = e(nc.sbuf_tensor("stat", [1, 2 * RPC], F32))
        uvbv_row = e(nc.sbuf_tensor("uvbv_row", [1, E], BF16))
        ones_col = e(nc.sbuf_tensor("ones_col", [128, 1], BF16))
        ones_row = e(nc.sbuf_tensor("ones_row", [1, 128], BF16))
        ones_row_f = e(nc.sbuf_tensor("ones_row_f", [1, 128], F32))
        ident_h = e(nc.sbuf_tensor("ident_h", [128, 128], BF16))
        sin_sb = e(nc.sbuf_tensor("sin_sb", [128, RPC], F32))
        cos_sb = e(nc.sbuf_tensor("cos_sb", [128, RPC], F32))
        sm = e(nc.sbuf_tensor("sm", [128, 96], F32))

        p_m = [e(nc.psum_tensor(f"p_m{i}", [128, RPC], F32)) for i in range(3)]
        p_v = [e(nc.psum_tensor(f"p_v{i}", [128, 512], F32)) for i in range(2)]
        p_base = e(nc.psum_tensor("p_base", [128, RPC], F32))
        p_h = e(nc.psum_tensor("p_h", [128, RPC], F32))
        p_stat = e(nc.psum_tensor("p_stat", [128, 512], F32))
        p_s = [p_stat[0:1, 0:RPC], p_stat[0:1, RPC:2 * RPC]]
        p_b = p_stat

        P = Prog(nc, es)
        psum_owner = {}

        def group_begin(key):
            if key in psum_owner:
                s, v = psum_owner[key]
                P.wait("tensor", s, v)

        smcol = {}
        _c = [0]
        def alloc_sm(name, ncols):
            smcol[name] = _c[0]
            _c[0] += ncols
        for name in ("ln_w", "ln_b", "o_b", "bf", "bi", "bc", "bo", "in_c"):
            alloc_sm(name, KC)
        alloc_sm("uvb_u", ET)
        alloc_sm("uvb_b", 1)
        alloc_sm("uvb_bs", 1)
        alloc_sm("gamma", 2)
        alloc_sm("beta", 2)
        alloc_sm("gammas", 2)
        alloc_sm("betas", 2)
        alloc_sm("g0p", 1)
        alloc_sm("b0p", 1)
        alloc_sm("g0ps", 1)
        alloc_sm("b0ps", 1)
        assert _c[0] <= 96

        def smc(name, i=0):
            return sm[:, smcol[name] + i: smcol[name] + i + 1]

        WG = {"uf": bb[1], "ui": bb[3], "uc": bb[5]}
        xg_buf = [[bb[4][:, (2 * g + i) * 1024:(2 * g + i + 1) * 1024]
                   for i in range(2)] for g in range(3)]
        wb_vk2 = [bb[0][:, i * 1024:(i + 1) * 1024] for i in range(2)]
        wv_vk = [bb[0][:, 2048 + i * 1024: 2048 + (i + 1) * 1024] for i in range(2)]
        kernelT = bb[0]
        kT_sb = bb[0][:, 4096:6144]
        qT_bufs = [bb[0][:, 6144:6400], bb[0][:, 6400:6656]]
        uT_bufs = [bb[1][:, 0:4096], bb[1][:, 4096:8192]]
        eT = bb[3][:, 4096:8192]
        biasT_sb = bb[2][:, 0:4096]
        uocT = bb[2][:, 4096:6144]
        xnnT = bb[2][:, 6144:8192]
        ow_strip = [bb[3][:, i * 2048:(i + 1) * 2048] for i in range(2)]
        woT_sb = bb[4]
        wu_strip = [bb[5][:, i * 1024:(i + 1) * 1024] for i in range(2)]
        wb_buf2 = [bb[5][:, 2048 + i * 1024: 2048 + (i + 1) * 1024] for i in range(2)]
        vstrip = [bb[5][:, 4096 + i * 2048: 4096 + (i + 1) * 2048] for i in range(2)]
        vout = [aux_h[:, i * 512:(i + 1) * 512] for i in range(2)]
        uo_strip = [aux_h[:, 1024 + i * 1024: 1024 + (i + 1) * 1024] for i in range(2)]
        x_bf = c_bfA
        uvbv_hlp = c_bfA
        cast_h = c_bfB

        import dataclasses as _dc

        def mkap(base_ap, offset, dims):
            return _dc.replace(base_ap, offset=offset,
                               ap=type(base_ap.ap)(dims))

        def dma(eng, sem, out, in_, slow=False):
            P.dsem(sem, eng)
            if slow:
                return P.emit(eng, lambda e, o=out, i=in_: e.dma_start(
                    out=o, in_=i, allow_slow_non_contiguous=True), inc=sem, amt=16)
            return P.emit(eng, lambda e, o=out, i=in_: e.dma_start(out=o, in_=i),
                          inc=sem, amt=16)

        for name in ("ln_w", "ln_b", "o_b", "bf", "bi", "bc", "bo", "in_c"):
            dma("sync", "LS", sm[:, smcol[name]:smcol[name] + KC],
                vec_in[name].rearrange("(t p) -> p t", p=128), slow=True)
        dma("sync", "LS", sm[:, smcol["uvb_u"]:smcol["uvb_u"] + ET],
            vec_in["uvb_u"].rearrange("(t p) -> p t", p=128), slow=True)
        dma("sync", "LS", sm[:, smcol["uvb_b"]:smcol["uvb_b"] + 1],
            vec_in["uvb_b"].rearrange("(t p) -> p t", p=128), slow=True)
        dma("sync", "LS", sm[:, smcol["uvb_bs"]:smcol["uvb_bs"] + 1],
            vec_in["uvb_bs"].rearrange("(t p) -> p t", p=128), slow=True)
        dma("sync", "LS", sm[:, smcol["gamma"]:smcol["gamma"] + 2],
            gamma_in.rearrange("h s -> s h"), slow=True)
        dma("sync", "LS", sm[:, smcol["beta"]:smcol["beta"] + 2],
            beta_in.rearrange("h s -> s h"), slow=True)
        dma("sync", "LS", sm[:, smcol["gammas"]:smcol["gammas"] + 2],
            gammas_in.rearrange("h s -> s h"), slow=True)
        dma("sync", "LS", sm[:, smcol["betas"]:smcol["betas"] + 2],
            betas_in.rearrange("h s -> s h"), slow=True)
        dma("sync", "LS", scr[:, :128], ident_in)
        dma("sync", "LS", sin_sb[:, :], sinT_in)
        dma("sync", "LS", cos_sb[:, :], cosT_in)

        P.wait("vector", "LS")
        P.emit("vector", lambda e: e.memset(ones_col[:, :], 1.0))
        P.emit("vector", lambda e: e.memset(ones_row[:, :], 1.0))
        P.emit("vector", lambda e: e.memset(ones_row_f[:, :], 1.0))
        P.emit("vector", lambda e: e.tensor_copy(ident_h[:, :], scr[:, :128]))
        for q in range(4):
            dma("sync", "L1", scr2[0:1, :512],
                uvb_v_in[q * 512:(q + 1) * 512].rearrange("(a m) -> a m", a=1))
            P.wait("vector", "L1")
            dvq = P.emit("vector", lambda e, q_=q: e.tensor_copy(
                uvbv_hlp[0:1, q_ * 512:(q_ + 1) * 512], scr2[0:1, :512]), inc="dv")
            P.wait("sync", "dv", dvq)
        P.wait("gpsimd", "dv")
        dma("gpsimd", "SU", uvbv_row[0:1, :], uvbv_hlp[0:1, 0:E])
        P.emit("vector", lambda e: e.tensor_scalar_mul(smc("g0p"), smc("gamma", 0), INV_SQRT_S))
        P.emit("vector", lambda e: e.tensor_scalar_mul(smc("b0p"), smc("beta", 0), INV_SQRT_S))
        P.emit("vector", lambda e: e.tensor_scalar_mul(smc("g0ps"), smc("gammas", 0), INV_SQRT_S))
        P.emit("vector", lambda e: e.tensor_scalar_mul(smc("b0ps"), smc("betas", 0), INV_SQRT_S),
               inc="dv")

        scr_guard = [0, 0]

        def cast_chunks(chunks):
            pass

        def cast_resident(src_ap, dst, n_out):
            i = 0
            for kc in range(KC):
                for off in range(0, n_out, 512):
                    wdt = min(512, n_out - off)
                    par = i % 2
                    buf = (scr, scr2)[par]
                    if scr_guard[par]:
                        P.wait("sync", "dv", scr_guard[par])
                    dma("sync", "L%d" % par, buf[:, :wdt],
                        src_ap[kc * 128:(kc + 1) * 128, off:off + wdt])
                    P.wait("vector", "L%d" % par)
                    dvv = P.emit("vector", lambda e, b=buf[:, :wdt],
                                 d=dst[:, kc * n_out + off: kc * n_out + off + wdt]:
                                 e.tensor_copy(d, b), inc="dv")
                    scr_guard[par] = dvv
                    i += 1

        cast_dram_guard = [0, 0]

        def cast_to_dram(src_ap, dst_dram, rows, cols):
            i = 0
            for r0 in range(0, rows, 128):
                for off in range(0, cols, 512):
                    wdt = min(512, cols - off)
                    par = i % 2
                    buf = (scr, scr2)[par]
                    if scr_guard[par]:
                        P.wait("sync", "dv", scr_guard[par])
                    dma("sync", "L%d" % par, buf[:, :wdt],
                        src_ap[r0:r0 + 128, off:off + wdt])
                    hbuf = cast_h[:, par * 1024: par * 1024 + wdt]
                    P.wait("vector", "L%d" % par)
                    if cast_dram_guard[par]:
                        P.wait("vector", "SC%d" % par)
                    dvv = P.emit("vector", lambda e, b=buf[:, :wdt], h=hbuf:
                                 e.tensor_copy(h, b), inc="dv")
                    scr_guard[par] = dvv
                    P.wait("gpsimd", "dv", dvv)
                    dma("gpsimd", "SC%d" % par, dst_dram.ap()[r0:r0 + 128, off:off + wdt], hbuf)
                    cast_dram_guard[par] = 1
                    i += 1

        for g in ("uf", "ui", "uc"):
            cast_resident(gateT[g], WG[g], H)
        cast_to_dram(gateT["wf"], wf_h, H, H)
        cast_to_dram(gateT["wi"], wi_h, H, H)
        cast_to_dram(gateT["wc"], wc_h, H, H)
        cast_to_dram(gateT["uo"], uo_h, H, H)
        cast_to_dram(wuT, wu_h, H, E)
        cast_to_dram(wvT, wv_h, H, E)
        cast_to_dram(wbT, wb_h, H, S)
        cast_to_dram(wbTs, wbs_h, H, S)
        cast_to_dram(owT, ow_h, E, H)

        P.barrier()

        st_mu = stat[0:1, 0:RPC]
        st_b = stat[0:1, RPC:2 * RPC]

        stats_guard = {"sq": 0, "bank": 0}

        def ln_stats(src_slices, src_dv=0):
            P.wait("tensor", "dv", max(src_dv, stats_guard["bank"]))
            P.wait("scalar", "dv", max(stats_guard["sq"], src_dv))
            if P.cnt.get("SK"):
                P.wait("scalar", "SK")
                P.wait("vector", "SK")
            for kc in range(KC):
                P.emit("tensor", lambda e, s=src_slices[kc], k=kc:
                       e.matmul(p_s[0], ones_col[:, :], s,
                                start=(k == 0), stop=(k == KC - 1)),
                       inc="pe")
            for kc in range(KC):
                sq = (sq_h, tmp_h)[kc % 2]
                if kc >= 2:
                    P.wait("scalar", "pe", pe_sq[kc - 2])
                av = P.emit("scalar", lambda e, s=src_slices[kc], q=sq:
                            e.activation(out=q[:, :], in_=s, func=AF.Square), inc="ac")
                P.wait("tensor", "ac", av)
                pv = P.emit("tensor", lambda e, k=kc, q=sq:
                            e.matmul(p_s[1], ones_col[:, :], q[:, :],
                                     start=(k == 0), stop=(k == KC - 1)),
                            inc="pe")
                if kc == 0:
                    pe_sq = {}
                pe_sq[kc] = pv
            P.wait("vector", "pe")
            P.emit("vector", lambda e: e.tensor_scalar_mul(st_mu, p_s[0], 1.0 / H))
            P.emit("vector", lambda e: e.tensor_scalar_mul(st_b, p_s[1], 1.0 / H))
            P.emit("vector", lambda e: e.tensor_mul(p_s[0], st_mu, st_mu))
            P.emit("vector", lambda e: e.tensor_sub(st_b, st_b, p_s[0]))
            P.emit("vector", lambda e: e.tensor_scalar_add(st_b, st_b, EPS), inc="dv")
            P.wait("scalar", "dv")
            P.emit("scalar", lambda e: e.activation(out=st_b, in_=st_b, func=AF.Sqrt),
                   inc="ac")
            P.wait("vector", "ac")
            recip_dv = P.emit("vector", lambda e: e.reciprocal(st_b, st_b), inc="dv")
            P.wait("tensor", "dv", recip_dv)
            P.emit("tensor", lambda e: e.matmul(p_b[:, 0:RPC], ones_row_f[:, :], st_mu,
                                                start=True, stop=True))
            P.emit("tensor", lambda e: e.matmul(p_b[:, RPC:2 * RPC], ones_row_f[:, :], st_b,
                                                start=True, stop=True), inc="pe")
            P.wait("vector", "pe")
            P.emit("vector", lambda e: e.tensor_copy(muB[:, :], p_b[:, 0:RPC]))
            stats_guard["bank"] = P.emit(
                "vector", lambda e: e.tensor_copy(rsB[:, :], p_b[:, RPC:2 * RPC]),
                inc="dv")

        def xnT_sl(kc, t):
            off = kc * (B * RPC) + t * RPC
            return xnT[:, off: off + RPC]

        for t in range(B):
            P.wait("vector", "pe")
            for kc in range(KC):
                par = kc % 2
                buf = (scr, scr2)[par]
                if scr_guard[par]:
                    P.wait("sync", "dv", scr_guard[par])
                dma("sync", "L%d" % par, buf[:, :RPC],
                    x_own[t, kc * 128:(kc + 1) * 128, :])
                P.wait("vector", "L%d" % par)
                dvv = P.emit("vector", lambda e, b=buf[:, :RPC],
                             d=x_bf[:, kc * RPC:(kc + 1) * RPC]:
                             e.tensor_copy(d, b), inc="dv")
                scr_guard[par] = dvv
            xs = [x_bf[:, kc * RPC:(kc + 1) * RPC] for kc in range(KC)]
            ln_stats(xs, src_dv=dvv)
            for kc in range(KC):
                P.emit("vector", lambda e, s=xs[kc]: e.tensor_sub(tmp_h[:, :], s, muB[:, :]))
                stats_guard["sq"] = P.emit(
                    "vector", lambda e, d=xnT_sl(kc, t):
                    e.tensor_mul(d, tmp_h[:, :], rsB[:, :]), inc="dv")

        P.barrier()

        def xnn_make(b):
            xs = [xnT_sl(kc, b) for kc in range(KC)]
            ln_stats(xs)
            P.wait("vector", "pe", produce_pe_end[0])
            for kc in range(KC):
                P.emit("vector", lambda e, s=xs[kc]: e.tensor_sub(tmp_h[:, :], s, muB[:, :]))
                P.emit("vector", lambda e: e.tensor_mul(sq_h[:, :], tmp_h[:, :], rsB[:, :]))
                stats_guard["sq"] = P.emit("vector", lambda e, k=kc: e.tensor_scalar(
                    out=xnnT[:, k * RPC:(k + 1) * RPC], in0=sq_h[:, :],
                    scalar1=smc("ln_w", k), scalar2=smc("ln_b", k),
                    op0=ALU.mult, op1=ALU.add), inc="dv")

        base_sw = c_f32[:, 0:RPC]
        relu_t = c_f32[:, RPC:2 * RPC]
        rope_dv_last = [0]
        produce_pe_end = [0]
        cc_of = {}
        qk_pe_end = {}
        eT_dv_end = {}

        def rope_make(dst, gcol, bcol, gscol, bscol, a0, a1,
                      bsrc=None, bssrc=None, tq=None):
            bsrc = gates[2][:, :] if bsrc is None else bsrc
            bssrc = base_sw if bssrc is None else bssrc
            tq = upd[:, :] if tq is None else tq
            P.emit("vector", lambda e, g=gcol, b_=bcol: e.tensor_scalar(
                out=tq, in0=bsrc, scalar1=g, scalar2=b_,
                op0=ALU.mult, op1=ALU.add))
            P.emit("vector", lambda e, g=gscol, b_=bscol: e.tensor_scalar(
                out=a0, in0=bssrc, scalar1=g, scalar2=b_,
                op0=ALU.mult, op1=ALU.add))
            P.emit("vector", lambda e: e.tensor_mul(a1, a0, sin_sb[:, :]))
            P.emit("vector", lambda e: e.tensor_mul(tq, tq, cos_sb[:, :]))
            return P.emit("vector", lambda e, d=dst:
                          e.tensor_add(d, tq, a1), inc="dv")

        vout_guard = [0, 0]
        out_guard = [0, 0]
        phaseA_dv = [0]

        vk_dv_xnn = [0]
        vk_base_pe = [0]

        vout_guard = [0, 0]
        out_guard = [0, 0]
        phaseA_dv = [0]

        vk_dv_xnn = [0]
        vk_base_pe = [0]
        v_pe = {}
        biasT_dv_h = [0]
        uoc_dv_h = [0]
        woT_dv_h = [0]

        def produce_vk_pre(b):
            xnn_make(b)
            vk_dv_xnn[0] = P.cnt["dv"]
            if vk_base_pe[0]:
                P.wait("gpsimd", "pe", vk_base_pe[0])
            for swp in range(2):
                src = (wb_h if swp == 0 else wbs_h).ap()
                dma("gpsimd", "LBk",
                    wb_vk2[swp][:, :1024].rearrange("p (kc c) -> p kc c", kc=KC),
                    mkap(src, 0, [[S, 128], [128 * S, KC], [1, S]]))

        def produce_vk(b):
            dv_xnn = vk_dv_xnn[0]
            P.wait("tensor", "dv", dv_xnn)
            P.wait("scalar", "dv", rope_dv_last[0])
            for swp, dst_sb, bias_name in (
                    (0, xres[0][:, :], "uvb_b"),
                    (1, xres[1][:, :], "uvb_bs")):
                group_begin("base")
                if swp == 0:
                    P.wait("tensor", "LBk")
                for kc in range(KC):
                    P.emit("tensor", lambda e, k=kc, w=wb_vk2[swp]:
                           e.matmul(p_base[:, :], w[:, k * 128:(k + 1) * 128],
                                    xnnT[:, k * RPC:(k + 1) * RPC],
                                    start=(k == 0), stop=(k == KC - 1)),
                           inc="pe" if kc == KC - 1 else None)
                pv = P.cnt["pe"]
                P.wait("scalar", "pe", pv)
                av = P.emit("scalar", lambda e, d=dst_sb, bn=bias_name: e.activation(
                    out=d, in_=p_base[:, :], func=AF.Silu,
                    bias=smc(bn)), inc="ac")
                psum_owner["base"] = ("ac", av)
            vk_base_pe[0] = P.cnt["pe"]
            P.wait("vector", "ac", av)
            kv = rope_make(tmp_h[:, :], smc("gamma", 1), smc("beta", 1),
                           smc("gammas", 1), smc("betas", 1),
                           outs[0][:, :], outs[1][:, :],
                           bsrc=xres[0][:, :], bssrc=xres[1][:, :],
                           tq=xres[0][:, :])
            rope_dv_last[0] = kv
            stats_guard["sq"] = max(stats_guard["sq"], kv)
            P.wait("gpsimd", "dv", kv)
            dma("gpsimd", "SK",
                ag_in_big.ap()[b * RPCB + RPC: b * RPCB + RPC + 16, :]
                .rearrange("a (c i) -> (a c) i", i=RPC),
                tmp_h[:, :])
            for nch in range(4):
                for kh in range(4):
                    idx = nch * 4 + kh
                    par = idx % 2
                    wb2 = wv_vk[par]
                    prev = idx - 2 if idx >= 2 else idx + 14
                    if prev in v_pe:
                        P.wait(("sync", "gpsimd")[par], "pe", v_pe[prev])
                    dma(("sync", "gpsimd")[par], "LVk%d" % par,
                        wb2[:, :].rearrange("p (kk c) -> p kk c", kk=2),
                        mkap(wv_h.ap(), (2 * kh) * 128 * E + nch * 512,
                             [[E, 128], [128 * E, 2], [1, 512]]))
                    pv = None
                    for kk in range(2):
                      kc = 2 * kh + kk
                      for rt in range(2):
                        key = "pv%d" % rt
                        if kc == 0:
                            group_begin(key)
                            P.emit("tensor", lambda e, p=p_v[rt], n=nch: e.matmul(
                                p[:, :], ones_row[:, :],
                                uvbv_row[0:1, n * 512:(n + 1) * 512],
                                start=True, stop=False))
                        if kk == 0 and rt == 0:
                            P.wait("tensor", "LVk%d" % par)
                        pv = P.emit("tensor",
                                    lambda e, p=p_v[rt], w=wb2, r=rt, k=kc, q=kk:
                                    e.matmul(p[:, :],
                                             xnnT[:, k * RPC + r * 128: k * RPC + r * 128 + 128],
                                             w[:, q * 512:(q + 1) * 512],
                                             start=False, stop=(k == KC - 1)),
                                    inc="pe")
                        v_pe[idx] = pv
                        if kc == KC - 1:
                            vo = vout[rt]
                            P.wait("scalar", "pe", pv)
                            if vout_guard[rt]:
                                P.wait("scalar", "SV%d" % rt)
                            av = P.emit("scalar", lambda e, p=p_v[rt], o=vo: e.activation(
                                out=o, in_=p[:, :], func=AF.Silu), inc="ac")
                            psum_owner[key] = ("ac", av)
                            P.wait("gpsimd", "ac", av)
                            dma("gpsimd", "SV%d" % rt,
                                ag_in_big.ap()[b * RPCB + rt * 128: b * RPCB + (rt + 1) * 128,
                                               nch * 512:(nch + 1) * 512], vo)
                            vout_guard[rt] = 1
            produce_pe_end[0] = P.cnt["pe"]

        b_dv_xnn = {}
        b_base_pe = [0]
        b_dv_kernel = {}
        b_qv = {}

        def produce_pre(b):
            xnn_make(b)
            b_dv_xnn[b] = P.cnt["dv"]
            if b_base_pe[0]:
                P.wait("sync", "pe", b_base_pe[0])
            for swp in range(2):
                src = (wb_h if swp == 0 else wbs_h).ap()
                dma("sync", "LB",
                    wb_buf2[swp][:, :1024].rearrange("p (kc c) -> p kc c", kc=KC),
                    mkap(src, 0, [[S, 128], [128 * S, KC], [1, S]]))

        def produce(b):
            uT = uT_bufs[b % 2]
            qT_sb = qT_bufs[b % 2]
            dv_xnn = b_dv_xnn[b]
            P.wait("tensor", "dv", dv_xnn)
            P.wait("scalar", "dv", rope_dv_last[0])
            for swp, dst_sb, bias_name in (
                    (0, gates[2][:, :], "uvb_b"),
                    (1, base_sw, "uvb_bs")):
                group_begin("base")
                if swp == 0:
                    P.wait("tensor", "LB")
                for kc in range(KC):
                    P.emit("tensor", lambda e, k=kc, w=wb_buf2[swp]:
                           e.matmul(p_base[:, :], w[:, k * 128:(k + 1) * 128],
                                    xnnT[:, k * RPC:(k + 1) * RPC],
                                    start=(k == 0), stop=(k == KC - 1)),
                           inc="pe" if kc == KC - 1 else None)
                pv = P.cnt["pe"]
                P.wait("scalar", "pe", pv)
                av = P.emit("scalar", lambda e, d=dst_sb, bn=bias_name: e.activation(
                    out=d, in_=p_base[:, :], func=AF.Silu,
                    bias=smc(bn)), inc="ac")
                psum_owner["base"] = ("ac", av)
            b_base_pe[0] = P.cnt["pe"]
            P.wait("vector", "ac", av)
            a0 = gates[0][:, :]
            a1 = gates[1][:, :]
            if b >= 2:
                P.wait("vector", "pe", qk_pe_end[b - 2])
            qv = rope_make(qT_sb, smc("g0p"), smc("b0p"),
                           smc("g0ps"), smc("b0ps"), a0, a1)
            rope_dv_last[0] = qv
            b_qv[b] = qv
            if b >= 2:
                P.wait("scalar", "dv", eT_dv_end[b - 2])

            def u_dma(et):
                if et >= ET:
                    return
                if et >= 2:
                    P.wait("scalar", "pe", u_pe[et - 2])
                dma("scalar", "LU%d" % (et % 2),
                    wu_strip[et % 2][:, :1024].rearrange("p (kc c) -> p kc c", kc=KC),
                    mkap(wu_h.ap(), et * 128, [[E, 128], [128 * E, KC], [1, 128]]))

            u_dma(0)
            u_dma(1)
            for et in range(ET):
                sb = wu_strip[et % 2]
                key = "pv%d" % (et % 2)
                pm = p_v[et % 2][:, 0:RPC]
                group_begin(key)
                P.wait("tensor", "LU%d" % (et % 2))
                if et == 0:
                    P.wait("tensor", "dv", dv_xnn)
                for kc in range(KC):
                    pv = P.emit("tensor", lambda e, p=pm, s=sb, k=kc:
                                e.matmul(p, s[:, k * 128:(k + 1) * 128],
                                         xnnT[:, k * RPC:(k + 1) * RPC],
                                         start=(k == 0), stop=(k == KC - 1)),
                                inc="pe")
                if et == 0:
                    u_pe = {}
                u_pe[et] = pv
                u_dma(et + 2)
                P.wait("scalar", "pe", pv)
                av = P.emit("scalar", lambda e, p=pm, t_=et, u_=uT: e.activation(
                    out=u_[:, t_ * RPC:(t_ + 1) * RPC], in_=p, func=AF.Silu,
                    bias=smc("uvb_u", t_)), inc="ac")
                psum_owner[key] = ("ac", av)
            produce_pe_end[0] = P.cnt["pe"]

        def consume_qk(b):
            qT_sb = qT_bufs[b % 2]
            agp = ag_out_pair[b // 2]
            P.wait("gpsimd", "cc", b // 2 + 1)
            if b > 0:
                P.wait("gpsimd", "pe", qk_pe_end[b - 1])
            for r in range(NCORES):
                base_row = r * (2 * RPCB) + (b % 2) * RPCB + RPC
                blk = agp.ap()[base_row: base_row + 16, :]
                dma("gpsimd", "LK", kT_sb[:, r * RPC:(r + 1) * RPC],
                    blk.rearrange("a (c i) -> (a c) i", i=RPC))
            for jt in range(ET):
                key = "m%d" % (jt % 3)
                pm = p_m[jt % 3]
                group_begin(key)
                if jt == 0:
                    P.wait("tensor", "LK")
                    P.wait("tensor", "dv",
                           max(b_qv[b], biasT_dv_h[0] if b == 0 else 0))
                P.emit("tensor", lambda e, p=pm, j=jt: e.matmul(
                    p[:, :], ident_h[:, :], biasT_sb[:, j * RPC:(j + 1) * RPC],
                    start=True, stop=False))
                pv = P.emit("tensor", lambda e, p=pm, j=jt: e.matmul(
                    p[:, :], kT_sb[:, j * 128:(j + 1) * 128], qT_sb,
                    start=False, stop=True), inc="pe")
                P.wait("scalar", "pe", pv)
                if jt > 0:
                    P.wait("scalar", "dv", sq_dv)
                av = P.emit("scalar", lambda e, p=pm: e.activation(
                    out=relu_t, in_=p[:, :], func=AF.Relu), inc="ac")
                psum_owner[key] = ("ac", av)
                P.wait("vector", "ac", av)
                sq_dv = P.emit("vector", lambda e, j=jt: e.tensor_mul(
                    kernelT[:, j * RPC:(j + 1) * RPC], relu_t, relu_t), inc="dv")
            qk_pe_end[b] = P.cnt["pe"]
            b_dv_kernel[b] = P.cnt["dv"]

        def consume(b):
            uT = uT_bufs[b % 2]
            dv_kernel = b_dv_kernel[b]
            agp = ag_out_pair[b // 2]
            for et in range(ET):
                vs = vstrip[et % 2]
                if et >= 2:
                    P.wait("gpsimd", "pe", kv_pe[et - 2])
                    P.wait("sync", "pe", kv_pe[et - 2])

                for q in range(2):
                    dma(("gpsimd", "sync")[q], "LW%d_%d" % (et % 2, q),
                        mkap(vs, vs.offset + q * 128,
                             [[8192, 128], [256, NCORES], [1, 128]]),
                        mkap(agp.ap(),
                             ((b % 2) * RPCB + q * 128) * E + et * 128,
                             [[E, 128], [2 * RPCB * E, NCORES], [1, 128]]))
                key = "m%d" % (et % 3)
                pm = p_m[et % 3]
                group_begin(key)
                P.wait("tensor", "LW%d_0" % (et % 2))
                P.wait("tensor", "LW%d_1" % (et % 2))
                if et == 0:
                    P.wait("tensor", "dv", dv_kernel)
                for jt in range(ET):
                    pv = P.emit("tensor", lambda e, p=pm, v_=vs, j=jt: e.matmul(
                        p[:, :], v_[:, j * 128:(j + 1) * 128],
                        kernelT[:, j * RPC:(j + 1) * RPC],
                        start=(j == 0), stop=(j == ET - 1)),
                        inc="pe" if jt == ET - 1 else None)
                if et == 0:
                    kv_pe = {}
                kv_pe[et] = pv
                P.wait("vector", "pe", pv)
                dvv = P.emit("vector", lambda e, p=pm, t_=et: e.tensor_mul(
                    eT[:, t_ * RPC:(t_ + 1) * RPC], p[:, :],
                    uT[:, t_ * RPC:(t_ + 1) * RPC]), inc="dv")
                psum_owner[key] = ("dv", dvv)
            dv_eT = P.cnt["dv"]
            eT_dv_end[b] = dv_eT
            for ft in range(KC):
                osb = ow_strip[ft % 2]
                if ft >= 2:
                    P.wait("scalar", "pe", o_pe[ft - 2])
                dma("scalar", "LO%d" % (ft % 2),
                    osb[:, :2048].rearrange("p (ec c) -> p ec c", ec=ET),
                    mkap(ow_h.ap(), ft * 128, [[H, 128], [128 * H, ET], [1, 128]]))
                group_begin("h")
                if b == 0 and ft == 0:
                    P.wait("tensor", "dv", max(uoc_dv_h[0], woT_dv_h[0]))
                P.emit("tensor", lambda e, f=ft: e.matmul(
                    p_h[:, :], ident_h[:, :], uocT[:, f * RPC:(f + 1) * RPC],
                    start=True, stop=False))
                for kc in range(KC):
                    P.emit("tensor", lambda e, f=ft, k=kc, b_=b: e.matmul(
                        p_h[:, :], woT_sb[:, k * H + f * 128: k * H + f * 128 + 128],
                        xnT_sl(k, b_), start=False, stop=(k == KC - 1)),
                        inc="pe" if kc == KC - 1 else None)
                pv_h = P.cnt["pe"]
                P.wait("scalar", "pe", pv_h)
                if ft > 0:
                    P.wait("scalar", "dv", comb_dv)
                av = P.emit("scalar", lambda e, f=ft: e.activation(
                    out=hT_b[:, :], in_=p_h[:, :], func=AF.Tanh, bias=smc("bo", f)),
                    inc="ac")
                psum_owner["h"] = ("ac", av)
                key = "m%d" % (ft % 3)
                pm = p_m[ft % 3]
                group_begin(key)
                P.emit("tensor", lambda e, p=pm, f=ft, b_=b: e.matmul(
                    p[:, :], ident_h[:, :], xnT_sl(f, b_), start=True, stop=False))
                P.wait("tensor", "LO%d" % (ft % 2))
                if ft == 0:
                    P.wait("tensor", "dv", dv_eT)
                for ec in range(ET):
                    P.emit("tensor", lambda e, p=pm, s=osb, c_=ec: e.matmul(
                        p[:, :], s[:, c_ * 128:(c_ + 1) * 128],
                        eT[:, c_ * RPC:(c_ + 1) * RPC],
                        start=False, stop=(c_ == ET - 1)),
                        inc="pe" if ec == ET - 1 else None)
                pv_o = P.cnt["pe"]
                if ft == 0:
                    o_pe = {}
                o_pe[ft] = pv_o
                xr = xres[ft % 2]
                if b == 0 and ft < 2:
                    P.wait("sync", "dv", phaseA_dv[0])
                dma("sync", "LX%d" % (ft % 2), xr[:, :],
                    x_own[b, ft * 128:(ft + 1) * 128, :])
                ob = outs[ft % 2]
                P.wait("vector", "pe", pv_o)
                P.wait("vector", "ac", av)
                if out_guard[ft % 2]:
                    P.wait("vector", "SO%d" % (ft % 2))
                P.emit("vector", lambda e, p=pm, f=ft, o=ob: e.scalar_tensor_tensor(
                    out=o[:, :], in0=p[:, :], scalar=smc("o_b", f), in1=hT_b[:, :],
                    op0=ALU.add, op1=ALU.add), inc="dv")
                psum_owner[key] = ("dv", P.cnt["dv"])
                P.wait("vector", "LX%d" % (ft % 2))
                comb_dv = P.emit("vector", lambda e, o=ob, x=xr: e.tensor_add(
                    o[:, :], o[:, :], x[:, :]), inc="dv")
                P.wait("gpsimd", "dv", comb_dv)
                dma("gpsimd", "SO%d" % (ft % 2), y_own[b, ft * 128:(ft + 1) * 128, :],
                    ob[:, :])
                out_guard[ft % 2] = 1
                P.wait("sync", "dv", comb_dv)

        for mt in range(KC):
            P.emit("vector", lambda e, m=mt: e.tensor_scalar(
                out=c_f32[:, m * RPC:(m + 1) * RPC], in0=muB[:, :],
                scalar1=0.0, scalar2=smc("in_c", m), op0=ALU.mult, op1=ALU.add))
        for mt in range(KC):
            P.emit("vector", lambda e, m=mt: e.tensor_copy(
                c_bfA[:, m * RPC:(m + 1) * RPC], c_f32[:, m * RPC:(m + 1) * RPC]),
                inc="dv")

        gatespec = [("wf", wf_h, "uf", "bf", AF.Sigmoid, 0),
                    ("wi", wi_h, "ui", "bi", AF.Sigmoid, 1),
                    ("wc", wc_h, "uc", "bc", AF.Tanh, 2)]
        xg_pe = [{} for _ in range(3)]

        def xg_dma(gi, gidx):
            t_, mt_ = divmod(gidx, KC)
            if t_ >= B:
                return
            wh = (wf_h, wi_h, wc_h)[gi]
            par = gidx % 2
            if gidx >= 2:
                P.wait("sync", "pe", xg_pe[gi][gidx - 2])
            dma("sync", "LG%d_%d" % (gi, par),
                xg_buf[gi][par][:, :1024].rearrange("p (kc c) -> p kc c", kc=KC),
                mkap(wh.ap(), mt_ * 128, [[H, 128], [128 * H, KC], [1, 128]]))

        dv_after_update = P.cnt["dv"]
        for gi in range(3):
            xg_dma(gi, 0)
            xg_dma(gi, 1)
        for t in range(B):
            produce_vk_pre(t)
            cur = c_bfA if t % 2 == 0 else c_bfB
            nxt = c_bfB if t % 2 == 0 else c_bfA
            first_group = True
            for mt in range(KC):
                gidx = t * KC + mt
                for (wg, wh, ug, bg, func, gi) in gatespec:
                    key = "m%d" % ((mt * 3 + gi) % 3)
                    pm = p_m[(mt * 3 + gi) % 3]
                    sbx = xg_buf[gi][gidx % 2]
                    group_begin(key)
                    P.wait("tensor", "LG%d_%d" % (gi, gidx % 2))
                    for kc in range(KC):
                        P.emit("tensor", lambda e, p=pm, w=sbx, k=kc, t_=t:
                               e.matmul(p[:, :],
                                        w[:, k * 128:(k + 1) * 128],
                                        xnT_sl(k, t_), start=(k == 0), stop=False))
                    if first_group:
                        P.wait("tensor", "dv", dv_after_update)
                        first_group = False
                    for kc in range(KC):
                        P.emit("tensor", lambda e, p=pm, u=WG[ug], m=mt, k=kc, c=cur:
                               e.matmul(p[:, :],
                                        u[:, k * H + m * 128: k * H + m * 128 + 128],
                                        c[:, k * RPC:(k + 1) * RPC],
                                        start=False, stop=(k == KC - 1)),
                               inc="pe" if kc == KC - 1 else None)
                    pv = P.cnt["pe"]
                    xg_pe[gi][gidx] = pv
                    xg_dma(gi, gidx + 2)
                    P.wait("scalar", "pe", pv)
                    P.wait("scalar", "dv", dv_after_update if mt == 0 else dv_mt)
                    av = P.emit("scalar", lambda e, p=pm, f=func, b=bg, m=mt, g=gi:
                                e.activation(out=gates[g][:, :], in_=p[:, :], func=f,
                                             bias=smc(b, m)), inc="ac")
                    psum_owner[key] = ("ac", av)
                P.wait("vector", "ac")
                P.emit("vector", lambda e, m=mt: e.tensor_mul(
                    upd[:, :], gates[0][:, :], c_f32[:, m * RPC:(m + 1) * RPC]))
                P.emit("vector", lambda e: e.tensor_mul(
                    gates[2][:, :], gates[1][:, :], gates[2][:, :]))
                P.emit("vector", lambda e, m=mt: e.tensor_add(
                    c_f32[:, m * RPC:(m + 1) * RPC], upd[:, :], gates[2][:, :]))
                dv_mt = P.emit("vector", lambda e, m=mt, n=nxt: e.tensor_copy(
                    n[:, m * RPC:(m + 1) * RPC], c_f32[:, m * RPC:(m + 1) * RPC]),
                    inc="dv")
            dv_after_update = P.cnt["dv"]
            produce_vk(t)
            if t % 2 == 1:
                pr = t // 2
                P.wait("gpsimd", "SV0")
                P.wait("gpsimd", "SV1")
                P.wait("gpsimd", "SK")
                P.emit("gpsimd", lambda e, p_=pr: e.collective_compute(
                    "AllGather", ALU.bypass, replica_groups=[list(range(NCORES))],
                    ins=[ag_in_big.ap()[(2 * p_) * RPCB:(2 * p_ + 2) * RPCB, :].opt()],
                    outs=[ag_out_pair[p_].ap().opt()]), inc="cc")

        c_fin = c_bfA if B % 2 == 0 else c_bfB

        rec_pe = P.cnt["pe"]
        phaseA_dv[0] = P.cnt["dv"]

        P.wait("tensor", "dv", phaseA_dv[0])
        uoc_pe = {}
        for ot in range(KC):
            ustr = uo_strip[ot % 2]
            if ot >= 2:
                P.wait("sync", "pe", uoc_pe[ot - 2])
            dma("sync", "LR%d" % (ot % 2),
                ustr[:, :1024].rearrange("p (mc c) -> p mc c", mc=KC),
                mkap(uo_h.ap(), ot * 128, [[H, 128], [128 * H, KC], [1, 128]]))
            key = "base"
            pm = p_base
            group_begin(key)
            P.wait("tensor", "LR%d" % (ot % 2))
            for mc in range(KC):
                P.emit("tensor", lambda e, p=pm, s=ustr, m=mc:
                       e.matmul(p[:, :], s[:, m * 128:(m + 1) * 128],
                                c_fin[:, m * RPC:(m + 1) * RPC],
                                start=(m == 0), stop=(m == KC - 1)),
                       inc="pe" if mc == KC - 1 else None)
            pv = P.cnt["pe"]
            uoc_pe[ot] = pv
            P.wait("vector", "pe", pv)
            dvv = P.emit("vector", lambda e, o=ot, p=pm: e.tensor_copy(
                uocT[:, o * RPC:(o + 1) * RPC], p[:, :]), inc="dv")
            psum_owner[key] = ("dv", dvv)
        uoc_dv_h[0] = P.cnt["dv"]

        for jt in range(ET):
            par = jt % 2
            buf = (scr, scr2)[par]
            if scr_guard[par]:
                P.wait("sync", "dv", scr_guard[par])
            dma("sync", "L%d" % par, buf[:, :RPC],
                biasT_own[jt * 128:(jt + 1) * 128, :])
            P.wait("vector", "L%d" % par)
            dvv = P.emit("vector", lambda e, b=buf[:, :RPC], j=jt:
                         e.tensor_copy(biasT_sb[:, j * RPC:(j + 1) * RPC], b), inc="dv")
            scr_guard[par] = dvv
        biasT_dv_h[0] = P.cnt["dv"]

        for q_ in ("sync", "gpsimd", "scalar"):
            P.wait(q_, "pe", rec_pe)

        produce_pre(0)
        produce(0)

        P.wait("vector", "pe", rec_pe)
        for kc in range(KC):
            for hh in range(2):
                par = (kc * 2 + hh) % 2
                buf = (scr, scr2)[par]
                if scr_guard[par]:
                    P.wait("sync", "dv", scr_guard[par])
                dma("sync", "L%d" % par, buf[:, :],
                    woT[kc * 128:(kc + 1) * 128, hh * 512:(hh + 1) * 512])
                P.wait("vector", "L%d" % par)
                dvv = P.emit("vector", lambda e, b=buf[:, :], k=kc, h_=hh:
                             e.tensor_copy(woT_sb[:, k * H + h_ * 512:
                                                  k * H + h_ * 512 + 512], b),
                             inc="dv")
                scr_guard[par] = dvv
        woT_dv_h[0] = P.cnt["dv"]

        for b in range(B):
            if b + 1 < B:
                produce_pre(b + 1)
            consume_qk(b)
            if b + 1 < B:
                produce(b + 1)
            consume(b)

        P.wait("gpsimd", "SO0")
        P.wait("gpsimd", "SO1")

        block = e(nc.Block())

        @block.sync
        def _(sync):
            for fn in P.ops["sync"]:
                fn(sync)

        @block.gpsimd
        def _(gpsimd):
            for fn in P.ops["gpsimd"]:
                fn(gpsimd)

        @block.tensor
        def _(tensor):
            for fn in P.ops["tensor"]:
                fn(tensor)

        @block.vector
        def _(vector):
            for fn in P.ops["vector"]:
                fn(vector)

        @block.scalar
        def _(scalar):
            for fn in P.ops["scalar"]:
                fn(scalar)

    return nc


_NC_CACHE = None


def kernel(**inputs):
    global _NC_CACHE
    x = np.asarray(inputs["L0_attn_x"], dtype=np.float32)
    sin_t, cos_t = _rope_tables()
    biasT = _bias_matrix(np.asarray(inputs["L0_attn_w"], np.float32))
    uv_w = np.asarray(inputs["L0_attn_uv_w"], np.float32)
    uv_b = np.asarray(inputs["L0_attn_uv_b"], np.float32)

    common = dict(
        ident_in=np.eye(128, dtype=np.float32),
        wuT=np.ascontiguousarray(uv_w[:E].T),
        wvT=np.ascontiguousarray(uv_w[E:2 * E].T),
        wbT=np.ascontiguousarray(uv_w[2 * E:].T),
        wbTs=np.ascontiguousarray(uv_w[2 * E:].T[:, _PERM]),
        owT=np.ascontiguousarray(np.asarray(inputs["L0_attn_o_w"], np.float32).T),
        woT=np.ascontiguousarray(np.asarray(inputs["L0_ffn_wo"], np.float32).T),
        wfT=np.ascontiguousarray(np.asarray(inputs["L0_ffn_wf"], np.float32).T),
        ufT=np.ascontiguousarray(np.asarray(inputs["L0_ffn_uf"], np.float32).T),
        wiT=np.ascontiguousarray(np.asarray(inputs["L0_ffn_wi"], np.float32).T),
        uiT=np.ascontiguousarray(np.asarray(inputs["L0_ffn_ui"], np.float32).T),
        wcT=np.ascontiguousarray(np.asarray(inputs["L0_ffn_wc"], np.float32).T),
        ucT=np.ascontiguousarray(np.asarray(inputs["L0_ffn_uc"], np.float32).T),
        uoT=np.ascontiguousarray(np.asarray(inputs["L0_ffn_uo"], np.float32).T),
        ln_w=np.asarray(inputs["L0_attn_ln_w"], np.float32),
        ln_b=np.asarray(inputs["L0_attn_ln_b"], np.float32),
        o_b=np.asarray(inputs["L0_attn_o_b"], np.float32),
        bf=np.asarray(inputs["L0_ffn_bf"], np.float32),
        bi=np.asarray(inputs["L0_ffn_bi"], np.float32),
        bc=np.asarray(inputs["L0_ffn_bc"], np.float32),
        bo=np.asarray(inputs["L0_ffn_bo"], np.float32),
        in_c=np.asarray(inputs["L0_ffn_in_c"], np.float32),
        uvb_u=np.ascontiguousarray(uv_b[:E]),
        uvb_v=np.ascontiguousarray(uv_b[E:2 * E]),
        uvb_b=np.ascontiguousarray(uv_b[2 * E:]),
        uvb_bs=np.ascontiguousarray(uv_b[2 * E:][_PERM]),
        gamma=np.asarray(inputs["L0_attn_gamma"], np.float32),
        beta=np.asarray(inputs["L0_attn_beta"], np.float32),
        gammas=np.ascontiguousarray(np.asarray(inputs["L0_attn_gamma"], np.float32)[:, _PERM]),
        betas=np.ascontiguousarray(np.asarray(inputs["L0_attn_beta"], np.float32)[:, _PERM]),
    )

    xT = np.ascontiguousarray(x.transpose(0, 2, 1))
    in_maps = []
    for c in range(NCORES):
        r0, r1 = c * RPC, (c + 1) * RPC
        m = dict(common)
        m["x_own"] = np.ascontiguousarray(xT[:, :, r0:r1])
        m["biasT_own"] = np.ascontiguousarray(biasT[:, r0:r1])
        st = sin_t[r0:r1].T
        ct = cos_t[r0:r1].T
        m["sinT"] = np.ascontiguousarray(np.concatenate([-st, st], axis=0))
        m["cosT"] = np.ascontiguousarray(np.concatenate([ct, ct], axis=0))
        in_maps.append(m)

    if _NC_CACHE is None:
        _NC_CACHE = build_program()
    res = run_bass_kernel_spmd(_NC_CACHE, in_maps, core_ids=list(range(NCORES)))
    shards = [res.results[c]["y_own"] for c in range(NCORES)]
    outT = np.concatenate(shards, axis=2)
    return np.ascontiguousarray(outT.transpose(0, 2, 1)).astype(np.float32)


# version 17
# speedup vs baseline: 1.2587x; 1.0080x over previous
import numpy as np

import concourse.bass as bass
import concourse.mybir as mybir
from concourse.bass_utils import run_bass_kernel_spmd

F32 = mybir.dt.float32
BF16 = mybir.dt.bfloat16
AF = mybir.ActivationFunctionType
ALU = mybir.AluOpType

NCORES = 8
B = 16
T = 2048
H = 1024
E = 2048
S = 128
RPC = T // NCORES
L = 512
INV_SQRT_S = 1.0 / float(np.sqrt(S))
EPS = 1e-5
KC = H // 128
ET = E // 128
RPCB = RPC + 16
_PERM = np.r_[64:128, 0:64]


def _rope_tables():
    try:
        import jax
        cpu = jax.devices("cpu")[0]
        import jax.numpy as jnp
        with jax.default_device(cpu):
            pos = jnp.arange(T, dtype=jnp.float32)
            inv_freq = 10000.0 ** (jnp.arange(64, dtype=jnp.float32) / 64)
            sinusoid = pos[:, None] * inv_freq[None, :]
            return (np.asarray(jnp.sin(sinusoid)), np.asarray(jnp.cos(sinusoid)))
    except Exception:
        pos = np.arange(T, dtype=np.float32)
        inv_freq = (10000.0 ** (np.arange(64, dtype=np.float32) /
                                np.float32(64))).astype(np.float32)
        sinusoid = pos[:, None] * inv_freq[None, :]
        return np.sin(sinusoid).astype(np.float32), np.cos(sinusoid).astype(np.float32)


def _bias_matrix(w):
    n = w.shape[0]
    rel = np.arange(T)[None, :] - np.arange(T)[:, None] + (L - 1)
    idx = np.where(rel < 0, rel + n, rel)
    idx = np.clip(idx, 0, n - 1)
    return np.ascontiguousarray(w[idx].T)


class Prog:
    ENGS = ("sync", "gpsimd", "tensor", "vector", "scalar")

    def __init__(self, nc, es):
        self.nc = nc
        self.es = es
        self.ops = {e: [] for e in self.ENGS}
        self.sem = {}
        self.cnt = {}
        self.dma_sems = {}
        for name in ("cc", "pe", "dv", "ac",
                     "b_sync", "b_gpsimd", "b_tensor", "b_vector", "b_scalar"):
            self.sem[name] = es.enter_context(nc.semaphore(name))
            self.cnt[name] = 0

    def dsem(self, name, issuer):
        if name not in self.sem:
            self.sem[name] = self.es.enter_context(self.nc.semaphore(name))
            self.cnt[name] = 0
            self.dma_sems[name] = issuer
        return name

    def wait(self, eng, sem, val=None):
        v = self.cnt[sem] if val is None else val
        h = self.sem[sem]
        self.ops[eng].append(lambda e, h=h, v=v: e.wait_ge(h, v))

    def emit(self, eng, fn, inc=None, amt=1):
        if eng == "vector":
            self.wait("vector", "dv")
            h = self.sem["dv"]
            self.cnt["dv"] += 1
            self.ops[eng].append(lambda e, fn=fn, h=h: fn(e).then_inc(h, 1))
            return self.cnt["dv"]
        if inc is None:
            self.ops[eng].append(fn)
            return None
        h = self.sem[inc]
        self.cnt[inc] += amt
        self.ops[eng].append(lambda e, fn=fn, h=h, a=amt: fn(e).then_inc(h, a))
        return self.cnt[inc]

    def barrier(self):
        for name, issuer in self.dma_sems.items():
            if self.cnt[name]:
                self.wait(issuer, name)
        self.wait("gpsimd", "cc")
        self.wait("vector", "dv")
        self.wait("scalar", "ac")
        self.wait("tensor", "pe")
        for e in self.ENGS:
            h = self.sem["b_" + e]
            self.cnt["b_" + e] += 1
            self.ops[e].append(lambda eng, h=h: eng.nop().then_inc(h, 1))
        for e in self.ENGS:
            for o in self.ENGS:
                if o != e:
                    self.wait(e, "b_" + o)


def build_program():
    nc = bass.Bass("TRN2", target_bir_lowering=False)
    dt = nc.dram_tensor

    x_own = dt("x_own", [B, H, RPC], F32, kind="ExternalInput").ap()
    biasT_own = dt("biasT_own", [T, RPC], F32, kind="ExternalInput").ap()
    sinT_in = dt("sinT", [128, RPC], F32, kind="ExternalInput").ap()
    cosT_in = dt("cosT", [128, RPC], F32, kind="ExternalInput").ap()
    ident_in = dt("ident_in", [128, 128], F32, kind="ExternalInput").ap()

    wuT = dt("wuT", [H, E], F32, kind="ExternalInput").ap()
    wvT = dt("wvT", [H, E], F32, kind="ExternalInput").ap()
    wbT = dt("wbT", [H, S], F32, kind="ExternalInput").ap()
    wbTs = dt("wbTs", [H, S], F32, kind="ExternalInput").ap()
    owT = dt("owT", [E, H], F32, kind="ExternalInput").ap()
    woT = dt("woT", [H, H], F32, kind="ExternalInput").ap()
    gateT = {g: dt(g + "T", [H, H], F32, kind="ExternalInput").ap()
             for g in ("wf", "uf", "wi", "ui", "wc", "uc", "uo")}

    vec_in = {}
    for name, n in (("ln_w", H), ("ln_b", H), ("o_b", H), ("bf", H), ("bi", H),
                    ("bc", H), ("bo", H), ("in_c", H), ("uvb_u", E), ("uvb_b", S), ("uvb_bs", S)):
        vec_in[name] = dt(name, [n], F32, kind="ExternalInput").ap()
    uvb_v_in = dt("uvb_v", [E], F32, kind="ExternalInput").ap()
    gamma_in = dt("gamma", [2, 128], F32, kind="ExternalInput").ap()
    gammas_in = dt("gammas", [2, 128], F32, kind="ExternalInput").ap()
    beta_in = dt("beta", [2, 128], F32, kind="ExternalInput").ap()
    betas_in = dt("betas", [2, 128], F32, kind="ExternalInput").ap()

    y_own = dt("y_own", [B, H, RPC], F32, kind="ExternalOutput").ap()

    wu_h = dt("wu_h", [H, E], BF16)
    wv_h = dt("wv_h", [H, E], BF16)
    wb_h = dt("wb_h", [H, S], BF16)
    wbs_h = dt("wbs_h", [H, S], BF16)
    ow_h = dt("ow_h", [E, H], BF16)
    uo_h = dt("uo_h", [H, H], BF16)
    wf_h = dt("wf_h", [H, H], BF16)
    wi_h = dt("wi_h", [H, H], BF16)
    wc_h = dt("wc_h", [H, H], BF16)
    ag_in_big = dt("ag_in_big", [B * RPCB, E], BF16)
    ag_out_pair = [dt(f"ag_out_p{p}", [NCORES * 2 * RPCB, E], BF16)
                   for p in range(B // 2)]

    from contextlib import ExitStack
    es = ExitStack()
    with es:
        e = es.enter_context

        xnT = e(nc.sbuf_tensor("xnT", [128, KC * B * RPC], BF16))
        bb = [e(nc.sbuf_tensor(f"bb{i}", [128, 8192], BF16)) for i in range(6)]
        aux_h = e(nc.sbuf_tensor("aux_h", [128, 3072], BF16))
        c_f32 = e(nc.sbuf_tensor("c_f32", [128, KC * RPC], F32))
        c_bfA = e(nc.sbuf_tensor("c_bfA", [128, KC * RPC], BF16))
        c_bfB = e(nc.sbuf_tensor("c_bfB", [128, KC * RPC], BF16))
        scr = e(nc.sbuf_tensor("scr", [128, 512], F32))
        scr2 = e(nc.sbuf_tensor("scr2", [128, 512], F32))
        gates = [e(nc.sbuf_tensor(f"gate{i}", [128, RPC], F32)) for i in range(3)]
        upd = e(nc.sbuf_tensor("upd", [128, RPC], F32))
        sq_h = e(nc.sbuf_tensor("sq_h", [128, RPC], BF16))
        tmp_h = e(nc.sbuf_tensor("tmp_h", [128, RPC], BF16))
        muB = e(nc.sbuf_tensor("muB", [128, RPC], BF16))
        rsB = e(nc.sbuf_tensor("rsB", [128, RPC], BF16))
        hT_b = e(nc.sbuf_tensor("hT_b", [128, RPC], BF16))
        xres = [e(nc.sbuf_tensor(f"xres{i}", [128, RPC], F32)) for i in range(2)]
        outs = [e(nc.sbuf_tensor(f"outs{i}", [128, RPC], F32)) for i in range(2)]
        stat = e(nc.sbuf_tensor("stat", [1, 2 * RPC], F32))
        uvbv_row = e(nc.sbuf_tensor("uvbv_row", [1, E], BF16))
        ones_col = e(nc.sbuf_tensor("ones_col", [128, 1], BF16))
        ones_row = e(nc.sbuf_tensor("ones_row", [1, 128], BF16))
        ones_row_f = e(nc.sbuf_tensor("ones_row_f", [1, 128], F32))
        ident_h = e(nc.sbuf_tensor("ident_h", [128, 128], BF16))
        sin_sb = e(nc.sbuf_tensor("sin_sb", [128, RPC], F32))
        cos_sb = e(nc.sbuf_tensor("cos_sb", [128, RPC], F32))
        sm = e(nc.sbuf_tensor("sm", [128, 96], F32))

        p_m = [e(nc.psum_tensor(f"p_m{i}", [128, RPC], F32)) for i in range(3)]
        p_v = [e(nc.psum_tensor(f"p_v{i}", [128, 512], F32)) for i in range(2)]
        p_base = e(nc.psum_tensor("p_base", [128, RPC], F32))
        p_h = e(nc.psum_tensor("p_h", [128, RPC], F32))
        p_stat = e(nc.psum_tensor("p_stat", [128, 512], F32))
        p_s = [p_stat[0:1, 0:RPC], p_stat[0:1, RPC:2 * RPC]]
        p_b = p_stat

        P = Prog(nc, es)
        psum_owner = {}

        def group_begin(key):
            if key in psum_owner:
                s, v = psum_owner[key]
                P.wait("tensor", s, v)

        smcol = {}
        _c = [0]
        def alloc_sm(name, ncols):
            smcol[name] = _c[0]
            _c[0] += ncols
        for name in ("ln_w", "ln_b", "o_b", "bf", "bi", "bc", "bo", "in_c"):
            alloc_sm(name, KC)
        alloc_sm("uvb_u", ET)
        alloc_sm("uvb_b", 1)
        alloc_sm("uvb_bs", 1)
        alloc_sm("gamma", 2)
        alloc_sm("beta", 2)
        alloc_sm("gammas", 2)
        alloc_sm("betas", 2)
        alloc_sm("g0p", 1)
        alloc_sm("b0p", 1)
        alloc_sm("g0ps", 1)
        alloc_sm("b0ps", 1)
        assert _c[0] <= 96

        def smc(name, i=0):
            return sm[:, smcol[name] + i: smcol[name] + i + 1]

        WG = {"uf": bb[1], "ui": bb[3], "uc": bb[5]}
        xg_buf = [[bb[4][:, (2 * g + i) * 1024:(2 * g + i + 1) * 1024]
                   for i in range(2)] for g in range(3)]
        wb_vk2 = [bb[0][:, i * 1024:(i + 1) * 1024] for i in range(2)]
        wv_vk = [bb[0][:, 2048 + i * 1024: 2048 + (i + 1) * 1024] for i in range(2)]
        kernelT = bb[0]
        kT_sb = bb[0][:, 4096:6144]
        qT_bufs = [bb[0][:, 6144:6400], bb[0][:, 6400:6656]]
        uT_bufs = [bb[1][:, 0:4096], bb[1][:, 4096:8192]]
        eT = bb[3][:, 4096:8192]
        biasT_sb = bb[2][:, 0:4096]
        uocT = bb[2][:, 4096:6144]
        xnnT = bb[2][:, 6144:8192]
        ow_strip = [bb[3][:, i * 2048:(i + 1) * 2048] for i in range(2)]
        woT_sb = bb[4]
        wu_strip = [bb[5][:, i * 1024:(i + 1) * 1024] for i in range(2)]
        wb_buf2 = [bb[5][:, 2048 + i * 1024: 2048 + (i + 1) * 1024] for i in range(2)]
        vstrip = [bb[5][:, 4096 + i * 2048: 4096 + (i + 1) * 2048] for i in range(2)]
        vout = [aux_h[:, i * 512:(i + 1) * 512] for i in range(2)]
        uo_strip = [aux_h[:, 1024 + i * 1024: 1024 + (i + 1) * 1024] for i in range(2)]
        x_bf = c_bfA
        uvbv_hlp = c_bfA
        cast_h = c_bfB

        import dataclasses as _dc

        def mkap(base_ap, offset, dims):
            return _dc.replace(base_ap, offset=offset,
                               ap=type(base_ap.ap)(dims))

        def dma(eng, sem, out, in_, slow=False):
            P.dsem(sem, eng)
            if slow:
                return P.emit(eng, lambda e, o=out, i=in_: e.dma_start(
                    out=o, in_=i, allow_slow_non_contiguous=True), inc=sem, amt=16)
            return P.emit(eng, lambda e, o=out, i=in_: e.dma_start(out=o, in_=i),
                          inc=sem, amt=16)

        for name in ("ln_w", "ln_b", "o_b", "bf", "bi", "bc", "bo", "in_c"):
            dma("sync", "LS", sm[:, smcol[name]:smcol[name] + KC],
                vec_in[name].rearrange("(t p) -> p t", p=128), slow=True)
        dma("sync", "LS", sm[:, smcol["uvb_u"]:smcol["uvb_u"] + ET],
            vec_in["uvb_u"].rearrange("(t p) -> p t", p=128), slow=True)
        dma("sync", "LS", sm[:, smcol["uvb_b"]:smcol["uvb_b"] + 1],
            vec_in["uvb_b"].rearrange("(t p) -> p t", p=128), slow=True)
        dma("sync", "LS", sm[:, smcol["uvb_bs"]:smcol["uvb_bs"] + 1],
            vec_in["uvb_bs"].rearrange("(t p) -> p t", p=128), slow=True)
        dma("sync", "LS", sm[:, smcol["gamma"]:smcol["gamma"] + 2],
            gamma_in.rearrange("h s -> s h"), slow=True)
        dma("sync", "LS", sm[:, smcol["beta"]:smcol["beta"] + 2],
            beta_in.rearrange("h s -> s h"), slow=True)
        dma("sync", "LS", sm[:, smcol["gammas"]:smcol["gammas"] + 2],
            gammas_in.rearrange("h s -> s h"), slow=True)
        dma("sync", "LS", sm[:, smcol["betas"]:smcol["betas"] + 2],
            betas_in.rearrange("h s -> s h"), slow=True)
        dma("sync", "LS", scr[:, :128], ident_in)
        dma("sync", "LS", sin_sb[:, :], sinT_in)
        dma("sync", "LS", cos_sb[:, :], cosT_in)

        P.wait("vector", "LS")
        P.emit("vector", lambda e: e.memset(ones_col[:, :], 1.0))
        P.emit("vector", lambda e: e.memset(ones_row[:, :], 1.0))
        P.emit("vector", lambda e: e.memset(ones_row_f[:, :], 1.0))
        P.emit("vector", lambda e: e.tensor_copy(ident_h[:, :], scr[:, :128]))
        for q in range(4):
            dma("sync", "L1", scr2[0:1, :512],
                uvb_v_in[q * 512:(q + 1) * 512].rearrange("(a m) -> a m", a=1))
            P.wait("vector", "L1")
            dvq = P.emit("vector", lambda e, q_=q: e.tensor_copy(
                uvbv_hlp[0:1, q_ * 512:(q_ + 1) * 512], scr2[0:1, :512]), inc="dv")
            P.wait("sync", "dv", dvq)
        P.wait("gpsimd", "dv")
        dma("gpsimd", "SU", uvbv_row[0:1, :], uvbv_hlp[0:1, 0:E])
        P.emit("vector", lambda e: e.tensor_scalar_mul(smc("g0p"), smc("gamma", 0), INV_SQRT_S))
        P.emit("vector", lambda e: e.tensor_scalar_mul(smc("b0p"), smc("beta", 0), INV_SQRT_S))
        P.emit("vector", lambda e: e.tensor_scalar_mul(smc("g0ps"), smc("gammas", 0), INV_SQRT_S))
        P.emit("vector", lambda e: e.tensor_scalar_mul(smc("b0ps"), smc("betas", 0), INV_SQRT_S),
               inc="dv")

        scr_guard = [0, 0]

        def cast_chunks(chunks):
            pass

        def cast_resident(src_ap, dst, n_out):
            i = 0
            for kc in range(KC):
                for off in range(0, n_out, 512):
                    wdt = min(512, n_out - off)
                    par = i % 2
                    buf = (scr, scr2)[par]
                    if scr_guard[par]:
                        P.wait("sync", "dv", scr_guard[par])
                    dma("sync", "L%d" % par, buf[:, :wdt],
                        src_ap[kc * 128:(kc + 1) * 128, off:off + wdt])
                    P.wait("vector", "L%d" % par)
                    dvv = P.emit("vector", lambda e, b=buf[:, :wdt],
                                 d=dst[:, kc * n_out + off: kc * n_out + off + wdt]:
                                 e.tensor_copy(d, b), inc="dv")
                    scr_guard[par] = dvv
                    i += 1

        cast_dram_guard = [0, 0]

        def cast_to_dram(src_ap, dst_dram, rows, cols):
            i = 0
            for r0 in range(0, rows, 128):
                for off in range(0, cols, 512):
                    wdt = min(512, cols - off)
                    par = i % 2
                    buf = (scr, scr2)[par]
                    if scr_guard[par]:
                        P.wait("sync", "dv", scr_guard[par])
                    dma("sync", "L%d" % par, buf[:, :wdt],
                        src_ap[r0:r0 + 128, off:off + wdt])
                    hbuf = cast_h[:, par * 1024: par * 1024 + wdt]
                    P.wait("vector", "L%d" % par)
                    if cast_dram_guard[par]:
                        P.wait("vector", "SC%d" % par)
                    dvv = P.emit("vector", lambda e, b=buf[:, :wdt], h=hbuf:
                                 e.tensor_copy(h, b), inc="dv")
                    scr_guard[par] = dvv
                    P.wait("gpsimd", "dv", dvv)
                    dma("gpsimd", "SC%d" % par, dst_dram.ap()[r0:r0 + 128, off:off + wdt], hbuf)
                    cast_dram_guard[par] = 1
                    i += 1

        for g in ("uf", "ui", "uc"):
            cast_resident(gateT[g], WG[g], H)
        cast_to_dram(gateT["wf"], wf_h, H, H)
        cast_to_dram(gateT["wi"], wi_h, H, H)
        cast_to_dram(gateT["wc"], wc_h, H, H)
        cast_to_dram(gateT["uo"], uo_h, H, H)
        cast_to_dram(wuT, wu_h, H, E)
        cast_to_dram(wvT, wv_h, H, E)
        cast_to_dram(wbT, wb_h, H, S)
        cast_to_dram(wbTs, wbs_h, H, S)
        cast_to_dram(owT, ow_h, E, H)

        P.barrier()

        st_mu = stat[0:1, 0:RPC]
        st_b = stat[0:1, RPC:2 * RPC]

        stats_guard = {"sq": 0, "bank": 0}

        def ln_stats(src_slices, src_dv=0):
            P.wait("tensor", "dv", max(src_dv, stats_guard["bank"]))
            P.wait("scalar", "dv", max(stats_guard["sq"], src_dv))
            if P.cnt.get("SK"):
                P.wait("scalar", "SK")
                P.wait("vector", "SK")
            for kc in range(KC):
                P.emit("tensor", lambda e, s=src_slices[kc], k=kc:
                       e.matmul(p_s[0], ones_col[:, :], s,
                                start=(k == 0), stop=(k == KC - 1)),
                       inc="pe")
            for kc in range(KC):
                sq = (sq_h, tmp_h)[kc % 2]
                if kc >= 2:
                    P.wait("scalar", "pe", pe_sq[kc - 2])
                av = P.emit("scalar", lambda e, s=src_slices[kc], q=sq:
                            e.activation(out=q[:, :], in_=s, func=AF.Square), inc="ac")
                P.wait("tensor", "ac", av)
                pv = P.emit("tensor", lambda e, k=kc, q=sq:
                            e.matmul(p_s[1], ones_col[:, :], q[:, :],
                                     start=(k == 0), stop=(k == KC - 1)),
                            inc="pe")
                if kc == 0:
                    pe_sq = {}
                pe_sq[kc] = pv
            P.wait("vector", "pe")
            P.emit("vector", lambda e: e.tensor_scalar_mul(st_mu, p_s[0], 1.0 / H))
            P.emit("vector", lambda e: e.tensor_scalar_mul(st_b, p_s[1], 1.0 / H))
            P.emit("vector", lambda e: e.tensor_mul(p_s[0], st_mu, st_mu))
            P.emit("vector", lambda e: e.tensor_sub(st_b, st_b, p_s[0]))
            P.emit("vector", lambda e: e.tensor_scalar_add(st_b, st_b, EPS), inc="dv")
            P.wait("scalar", "dv")
            P.emit("scalar", lambda e: e.activation(out=st_b, in_=st_b, func=AF.Sqrt),
                   inc="ac")
            P.wait("vector", "ac")
            recip_dv = P.emit("vector", lambda e: e.reciprocal(st_b, st_b), inc="dv")
            P.wait("tensor", "dv", recip_dv)
            P.emit("tensor", lambda e: e.matmul(p_b[:, 0:RPC], ones_row_f[:, :], st_mu,
                                                start=True, stop=True))
            P.emit("tensor", lambda e: e.matmul(p_b[:, RPC:2 * RPC], ones_row_f[:, :], st_b,
                                                start=True, stop=True), inc="pe")
            P.wait("vector", "pe")
            P.emit("vector", lambda e: e.tensor_copy(muB[:, :], p_b[:, 0:RPC]))
            stats_guard["bank"] = P.emit(
                "vector", lambda e: e.tensor_copy(rsB[:, :], p_b[:, RPC:2 * RPC]),
                inc="dv")

        def xnT_sl(kc, t):
            off = kc * (B * RPC) + t * RPC
            return xnT[:, off: off + RPC]

        for t in range(B):
            P.wait("vector", "pe")
            for kc in range(KC):
                par = kc % 2
                buf = (scr, scr2)[par]
                if scr_guard[par]:
                    P.wait("sync", "dv", scr_guard[par])
                dma("sync", "L%d" % par, buf[:, :RPC],
                    x_own[t, kc * 128:(kc + 1) * 128, :])
                P.wait("vector", "L%d" % par)
                dvv = P.emit("vector", lambda e, b=buf[:, :RPC],
                             d=x_bf[:, kc * RPC:(kc + 1) * RPC]:
                             e.tensor_copy(d, b), inc="dv")
                scr_guard[par] = dvv
            xs = [x_bf[:, kc * RPC:(kc + 1) * RPC] for kc in range(KC)]
            ln_stats(xs, src_dv=dvv)
            for kc in range(KC):
                P.emit("vector", lambda e, s=xs[kc]: e.tensor_sub(tmp_h[:, :], s, muB[:, :]))
                stats_guard["sq"] = P.emit(
                    "vector", lambda e, d=xnT_sl(kc, t):
                    e.tensor_mul(d, tmp_h[:, :], rsB[:, :]), inc="dv")

        P.barrier()

        def xnn_make(b):
            xs = [xnT_sl(kc, b) for kc in range(KC)]
            ln_stats(xs)
            P.wait("vector", "pe", produce_pe_end[0])
            for kc in range(KC):
                P.emit("vector", lambda e, s=xs[kc]: e.tensor_sub(tmp_h[:, :], s, muB[:, :]))
                P.emit("vector", lambda e: e.tensor_mul(sq_h[:, :], tmp_h[:, :], rsB[:, :]))
                stats_guard["sq"] = P.emit("vector", lambda e, k=kc: e.tensor_scalar(
                    out=xnnT[:, k * RPC:(k + 1) * RPC], in0=sq_h[:, :],
                    scalar1=smc("ln_w", k), scalar2=smc("ln_b", k),
                    op0=ALU.mult, op1=ALU.add), inc="dv")

        base_sw = c_f32[:, 0:RPC]
        relu_t = c_f32[:, RPC:2 * RPC]
        rope_dv_last = [0]
        produce_pe_end = [0]
        cc_of = {}
        qk_pe_end = {}
        eT_dv_end = {}

        def rope_make(dst, gcol, bcol, gscol, bscol, a0, a1,
                      bsrc=None, bssrc=None, tq=None):
            bsrc = gates[2][:, :] if bsrc is None else bsrc
            bssrc = base_sw if bssrc is None else bssrc
            tq = upd[:, :] if tq is None else tq
            P.emit("vector", lambda e, g=gcol, b_=bcol: e.tensor_scalar(
                out=tq, in0=bsrc, scalar1=g, scalar2=b_,
                op0=ALU.mult, op1=ALU.add))
            P.emit("vector", lambda e, g=gscol, b_=bscol: e.tensor_scalar(
                out=a0, in0=bssrc, scalar1=g, scalar2=b_,
                op0=ALU.mult, op1=ALU.add))
            P.emit("vector", lambda e: e.tensor_mul(a1, a0, sin_sb[:, :]))
            P.emit("vector", lambda e: e.tensor_mul(tq, tq, cos_sb[:, :]))
            return P.emit("vector", lambda e, d=dst:
                          e.tensor_add(d, tq, a1), inc="dv")

        vout_guard = [0, 0]
        out_guard = [0, 0]
        phaseA_dv = [0]

        vk_dv_xnn = [0]
        vk_base_pe = [0]

        vout_guard = [0, 0]
        out_guard = [0, 0]
        phaseA_dv = [0]

        vk_dv_xnn = [0]
        vk_base_pe = [0]
        v_pe = {}
        biasT_dv_h = [0]
        uoc_dv_h = [0]
        woT_dv_h = [0]

        def produce_vk_pre(b):
            xnn_make(b)
            vk_dv_xnn[0] = P.cnt["dv"]
            if vk_base_pe[0]:
                P.wait("gpsimd", "pe", vk_base_pe[0])
            for swp in range(2):
                src = (wb_h if swp == 0 else wbs_h).ap()
                dma("gpsimd", "LBk",
                    wb_vk2[swp][:, :1024].rearrange("p (kc c) -> p kc c", kc=KC),
                    mkap(src, 0, [[S, 128], [128 * S, KC], [1, S]]))

        def produce_vk(b):
            dv_xnn = vk_dv_xnn[0]
            P.wait("tensor", "dv", dv_xnn)
            P.wait("scalar", "dv", rope_dv_last[0])
            for swp, dst_sb, bias_name in (
                    (0, xres[0][:, :], "uvb_b"),
                    (1, xres[1][:, :], "uvb_bs")):
                group_begin("base")
                if swp == 0:
                    P.wait("tensor", "LBk")
                for kc in range(KC):
                    P.emit("tensor", lambda e, k=kc, w=wb_vk2[swp]:
                           e.matmul(p_base[:, :], w[:, k * 128:(k + 1) * 128],
                                    xnnT[:, k * RPC:(k + 1) * RPC],
                                    start=(k == 0), stop=(k == KC - 1)),
                           inc="pe" if kc == KC - 1 else None)
                pv = P.cnt["pe"]
                P.wait("scalar", "pe", pv)
                av = P.emit("scalar", lambda e, d=dst_sb, bn=bias_name: e.activation(
                    out=d, in_=p_base[:, :], func=AF.Silu,
                    bias=smc(bn)), inc="ac")
                psum_owner["base"] = ("ac", av)
            vk_base_pe[0] = P.cnt["pe"]
            P.wait("vector", "ac", av)
            kv = rope_make(tmp_h[:, :], smc("gamma", 1), smc("beta", 1),
                           smc("gammas", 1), smc("betas", 1),
                           outs[0][:, :], outs[1][:, :],
                           bsrc=xres[0][:, :], bssrc=xres[1][:, :],
                           tq=xres[0][:, :])
            rope_dv_last[0] = kv
            stats_guard["sq"] = max(stats_guard["sq"], kv)
            P.wait("gpsimd", "dv", kv)
            dma("gpsimd", "SK",
                ag_in_big.ap()[b * RPCB + RPC: b * RPCB + RPC + 16, :]
                .rearrange("a (c i) -> (a c) i", i=RPC),
                tmp_h[:, :])
            for nch in range(4):
                for kh in range(4):
                    idx = nch * 4 + kh
                    par = idx % 2
                    wb2 = wv_vk[par]
                    prev = idx - 2 if idx >= 2 else idx + 14
                    if prev in v_pe:
                        P.wait(("sync", "gpsimd")[par], "pe", v_pe[prev])
                    dma(("sync", "gpsimd")[par], "LVk%d" % par,
                        wb2[:, :].rearrange("p (kk c) -> p kk c", kk=2),
                        mkap(wv_h.ap(), (2 * kh) * 128 * E + nch * 512,
                             [[E, 128], [128 * E, 2], [1, 512]]))
                    pv = None
                    for kk in range(2):
                      kc = 2 * kh + kk
                      for rt in range(2):
                        key = "pv%d" % rt
                        if kc == 0:
                            group_begin(key)
                            P.emit("tensor", lambda e, p=p_v[rt], n=nch: e.matmul(
                                p[:, :], ones_row[:, :],
                                uvbv_row[0:1, n * 512:(n + 1) * 512],
                                start=True, stop=False))
                        if kk == 0 and rt == 0:
                            P.wait("tensor", "LVk%d" % par)
                        pv = P.emit("tensor",
                                    lambda e, p=p_v[rt], w=wb2, r=rt, k=kc, q=kk:
                                    e.matmul(p[:, :],
                                             xnnT[:, k * RPC + r * 128: k * RPC + r * 128 + 128],
                                             w[:, q * 512:(q + 1) * 512],
                                             start=False, stop=(k == KC - 1)),
                                    inc="pe")
                        v_pe[idx] = pv
                        if kc == KC - 1:
                            vo = vout[rt]
                            P.wait("scalar", "pe", pv)
                            if vout_guard[rt]:
                                P.wait("scalar", "SV%d" % rt)
                            av = P.emit("scalar", lambda e, p=p_v[rt], o=vo: e.activation(
                                out=o, in_=p[:, :], func=AF.Silu), inc="ac")
                            psum_owner[key] = ("ac", av)
                            P.wait("gpsimd", "ac", av)
                            dma("gpsimd", "SV%d" % rt,
                                ag_in_big.ap()[b * RPCB + rt * 128: b * RPCB + (rt + 1) * 128,
                                               nch * 512:(nch + 1) * 512], vo)
                            vout_guard[rt] = 1
            produce_pe_end[0] = P.cnt["pe"]

        b_dv_xnn = {}
        b_base_pe = [0]
        b_dv_kernel = {}
        b_qv = {}

        def produce_pre(b):
            xnn_make(b)
            b_dv_xnn[b] = P.cnt["dv"]
            if b_base_pe[0]:
                P.wait("sync", "pe", b_base_pe[0])
            for swp in range(2):
                src = (wb_h if swp == 0 else wbs_h).ap()
                dma("sync", "LB",
                    wb_buf2[swp][:, :1024].rearrange("p (kc c) -> p kc c", kc=KC),
                    mkap(src, 0, [[S, 128], [128 * S, KC], [1, S]]))

        def produce(b):
            uT = uT_bufs[b % 2]
            qT_sb = qT_bufs[b % 2]
            dv_xnn = b_dv_xnn[b]
            P.wait("tensor", "dv", dv_xnn)
            P.wait("scalar", "dv", rope_dv_last[0])
            for swp, dst_sb, bias_name in (
                    (0, gates[2][:, :], "uvb_b"),
                    (1, base_sw, "uvb_bs")):
                group_begin("base")
                if swp == 0:
                    P.wait("tensor", "LB")
                for kc in range(KC):
                    P.emit("tensor", lambda e, k=kc, w=wb_buf2[swp]:
                           e.matmul(p_base[:, :], w[:, k * 128:(k + 1) * 128],
                                    xnnT[:, k * RPC:(k + 1) * RPC],
                                    start=(k == 0), stop=(k == KC - 1)),
                           inc="pe" if kc == KC - 1 else None)
                pv = P.cnt["pe"]
                P.wait("scalar", "pe", pv)
                av = P.emit("scalar", lambda e, d=dst_sb, bn=bias_name: e.activation(
                    out=d, in_=p_base[:, :], func=AF.Silu,
                    bias=smc(bn)), inc="ac")
                psum_owner["base"] = ("ac", av)
            b_base_pe[0] = P.cnt["pe"]
            P.wait("vector", "ac", av)
            a0 = gates[0][:, :]
            a1 = gates[1][:, :]
            if b >= 2:
                P.wait("vector", "pe", qk_pe_end[b - 2])
            qv = rope_make(qT_sb, smc("g0p"), smc("b0p"),
                           smc("g0ps"), smc("b0ps"), a0, a1)
            rope_dv_last[0] = qv
            b_qv[b] = qv
            if b >= 2:
                P.wait("scalar", "dv", eT_dv_end[b - 2])

            def u_dma(et):
                if et >= ET:
                    return
                if et >= 2:
                    P.wait("scalar", "pe", u_pe[et - 2])
                dma("scalar", "LU%d" % (et % 2),
                    wu_strip[et % 2][:, :1024].rearrange("p (kc c) -> p kc c", kc=KC),
                    mkap(wu_h.ap(), et * 128, [[E, 128], [128 * E, KC], [1, 128]]))

            u_dma(0)
            u_dma(1)
            for et in range(ET):
                sb = wu_strip[et % 2]
                key = "pv%d" % (et % 2)
                pm = p_v[et % 2][:, 0:RPC]
                group_begin(key)
                P.wait("tensor", "LU%d" % (et % 2))
                if et == 0:
                    P.wait("tensor", "dv", dv_xnn)
                for kc in range(KC):
                    pv = P.emit("tensor", lambda e, p=pm, s=sb, k=kc:
                                e.matmul(p, s[:, k * 128:(k + 1) * 128],
                                         xnnT[:, k * RPC:(k + 1) * RPC],
                                         start=(k == 0), stop=(k == KC - 1)),
                                inc="pe")
                if et == 0:
                    u_pe = {}
                u_pe[et] = pv
                u_dma(et + 2)
                P.wait("scalar", "pe", pv)
                av = P.emit("scalar", lambda e, p=pm, t_=et, u_=uT: e.activation(
                    out=u_[:, t_ * RPC:(t_ + 1) * RPC], in_=p, func=AF.Silu,
                    bias=smc("uvb_u", t_)), inc="ac")
                psum_owner[key] = ("ac", av)
            produce_pe_end[0] = P.cnt["pe"]

        def consume_qk(b):
            qT_sb = qT_bufs[b % 2]
            agp = ag_out_pair[b // 2]
            P.wait("gpsimd", "cc", b // 2 + 1)
            P.wait("sync", "cc", b // 2 + 1)
            if b > 0:
                P.wait("gpsimd", "pe", qk_pe_end[b - 1])
            for r in range(NCORES):
                base_row = r * (2 * RPCB) + (b % 2) * RPCB + RPC
                blk = agp.ap()[base_row: base_row + 16, :]
                dma("gpsimd", "LK", kT_sb[:, r * RPC:(r + 1) * RPC],
                    blk.rearrange("a (c i) -> (a c) i", i=RPC))
            for jt in range(ET):
                key = "m%d" % (jt % 3)
                pm = p_m[jt % 3]
                group_begin(key)
                if jt == 0:
                    P.wait("tensor", "LK")
                    P.wait("tensor", "dv",
                           max(b_qv[b], biasT_dv_h[0] if b == 0 else 0))
                P.emit("tensor", lambda e, p=pm, j=jt: e.matmul(
                    p[:, :], ident_h[:, :], biasT_sb[:, j * RPC:(j + 1) * RPC],
                    start=True, stop=False))
                pv = P.emit("tensor", lambda e, p=pm, j=jt: e.matmul(
                    p[:, :], kT_sb[:, j * 128:(j + 1) * 128], qT_sb,
                    start=False, stop=True), inc="pe")
                P.wait("scalar", "pe", pv)
                if jt > 0:
                    P.wait("scalar", "dv", sq_dv)
                av = P.emit("scalar", lambda e, p=pm: e.activation(
                    out=relu_t, in_=p[:, :], func=AF.Relu), inc="ac")
                psum_owner[key] = ("ac", av)
                P.wait("vector", "ac", av)
                sq_dv = P.emit("vector", lambda e, j=jt: e.tensor_mul(
                    kernelT[:, j * RPC:(j + 1) * RPC], relu_t, relu_t), inc="dv")
            qk_pe_end[b] = P.cnt["pe"]
            b_dv_kernel[b] = P.cnt["dv"]

        def consume(b):
            uT = uT_bufs[b % 2]
            dv_kernel = b_dv_kernel[b]
            agp = ag_out_pair[b // 2]
            for et in range(ET):
                vs = vstrip[et % 2]
                if et >= 2:
                    P.wait("gpsimd", "pe", kv_pe[et - 2])
                    P.wait("sync", "pe", kv_pe[et - 2])

                for q in range(2):
                    dma(("gpsimd", "sync")[q], "LW%d_%d" % (et % 2, q),
                        mkap(vs, vs.offset + q * 128,
                             [[8192, 128], [256, NCORES], [1, 128]]),
                        mkap(agp.ap(),
                             ((b % 2) * RPCB + q * 128) * E + et * 128,
                             [[E, 128], [2 * RPCB * E, NCORES], [1, 128]]))
                key = "m%d" % (et % 3)
                pm = p_m[et % 3]
                group_begin(key)
                P.wait("tensor", "LW%d_0" % (et % 2))
                P.wait("tensor", "LW%d_1" % (et % 2))
                if et == 0:
                    P.wait("tensor", "dv", dv_kernel)
                for jt in range(ET):
                    pv = P.emit("tensor", lambda e, p=pm, v_=vs, j=jt: e.matmul(
                        p[:, :], v_[:, j * 128:(j + 1) * 128],
                        kernelT[:, j * RPC:(j + 1) * RPC],
                        start=(j == 0), stop=(j == ET - 1)),
                        inc="pe" if jt == ET - 1 else None)
                if et == 0:
                    kv_pe = {}
                kv_pe[et] = pv
                P.wait("vector", "pe", pv)
                dvv = P.emit("vector", lambda e, p=pm, t_=et: e.tensor_mul(
                    eT[:, t_ * RPC:(t_ + 1) * RPC], p[:, :],
                    uT[:, t_ * RPC:(t_ + 1) * RPC]), inc="dv")
                psum_owner[key] = ("dv", dvv)
            dv_eT = P.cnt["dv"]
            eT_dv_end[b] = dv_eT
            for ft in range(KC):
                osb = ow_strip[ft % 2]
                if ft >= 2:
                    P.wait("scalar", "pe", o_pe[ft - 2])
                dma("scalar", "LO%d" % (ft % 2),
                    osb[:, :2048].rearrange("p (ec c) -> p ec c", ec=ET),
                    mkap(ow_h.ap(), ft * 128, [[H, 128], [128 * H, ET], [1, 128]]))
                group_begin("h")
                if b == 0 and ft == 0:
                    P.wait("tensor", "dv", max(uoc_dv_h[0], woT_dv_h[0]))
                P.emit("tensor", lambda e, f=ft: e.matmul(
                    p_h[:, :], ident_h[:, :], uocT[:, f * RPC:(f + 1) * RPC],
                    start=True, stop=False))
                for kc in range(KC):
                    P.emit("tensor", lambda e, f=ft, k=kc, b_=b: e.matmul(
                        p_h[:, :], woT_sb[:, k * H + f * 128: k * H + f * 128 + 128],
                        xnT_sl(k, b_), start=False, stop=(k == KC - 1)),
                        inc="pe" if kc == KC - 1 else None)
                pv_h = P.cnt["pe"]
                P.wait("scalar", "pe", pv_h)
                if ft > 0:
                    P.wait("scalar", "dv", comb_dv)
                av = P.emit("scalar", lambda e, f=ft: e.activation(
                    out=hT_b[:, :], in_=p_h[:, :], func=AF.Tanh, bias=smc("bo", f)),
                    inc="ac")
                psum_owner["h"] = ("ac", av)
                key = "m%d" % (ft % 3)
                pm = p_m[ft % 3]
                group_begin(key)
                P.emit("tensor", lambda e, p=pm, f=ft, b_=b: e.matmul(
                    p[:, :], ident_h[:, :], xnT_sl(f, b_), start=True, stop=False))
                P.wait("tensor", "LO%d" % (ft % 2))
                if ft == 0:
                    P.wait("tensor", "dv", dv_eT)
                for ec in range(ET):
                    P.emit("tensor", lambda e, p=pm, s=osb, c_=ec: e.matmul(
                        p[:, :], s[:, c_ * 128:(c_ + 1) * 128],
                        eT[:, c_ * RPC:(c_ + 1) * RPC],
                        start=False, stop=(c_ == ET - 1)),
                        inc="pe" if ec == ET - 1 else None)
                pv_o = P.cnt["pe"]
                if ft == 0:
                    o_pe = {}
                o_pe[ft] = pv_o
                xr = xres[ft % 2]
                if b == 0 and ft < 2:
                    P.wait("sync", "dv", phaseA_dv[0])
                dma("sync", "LX%d" % (ft % 2), xr[:, :],
                    x_own[b, ft * 128:(ft + 1) * 128, :])
                ob = outs[ft % 2]
                P.wait("vector", "pe", pv_o)
                P.wait("vector", "ac", av)
                if out_guard[ft % 2]:
                    P.wait("vector", "SO%d" % (ft % 2))
                P.emit("vector", lambda e, p=pm, f=ft, o=ob: e.scalar_tensor_tensor(
                    out=o[:, :], in0=p[:, :], scalar=smc("o_b", f), in1=hT_b[:, :],
                    op0=ALU.add, op1=ALU.add), inc="dv")
                psum_owner[key] = ("dv", P.cnt["dv"])
                P.wait("vector", "LX%d" % (ft % 2))
                comb_dv = P.emit("vector", lambda e, o=ob, x=xr: e.tensor_add(
                    o[:, :], o[:, :], x[:, :]), inc="dv")
                P.wait("gpsimd", "dv", comb_dv)
                dma("gpsimd", "SO%d" % (ft % 2), y_own[b, ft * 128:(ft + 1) * 128, :],
                    ob[:, :])
                out_guard[ft % 2] = 1
                P.wait("sync", "dv", comb_dv)

        for mt in range(KC):
            P.emit("vector", lambda e, m=mt: e.tensor_scalar(
                out=c_f32[:, m * RPC:(m + 1) * RPC], in0=muB[:, :],
                scalar1=0.0, scalar2=smc("in_c", m), op0=ALU.mult, op1=ALU.add))
        for mt in range(KC):
            P.emit("vector", lambda e, m=mt: e.tensor_copy(
                c_bfA[:, m * RPC:(m + 1) * RPC], c_f32[:, m * RPC:(m + 1) * RPC]),
                inc="dv")

        gatespec = [("wf", wf_h, "uf", "bf", AF.Sigmoid, 0),
                    ("wi", wi_h, "ui", "bi", AF.Sigmoid, 1),
                    ("wc", wc_h, "uc", "bc", AF.Tanh, 2)]
        xg_pe = [{} for _ in range(3)]

        def xg_dma(gi, gidx):
            t_, mt_ = divmod(gidx, KC)
            if t_ >= B:
                return
            wh = (wf_h, wi_h, wc_h)[gi]
            par = gidx % 2
            if gidx >= 2:
                P.wait("sync", "pe", xg_pe[gi][gidx - 2])
            dma("sync", "LG%d_%d" % (gi, par),
                xg_buf[gi][par][:, :1024].rearrange("p (kc c) -> p kc c", kc=KC),
                mkap(wh.ap(), mt_ * 128, [[H, 128], [128 * H, KC], [1, 128]]))

        dv_after_update = P.cnt["dv"]
        for gi in range(3):
            xg_dma(gi, 0)
            xg_dma(gi, 1)
        for t in range(B):
            produce_vk_pre(t)
            cur = c_bfA if t % 2 == 0 else c_bfB
            nxt = c_bfB if t % 2 == 0 else c_bfA
            first_group = True
            for mt in range(KC):
                gidx = t * KC + mt
                for (wg, wh, ug, bg, func, gi) in gatespec:
                    key = "m%d" % ((mt * 3 + gi) % 3)
                    pm = p_m[(mt * 3 + gi) % 3]
                    sbx = xg_buf[gi][gidx % 2]
                    group_begin(key)
                    P.wait("tensor", "LG%d_%d" % (gi, gidx % 2))
                    for kc in range(KC):
                        P.emit("tensor", lambda e, p=pm, w=sbx, k=kc, t_=t:
                               e.matmul(p[:, :],
                                        w[:, k * 128:(k + 1) * 128],
                                        xnT_sl(k, t_), start=(k == 0), stop=False))
                    if first_group:
                        P.wait("tensor", "dv", dv_after_update)
                        first_group = False
                    for kc in range(KC):
                        P.emit("tensor", lambda e, p=pm, u=WG[ug], m=mt, k=kc, c=cur:
                               e.matmul(p[:, :],
                                        u[:, k * H + m * 128: k * H + m * 128 + 128],
                                        c[:, k * RPC:(k + 1) * RPC],
                                        start=False, stop=(k == KC - 1)),
                               inc="pe" if kc == KC - 1 else None)
                    pv = P.cnt["pe"]
                    xg_pe[gi][gidx] = pv
                    xg_dma(gi, gidx + 2)
                    P.wait("scalar", "pe", pv)
                    P.wait("scalar", "dv", dv_after_update if mt == 0 else dv_mt)
                    av = P.emit("scalar", lambda e, p=pm, f=func, b=bg, m=mt, g=gi:
                                e.activation(out=gates[g][:, :], in_=p[:, :], func=f,
                                             bias=smc(b, m)), inc="ac")
                    psum_owner[key] = ("ac", av)
                P.wait("vector", "ac")
                P.emit("vector", lambda e, m=mt: e.tensor_mul(
                    upd[:, :], gates[0][:, :], c_f32[:, m * RPC:(m + 1) * RPC]))
                P.emit("vector", lambda e: e.tensor_mul(
                    gates[2][:, :], gates[1][:, :], gates[2][:, :]))
                P.emit("vector", lambda e, m=mt: e.tensor_add(
                    c_f32[:, m * RPC:(m + 1) * RPC], upd[:, :], gates[2][:, :]))
                dv_mt = P.emit("vector", lambda e, m=mt, n=nxt: e.tensor_copy(
                    n[:, m * RPC:(m + 1) * RPC], c_f32[:, m * RPC:(m + 1) * RPC]),
                    inc="dv")
            dv_after_update = P.cnt["dv"]
            produce_vk(t)
            if t % 2 == 1:
                pr = t // 2
                P.wait("gpsimd", "SV0")
                P.wait("gpsimd", "SV1")
                P.wait("gpsimd", "SK")
                P.emit("gpsimd", lambda e, p_=pr: e.collective_compute(
                    "AllGather", ALU.bypass, replica_groups=[list(range(NCORES))],
                    ins=[ag_in_big.ap()[(2 * p_) * RPCB:(2 * p_ + 2) * RPCB, :].opt()],
                    outs=[ag_out_pair[p_].ap().opt()]), inc="cc")

        c_fin = c_bfA if B % 2 == 0 else c_bfB

        rec_pe = P.cnt["pe"]
        phaseA_dv[0] = P.cnt["dv"]

        P.wait("tensor", "dv", phaseA_dv[0])
        uoc_pe = {}
        for ot in range(KC):
            ustr = uo_strip[ot % 2]
            if ot >= 2:
                P.wait("sync", "pe", uoc_pe[ot - 2])
            dma("sync", "LR%d" % (ot % 2),
                ustr[:, :1024].rearrange("p (mc c) -> p mc c", mc=KC),
                mkap(uo_h.ap(), ot * 128, [[H, 128], [128 * H, KC], [1, 128]]))
            key = "base"
            pm = p_base
            group_begin(key)
            P.wait("tensor", "LR%d" % (ot % 2))
            for mc in range(KC):
                P.emit("tensor", lambda e, p=pm, s=ustr, m=mc:
                       e.matmul(p[:, :], s[:, m * 128:(m + 1) * 128],
                                c_fin[:, m * RPC:(m + 1) * RPC],
                                start=(m == 0), stop=(m == KC - 1)),
                       inc="pe" if mc == KC - 1 else None)
            pv = P.cnt["pe"]
            uoc_pe[ot] = pv
            P.wait("vector", "pe", pv)
            dvv = P.emit("vector", lambda e, o=ot, p=pm: e.tensor_copy(
                uocT[:, o * RPC:(o + 1) * RPC], p[:, :]), inc="dv")
            psum_owner[key] = ("dv", dvv)
        uoc_dv_h[0] = P.cnt["dv"]

        for jt in range(ET):
            par = jt % 2
            buf = (scr, scr2)[par]
            if scr_guard[par]:
                P.wait("sync", "dv", scr_guard[par])
            dma("sync", "L%d" % par, buf[:, :RPC],
                biasT_own[jt * 128:(jt + 1) * 128, :])
            P.wait("vector", "L%d" % par)
            dvv = P.emit("vector", lambda e, b=buf[:, :RPC], j=jt:
                         e.tensor_copy(biasT_sb[:, j * RPC:(j + 1) * RPC], b), inc="dv")
            scr_guard[par] = dvv
        biasT_dv_h[0] = P.cnt["dv"]

        for q_ in ("sync", "gpsimd", "scalar"):
            P.wait(q_, "pe", rec_pe)

        produce_pre(0)
        produce(0)

        P.wait("vector", "pe", rec_pe)
        for kc in range(KC):
            for hh in range(2):
                par = (kc * 2 + hh) % 2
                buf = (scr, scr2)[par]
                if scr_guard[par]:
                    P.wait("sync", "dv", scr_guard[par])
                dma("sync", "L%d" % par, buf[:, :],
                    woT[kc * 128:(kc + 1) * 128, hh * 512:(hh + 1) * 512])
                P.wait("vector", "L%d" % par)
                dvv = P.emit("vector", lambda e, b=buf[:, :], k=kc, h_=hh:
                             e.tensor_copy(woT_sb[:, k * H + h_ * 512:
                                                  k * H + h_ * 512 + 512], b),
                             inc="dv")
                scr_guard[par] = dvv
        woT_dv_h[0] = P.cnt["dv"]

        for b in range(B):
            if b + 1 < B:
                produce_pre(b + 1)
            consume_qk(b)
            if b + 1 < B:
                produce(b + 1)
            consume(b)

        P.wait("gpsimd", "SO0")
        P.wait("gpsimd", "SO1")

        block = e(nc.Block())

        @block.sync
        def _(sync):
            for fn in P.ops["sync"]:
                fn(sync)

        @block.gpsimd
        def _(gpsimd):
            for fn in P.ops["gpsimd"]:
                fn(gpsimd)

        @block.tensor
        def _(tensor):
            for fn in P.ops["tensor"]:
                fn(tensor)

        @block.vector
        def _(vector):
            for fn in P.ops["vector"]:
                fn(vector)

        @block.scalar
        def _(scalar):
            for fn in P.ops["scalar"]:
                fn(scalar)

    return nc


_NC_CACHE = None


def kernel(**inputs):
    global _NC_CACHE
    x = np.asarray(inputs["L0_attn_x"], dtype=np.float32)
    sin_t, cos_t = _rope_tables()
    biasT = _bias_matrix(np.asarray(inputs["L0_attn_w"], np.float32))
    uv_w = np.asarray(inputs["L0_attn_uv_w"], np.float32)
    uv_b = np.asarray(inputs["L0_attn_uv_b"], np.float32)

    common = dict(
        ident_in=np.eye(128, dtype=np.float32),
        wuT=np.ascontiguousarray(uv_w[:E].T),
        wvT=np.ascontiguousarray(uv_w[E:2 * E].T),
        wbT=np.ascontiguousarray(uv_w[2 * E:].T),
        wbTs=np.ascontiguousarray(uv_w[2 * E:].T[:, _PERM]),
        owT=np.ascontiguousarray(np.asarray(inputs["L0_attn_o_w"], np.float32).T),
        woT=np.ascontiguousarray(np.asarray(inputs["L0_ffn_wo"], np.float32).T),
        wfT=np.ascontiguousarray(np.asarray(inputs["L0_ffn_wf"], np.float32).T),
        ufT=np.ascontiguousarray(np.asarray(inputs["L0_ffn_uf"], np.float32).T),
        wiT=np.ascontiguousarray(np.asarray(inputs["L0_ffn_wi"], np.float32).T),
        uiT=np.ascontiguousarray(np.asarray(inputs["L0_ffn_ui"], np.float32).T),
        wcT=np.ascontiguousarray(np.asarray(inputs["L0_ffn_wc"], np.float32).T),
        ucT=np.ascontiguousarray(np.asarray(inputs["L0_ffn_uc"], np.float32).T),
        uoT=np.ascontiguousarray(np.asarray(inputs["L0_ffn_uo"], np.float32).T),
        ln_w=np.asarray(inputs["L0_attn_ln_w"], np.float32),
        ln_b=np.asarray(inputs["L0_attn_ln_b"], np.float32),
        o_b=np.asarray(inputs["L0_attn_o_b"], np.float32),
        bf=np.asarray(inputs["L0_ffn_bf"], np.float32),
        bi=np.asarray(inputs["L0_ffn_bi"], np.float32),
        bc=np.asarray(inputs["L0_ffn_bc"], np.float32),
        bo=np.asarray(inputs["L0_ffn_bo"], np.float32),
        in_c=np.asarray(inputs["L0_ffn_in_c"], np.float32),
        uvb_u=np.ascontiguousarray(uv_b[:E]),
        uvb_v=np.ascontiguousarray(uv_b[E:2 * E]),
        uvb_b=np.ascontiguousarray(uv_b[2 * E:]),
        uvb_bs=np.ascontiguousarray(uv_b[2 * E:][_PERM]),
        gamma=np.asarray(inputs["L0_attn_gamma"], np.float32),
        beta=np.asarray(inputs["L0_attn_beta"], np.float32),
        gammas=np.ascontiguousarray(np.asarray(inputs["L0_attn_gamma"], np.float32)[:, _PERM]),
        betas=np.ascontiguousarray(np.asarray(inputs["L0_attn_beta"], np.float32)[:, _PERM]),
    )

    xT = np.ascontiguousarray(x.transpose(0, 2, 1))
    in_maps = []
    for c in range(NCORES):
        r0, r1 = c * RPC, (c + 1) * RPC
        m = dict(common)
        m["x_own"] = np.ascontiguousarray(xT[:, :, r0:r1])
        m["biasT_own"] = np.ascontiguousarray(biasT[:, r0:r1])
        st = sin_t[r0:r1].T
        ct = cos_t[r0:r1].T
        m["sinT"] = np.ascontiguousarray(np.concatenate([-st, st], axis=0))
        m["cosT"] = np.ascontiguousarray(np.concatenate([ct, ct], axis=0))
        in_maps.append(m)

    if _NC_CACHE is None:
        _NC_CACHE = build_program()
    res = run_bass_kernel_spmd(_NC_CACHE, in_maps, core_ids=list(range(NCORES)))
    shards = [res.results[c]["y_own"] for c in range(NCORES)]
    outT = np.concatenate(shards, axis=2)
    return np.ascontiguousarray(outT.transpose(0, 2, 1)).astype(np.float32)


# version 18
# speedup vs baseline: 1.2632x; 1.0036x over previous
import numpy as np

import concourse.bass as bass
import concourse.mybir as mybir
from concourse.bass_utils import run_bass_kernel_spmd

F32 = mybir.dt.float32
BF16 = mybir.dt.bfloat16
AF = mybir.ActivationFunctionType
ALU = mybir.AluOpType

NCORES = 8
B = 16
T = 2048
H = 1024
E = 2048
S = 128
RPC = T // NCORES
L = 512
INV_SQRT_S = 1.0 / float(np.sqrt(S))
EPS = 1e-5
KC = H // 128
ET = E // 128
RPCB = RPC + 16
_PERM = np.r_[64:128, 0:64]


def _rope_tables():
    try:
        import jax
        cpu = jax.devices("cpu")[0]
        import jax.numpy as jnp
        with jax.default_device(cpu):
            pos = jnp.arange(T, dtype=jnp.float32)
            inv_freq = 10000.0 ** (jnp.arange(64, dtype=jnp.float32) / 64)
            sinusoid = pos[:, None] * inv_freq[None, :]
            return (np.asarray(jnp.sin(sinusoid)), np.asarray(jnp.cos(sinusoid)))
    except Exception:
        pos = np.arange(T, dtype=np.float32)
        inv_freq = (10000.0 ** (np.arange(64, dtype=np.float32) /
                                np.float32(64))).astype(np.float32)
        sinusoid = pos[:, None] * inv_freq[None, :]
        return np.sin(sinusoid).astype(np.float32), np.cos(sinusoid).astype(np.float32)


def _bias_matrix(w):
    n = w.shape[0]
    rel = np.arange(T)[None, :] - np.arange(T)[:, None] + (L - 1)
    idx = np.where(rel < 0, rel + n, rel)
    idx = np.clip(idx, 0, n - 1)
    return np.ascontiguousarray(w[idx].T)


class Prog:
    ENGS = ("sync", "gpsimd", "tensor", "vector", "scalar")

    def __init__(self, nc, es):
        self.nc = nc
        self.es = es
        self.ops = {e: [] for e in self.ENGS}
        self.sem = {}
        self.cnt = {}
        self.dma_sems = {}
        for name in ("cc", "pe", "dv", "ac",
                     "b_sync", "b_gpsimd", "b_tensor", "b_vector", "b_scalar"):
            self.sem[name] = es.enter_context(nc.semaphore(name))
            self.cnt[name] = 0

    def dsem(self, name, issuer):
        if name not in self.sem:
            self.sem[name] = self.es.enter_context(self.nc.semaphore(name))
            self.cnt[name] = 0
            self.dma_sems[name] = issuer
        return name

    def wait(self, eng, sem, val=None):
        v = self.cnt[sem] if val is None else val
        h = self.sem[sem]
        self.ops[eng].append(lambda e, h=h, v=v: e.wait_ge(h, v))

    def emit(self, eng, fn, inc=None, amt=1):
        if eng == "vector":
            self.wait("vector", "dv")
            h = self.sem["dv"]
            self.cnt["dv"] += 1
            self.ops[eng].append(lambda e, fn=fn, h=h: fn(e).then_inc(h, 1))
            return self.cnt["dv"]
        if inc is None:
            self.ops[eng].append(fn)
            return None
        h = self.sem[inc]
        self.cnt[inc] += amt
        self.ops[eng].append(lambda e, fn=fn, h=h, a=amt: fn(e).then_inc(h, a))
        return self.cnt[inc]

    def barrier(self):
        for name, issuer in self.dma_sems.items():
            if self.cnt[name]:
                self.wait(issuer, name)
        self.wait("gpsimd", "cc")
        self.wait("vector", "dv")
        self.wait("scalar", "ac")
        self.wait("tensor", "pe")
        for e in self.ENGS:
            h = self.sem["b_" + e]
            self.cnt["b_" + e] += 1
            self.ops[e].append(lambda eng, h=h: eng.nop().then_inc(h, 1))
        for e in self.ENGS:
            for o in self.ENGS:
                if o != e:
                    self.wait(e, "b_" + o)


def build_program():
    nc = bass.Bass("TRN2", target_bir_lowering=False)
    dt = nc.dram_tensor

    x_own = dt("x_own", [B, H, RPC], F32, kind="ExternalInput").ap()
    biasT_own = dt("biasT_own", [T, RPC], F32, kind="ExternalInput").ap()
    sinT_in = dt("sinT", [128, RPC], F32, kind="ExternalInput").ap()
    cosT_in = dt("cosT", [128, RPC], F32, kind="ExternalInput").ap()
    ident_in = dt("ident_in", [128, 128], F32, kind="ExternalInput").ap()

    wuT = dt("wuT", [H, E], F32, kind="ExternalInput").ap()
    wvT = dt("wvT", [H, E], F32, kind="ExternalInput").ap()
    wbT = dt("wbT", [H, S], F32, kind="ExternalInput").ap()
    wbTs = dt("wbTs", [H, S], F32, kind="ExternalInput").ap()
    owT = dt("owT", [E, H], F32, kind="ExternalInput").ap()
    woT = dt("woT", [H, H], F32, kind="ExternalInput").ap()
    gateT = {g: dt(g + "T", [H, H], F32, kind="ExternalInput").ap()
             for g in ("wf", "uf", "wi", "ui", "wc", "uc", "uo")}

    vec_in = {}
    for name, n in (("ln_w", H), ("ln_b", H), ("o_b", H), ("bf", H), ("bi", H),
                    ("bc", H), ("bo", H), ("in_c", H), ("uvb_u", E), ("uvb_b", S), ("uvb_bs", S)):
        vec_in[name] = dt(name, [n], F32, kind="ExternalInput").ap()
    uvb_v_in = dt("uvb_v", [E], F32, kind="ExternalInput").ap()
    gamma_in = dt("gamma", [2, 128], F32, kind="ExternalInput").ap()
    gammas_in = dt("gammas", [2, 128], F32, kind="ExternalInput").ap()
    beta_in = dt("beta", [2, 128], F32, kind="ExternalInput").ap()
    betas_in = dt("betas", [2, 128], F32, kind="ExternalInput").ap()

    y_own = dt("y_own", [B, H, RPC], F32, kind="ExternalOutput").ap()

    wu_h = dt("wu_h", [H, E], BF16)
    wv_h = dt("wv_h", [H, E], BF16)
    wb_h = dt("wb_h", [H, S], BF16)
    wbs_h = dt("wbs_h", [H, S], BF16)
    ow_h = dt("ow_h", [E, H], BF16)
    uo_h = dt("uo_h", [H, H], BF16)
    xgpre_h = [dt("xgpre%d" % g, [H, B * RPC], BF16) for g in range(3)]
    ag_in_big = dt("ag_in_big", [B * RPCB, E], BF16)
    ag_out_pair = [dt(f"ag_out_p{p}", [NCORES * 2 * RPCB, E], BF16)
                   for p in range(B // 2)]

    from contextlib import ExitStack
    es = ExitStack()
    with es:
        e = es.enter_context

        xnT = e(nc.sbuf_tensor("xnT", [128, KC * B * RPC], BF16))
        bb = [e(nc.sbuf_tensor(f"bb{i}", [128, 8192], BF16)) for i in range(6)]
        aux_h = e(nc.sbuf_tensor("aux_h", [128, 3072], BF16))
        c_f32 = e(nc.sbuf_tensor("c_f32", [128, KC * RPC], F32))
        c_bfA = e(nc.sbuf_tensor("c_bfA", [128, KC * RPC], BF16))
        c_bfB = e(nc.sbuf_tensor("c_bfB", [128, KC * RPC], BF16))
        scr = e(nc.sbuf_tensor("scr", [128, 512], F32))
        scr2 = e(nc.sbuf_tensor("scr2", [128, 512], F32))
        gates = [e(nc.sbuf_tensor(f"gate{i}", [128, RPC], F32)) for i in range(3)]
        upd = e(nc.sbuf_tensor("upd", [128, RPC], F32))
        sq_h = e(nc.sbuf_tensor("sq_h", [128, RPC], BF16))
        tmp_h = e(nc.sbuf_tensor("tmp_h", [128, RPC], BF16))
        muB = e(nc.sbuf_tensor("muB", [128, RPC], BF16))
        rsB = e(nc.sbuf_tensor("rsB", [128, RPC], BF16))
        hT_b = e(nc.sbuf_tensor("hT_b", [128, RPC], BF16))
        xres = [e(nc.sbuf_tensor(f"xres{i}", [128, RPC], F32)) for i in range(2)]
        outs = [e(nc.sbuf_tensor(f"outs{i}", [128, RPC], F32)) for i in range(2)]
        stat = e(nc.sbuf_tensor("stat", [1, 2 * RPC], F32))
        uvbv_row = e(nc.sbuf_tensor("uvbv_row", [1, E], BF16))
        ones_col = e(nc.sbuf_tensor("ones_col", [128, 1], BF16))
        ones_row = e(nc.sbuf_tensor("ones_row", [1, 128], BF16))
        ones_row_f = e(nc.sbuf_tensor("ones_row_f", [1, 128], F32))
        ident_h = e(nc.sbuf_tensor("ident_h", [128, 128], BF16))
        sin_sb = e(nc.sbuf_tensor("sin_sb", [128, RPC], F32))
        cos_sb = e(nc.sbuf_tensor("cos_sb", [128, RPC], F32))
        sm = e(nc.sbuf_tensor("sm", [128, 96], F32))

        p_m = [e(nc.psum_tensor(f"p_m{i}", [128, RPC], F32)) for i in range(3)]
        p_v = [e(nc.psum_tensor(f"p_v{i}", [128, 512], F32)) for i in range(2)]
        p_base = e(nc.psum_tensor("p_base", [128, RPC], F32))
        p_h = e(nc.psum_tensor("p_h", [128, RPC], F32))
        p_stat = e(nc.psum_tensor("p_stat", [128, 512], F32))
        p_s = [p_stat[0:1, 0:RPC], p_stat[0:1, RPC:2 * RPC]]
        p_b = p_stat

        P = Prog(nc, es)
        psum_owner = {}

        def group_begin(key):
            if key in psum_owner:
                s, v = psum_owner[key]
                P.wait("tensor", s, v)

        smcol = {}
        _c = [0]
        def alloc_sm(name, ncols):
            smcol[name] = _c[0]
            _c[0] += ncols
        for name in ("ln_w", "ln_b", "o_b", "bf", "bi", "bc", "bo", "in_c"):
            alloc_sm(name, KC)
        alloc_sm("uvb_u", ET)
        alloc_sm("uvb_b", 1)
        alloc_sm("uvb_bs", 1)
        alloc_sm("gamma", 2)
        alloc_sm("beta", 2)
        alloc_sm("gammas", 2)
        alloc_sm("betas", 2)
        alloc_sm("g0p", 1)
        alloc_sm("b0p", 1)
        alloc_sm("g0ps", 1)
        alloc_sm("b0ps", 1)
        assert _c[0] <= 96

        def smc(name, i=0):
            return sm[:, smcol[name] + i: smcol[name] + i + 1]

        WG = {"uf": bb[1], "ui": bb[3], "uc": bb[5]}
        WX = {"wf": bb[0], "wi": bb[2], "wc": bb[4]}
        xg_buf = [[bb[0][:, 2048 + (4 * g + i) * 256: 2048 + (4 * g + i + 1) * 256]
                   for i in range(4)] for g in range(3)]
        wb_vk2 = [bb[0][:, i * 1024:(i + 1) * 1024] for i in range(2)]
        wv_vk = [bb[4][:, i * 1024:(i + 1) * 1024] for i in range(4)]
        kernelT = bb[0]
        kT_sb = bb[0][:, 4096:6144]
        qT_bufs = [bb[0][:, 6144:6400], bb[0][:, 6400:6656]]
        uT_bufs = [bb[1][:, 0:4096], bb[1][:, 4096:8192]]
        eT = bb[3][:, 4096:8192]
        biasT_sb = bb[2][:, 0:4096]
        uocT = bb[2][:, 4096:6144]
        xnnT = bb[2][:, 6144:8192]
        ow_strip = [bb[3][:, i * 2048:(i + 1) * 2048] for i in range(2)]
        woT_sb = bb[4]
        wu_strip = [bb[5][:, i * 1024:(i + 1) * 1024] for i in range(2)]
        wb_buf2 = [bb[5][:, 2048 + i * 1024: 2048 + (i + 1) * 1024] for i in range(2)]
        vstrip = [bb[5][:, 4096 + i * 2048: 4096 + (i + 1) * 2048] for i in range(2)]
        vout = [aux_h[:, i * 512:(i + 1) * 512] for i in range(2)]
        uo_strip = [aux_h[:, 1024 + i * 1024: 1024 + (i + 1) * 1024] for i in range(2)]
        x_bf = c_bfA
        uvbv_hlp = c_bfA
        cast_h = c_bfB

        import dataclasses as _dc

        def mkap(base_ap, offset, dims):
            return _dc.replace(base_ap, offset=offset,
                               ap=type(base_ap.ap)(dims))

        def dma(eng, sem, out, in_, slow=False):
            P.dsem(sem, eng)
            if slow:
                return P.emit(eng, lambda e, o=out, i=in_: e.dma_start(
                    out=o, in_=i, allow_slow_non_contiguous=True), inc=sem, amt=16)
            return P.emit(eng, lambda e, o=out, i=in_: e.dma_start(out=o, in_=i),
                          inc=sem, amt=16)

        for name in ("ln_w", "ln_b", "o_b", "bf", "bi", "bc", "bo", "in_c"):
            dma("sync", "LS", sm[:, smcol[name]:smcol[name] + KC],
                vec_in[name].rearrange("(t p) -> p t", p=128), slow=True)
        dma("sync", "LS", sm[:, smcol["uvb_u"]:smcol["uvb_u"] + ET],
            vec_in["uvb_u"].rearrange("(t p) -> p t", p=128), slow=True)
        dma("sync", "LS", sm[:, smcol["uvb_b"]:smcol["uvb_b"] + 1],
            vec_in["uvb_b"].rearrange("(t p) -> p t", p=128), slow=True)
        dma("sync", "LS", sm[:, smcol["uvb_bs"]:smcol["uvb_bs"] + 1],
            vec_in["uvb_bs"].rearrange("(t p) -> p t", p=128), slow=True)
        dma("sync", "LS", sm[:, smcol["gamma"]:smcol["gamma"] + 2],
            gamma_in.rearrange("h s -> s h"), slow=True)
        dma("sync", "LS", sm[:, smcol["beta"]:smcol["beta"] + 2],
            beta_in.rearrange("h s -> s h"), slow=True)
        dma("sync", "LS", sm[:, smcol["gammas"]:smcol["gammas"] + 2],
            gammas_in.rearrange("h s -> s h"), slow=True)
        dma("sync", "LS", sm[:, smcol["betas"]:smcol["betas"] + 2],
            betas_in.rearrange("h s -> s h"), slow=True)
        dma("sync", "LS", scr[:, :128], ident_in)
        dma("sync", "LS", sin_sb[:, :], sinT_in)
        dma("sync", "LS", cos_sb[:, :], cosT_in)

        P.wait("vector", "LS")
        P.emit("vector", lambda e: e.memset(ones_col[:, :], 1.0))
        P.emit("vector", lambda e: e.memset(ones_row[:, :], 1.0))
        P.emit("vector", lambda e: e.memset(ones_row_f[:, :], 1.0))
        P.emit("vector", lambda e: e.tensor_copy(ident_h[:, :], scr[:, :128]))
        for q in range(4):
            dma("sync", "L1", scr2[0:1, :512],
                uvb_v_in[q * 512:(q + 1) * 512].rearrange("(a m) -> a m", a=1))
            P.wait("vector", "L1")
            dvq = P.emit("vector", lambda e, q_=q: e.tensor_copy(
                uvbv_hlp[0:1, q_ * 512:(q_ + 1) * 512], scr2[0:1, :512]), inc="dv")
            P.wait("sync", "dv", dvq)
        P.wait("gpsimd", "dv")
        dma("gpsimd", "SU", uvbv_row[0:1, :], uvbv_hlp[0:1, 0:E])
        P.emit("vector", lambda e: e.tensor_scalar_mul(smc("g0p"), smc("gamma", 0), INV_SQRT_S))
        P.emit("vector", lambda e: e.tensor_scalar_mul(smc("b0p"), smc("beta", 0), INV_SQRT_S))
        P.emit("vector", lambda e: e.tensor_scalar_mul(smc("g0ps"), smc("gammas", 0), INV_SQRT_S))
        P.emit("vector", lambda e: e.tensor_scalar_mul(smc("b0ps"), smc("betas", 0), INV_SQRT_S),
               inc="dv")

        scr_guard = [0, 0]

        def cast_chunks(chunks):
            pass

        def cast_resident(src_ap, dst, n_out):
            i = 0
            for kc in range(KC):
                for off in range(0, n_out, 512):
                    wdt = min(512, n_out - off)
                    par = i % 2
                    buf = (scr, scr2)[par]
                    if scr_guard[par]:
                        P.wait("sync", "dv", scr_guard[par])
                    dma("sync", "L%d" % par, buf[:, :wdt],
                        src_ap[kc * 128:(kc + 1) * 128, off:off + wdt])
                    P.wait("vector", "L%d" % par)
                    dvv = P.emit("vector", lambda e, b=buf[:, :wdt],
                                 d=dst[:, kc * n_out + off: kc * n_out + off + wdt]:
                                 e.tensor_copy(d, b), inc="dv")
                    scr_guard[par] = dvv
                    i += 1

        cast_dram_guard = [0, 0]

        def cast_to_dram(src_ap, dst_dram, rows, cols):
            i = 0
            for r0 in range(0, rows, 128):
                for off in range(0, cols, 512):
                    wdt = min(512, cols - off)
                    par = i % 2
                    buf = (scr, scr2)[par]
                    if scr_guard[par]:
                        P.wait("sync", "dv", scr_guard[par])
                    dma("sync", "L%d" % par, buf[:, :wdt],
                        src_ap[r0:r0 + 128, off:off + wdt])
                    hbuf = cast_h[:, par * 1024: par * 1024 + wdt]
                    P.wait("vector", "L%d" % par)
                    if cast_dram_guard[par]:
                        P.wait("vector", "SC%d" % par)
                    dvv = P.emit("vector", lambda e, b=buf[:, :wdt], h=hbuf:
                                 e.tensor_copy(h, b), inc="dv")
                    scr_guard[par] = dvv
                    P.wait("gpsimd", "dv", dvv)
                    dma("gpsimd", "SC%d" % par, dst_dram.ap()[r0:r0 + 128, off:off + wdt], hbuf)
                    cast_dram_guard[par] = 1
                    i += 1

        for g in ("uf", "ui", "uc"):
            cast_resident(gateT[g], WG[g], H)
        for g, dst in (("wf", bb[0]), ("wi", bb[2]), ("wc", bb[4])):
            cast_resident(gateT[g], dst, H)
        cast_to_dram(gateT["uo"], uo_h, H, H)
        cast_to_dram(wuT, wu_h, H, E)
        cast_to_dram(wvT, wv_h, H, E)
        cast_to_dram(wbT, wb_h, H, S)
        cast_to_dram(wbTs, wbs_h, H, S)
        cast_to_dram(owT, ow_h, E, H)

        P.barrier()

        st_mu = stat[0:1, 0:RPC]
        st_b = stat[0:1, RPC:2 * RPC]

        stats_guard = {"sq": 0, "bank": 0}

        def ln_stats(src_slices, src_dv=0):
            P.wait("tensor", "dv", max(src_dv, stats_guard["bank"]))
            P.wait("scalar", "dv", max(stats_guard["sq"], src_dv))
            if P.cnt.get("SK"):
                P.wait("scalar", "SK")
                P.wait("vector", "SK")
            for kc in range(KC):
                P.emit("tensor", lambda e, s=src_slices[kc], k=kc:
                       e.matmul(p_s[0], ones_col[:, :], s,
                                start=(k == 0), stop=(k == KC - 1)),
                       inc="pe")
            stats_guard["srcpe"] = P.cnt["pe"]
            for kc in range(KC):
                sq = (sq_h, tmp_h)[kc % 2]
                if kc >= 2:
                    P.wait("scalar", "pe", pe_sq[kc - 2])
                av = P.emit("scalar", lambda e, s=src_slices[kc], q=sq:
                            e.activation(out=q[:, :], in_=s, func=AF.Square), inc="ac")
                P.wait("tensor", "ac", av)
                pv = P.emit("tensor", lambda e, k=kc, q=sq:
                            e.matmul(p_s[1], ones_col[:, :], q[:, :],
                                     start=(k == 0), stop=(k == KC - 1)),
                            inc="pe")
                if kc == 0:
                    pe_sq = {}
                pe_sq[kc] = pv
            stats_guard["srcac"] = P.cnt["ac"]
            P.wait("vector", "pe")
            P.emit("vector", lambda e: e.tensor_scalar_mul(st_mu, p_s[0], 1.0 / H))
            P.emit("vector", lambda e: e.tensor_scalar_mul(st_b, p_s[1], 1.0 / H))
            P.emit("vector", lambda e: e.tensor_mul(p_s[0], st_mu, st_mu))
            P.emit("vector", lambda e: e.tensor_sub(st_b, st_b, p_s[0]))
            P.emit("vector", lambda e: e.tensor_scalar_add(st_b, st_b, EPS), inc="dv")
            P.wait("scalar", "dv")
            P.emit("scalar", lambda e: e.activation(out=st_b, in_=st_b, func=AF.Sqrt),
                   inc="ac")
            P.wait("vector", "ac")
            recip_dv = P.emit("vector", lambda e: e.reciprocal(st_b, st_b), inc="dv")
            P.wait("tensor", "dv", recip_dv)
            P.emit("tensor", lambda e: e.matmul(p_b[:, 0:RPC], ones_row_f[:, :], st_mu,
                                                start=True, stop=True))
            P.emit("tensor", lambda e: e.matmul(p_b[:, RPC:2 * RPC], ones_row_f[:, :], st_b,
                                                start=True, stop=True), inc="pe")
            P.wait("vector", "pe")
            P.emit("vector", lambda e: e.tensor_copy(muB[:, :], p_b[:, 0:RPC]))
            stats_guard["bank"] = P.emit(
                "vector", lambda e: e.tensor_copy(rsB[:, :], p_b[:, RPC:2 * RPC]),
                inc="dv")

        def xnT_sl(kc, t):
            off = kc * (B * RPC) + t * RPC
            return xnT[:, off: off + RPC]

        xgst = [aux_h[:, 0:256], aux_h[:, 256:512]]
        xgst_used = [0, 0]
        for t in range(B):
            if t > 0:
                P.wait("vector", "pe", stats_guard["srcpe"])
                P.wait("vector", "ac", stats_guard["srcac"])
            for kc in range(KC):
                par = kc % 2
                buf = (scr, scr2)[par]
                if scr_guard[par]:
                    P.wait("sync", "dv", scr_guard[par])
                dma("sync", "L%d" % par, buf[:, :RPC],
                    x_own[t, kc * 128:(kc + 1) * 128, :])
                P.wait("vector", "L%d" % par)
                dvv = P.emit("vector", lambda e, b=buf[:, :RPC],
                             d=x_bf[:, kc * RPC:(kc + 1) * RPC]:
                             e.tensor_copy(d, b), inc="dv")
                scr_guard[par] = dvv
            xs = [x_bf[:, kc * RPC:(kc + 1) * RPC] for kc in range(KC)]
            ln_stats(xs, src_dv=dvv)
            for kc in range(KC):
                P.emit("vector", lambda e, s=xs[kc]: e.tensor_sub(tmp_h[:, :], s, muB[:, :]))
                stats_guard["sq"] = P.emit(
                    "vector", lambda e, d=xnT_sl(kc, t):
                    e.tensor_mul(d, tmp_h[:, :], rsB[:, :]), inc="dv")
            nrm_dv = P.cnt["dv"]
            for gi, wg in enumerate(("wf", "wi", "wc")):
                wres = WX[wg]
                for mt in range(KC):
                    gidx3 = gi * KC + mt
                    key = "m%d" % (gidx3 % 3)
                    pm = p_m[gidx3 % 3]
                    group_begin(key)
                    if gidx3 == 0:
                        P.wait("tensor", "dv", nrm_dv)
                    for kc in range(KC):
                        P.emit("tensor", lambda e, p=pm, w=wres, m=mt, k=kc, t_=t:
                               e.matmul(p[:, :],
                                        w[:, k * H + m * 128: k * H + m * 128 + 128],
                                        xnT_sl(k, t_), start=(k == 0),
                                        stop=(k == KC - 1)),
                               inc="pe" if kc == KC - 1 else None)
                    pv = P.cnt["pe"]
                    par = gidx3 % 2
                    P.wait("scalar", "pe", pv)
                    if xgst_used[par]:
                        P.wait("scalar", "XS%d" % par)
                    av = P.emit("scalar", lambda e, p=pm, s=xgst[par]: e.activation(
                        out=s, in_=p[:, :], func=AF.Copy), inc="ac")
                    psum_owner[key] = ("ac", av)
                    P.wait("gpsimd", "ac", av)
                    dma("gpsimd", "XS%d" % par,
                        xgpre_h[gi].ap()[mt * 128:(mt + 1) * 128,
                                         t * RPC:(t + 1) * RPC], xgst[par])
                    xgst_used[par] = 1

        P.barrier()

        def xnn_make(b):
            xs = [xnT_sl(kc, b) for kc in range(KC)]
            ln_stats(xs)
            P.wait("vector", "pe", produce_pe_end[0])
            for kc in range(KC):
                P.emit("vector", lambda e, s=xs[kc]: e.tensor_sub(tmp_h[:, :], s, muB[:, :]))
                P.emit("vector", lambda e: e.tensor_mul(sq_h[:, :], tmp_h[:, :], rsB[:, :]))
                stats_guard["sq"] = P.emit("vector", lambda e, k=kc: e.tensor_scalar(
                    out=xnnT[:, k * RPC:(k + 1) * RPC], in0=sq_h[:, :],
                    scalar1=smc("ln_w", k), scalar2=smc("ln_b", k),
                    op0=ALU.mult, op1=ALU.add), inc="dv")

        base_sw = c_f32[:, 0:RPC]
        relu_t = c_f32[:, RPC:2 * RPC]
        rope_dv_last = [0]
        produce_pe_end = [0]
        cc_of = {}
        qk_pe_end = {}
        eT_dv_end = {}

        def rope_make(dst, gcol, bcol, gscol, bscol, a0, a1,
                      bsrc=None, bssrc=None, tq=None):
            bsrc = gates[2][:, :] if bsrc is None else bsrc
            bssrc = base_sw if bssrc is None else bssrc
            tq = upd[:, :] if tq is None else tq
            P.emit("vector", lambda e, g=gcol, b_=bcol: e.tensor_scalar(
                out=tq, in0=bsrc, scalar1=g, scalar2=b_,
                op0=ALU.mult, op1=ALU.add))
            P.emit("vector", lambda e, g=gscol, b_=bscol: e.tensor_scalar(
                out=a0, in0=bssrc, scalar1=g, scalar2=b_,
                op0=ALU.mult, op1=ALU.add))
            P.emit("vector", lambda e: e.tensor_mul(a1, a0, sin_sb[:, :]))
            P.emit("vector", lambda e: e.tensor_mul(tq, tq, cos_sb[:, :]))
            return P.emit("vector", lambda e, d=dst:
                          e.tensor_add(d, tq, a1), inc="dv")

        vout_guard = [0, 0]
        out_guard = [0, 0]
        phaseA_dv = [0]

        vk_dv_xnn = [0]
        vk_base_pe = [0]

        vout_guard = [0, 0]
        out_guard = [0, 0]
        phaseA_dv = [0]

        vk_dv_xnn = [0]
        vk_base_pe = [0]
        v_pe = {}
        biasT_dv_h = [0]
        uoc_dv_h = [0]
        woT_dv_h = [0]

        def produce_vk_pre(b):
            xnn_make(b)
            vk_dv_xnn[0] = P.cnt["dv"]
            if vk_base_pe[0]:
                P.wait("gpsimd", "pe", vk_base_pe[0])
            for swp in range(2):
                src = (wb_h if swp == 0 else wbs_h).ap()
                dma("gpsimd", "LBk",
                    wb_vk2[swp][:, :1024].rearrange("p (kc c) -> p kc c", kc=KC),
                    mkap(src, 0, [[S, 128], [128 * S, KC], [1, S]]))

        def produce_vk(b):
            dv_xnn = vk_dv_xnn[0]
            P.wait("tensor", "dv", dv_xnn)
            P.wait("scalar", "dv", rope_dv_last[0])
            for swp, dst_sb, bias_name in (
                    (0, xres[0][:, :], "uvb_b"),
                    (1, xres[1][:, :], "uvb_bs")):
                group_begin("base")
                if swp == 0:
                    P.wait("tensor", "LBk")
                for kc in range(KC):
                    P.emit("tensor", lambda e, k=kc, w=wb_vk2[swp]:
                           e.matmul(p_base[:, :], w[:, k * 128:(k + 1) * 128],
                                    xnnT[:, k * RPC:(k + 1) * RPC],
                                    start=(k == 0), stop=(k == KC - 1)),
                           inc="pe" if kc == KC - 1 else None)
                pv = P.cnt["pe"]
                P.wait("scalar", "pe", pv)
                av = P.emit("scalar", lambda e, d=dst_sb, bn=bias_name: e.activation(
                    out=d, in_=p_base[:, :], func=AF.Silu,
                    bias=smc(bn)), inc="ac")
                psum_owner["base"] = ("ac", av)
            vk_base_pe[0] = P.cnt["pe"]
            P.wait("vector", "ac", av)
            kv = rope_make(tmp_h[:, :], smc("gamma", 1), smc("beta", 1),
                           smc("gammas", 1), smc("betas", 1),
                           outs[0][:, :], outs[1][:, :],
                           bsrc=xres[0][:, :], bssrc=xres[1][:, :],
                           tq=xres[0][:, :])
            rope_dv_last[0] = kv
            stats_guard["sq"] = max(stats_guard["sq"], kv)
            P.wait("gpsimd", "dv", kv)
            dma("gpsimd", "SK",
                ag_in_big.ap()[b * RPCB + RPC: b * RPCB + RPC + 16, :]
                .rearrange("a (c i) -> (a c) i", i=RPC),
                tmp_h[:, :])
            for nch in range(4):
                for kh in range(4):
                    idx = nch * 4 + kh
                    par = idx % 4
                    qpar = idx % 2
                    wb2 = wv_vk[par]
                    prev = idx - 4 if idx >= 4 else idx + 12
                    if prev in v_pe:
                        P.wait(("sync", "gpsimd")[qpar], "pe", v_pe[prev])
                    dma(("sync", "gpsimd")[qpar], "LVk%d" % par,
                        wb2[:, :].rearrange("p (kk c) -> p kk c", kk=2),
                        mkap(wv_h.ap(), (2 * kh) * 128 * E + nch * 512,
                             [[E, 128], [128 * E, 2], [1, 512]]))
                    pv = None
                    for kk in range(2):
                      kc = 2 * kh + kk
                      for rt in range(2):
                        key = "pv%d" % rt
                        if kc == 0:
                            group_begin(key)
                            P.emit("tensor", lambda e, p=p_v[rt], n=nch: e.matmul(
                                p[:, :], ones_row[:, :],
                                uvbv_row[0:1, n * 512:(n + 1) * 512],
                                start=True, stop=False))
                        if kk == 0 and rt == 0:
                            P.wait("tensor", "LVk%d" % par)
                        pv = P.emit("tensor",
                                    lambda e, p=p_v[rt], w=wb2, r=rt, k=kc, q=kk:
                                    e.matmul(p[:, :],
                                             xnnT[:, k * RPC + r * 128: k * RPC + r * 128 + 128],
                                             w[:, q * 512:(q + 1) * 512],
                                             start=False, stop=(k == KC - 1)),
                                    inc="pe")
                        v_pe[idx] = pv
                        if kc == KC - 1:
                            vo = vout[rt]
                            P.wait("scalar", "pe", pv)
                            if vout_guard[rt]:
                                P.wait("scalar", "SV%d" % rt)
                            av = P.emit("scalar", lambda e, p=p_v[rt], o=vo: e.activation(
                                out=o, in_=p[:, :], func=AF.Silu), inc="ac")
                            psum_owner[key] = ("ac", av)
                            P.wait("gpsimd", "ac", av)
                            dma("gpsimd", "SV%d" % rt,
                                ag_in_big.ap()[b * RPCB + rt * 128: b * RPCB + (rt + 1) * 128,
                                               nch * 512:(nch + 1) * 512], vo)
                            vout_guard[rt] = 1
            produce_pe_end[0] = P.cnt["pe"]

        b_dv_xnn = {}
        b_base_pe = [0]
        b_dv_kernel = {}
        b_qv = {}

        def produce_pre(b):
            xnn_make(b)
            b_dv_xnn[b] = P.cnt["dv"]
            if b_base_pe[0]:
                P.wait("sync", "pe", b_base_pe[0])
            for swp in range(2):
                src = (wb_h if swp == 0 else wbs_h).ap()
                dma("sync", "LB",
                    wb_buf2[swp][:, :1024].rearrange("p (kc c) -> p kc c", kc=KC),
                    mkap(src, 0, [[S, 128], [128 * S, KC], [1, S]]))

        def produce(b):
            uT = uT_bufs[b % 2]
            qT_sb = qT_bufs[b % 2]
            dv_xnn = b_dv_xnn[b]
            P.wait("tensor", "dv", dv_xnn)
            P.wait("scalar", "dv", rope_dv_last[0])
            for swp, dst_sb, bias_name in (
                    (0, gates[2][:, :], "uvb_b"),
                    (1, base_sw, "uvb_bs")):
                group_begin("base")
                if swp == 0:
                    P.wait("tensor", "LB")
                for kc in range(KC):
                    P.emit("tensor", lambda e, k=kc, w=wb_buf2[swp]:
                           e.matmul(p_base[:, :], w[:, k * 128:(k + 1) * 128],
                                    xnnT[:, k * RPC:(k + 1) * RPC],
                                    start=(k == 0), stop=(k == KC - 1)),
                           inc="pe" if kc == KC - 1 else None)
                pv = P.cnt["pe"]
                P.wait("scalar", "pe", pv)
                av = P.emit("scalar", lambda e, d=dst_sb, bn=bias_name: e.activation(
                    out=d, in_=p_base[:, :], func=AF.Silu,
                    bias=smc(bn)), inc="ac")
                psum_owner["base"] = ("ac", av)
            b_base_pe[0] = P.cnt["pe"]
            P.wait("vector", "ac", av)
            a0 = gates[0][:, :]
            a1 = gates[1][:, :]
            if b >= 2:
                P.wait("vector", "pe", qk_pe_end[b - 2])
            qv = rope_make(qT_sb, smc("g0p"), smc("b0p"),
                           smc("g0ps"), smc("b0ps"), a0, a1)
            rope_dv_last[0] = qv
            b_qv[b] = qv
            if b >= 2:
                P.wait("scalar", "dv", eT_dv_end[b - 2])

            def u_dma(et):
                if et >= ET:
                    return
                if et >= 2:
                    P.wait("scalar", "pe", u_pe[et - 2])
                dma("scalar", "LU%d" % (et % 2),
                    wu_strip[et % 2][:, :1024].rearrange("p (kc c) -> p kc c", kc=KC),
                    mkap(wu_h.ap(), et * 128, [[E, 128], [128 * E, KC], [1, 128]]))

            u_dma(0)
            u_dma(1)
            for et in range(ET):
                sb = wu_strip[et % 2]
                key = "pv%d" % (et % 2)
                pm = p_v[et % 2][:, 0:RPC]
                group_begin(key)
                P.wait("tensor", "LU%d" % (et % 2))
                if et == 0:
                    P.wait("tensor", "dv", dv_xnn)
                for kc in range(KC):
                    pv = P.emit("tensor", lambda e, p=pm, s=sb, k=kc:
                                e.matmul(p, s[:, k * 128:(k + 1) * 128],
                                         xnnT[:, k * RPC:(k + 1) * RPC],
                                         start=(k == 0), stop=(k == KC - 1)),
                                inc="pe")
                if et == 0:
                    u_pe = {}
                u_pe[et] = pv
                u_dma(et + 2)
                P.wait("scalar", "pe", pv)
                av = P.emit("scalar", lambda e, p=pm, t_=et, u_=uT: e.activation(
                    out=u_[:, t_ * RPC:(t_ + 1) * RPC], in_=p, func=AF.Silu,
                    bias=smc("uvb_u", t_)), inc="ac")
                psum_owner[key] = ("ac", av)
            produce_pe_end[0] = P.cnt["pe"]

        def consume_qk(b):
            qT_sb = qT_bufs[b % 2]
            agp = ag_out_pair[b // 2]
            P.wait("gpsimd", "cc", b // 2 + 1)
            P.wait("sync", "cc", b // 2 + 1)
            if b > 0:
                P.wait("gpsimd", "pe", qk_pe_end[b - 1])
            for r in range(NCORES):
                base_row = r * (2 * RPCB) + (b % 2) * RPCB + RPC
                blk = agp.ap()[base_row: base_row + 16, :]
                dma("gpsimd", "LK", kT_sb[:, r * RPC:(r + 1) * RPC],
                    blk.rearrange("a (c i) -> (a c) i", i=RPC))
            for jt in range(ET):
                key = "m%d" % (jt % 3)
                pm = p_m[jt % 3]
                group_begin(key)
                if jt == 0:
                    P.wait("tensor", "LK")
                    P.wait("tensor", "dv",
                           max(b_qv[b], biasT_dv_h[0] if b == 0 else 0))
                P.emit("tensor", lambda e, p=pm, j=jt: e.matmul(
                    p[:, :], ident_h[:, :], biasT_sb[:, j * RPC:(j + 1) * RPC],
                    start=True, stop=False))
                pv = P.emit("tensor", lambda e, p=pm, j=jt: e.matmul(
                    p[:, :], kT_sb[:, j * 128:(j + 1) * 128], qT_sb,
                    start=False, stop=True), inc="pe")
                P.wait("scalar", "pe", pv)
                if jt > 0:
                    P.wait("scalar", "dv", sq_dv)
                av = P.emit("scalar", lambda e, p=pm: e.activation(
                    out=relu_t, in_=p[:, :], func=AF.Relu), inc="ac")
                psum_owner[key] = ("ac", av)
                P.wait("vector", "ac", av)
                sq_dv = P.emit("vector", lambda e, j=jt: e.tensor_mul(
                    kernelT[:, j * RPC:(j + 1) * RPC], relu_t, relu_t), inc="dv")
            qk_pe_end[b] = P.cnt["pe"]
            b_dv_kernel[b] = P.cnt["dv"]

        def consume(b):
            uT = uT_bufs[b % 2]
            dv_kernel = b_dv_kernel[b]
            agp = ag_out_pair[b // 2]
            for et in range(ET):
                vs = vstrip[et % 2]
                if et >= 2:
                    P.wait("gpsimd", "pe", kv_pe[et - 2])
                    P.wait("sync", "pe", kv_pe[et - 2])

                for q in range(2):
                    dma(("gpsimd", "sync")[q], "LW%d_%d" % (et % 2, q),
                        mkap(vs, vs.offset + q * 128,
                             [[8192, 128], [256, NCORES], [1, 128]]),
                        mkap(agp.ap(),
                             ((b % 2) * RPCB + q * 128) * E + et * 128,
                             [[E, 128], [2 * RPCB * E, NCORES], [1, 128]]))
                key = "m%d" % (et % 3)
                pm = p_m[et % 3]
                group_begin(key)
                P.wait("tensor", "LW%d_0" % (et % 2))
                P.wait("tensor", "LW%d_1" % (et % 2))
                if et == 0:
                    P.wait("tensor", "dv", dv_kernel)
                for jt in range(ET):
                    pv = P.emit("tensor", lambda e, p=pm, v_=vs, j=jt: e.matmul(
                        p[:, :], v_[:, j * 128:(j + 1) * 128],
                        kernelT[:, j * RPC:(j + 1) * RPC],
                        start=(j == 0), stop=(j == ET - 1)),
                        inc="pe" if jt == ET - 1 else None)
                if et == 0:
                    kv_pe = {}
                kv_pe[et] = pv
                P.wait("vector", "pe", pv)
                dvv = P.emit("vector", lambda e, p=pm, t_=et: e.tensor_mul(
                    eT[:, t_ * RPC:(t_ + 1) * RPC], p[:, :],
                    uT[:, t_ * RPC:(t_ + 1) * RPC]), inc="dv")
                psum_owner[key] = ("dv", dvv)
            dv_eT = P.cnt["dv"]
            eT_dv_end[b] = dv_eT
            for ft in range(KC):
                osb = ow_strip[ft % 2]
                if ft >= 2:
                    P.wait("scalar", "pe", o_pe[ft - 2])
                dma("scalar", "LO%d" % (ft % 2),
                    osb[:, :2048].rearrange("p (ec c) -> p ec c", ec=ET),
                    mkap(ow_h.ap(), ft * 128, [[H, 128], [128 * H, ET], [1, 128]]))
                group_begin("h")
                if b == 0 and ft == 0:
                    P.wait("tensor", "dv", max(uoc_dv_h[0], woT_dv_h[0]))
                P.emit("tensor", lambda e, f=ft: e.matmul(
                    p_h[:, :], ident_h[:, :], uocT[:, f * RPC:(f + 1) * RPC],
                    start=True, stop=False))
                for kc in range(KC):
                    P.emit("tensor", lambda e, f=ft, k=kc, b_=b: e.matmul(
                        p_h[:, :], woT_sb[:, k * H + f * 128: k * H + f * 128 + 128],
                        xnT_sl(k, b_), start=False, stop=(k == KC - 1)),
                        inc="pe" if kc == KC - 1 else None)
                pv_h = P.cnt["pe"]
                P.wait("scalar", "pe", pv_h)
                if ft > 0:
                    P.wait("scalar", "dv", comb_dv)
                av = P.emit("scalar", lambda e, f=ft: e.activation(
                    out=hT_b[:, :], in_=p_h[:, :], func=AF.Tanh, bias=smc("bo", f)),
                    inc="ac")
                psum_owner["h"] = ("ac", av)
                key = "m%d" % (ft % 3)
                pm = p_m[ft % 3]
                group_begin(key)
                P.emit("tensor", lambda e, p=pm, f=ft, b_=b: e.matmul(
                    p[:, :], ident_h[:, :], xnT_sl(f, b_), start=True, stop=False))
                P.wait("tensor", "LO%d" % (ft % 2))
                if ft == 0:
                    P.wait("tensor", "dv", dv_eT)
                for ec in range(ET):
                    P.emit("tensor", lambda e, p=pm, s=osb, c_=ec: e.matmul(
                        p[:, :], s[:, c_ * 128:(c_ + 1) * 128],
                        eT[:, c_ * RPC:(c_ + 1) * RPC],
                        start=False, stop=(c_ == ET - 1)),
                        inc="pe" if ec == ET - 1 else None)
                pv_o = P.cnt["pe"]
                if ft == 0:
                    o_pe = {}
                o_pe[ft] = pv_o
                xr = xres[ft % 2]
                if b == 0 and ft < 2:
                    P.wait("sync", "dv", phaseA_dv[0])
                dma("sync", "LX%d" % (ft % 2), xr[:, :],
                    x_own[b, ft * 128:(ft + 1) * 128, :])
                ob = outs[ft % 2]
                P.wait("vector", "pe", pv_o)
                P.wait("vector", "ac", av)
                if out_guard[ft % 2]:
                    P.wait("vector", "SO%d" % (ft % 2))
                P.emit("vector", lambda e, p=pm, f=ft, o=ob: e.scalar_tensor_tensor(
                    out=o[:, :], in0=p[:, :], scalar=smc("o_b", f), in1=hT_b[:, :],
                    op0=ALU.add, op1=ALU.add), inc="dv")
                psum_owner[key] = ("dv", P.cnt["dv"])
                P.wait("vector", "LX%d" % (ft % 2))
                comb_dv = P.emit("vector", lambda e, o=ob, x=xr: e.tensor_add(
                    o[:, :], o[:, :], x[:, :]), inc="dv")
                P.wait("gpsimd", "dv", comb_dv)
                dma("gpsimd", "SO%d" % (ft % 2), y_own[b, ft * 128:(ft + 1) * 128, :],
                    ob[:, :])
                out_guard[ft % 2] = 1
                P.wait("sync", "dv", comb_dv)

        for mt in range(KC):
            P.emit("vector", lambda e, m=mt: e.tensor_scalar(
                out=c_f32[:, m * RPC:(m + 1) * RPC], in0=muB[:, :],
                scalar1=0.0, scalar2=smc("in_c", m), op0=ALU.mult, op1=ALU.add))
        for mt in range(KC):
            P.emit("vector", lambda e, m=mt: e.tensor_copy(
                c_bfA[:, m * RPC:(m + 1) * RPC], c_f32[:, m * RPC:(m + 1) * RPC]),
                inc="dv")

        gatespec = [("wf", "uf", "bf", AF.Sigmoid, 0),
                    ("wi", "ui", "bi", AF.Sigmoid, 1),
                    ("wc", "uc", "bc", AF.Tanh, 2)]
        xg_pe = [{} for _ in range(3)]

        def xg_dma(gi, gidx):
            t_, mt_ = divmod(gidx, KC)
            if t_ >= B:
                return
            par = gidx % 4
            if gidx >= 4:
                P.wait("sync", "pe", xg_pe[gi][gidx - 4])
            dma("sync", "LG%d_%d" % (gi, par), xg_buf[gi][par][:, :],
                xgpre_h[gi].ap()[mt_ * 128:(mt_ + 1) * 128,
                                 t_ * RPC:(t_ + 1) * RPC])

        dv_after_update = P.cnt["dv"]
        for gi in range(3):
            for pf in range(4):
                xg_dma(gi, pf)
        for t in range(B):
            produce_vk_pre(t)
            cur = c_bfA if t % 2 == 0 else c_bfB
            nxt = c_bfB if t % 2 == 0 else c_bfA
            first_group = True
            for mt in range(KC):
                gidx = t * KC + mt
                for (wg, ug, bg, func, gi) in gatespec:
                    key = "m%d" % ((mt * 3 + gi) % 3)
                    pm = p_m[(mt * 3 + gi) % 3]
                    sbx = xg_buf[gi][gidx % 4]
                    group_begin(key)
                    P.wait("tensor", "LG%d_%d" % (gi, gidx % 4))
                    P.emit("tensor", lambda e, p=pm, w=sbx: e.matmul(
                        p[:, :], ident_h[:, :], w[:, :], start=True, stop=False))
                    if first_group:
                        P.wait("tensor", "dv", dv_after_update)
                        first_group = False
                    for kc in range(KC):
                        P.emit("tensor", lambda e, p=pm, u=WG[ug], m=mt, k=kc, c=cur:
                               e.matmul(p[:, :],
                                        u[:, k * H + m * 128: k * H + m * 128 + 128],
                                        c[:, k * RPC:(k + 1) * RPC],
                                        start=False, stop=(k == KC - 1)),
                               inc="pe" if kc == KC - 1 else None)
                    pv = P.cnt["pe"]
                    xg_pe[gi][gidx] = pv
                    xg_dma(gi, gidx + 4)
                    P.wait("scalar", "pe", pv)
                    P.wait("scalar", "dv", dv_after_update if mt == 0 else dv_mt)
                    av = P.emit("scalar", lambda e, p=pm, f=func, b=bg, m=mt, g=gi:
                                e.activation(out=gates[g][:, :], in_=p[:, :], func=f,
                                             bias=smc(b, m)), inc="ac")
                    psum_owner[key] = ("ac", av)
                P.wait("vector", "ac")
                P.emit("vector", lambda e, m=mt: e.tensor_mul(
                    upd[:, :], gates[0][:, :], c_f32[:, m * RPC:(m + 1) * RPC]))
                P.emit("vector", lambda e: e.tensor_mul(
                    gates[2][:, :], gates[1][:, :], gates[2][:, :]))
                P.emit("vector", lambda e, m=mt: e.tensor_add(
                    c_f32[:, m * RPC:(m + 1) * RPC], upd[:, :], gates[2][:, :]))
                dv_mt = P.emit("vector", lambda e, m=mt, n=nxt: e.tensor_copy(
                    n[:, m * RPC:(m + 1) * RPC], c_f32[:, m * RPC:(m + 1) * RPC]),
                    inc="dv")
            dv_after_update = P.cnt["dv"]
            produce_vk(t)
            if t % 2 == 1:
                pr = t // 2
                P.wait("gpsimd", "SV0")
                P.wait("gpsimd", "SV1")
                P.wait("gpsimd", "SK")
                P.emit("gpsimd", lambda e, p_=pr: e.collective_compute(
                    "AllGather", ALU.bypass, replica_groups=[list(range(NCORES))],
                    ins=[ag_in_big.ap()[(2 * p_) * RPCB:(2 * p_ + 2) * RPCB, :].opt()],
                    outs=[ag_out_pair[p_].ap().opt()]), inc="cc")

        c_fin = c_bfA if B % 2 == 0 else c_bfB

        rec_pe = P.cnt["pe"]
        phaseA_dv[0] = P.cnt["dv"]

        P.wait("tensor", "dv", phaseA_dv[0])
        uoc_pe = {}
        for ot in range(KC):
            ustr = uo_strip[ot % 2]
            if ot >= 2:
                P.wait("sync", "pe", uoc_pe[ot - 2])
            dma("sync", "LR%d" % (ot % 2),
                ustr[:, :1024].rearrange("p (mc c) -> p mc c", mc=KC),
                mkap(uo_h.ap(), ot * 128, [[H, 128], [128 * H, KC], [1, 128]]))
            key = "base"
            pm = p_base
            group_begin(key)
            P.wait("tensor", "LR%d" % (ot % 2))
            for mc in range(KC):
                P.emit("tensor", lambda e, p=pm, s=ustr, m=mc:
                       e.matmul(p[:, :], s[:, m * 128:(m + 1) * 128],
                                c_fin[:, m * RPC:(m + 1) * RPC],
                                start=(m == 0), stop=(m == KC - 1)),
                       inc="pe" if mc == KC - 1 else None)
            pv = P.cnt["pe"]
            uoc_pe[ot] = pv
            P.wait("vector", "pe", pv)
            dvv = P.emit("vector", lambda e, o=ot, p=pm: e.tensor_copy(
                uocT[:, o * RPC:(o + 1) * RPC], p[:, :]), inc="dv")
            psum_owner[key] = ("dv", dvv)
        uoc_dv_h[0] = P.cnt["dv"]

        for jt in range(ET):
            par = jt % 2
            buf = (scr, scr2)[par]
            if scr_guard[par]:
                P.wait("sync", "dv", scr_guard[par])
            dma("sync", "L%d" % par, buf[:, :RPC],
                biasT_own[jt * 128:(jt + 1) * 128, :])
            P.wait("vector", "L%d" % par)
            dvv = P.emit("vector", lambda e, b=buf[:, :RPC], j=jt:
                         e.tensor_copy(biasT_sb[:, j * RPC:(j + 1) * RPC], b), inc="dv")
            scr_guard[par] = dvv
        biasT_dv_h[0] = P.cnt["dv"]

        for q_ in ("sync", "gpsimd", "scalar"):
            P.wait(q_, "pe", rec_pe)

        produce_pre(0)
        produce(0)

        P.wait("vector", "pe", rec_pe)
        for kc in range(KC):
            for hh in range(2):
                par = (kc * 2 + hh) % 2
                buf = (scr, scr2)[par]
                if scr_guard[par]:
                    P.wait("sync", "dv", scr_guard[par])
                dma("sync", "L%d" % par, buf[:, :],
                    woT[kc * 128:(kc + 1) * 128, hh * 512:(hh + 1) * 512])
                P.wait("vector", "L%d" % par)
                dvv = P.emit("vector", lambda e, b=buf[:, :], k=kc, h_=hh:
                             e.tensor_copy(woT_sb[:, k * H + h_ * 512:
                                                  k * H + h_ * 512 + 512], b),
                             inc="dv")
                scr_guard[par] = dvv
        woT_dv_h[0] = P.cnt["dv"]

        for b in range(B):
            if b + 1 < B:
                produce_pre(b + 1)
            consume_qk(b)
            if b + 1 < B:
                produce(b + 1)
            consume(b)

        P.wait("gpsimd", "SO0")
        P.wait("gpsimd", "SO1")

        block = e(nc.Block())

        @block.sync
        def _(sync):
            for fn in P.ops["sync"]:
                fn(sync)

        @block.gpsimd
        def _(gpsimd):
            for fn in P.ops["gpsimd"]:
                fn(gpsimd)

        @block.tensor
        def _(tensor):
            for fn in P.ops["tensor"]:
                fn(tensor)

        @block.vector
        def _(vector):
            for fn in P.ops["vector"]:
                fn(vector)

        @block.scalar
        def _(scalar):
            for fn in P.ops["scalar"]:
                fn(scalar)

    return nc


_NC_CACHE = None


def kernel(**inputs):
    global _NC_CACHE
    x = np.asarray(inputs["L0_attn_x"], dtype=np.float32)
    sin_t, cos_t = _rope_tables()
    biasT = _bias_matrix(np.asarray(inputs["L0_attn_w"], np.float32))
    uv_w = np.asarray(inputs["L0_attn_uv_w"], np.float32)
    uv_b = np.asarray(inputs["L0_attn_uv_b"], np.float32)

    common = dict(
        ident_in=np.eye(128, dtype=np.float32),
        wuT=np.ascontiguousarray(uv_w[:E].T),
        wvT=np.ascontiguousarray(uv_w[E:2 * E].T),
        wbT=np.ascontiguousarray(uv_w[2 * E:].T),
        wbTs=np.ascontiguousarray(uv_w[2 * E:].T[:, _PERM]),
        owT=np.ascontiguousarray(np.asarray(inputs["L0_attn_o_w"], np.float32).T),
        woT=np.ascontiguousarray(np.asarray(inputs["L0_ffn_wo"], np.float32).T),
        wfT=np.ascontiguousarray(np.asarray(inputs["L0_ffn_wf"], np.float32).T),
        ufT=np.ascontiguousarray(np.asarray(inputs["L0_ffn_uf"], np.float32).T),
        wiT=np.ascontiguousarray(np.asarray(inputs["L0_ffn_wi"], np.float32).T),
        uiT=np.ascontiguousarray(np.asarray(inputs["L0_ffn_ui"], np.float32).T),
        wcT=np.ascontiguousarray(np.asarray(inputs["L0_ffn_wc"], np.float32).T),
        ucT=np.ascontiguousarray(np.asarray(inputs["L0_ffn_uc"], np.float32).T),
        uoT=np.ascontiguousarray(np.asarray(inputs["L0_ffn_uo"], np.float32).T),
        ln_w=np.asarray(inputs["L0_attn_ln_w"], np.float32),
        ln_b=np.asarray(inputs["L0_attn_ln_b"], np.float32),
        o_b=np.asarray(inputs["L0_attn_o_b"], np.float32),
        bf=np.asarray(inputs["L0_ffn_bf"], np.float32),
        bi=np.asarray(inputs["L0_ffn_bi"], np.float32),
        bc=np.asarray(inputs["L0_ffn_bc"], np.float32),
        bo=np.asarray(inputs["L0_ffn_bo"], np.float32),
        in_c=np.asarray(inputs["L0_ffn_in_c"], np.float32),
        uvb_u=np.ascontiguousarray(uv_b[:E]),
        uvb_v=np.ascontiguousarray(uv_b[E:2 * E]),
        uvb_b=np.ascontiguousarray(uv_b[2 * E:]),
        uvb_bs=np.ascontiguousarray(uv_b[2 * E:][_PERM]),
        gamma=np.asarray(inputs["L0_attn_gamma"], np.float32),
        beta=np.asarray(inputs["L0_attn_beta"], np.float32),
        gammas=np.ascontiguousarray(np.asarray(inputs["L0_attn_gamma"], np.float32)[:, _PERM]),
        betas=np.ascontiguousarray(np.asarray(inputs["L0_attn_beta"], np.float32)[:, _PERM]),
    )

    xT = np.ascontiguousarray(x.transpose(0, 2, 1))
    in_maps = []
    for c in range(NCORES):
        r0, r1 = c * RPC, (c + 1) * RPC
        m = dict(common)
        m["x_own"] = np.ascontiguousarray(xT[:, :, r0:r1])
        m["biasT_own"] = np.ascontiguousarray(biasT[:, r0:r1])
        st = sin_t[r0:r1].T
        ct = cos_t[r0:r1].T
        m["sinT"] = np.ascontiguousarray(np.concatenate([-st, st], axis=0))
        m["cosT"] = np.ascontiguousarray(np.concatenate([ct, ct], axis=0))
        in_maps.append(m)

    if _NC_CACHE is None:
        _NC_CACHE = build_program()
    res = run_bass_kernel_spmd(_NC_CACHE, in_maps, core_ids=list(range(NCORES)))
    shards = [res.results[c]["y_own"] for c in range(NCORES)]
    outT = np.concatenate(shards, axis=2)
    return np.ascontiguousarray(outT.transpose(0, 2, 1)).astype(np.float32)
